# version 01
# baseline (speedup 1.0000x reference)
import numpy as np
import ml_dtypes
from contextlib import ExitStack

import concourse.bass as bass
import concourse.mybir as mybir
from concourse.bass_utils import run_bass_kernel_spmd

F32 = mybir.dt.float32
BF16 = mybir.dt.bfloat16
AF = mybir.ActivationFunctionType
ALU = mybir.AluOpType
AX = mybir.AxisListType.X

MAGIC = 12582912.0
D = 1024
NT = 4096
OWN = 2048
W = 2050
HE = 64
NE = 16
LNEPS = 1e-5
WSL = [(0, 512), (512, 512), (1024, 512), (1536, 512), (2048, 2)]
NSL = [(i * 512, 512) for i in range(8)]


class Sem:
    def __init__(self, nc, st, name):
        self.h = st.enter_context(nc.semaphore(name))
        self.v = 0

    def inc(self, instr, k=1):
        instr.then_inc(self.h, k)
        self.v += k
        return self.v

    def wait(self, eng, v=None):
        eng.wait_ge(self.h, self.v if v is None else v)


def _s(lo, n):
    return slice(lo, lo + n)


class Prog:

    def __init__(self, upto=99, debug=0, p1_pairs=8, p1_proj=True, p1_sect=9, p3_sect=9):
        self.upto = upto
        self.debug = debug
        self.p1_pairs = p1_pairs
        self.p1_proj = p1_proj
        self.p1_sect = p1_sect
        self.p1_norm = 3
        self.p3_sect = p3_sect
        nc = self.nc = bass.Bass("TRN2", target_bir_lowering=False)
        self.te, self.ve, self.sc, self.gp, self.sy = (
            nc.tensor, nc.vector, nc.scalar, nc.gpsimd, nc.sync)
        self.st = ExitStack()
        self.din = {}
        self._declare_dram()
        self._alloc_common()

    def _declare_dram(self):
        nc = self.nc

        def DI(name, shape):
            self.din[name] = nc.dram_tensor(name, list(shape), F32, kind="ExternalInput").ap()

        DI("xTf", (D, NT)); DI("xTw", (D, W)); DI("masks", (128, 2))
        DI("wqT", (D, D)); DI("wkT", (D, D)); DI("wvT", (D, D)); DI("wpT", (D, D)); DI("bp_r", (128, 8))
        DI("w1T", (D, 4096)); DI("cw", (128, 96)); DI("cb", (128, 32)); DI("w2T", (4096, D))
        DI("qw1T", (D, D)); DI("qb1_r", (128, 8)); DI("kw1T", (D, D)); DI("kb1_r", (128, 8))
        DI("qw2Ts", (D, 512)); DI("qb2_row", (1, 512)); DI("kw2Ts", (D, 512)); DI("kb2_row", (1, 512))
        DI("vwTs", (D, 512)); DI("vb_s", (128, 4)); DI("gwTs", (D, 512)); DI("gb_s", (128, 4))
        DI("pwT", (D, D)); DI("pb_r", (128, 8))
        DI("gwT", (D, 16)); DI("gb_row", (1, 16))
        DI("We1", (NE, D, 512)); DI("be1_c", (128, 64)); DI("We2", (NE, 512, D)); DI("be2", (NE, D))
        DI("We3", (NE, D, 512)); DI("be3_c", (128, 64)); DI("E16", (16, 2048))
        DI("ws1", (D, D)); DI("bs1_r", (128, 8)); DI("ws2", (D, D)); DI("bs2_r", (128, 8))
        DI("ws3", (D, D)); DI("bs3_r", (128, 8))
        DI("ident", (128, 128)); DI("ones_row", (1, 128)); DI("e2a", (128, 66)); DI("e2b", (128, 66))
        DI("swap64", (128, 128))

        self.y_out = nc.dram_tensor("y", [OWN, D], F32, kind="ExternalOutput").ap()
        self.dbg = {}
        if self.debug:
            self.dbg["d_r1"] = nc.dram_tensor("d_r1", [D, W], F32, kind="ExternalOutput").ap()
            self.dbg["d_att"] = nc.dram_tensor("d_att", [D, W], F32, kind="ExternalOutput").ap()
            self.dbg["d_r2"] = nc.dram_tensor("d_r2", [D, W], F32, kind="ExternalOutput").ap()
            self.dbg["d_r3"] = nc.dram_tensor("d_r3", [D, OWN], F32, kind="ExternalOutput").ap()
            self.dbg["d_cmb"] = nc.dram_tensor("d_cmb", [16, OWN], F32, kind="ExternalOutput").ap()
            self.dbg["d_z"] = nc.dram_tensor("d_z", [D, OWN], F32, kind="ExternalOutput").ap()

        self.w1q_d = nc.dram_tensor("w1q_d", [32, 8, 128, 128], BF16).ap()
        self.w2q_d = nc.dram_tensor("w2q_d", [8, 32, 128, 128], BF16).ap()
        self.h2_d = nc.dram_tensor("h2_d", [32, 128, OWN], BF16).ap()
        self.x2own_d = nc.dram_tensor("x2own_d", [8, 128, OWN], F32).ap()
        self.cc_in = nc.dram_tensor("cc_in", [D, OWN], BF16).ap()
        self.cc_out = nc.dram_tensor("cc_out", [2 * D, OWN], BF16).ap()

    def _alloc_common(self):
        nc, st = self.nc, self.st
        E = st.enter_context

        def SB(name, p, f, dt, side="left"):
            return E(nc.sbuf_tensor(name, [p, f], dt, side=side))
        self.SB = SB

        self.ident_f = SB("ident_f", 128, 128, F32)
        self.ident_b = SB("ident_b", 128, 128, BF16)
        self.ones_row_f = SB("ones_row_f", 1, 128, F32)
        self.ones_row_b = SB("ones_row_b", 1, 128, BF16)
        self.ones_col_b = SB("ones_col_b", 128, 1, BF16)
        self.ones_col_f = SB("ones_col_f", 128, 1, F32)
        self.e2a = SB("e2a_sb", 128, 66, BF16)
        self.e2b = SB("e2b_sb", 128, 66, BF16)
        self.okk = SB("okk", 66, 128, BF16)
        self.swap64 = SB("swap64_sb", 128, 128, BF16)
        self.masks_sb = SB("masks_sb", 128, 2, F32)
        self.bp_sb = SB("bp_sb", 128, 8, F32)
        self.cw_sb = SB("cw_sb", 128, 96, F32)
        self.cb_sb = SB("cb_sb", 128, 32, F32)
        self.sc0 = SB("sc0", 1, 16, F32)
        self.pr = SB("pr", 1, 6528, F32, side="right")
        self.PRA, self.PRB, self.PRC = 0, 2176, 4352

        self.sW = Sem(nc, st, "sW")
        self.sX = Sem(nc, st, "sX")
        self.sT = Sem(nc, st, "sT")
        self.sV = Sem(nc, st, "sV")
        self.sS = Sem(nc, st, "sS")
        self.sG = Sem(nc, st, "sG")
        self.sO = Sem(nc, st, "sO")

        sy, gp = self.sy, self.gp
        d = self.din
        for src, dst in (("ident", self.ident_f), ("ones_row", self.ones_row_f),
                         ("masks", self.masks_sb), ("bp_r", self.bp_sb),
                         ("cw", self.cw_sb), ("cb", self.cb_sb)):
            self.sX.inc(sy.dma_start(out=dst[:], in_=d[src]), 16)
        for src, dst in (("ident", self.ident_b), ("ones_row", self.ones_row_b),
                         ("e2a", self.e2a), ("e2b", self.e2b), ("swap64", self.swap64)):
            self.sW.inc(gp.dma_start(out=dst[:], in_=d[src]), 16)
        gp.wait_ge(self.sW.h, self.sW.v)
        gp.memset(self.ones_col_b[:], 1.0)
        gp.memset(self.ones_col_f[:], 1.0)
        gp.memset(self.okk[0:1, :], 1.0)
        gp.memset(self.okk[64:65, :], 1.0)
        self._consts = {}
        for v in (LNEPS, MAGIC, -MAGIC, 1.0, -1.0, -1.5):
            t = SB(f"c_{len(self._consts)}", 128, 1, F32)
            gp.memset(t[:], float(v))
            nc.const_aps.register(F32, float(v), t[:])
            self._consts[float(v)] = t
        self.sG.inc(self.gp.drain())

    def build(self):
        lp = self.nc.allow_low_precision("bf16 matmul operands are intentional")
        lp.__enter__()
        if self.upto >= 1:
            self.phase1()
        if self.upto >= 2:
            self.phase2()
        if self.upto >= 3:
            self.phase3()
        if self.upto >= 4:
            self.phase4()
            self.phase5()
        if self.upto < 4:
            self.debug_tail()
        self.st.close()
        return self.nc

    def qbarrier(self, *engs):
        sems = [self.sT, self.sV, self.sS, self.sG, self.sW, self.sX, self.sO]
        sems += getattr(self, "extra_sems", [])
        for e in engs:
            for s in sems:
                e.wait_ge(s.h, s.v)

    def emit_ln(self, x_tiles, off, width, ps_big, scr_b):
        te, ve, sc = self.te, self.ve, self.sc
        sT, sV, sS = self.sT, self.sV, self.sS
        pr, PRA, PRB, PRC = self.pr, self.PRA, self.PRB, self.PRC
        sls = [(o, n) for (o, n) in
               [(i * 512, min(512, width - i * 512)) for i in range((width + 511) // 512)]]
        for which, fn, seg in ((0, AF.Identity, PRA), (1, AF.Square, PRB)):
            for dc in range(8):
                sc.wait_ge(sT.h, sT.v)
                sc.activation(scr_b[:, 0:width], x_tiles[dc][:, _s(off, width)], fn)
                self.sS.inc(sc.drain())
                te.wait_ge(sS.h, sS.v)
                if dc == 0:
                    te.wait_ge(sV.h, sV.v)
                for (o, n) in sls:
                    ins = te.matmul(ps_big[0:1, _s(o, n)], self.ones_col_b[:],
                                    scr_b[:, _s(o, n)], start=(dc == 0), stop=(dc == 7))
                sT.inc(ins)
            ve.wait_ge(sT.h, sT.v)
            ins = ve.tensor_copy(pr[0:1, _s(seg, width)], ps_big[0:1, 0:width])
            sV.inc(ve.drain())
        ve.tensor_scalar_mul(pr[0:1, _s(PRA, width)], pr[0:1, _s(PRA, width)], 1.0 / 1024.0)
        ve.tensor_scalar_mul(pr[0:1, _s(PRB, width)], pr[0:1, _s(PRB, width)], 1.0 / 1024.0)
        sV.inc(ve.drain())
        sc.wait_ge(sV.h, sV.v)
        sc.activation(pr[0:1, _s(PRC, width)], pr[0:1, _s(PRA, width)], AF.Square)
        self.sS.inc(sc.drain())
        ve.wait_ge(sS.h, sS.v)
        ve.tensor_sub(pr[0:1, _s(PRB, width)], pr[0:1, _s(PRB, width)], pr[0:1, _s(PRC, width)])
        sV.inc(ve.drain())
        sc.wait_ge(sV.h, sV.v)
        sc.activation(pr[0:1, _s(PRC, width)], pr[0:1, _s(PRB, width)], AF.Sqrt, bias=LNEPS)
        self.sS.inc(sc.drain())
        ve.wait_ge(sS.h, sS.v)
        ve.reciprocal(pr[0:1, _s(PRB, width)], pr[0:1, _s(PRC, width)])
        sV.inc(ve.drain())
        for seg, op in ((PRA, ALU.subtract), (PRB, ALU.mult)):
            te.wait_ge(sV.h, sV.v)
            for (o, n) in sls:
                ins = te.matmul(ps_big[0:128, _s(o, n)], self.ones_row_f[:],
                                pr[0:1, _s(seg + o, n)], start=True, stop=True)
            sT.inc(ins)
            ve.wait_ge(sT.h, sT.v)
            for dc in range(8):
                ins = ve.tensor_tensor(x_tiles[dc][:, _s(off, width)],
                                       x_tiles[dc][:, _s(off, width)],
                                       ps_big[0:128, 0:width], op)
            sV.inc(ve.drain())
        te.wait_ge(sV.h, sV.v)
        self.sT.inc(te.nop())

    def phase1(self):
        nc, te, ve, sc, gp, sy = self.nc, self.te, self.ve, self.sc, self.gp, self.sy
        d, SB = self.din, self.SB
        sW, sX, sT, sV, sS, sG = self.sW, self.sX, self.sT, self.sV, self.sS, self.sG

        p1st = ExitStack()
        att = [p1st.enter_context(nc.sbuf_tensor(f"att{i}", [128, W], BF16))
               for i in range(8)]

        with ExitStack() as ph:
            EE = ph.enter_context

            def S(name, p, f, dt):
                return EE(nc.sbuf_tensor(name, [p, f], dt))

            xTf_b = [S(f"xTf{i}", 128, NT, BF16) for i in range(8)]
            xTw_b = [S(f"xTw{i}", 128, W, BF16) for i in range(8)]
            kp = S("kp", 128, NT, BF16)
            qp = S("qp", 128, W, BF16)
            vg = S("vg", 128, 32 * 130, BF16)
            pt = [S("pt0", 128, 1024, BF16), S("pt1", 128, 1024, BF16)]
            wq_p = S("wq_p", 128, 1024, BF16)
            wk_p = S("wk_p", 128, 1024, BF16)
            wv_p = S("wv_p", 128, 1024, BF16)
            m0 = S("m0", 1, 2112, BF16)
            sq_b = S("sq_b", 128, 512, BF16)
            n64 = S("n64", 66, 2112, BF16)
            n64b = S("n64b", 66, 2112, BF16)
            t64 = S("t64", 66, 16, F32)
            rb = S("rb", 64, 1024, BF16)
            osc = S("osc", 64, 2112, BF16)

            ps_S = [EE(nc.psum_tensor("ps_S0", [128, 1024], F32)),
                    EE(nc.psum_tensor("ps_S1", [128, 1024], F32))]
            ps_av = EE(nc.psum_tensor("ps_av", [66, 1024], F32))
            ps_x = [EE(nc.psum_tensor("ps_x0", [128, 512], F32)),
                    EE(nc.psum_tensor("ps_x1", [128, 512], F32))]

            for dc in range(8):
                sW.inc(gp.dma_start(out=xTf_b[dc][:], in_=d["xTf"][_s(dc * 128, 128), :]), 16)
                sW.inc(gp.dma_start(out=xTw_b[dc][:], in_=d["xTw"][_s(dc * 128, 128), :]), 16)
            ld_x_done = sW.v

            def load_pair_w(p):
                for nm, dst in (("wqT", wq_p), ("wkT", wk_p)):
                    src = d[nm][:, _s(p * 128, 128)].rearrange("(dc r) c -> r dc c", r=128)
                    dstv = dst[:].rearrange("r (dc c) -> r dc c", c=128)
                    sW.inc(gp.dma_start(out=dstv, in_=src), 16)
                return sW.v

            def load_pair_wv(p):
                src = d["wvT"][:, _s(p * 128, 128)].rearrange("(dc r) c -> r dc c", r=128)
                dstv = wv_p[:].rearrange("r (dc c) -> r dc c", c=128)
                sW.inc(gp.dma_start(out=dstv, in_=src), 16)
                return sW.v

            vg_ev = [0, 0]
            kp_ev = [0, 0]
            kp_sq = [0, 0]
            dma_osc = 0

            for p in range(self.p1_pairs):
                wv_done = load_pair_wv(p)
                w_done = load_pair_w(p)

                gp.wait_ge(sW.h, sW.v)
                gp.wait_ge(sT.h, sT.v)
                gp.memset(vg[:, :], 1.0)
                self.sG.inc(gp.drain())
                te.wait_ge(sG.h, sG.v)
                te.wait_ge(sV.h, sV.v)
                for kt in range(32):
                    par = kt % 2
                    if kt >= 2:
                        te.wait_ge(sV.h, vg_ev[par])
                    for dc in range(8):
                        ins = te.matmul(ps_x[par][:, 0:128],
                                        xTf_b[dc][:, _s(kt * 128, 128)],
                                        wv_p[:, _s(dc * 128, 128)],
                                        start=(dc == 0), stop=(dc == 7))
                    tv = sT.inc(ins)
                    ve.wait_ge(sT.h, tv)
                    dst = vg[:, _s(kt * 130, 130)].rearrange("r (h c) -> r h c", c=65)[:, :, 0:64]
                    src = ps_x[par][:, 0:128].rearrange("r (h c) -> r h c", c=64)
                    ins = ve.tensor_copy(dst, src)
                    vg_ev[par] = sV.inc(ve.drain())

                if self.p1_sect < 2:
                    continue
                gp.wait_ge(sW.h, sW.v)
                self.sG.inc(gp.drain())
                ve.wait_ge(sG.h, sG.v)
                ve.memset(t64[64:65, 0:2], 0.0)
                sV.inc(ve.drain())
                te.wait_ge(sG.h, sG.v)
                te.wait_ge(sV.h, sV.v)
                for si, (o, n) in enumerate(NSL):
                    par = si % 2
                    if si >= 2:
                        te.wait_ge(sS.h, kp_sq[par])
                        te.wait_ge(sV.h, kp_ev[par])
                    for dc in range(8):
                        ins = te.matmul(ps_x[par][:, 0:n],
                                        wk_p[:, _s(dc * 128, 128)],
                                        xTf_b[dc][:, _s(o, n)],
                                        start=(dc == 0), stop=(dc == 7))
                    tv = sT.inc(ins)
                    ve.wait_ge(sT.h, tv)
                    ins = ve.tensor_copy(kp[:, _s(o, n)], ps_x[par][:, 0:n])
                    kp_ev[par] = sV.inc(ve.drain())
                    if self.p1_norm >= 1:
                        sc.wait_ge(sT.h, tv)
                        sc.wait_ge(sV.h, kp_ev[par])
                        sc.activation(sq_b[:, 0:n], ps_x[par][:, 0:n], AF.Square)
                        kp_sq[par] = self.sS.inc(sc.drain())
                    if self.p1_norm >= 2:
                        te.wait_ge(sS.h, kp_sq[par])
                        if si >= 1:
                            te.wait_ge(sV.h, sV.v)
                        ins = te.matmul(ps_av[:, 0:n], self.e2a[:], sq_b[:, 0:n], start=True, stop=True)
                        ins = te.matmul(ps_av[:, _s(512, n)], self.e2b[:], sq_b[:, 0:n], start=True, stop=True)
                        tv = sT.inc(ins)
                    if self.p1_norm >= 3:
                        ve.wait_ge(sT.h, tv)
                        ve.reduce_max(t64[64:65, 2:3], ps_av[64:65, 0:n], axis=AX)
                        ve.reduce_max(t64[64:65, 3:4], ps_av[64:65, _s(512, n)], axis=AX)
                        ve.drain()
                        ve.tensor_max(t64[64:65, 0:1], t64[64:65, 0:1], t64[64:65, 2:3])
                        ve.tensor_max(t64[64:65, 1:2], t64[64:65, 1:2], t64[64:65, 3:4])
                        sV.inc(ve.drain())
                te.wait_ge(sV.h, sV.v)

                if self.p1_sect < 3:
                    continue
                for si, (o, n) in enumerate(WSL):
                    par = si % 2
                    for dc in range(8):
                        ins = te.matmul(ps_x[par][:, 0:n],
                                        wq_p[:, _s(dc * 128, 128)],
                                        xTw_b[dc][:, _s(o, n)],
                                        start=(dc == 0), stop=(dc == 7))
                    tv = sT.inc(ins)
                    ve.wait_ge(sT.h, tv)
                    ins = ve.tensor_copy(qp[:, _s(o, n)], ps_x[par][:, 0:n])
                    sV.inc(ve.drain())
                    sc.wait_ge(sT.h, tv)
                    sc.wait_ge(sV.h, sV.v)
                    sc.activation(sq_b[:, 0:n], ps_x[par][:, 0:n], AF.Square)
                    sv = self.sS.inc(sc.drain())
                    te.wait_ge(sS.h, sv)
                    ins = te.matmul(ps_av[:, 0:n], self.e2a[:], sq_b[:, 0:n], start=True, stop=True)
                    ins = te.matmul(ps_av[:, _s(512, n)], self.e2b[:], sq_b[:, 0:n], start=True, stop=True)
                    tv = sT.inc(ins)
                    ve.wait_ge(sT.h, tv)
                    ve.tensor_copy(n64[64:65, _s(o, n)], ps_av[64:65, 0:n])
                    ve.tensor_copy(n64b[64:65, _s(o, n)], ps_av[64:65, _s(512, n)])
                    sV.inc(ve.drain())
                    te.wait_ge(sV.h, sV.v)

                if self.p1_sect < 4:
                    continue
                sc.wait_ge(sV.h, sV.v)
                sc.activation(n64[64:65, 0:W], n64[64:65, 0:W], AF.Sqrt, scale=t64[64:65, 0:1])
                sc.drain()
                sc.activation(n64b[64:65, 0:W], n64b[64:65, 0:W], AF.Sqrt, scale=t64[64:65, 1:2])
                sv = self.sS.inc(sc.drain())
                ve.wait_ge(sS.h, sv)
                ve.tensor_scalar_mul(n64[64:65, 0:W], n64[64:65, 0:W], -1.0)
                ve.tensor_scalar_mul(n64b[64:65, 0:W], n64b[64:65, 0:W], -1.0)
                sV.inc(ve.drain())
                gp.wait_ge(sV.h, sV.v)
                sW.inc(gp.dma_start(out=m0[0:1, 0:W], in_=n64[64:65, 0:W]), 16)
                gp.wait_ge(sW.h, sW.v)
                mh0_done = self.sG.inc(gp.drain())

                if self.p1_sect < 5:
                    continue
                for hh in range(2):
                    h = 2 * p + hh
                    vbase = hh

                    rb_lo = hh * 64
                    mrow = m0[0:1, :] if hh == 0 else n64b[64:65, :]
                    ones1 = self.okk[0:1, 0:128] if hh == 0 else self.okk[64:65, 0:128]

                    def s_mm(par, cl, cn, ql, qn):
                        te.matmul(ps_S[par][:, _s(cl, cn)],
                                  kp[_s(rb_lo, 64), _s(kt_ * 128, 128)],
                                  qp[_s(rb_lo, 64), _s(ql, qn)], start=True, stop=False)
                        return te.matmul(ps_S[par][:, _s(cl, cn)], ones1,
                                         mrow[:, _s(ql, qn)], start=False, stop=True)

                    te.wait_ge(sV.h, sV.v)
                    te.wait_ge(sG.h, mh0_done)
                    if hh == 1:
                        te.wait_ge(sG.h, dma_osc)
                    for qh in range(2):
                        qo = qh * 1024
                        exp_ct = [0, 0]
                        av_last = 0

                        def emit_av(kt_a):
                            par_a = kt_a % 2
                            te.wait_ge(sS.h, exp_ct[par_a])
                            for ss_ in range(2):
                                ins_ = te.matmul(ps_av[0:65, _s(ss_ * 512, 512)],
                                                 vg[:, _s(kt_a * 130 + vbase * 65, 65)],
                                                 pt[par_a][:, _s(ss_ * 512, 512)],
                                                 start=(kt_a == 0), stop=(kt_a == 31))
                            return sT.inc(ins_)

                        for kt_ in range(32):
                            par = kt_ % 2
                            if kt_ >= 2:
                                te.wait_ge(sS.h, exp_ct[par])
                            for ss in range(2):
                                ins = s_mm(par, ss * 512, 512, qo + ss * 512, 512)
                            tv = sT.inc(ins)
                            sc.wait_ge(sT.h, tv)
                            sc.activation(pt[par][:, 0:1024], ps_S[par][:, 0:1024],
                                          AF.Exp, scale=0.125)
                            exp_ct[par] = self.sS.inc(sc.drain())
                            if kt_ >= 1:
                                av_last = emit_av(kt_ - 1)
                        av_last = emit_av(31)
                        ve.wait_ge(sT.h, av_last)
                        ve.reciprocal(n64[64:65, _s(0, 1024)], ps_av[64:65, 0:1024])
                        sv = sV.inc(ve.drain())
                        te.wait_ge(sV.h, sv)
                        for ss in range(2):
                            ins = te.matmul(ps_S[0][0:64, _s(ss * 512, 512)],
                                            self.okk[64:65, 0:64],
                                            n64[64:65, _s(ss * 512, 512)],
                                            start=True, stop=True)
                        tv = sT.inc(ins)
                        ve.wait_ge(sT.h, tv)
                        ve.tensor_copy(rb[:, 0:1024], ps_S[0][0:64, 0:1024])
                        ve.drain()
                        if hh == 0:
                            ins = ve.tensor_mul(att[h // 2][0:64, _s(qo, 1024)],
                                                ps_av[0:64, 0:1024], rb[:, 0:1024])
                        else:
                            ins = ve.tensor_mul(osc[:, _s(qo, 1024)],
                                                ps_av[0:64, 0:1024], rb[:, 0:1024])
                        sV.inc(ve.drain())
                        te.wait_ge(sV.h, sV.v)
                    if self.p1_sect < 6:
                        continue
                    for kt_ in range(32):
                        te.matmul(ps_S[0][:, _s(kt_ * 2, 2)],
                                  kp[_s(rb_lo, 64), _s(kt_ * 128, 128)],
                                  qp[_s(rb_lo, 64), 2048:2050],
                                  start=(kt_ == 0), stop=False)
                        ins = te.matmul(ps_S[0][:, _s(kt_ * 2, 2)], ones1,
                                        mrow[:, 2048:2050],
                                        start=False, stop=(kt_ == 31))
                    tv = sT.inc(ins)
                    sc.wait_ge(sT.h, tv)
                    sc.activation(pt[0][:, 0:64], ps_S[0][:, 0:64], AF.Exp, scale=0.125)
                    sv = self.sS.inc(sc.drain())
                    te.wait_ge(sS.h, sv)
                    for kt_ in range(32):
                        ins = te.matmul(ps_av[0:65, 0:2],
                                        vg[:, _s(kt_ * 130 + vbase * 65, 65)],
                                        pt[0][:, _s(kt_ * 2, 2)],
                                        start=(kt_ == 0), stop=(kt_ == 31))
                    tv = sT.inc(ins)
                    ve.wait_ge(sT.h, tv)
                    ve.reciprocal(n64[64:65, _s(0, 2)], ps_av[64:65, 0:2])
                    sv = sV.inc(ve.drain())
                    te.wait_ge(sV.h, sv)
                    ins = te.matmul(ps_S[0][0:64, 0:2], self.okk[64:65, 0:64],
                                    n64[64:65, _s(0, 2)], start=True, stop=True)
                    tv = sT.inc(ins)
                    ve.wait_ge(sT.h, tv)
                    ve.tensor_copy(rb[:, 0:2], ps_S[0][0:64, 0:2])
                    ve.drain()
                    if hh == 0:
                        ins = ve.tensor_mul(att[h // 2][0:64, 2048:2050],
                                            ps_av[0:64, 0:2], rb[:, 0:2])
                    else:
                        ins = ve.tensor_mul(osc[:, 2048:2050], ps_av[0:64, 0:2], rb[:, 0:2])
                    sV.inc(ve.drain())
                    te.wait_ge(sV.h, sV.v)
                    if hh == 1:
                        gp.wait_ge(sV.h, sV.v)
                        sW.inc(gp.dma_start(out=att[h // 2][64:128, 0:W], in_=osc[:, 0:W]), 16)
                        gp.wait_ge(sW.h, sW.v)
                        dma_osc = self.sG.inc(gp.drain())

        if not self.p1_proj:
            if self.debug:
                gp.wait_ge(self.sV.h, self.sV.v)
                gp.wait_ge(self.sT.h, self.sT.v)
                for dc in range(8):
                    self.sW.inc(gp.dma_start(out=self.dbg["d_att"][_s(dc * 128, 128), :],
                                             in_=att[dc][:, 0:W]), 16)
            p1st.close()
            return
        self.xw_stack = ExitStack()
        xw = [self.xw_stack.enter_context(
            self.nc.sbuf_tensor(f"xw{i}", [128, W], F32, side="right")) for i in range(8)]
        self.xw = xw
        with ExitStack() as ph:
            EE = ph.enter_context
            wp_full = EE(nc.sbuf_tensor("wp_full", [128, 8192], BF16))
            scr_b = EE(nc.sbuf_tensor("p1scr", [128, W], BF16))
            ps_big = EE(nc.psum_tensor("ps_big", [128, 2560], F32))

            for eng in (sy, gp):
                eng.wait_ge(sT.h, sT.v)
                eng.wait_ge(sV.h, sV.v)
                eng.wait_ge(sS.h, sS.v)
                eng.wait_ge(sW.h, sW.v)
                eng.wait_ge(self.sG.h, self.sG.v)
            for dc in range(8):
                sX.inc(sy.dma_start(out=xw[dc][:], in_=d["xTw"][_s(dc * 128, 128), :]), 16)
            srcv = d["wpT"][:].rearrange("(dc r) c -> r dc c", r=128)
            dstv = wp_full[:].rearrange("r (dc c) -> r dc c", c=1024)
            wp_done = sW.inc(gp.dma_start(out=dstv, in_=srcv), 16)
            ld_x = sX.v
            proj_ev = 0
            te.wait_ge(sW.h, wp_done)
            for o in range(8):
                if o >= 1:
                    te.wait_ge(sV.h, proj_ev)
                for dc in range(8):
                    for (off, n) in WSL:
                        ins = te.matmul(ps_big[:, _s(off, n)],
                                        wp_full[:, _s(dc * 1024 + o * 128, 128)],
                                        att[dc][:, _s(off, n)], start=(dc == 0), stop=(dc == 7))
                tv = sT.inc(ins)
                ve.wait_ge(sT.h, tv)
                ve.wait_ge(sX.h, ld_x)
                ins = ve.scalar_tensor_tensor(xw[o][:, 0:W], ps_big[0:128, 0:W],
                                              self.bp_sb[:, _s(o, 1)], xw[o][:, 0:W],
                                              ALU.add, ALU.add)
                proj_ev = sV.inc(ve.drain())
            if self.debug:
                gp.wait_ge(self.sV.h, self.sV.v)
                for dc in range(8):
                    self.sW.inc(gp.dma_start(out=self.dbg["d_att"][_s(dc * 128, 128), :],
                                             in_=att[dc][:, 0:W]), 16)
            te.wait_ge(sV.h, sV.v)
            self.sT.inc(te.nop())
            self.emit_ln(xw, 0, W, ps_big, scr_b)
            if self.debug:
                sy.wait_ge(self.sV.h, self.sV.v)
                sy.wait_ge(self.sT.h, self.sT.v)
                for dc in range(8):
                    self.sO.inc(sy.dma_start(out=self.dbg["d_r1"][_s(dc * 128, 128), :],
                                             in_=xw[dc][:, 0:W]), 16)
        p1st.close()

    def phase2(self):
        nc, te, ve, sc, gp, sy = self.nc, self.te, self.ve, self.sc, self.gp, self.sy
        d = self.din
        sW, sX, sT, sV, sS, sG = self.sW, self.sX, self.sT, self.sV, self.sS, self.sG
        pr, PRA, PRB, PRC = self.pr, self.PRA, self.PRB, self.PRC
        xw = self.xw
        st2 = ExitStack()

        def S2(name, p, f, dt):
            return st2.enter_context(nc.sbuf_tensor(name, [p, f], dt))

        ps_big = st2.enter_context(nc.psum_tensor("p2big", [128, 2560], F32))
        ps_tr = st2.enter_context(nc.psum_tensor("p2tr", [128, 2176], BF16))
        sF = self.sF = [Sem(nc, self.st, "sF0"), Sem(nc, self.st, "sF1")]
        sH = [Sem(nc, self.st, "sH0"), Sem(nc, self.st, "sH1")]
        sH2 = [Sem(nc, self.st, "sH20"), Sem(nc, self.st, "sH21")]
        sW2 = [Sem(nc, self.st, "sW20"), Sem(nc, self.st, "sW21")]
        self.extra_sems = sF + sH + sH2 + sW2

        mcol = S2("mcol", 128, 4, F32)
        self.qbarrier(gp, sy, te, ve, sc)
        cwm = S2("cwm", 128, 96, F32)
        acc2 = S2("acc2", 128, OWN, BF16)
        am_cols = S2("am_cols", 128, 17, F32)
        amT = S2("amT", 17, 128, F32)
        gp.memset(am_cols[:], 0.0)
        self.sG.inc(gp.drain())

        def wstats_quant(nm, nrow, ncol, spill, spill_rear, mc):
            with ExitStack() as ws:
                wf = [ws.enter_context(nc.sbuf_tensor(f"wf_{nm}{i}", [128, ncol], F32))
                      for i in range(2)]
                wq = ws.enter_context(nc.sbuf_tensor(f"wqq_{nm}", [128, ncol], BF16))
                accc = ws.enter_context(nc.sbuf_tensor(f"accc_{nm}", [128, 2], F32))
                nch = nrow // 128
                sF = self.sF
                lded = [0, 0]
                for ch in range(2):
                    lded[ch] = sF[ch].inc(
                        sy.dma_start(out=wf[ch][:], in_=d[nm][_s(ch * 128, 128), :]), 16)
                for ch in range(nch):
                    par = ch % 2
                    ve.wait_ge(sF[par].h, lded[par])
                    ve.reduce_sum(accc[:, 1:2], wf[par][:, :], axis=AX,
                                  apply_absolute_value=True)
                    ve.drain()
                    if ch == 0:
                        ve.tensor_copy(accc[:, 0:1], accc[:, 1:2])
                    else:
                        ve.tensor_add(accc[:, 0:1], accc[:, 0:1], accc[:, 1:2])
                    sV.inc(ve.drain())
                    if ch + 2 < nch:
                        sy.wait_ge(sV.h, sV.v)
                        lded[par] = sF[par].inc(
                            sy.dma_start(out=wf[par][:], in_=d[nm][_s((ch + 2) * 128, 128), :]), 16)
                te.wait_ge(sV.h, sV.v)
                ins = te.matmul(ps_big[0:1, 0:1], self.ones_col_f[:], accc[:, 0:1],
                                start=True, stop=True)
                tv = sT.inc(ins)
                ve.wait_ge(sT.h, tv)
                ve.tensor_scalar(pr[0:1, 0:1], ps_big[0:1, 0:1], 1.0 / (nrow * ncol), 1e-5,
                                 ALU.mult, ALU.max)
                ve.drain()
                ve.reciprocal(pr[0:1, 1:2], pr[0:1, 0:1])
                sv = sV.inc(ve.drain())
                te.wait_ge(sV.h, sv)
                te.matmul(ps_big[0:128, 0:1], self.ones_row_f[:], pr[0:1, 0:1],
                          start=True, stop=True)
                ins = te.matmul(ps_big[0:128, 1:2], self.ones_row_f[:], pr[0:1, 1:2],
                                start=True, stop=True)
                tv = sT.inc(ins)
                ve.wait_ge(sT.h, tv)
                ve.tensor_copy(mcol[:, _s(mc, 2)], ps_big[0:128, 0:2])
                sv = sV.inc(ve.drain())
                prev_spill = self.sW.v
                for ch in range(nch):
                    par = ch % 2
                    sy.wait_ge(sV.h, sV.v)
                    lded[par] = sF[par].inc(
                        sy.dma_start(out=wf[par][:], in_=d[nm][_s(ch * 128, 128), :]), 16)
                    ve.wait_ge(sF[par].h, lded[par])
                    ve.tensor_scalar(wf[par][:], wf[par][:], mcol[:, _s(mc + 1, 1)], MAGIC,
                                     ALU.mult, ALU.add)
                    ve.drain()
                    ve.tensor_scalar(wf[par][:], wf[par][:], -MAGIC, None, ALU.add)
                    ve.drain()
                    ve.wait_ge(self.sW.h, prev_spill)
                    ve.tensor_scalar(wq[:, :], wf[par][:], -1.0, 1.0, ALU.max, ALU.min)
                    sv = sV.inc(ve.drain())
                    gp.wait_ge(sV.h, sv)
                    oap, iap = spill_rear(ch, wq)
                    prev_spill = self.sW.inc(gp.dma_start(out=oap, in_=iap), 16)
                gp.wait_ge(self.sW.h, self.sW.v)
                self.sG.inc(gp.drain())

        def w1_rear(ch, wq):
            return (self.w1q_d[:, ch].rearrange("jc r c -> r jc c"),
                    wq[:].rearrange("r (jc c) -> r jc c", c=128))

        def w2_rear(ch, wq):
            return (self.w2q_d[:, ch].rearrange("o r c -> r o c"),
                    wq[:].rearrange("r (o c) -> r o c", c=128))

        wstats_quant("w1T", D, 4096, self.w1q_d, w1_rear, 0)

        ve.wait_ge(sG.h, sG.v)
        ve.tensor_scalar(cwm[:, :], self.cw_sb[:, :], mcol[:, 0:1], None, ALU.mult)
        sV.inc(ve.drain())

        with ExitStack() as fs:
            def SF(name, p, f, dt):
                return fs.enter_context(nc.sbuf_tensor(name, [p, f], dt))
            xq = [SF(f"xq{i}", 128, W, BF16) for i in range(8)]
            scrf = SF("scrf", 128, W, F32)
            scr_b = SF("scr2b", 128, W, BF16)
            acc_b = SF("acc_b", 128, W, BF16)
            sc_bc = SF("sc_bc", 128, W, F32)
            inv_bc = SF("inv_bc", 128, W, F32)
            h_int = SF("h_int", 128, W, F32)
            t0 = SF("t0", 128, OWN, F32)
            h2b = [SF("h2b0", 128, OWN, BF16), SF("h2b1", 128, OWN, BF16)]
            w1b = [SF("w1b0", 128, 1024, BF16), SF("w1b1", 128, 1024, BF16)]

            for dc in range(8):
                sc.wait_ge(sV.h, sV.v)
                sc.wait_ge(sT.h, sT.v)
                sc.activation(scr_b[:, 0:W] if dc else acc_b[:, 0:W],
                              xw[dc][:, 0:W], AF.Abs)
                ssv = self.sS.inc(sc.drain())
                if dc:
                    ve.wait_ge(sS.h, ssv)
                    ve.tensor_max(acc_b[:, 0:W], acc_b[:, 0:W], scr_b[:, 0:W])
                    sV.inc(ve.drain())

            def cross_part_amax(acc_t, width, seg):
                ntile = (width + 127) // 128
                te.wait_ge(sV.h, sV.v)
                for t in range(ntile):
                    n = min(128, width - t * 128)
                    ins = te.transpose(ps_tr[0:n, _s(t * 128, 128)],
                                       acc_t[:, _s(t * 128, n)], self.ident_b[:])
                tv = sT.inc(ins)
                ve.wait_ge(sT.h, tv)
                for t in range(ntile):
                    n = min(128, width - t * 128)
                    ve.reduce_max(am_cols[0:n, _s(t, 1)], ps_tr[0:n, _s(t * 128, 128)],
                                  axis=AX)
                sv = sV.inc(ve.drain())
                te.wait_ge(sV.h, sv)
                ins = te.transpose(ps_big[0:17, 0:128], am_cols[:, 0:17], self.ident_f[:])
                tv = sT.inc(ins)
                ve.wait_ge(sT.h, tv)
                ve.tensor_copy(amT[:, :], ps_big[0:17, 0:128])
                sv = sV.inc(ve.drain())
                gp.wait_ge(sV.h, sv)
                self.sW.inc(gp.dma_start(out=pr[0:1, _s(seg, 2176)], in_=amT[:, :]), 16)
                gp.wait_ge(self.sW.h, self.sW.v)
                return self.sG.inc(gp.drain())

            gv = cross_part_amax(acc_b, W, PRA)
            ve.wait_ge(sG.h, gv)
            ve.tensor_scalar(pr[0:1, _s(PRA, W)], pr[0:1, _s(PRA, W)], 1e-5, None, ALU.max)
            ve.drain()
            ve.reciprocal(pr[0:1, _s(PRB, W)], pr[0:1, _s(PRA, W)])
            ve.drain()
            ve.tensor_scalar(pr[0:1, _s(PRB, W)], pr[0:1, _s(PRB, W)], 127.0, None, ALU.mult)
            ve.drain()
            ve.tensor_scalar(pr[0:1, _s(PRC, W)], pr[0:1, _s(PRA, W)], 1.0 / 127.0, None, ALU.mult)
            sv = sV.inc(ve.drain())

            def bcast_row(seg, width, dst_t):
                te.wait_ge(sV.h, sV.v)
                for (o, n) in [(i * 512, min(512, width - i * 512))
                               for i in range((width + 511) // 512)]:
                    ins = te.matmul(ps_big[0:128, _s(o, n)], self.ones_row_f[:],
                                    pr[0:1, _s(seg + o, n)], start=True, stop=True)
                tv = sT.inc(ins)
                ve.wait_ge(sT.h, tv)
                ve.tensor_copy(dst_t[:, 0:width], ps_big[0:128, 0:width])
                return sV.inc(ve.drain())

            bcast_row(PRB, W, sc_bc)
            bcast_row(PRC, W, inv_bc)
            for dc in range(8):
                ve.tensor_mul(scrf[:, 0:W], xw[dc][:, 0:W], sc_bc[:, 0:W])
                ve.drain()
                ve.tensor_scalar(scrf[:, 0:W], scrf[:, 0:W], MAGIC, -MAGIC, ALU.add, ALU.add)
                ve.drain()
                ve.tensor_mul(xq[dc][:, 0:W], scrf[:, 0:W], inv_bc[:, 0:W])
                sV.inc(ve.drain())

            sy.wait_ge(self.sG.h, self.sG.v)
            sy.wait_ge(self.sW.h, self.sW.v)
            ldv = [0, 0]
            for jc in range(2):
                ldv[jc] = sF[jc].inc(
                    sy.dma_start(out=w1b[jc][:].rearrange("r (dc c) -> r dc c", c=128),
                                 in_=self.w1q_d[jc].rearrange("dc r c -> r dc c")), 16)
            ev_ct = 0
            for jc in range(32):
                par = jc % 2
                te.wait_ge(sF[par].h, ldv[par])
                te.wait_ge(sV.h, sV.v)
                if jc >= 1:
                    te.wait_ge(sS.h, ev_ct)
                for dc in range(8):
                    for (off, n) in WSL:
                        ins = te.matmul(ps_big[:, _s(off, n)], w1b[par][:, _s(dc * 128, 128)],
                                        xq[dc][:, _s(off, n)], start=(dc == 0), stop=(dc == 7))
                tv = sT.inc(ins)
                if jc + 2 < 32:
                    sy.wait_ge(sT.h, tv)
                    ldv[par] = sF[par].inc(
                        sy.dma_start(out=w1b[par][:].rearrange("r (dc c) -> r dc c", c=128),
                                     in_=self.w1q_d[jc + 2].rearrange("dc r c -> r dc c")), 16)
                ve.wait_ge(sT.h, tv)
                ve.tensor_scalar(ps_big[:, 0:1], ps_big[:, 0:1], self.masks_sb[:, 0:1],
                                 None, ALU.mult)
                ve.tensor_scalar(ps_big[:, 2049:2050], ps_big[:, 2049:2050],
                                 self.masks_sb[:, 1:2], None, ALU.mult)
                mv = sV.inc(ve.drain())
                sc.wait_ge(sV.h, mv)
                sc.activation(h_int[:, 0:W], ps_big[0:128, 0:W], AF.Identity)
                ev_ct = self.sS.inc(sc.drain())
                sc.activation(t0[:, 0:OWN], h_int[:, 0:OWN], AF.Identity,
                              scale=cwm[:, _s(jc * 3, 1)], bias=self.cb_sb[:, _s(jc, 1)])
                cv = self.sS.inc(sc.drain())
                ve.wait_ge(sS.h, cv)
                ve.scalar_tensor_tensor(t0[:, 0:OWN], h_int[:, 1:1 + OWN],
                                        cwm[:, _s(jc * 3 + 1, 1)], t0[:, 0:OWN],
                                        ALU.mult, ALU.add)
                ve.drain()
                ve.scalar_tensor_tensor(t0[:, 0:OWN], h_int[:, 2:2 + OWN],
                                        cwm[:, _s(jc * 3 + 2, 1)], t0[:, 0:OWN],
                                        ALU.mult, ALU.add)
                cv2 = sV.inc(ve.drain())
                sc.wait_ge(sV.h, cv2)
                if jc >= 2:
                    sc.wait_ge(sH[par].h, 16 * (jc // 2))
                sc.activation(h2b[par][:, 0:OWN], t0[:, 0:OWN], AF.Silu)
                sc.drain()
                sc.activation(scr_b[:, 0:OWN], t0[:, 0:OWN], AF.Abs)
                av = self.sS.inc(sc.drain())
                ve.wait_ge(sS.h, av)
                if jc == 0:
                    ve.tensor_copy(acc2[:, :], scr_b[:, 0:OWN])
                else:
                    ve.tensor_max(acc2[:, :], acc2[:, :], scr_b[:, 0:OWN])
                sV.inc(ve.drain())
                sy.wait_ge(sS.h, av)
                sH[par].inc(sy.dma_start(out=self.h2_d[jc], in_=h2b[par][:, 0:OWN]), 16)

            gv = cross_part_amax(acc2, OWN, PRA)
            ve.wait_ge(sG.h, gv)
            ve.tensor_scalar(pr[0:1, _s(PRA, OWN)], pr[0:1, _s(PRA, OWN)], 1e-5, None, ALU.max)
            ve.drain()
            ve.reciprocal(pr[0:1, _s(PRB, OWN)], pr[0:1, _s(PRA, OWN)])
            ve.drain()
            ve.tensor_scalar(pr[0:1, _s(PRB, OWN)], pr[0:1, _s(PRB, OWN)], 127.0, None, ALU.mult)
            ve.drain()
            ve.tensor_scalar(pr[0:1, _s(PRC, OWN)], pr[0:1, _s(PRA, OWN)], 1.0 / 127.0,
                             None, ALU.mult)
            sV.inc(ve.drain())

        self.qbarrier(gp, sy, te, ve, sc)
        wstats_quant("w2T", 4096, D, self.w2q_d, w2_rear, 2)

        self.qbarrier(gp, sy, te, ve, sc)
        with ExitStack() as fs:
            def SF(name, p, f, dt):
                return fs.enter_context(nc.sbuf_tensor(name, [p, f], dt))
            h2q = [SF(f"h2q{i}", 128, 1024, BF16) for i in range(32)]
            h2l = [SF("h2l0", 128, 1024, BF16), SF("h2l1", 128, 1024, BF16)]
            w2b = [SF("w2b0", 128, 32 * 128, BF16), SF("w2b1", 128, 32 * 128, BF16)]
            s2bc = SF("s2bc", 128, 1024, F32)
            i2bc = SF("i2bc", 128, 1024, F32)
            scrf2 = SF("scrf2", 128, 1024, F32)

            for tg in range(2):
                to = tg * 1024
                bcast_done = 0
                te.wait_ge(sV.h, sV.v)
                for (o, n) in ((0, 512), (512, 512)):
                    ins = te.matmul(ps_big[0:128, _s(o, n)], self.ones_row_f[:],
                                    pr[0:1, _s(PRB + to + o, n)], start=True, stop=True)
                    ins2 = te.matmul(ps_big[0:128, _s(1024 + o, n)], self.ones_row_f[:],
                                     pr[0:1, _s(PRC + to + o, n)], start=True, stop=True)
                tv = sT.inc(ins2)
                ve.wait_ge(sT.h, tv)
                ve.tensor_copy(s2bc[:, :], ps_big[0:128, 0:1024])
                ve.tensor_copy(i2bc[:, :], ps_big[0:128, 1024:2048])
                bcast_done = sV.inc(ve.drain())
                ld2 = [0, 0]
                if tg == 0:
                    sy.wait_ge(sH[0].h, sH[0].v)
                    sy.wait_ge(sH[1].h, sH[1].v)
                for jc in range(2):
                    ld2[jc] = sH2[jc].inc(
                        sy.dma_start(out=h2l[jc][:], in_=self.h2_d[jc][:, _s(to, 1024)]), 16)
                for jc in range(32):
                    par = jc % 2
                    ve.wait_ge(sH2[par].h, ld2[par])
                    ve.wait_ge(sT.h, sT.v)
                    ve.tensor_mul(scrf2[:, :], h2l[par][:, :], s2bc[:, :])
                    ve.drain()
                    ve.tensor_scalar(scrf2[:, :], scrf2[:, :], MAGIC, -MAGIC, ALU.add, ALU.add)
                    ve.drain()
                    ve.tensor_mul(h2q[jc][:, :], scrf2[:, :], i2bc[:, :])
                    qv = sV.inc(ve.drain())
                    if jc + 2 < 32:
                        sy.wait_ge(sV.h, qv)
                        ld2[par] = sH2[par].inc(
                            sy.dma_start(out=h2l[par][:], in_=self.h2_d[jc + 2][:, _s(to, 1024)]), 16)
                ldw = [0, 0]
                if tg == 0:
                    sy.wait_ge(sG.h, sG.v)
                for o in range(2):
                    ldw[o] = sW2[o].inc(
                        sy.dma_start(out=w2b[o][:].rearrange("r (jc c) -> r jc c", c=128),
                                     in_=self.w2q_d[o].rearrange("jc r c -> r jc c")), 16)
                ev2 = 0
                for o in range(8):
                    par = o % 2
                    te.wait_ge(sW2[par].h, ldw[par])
                    te.wait_ge(sV.h, sV.v if o == 0 else ev2)
                    pp = ps_big[:, _s(par * 1024, 1024)]
                    for jc in range(32):
                        for (off, n) in ((0, 512), (512, 512)):
                            ins = te.matmul(pp[:, _s(off, n)], w2b[par][:, _s(jc * 128, 128)],
                                            h2q[jc][:, _s(off, n)],
                                            start=(jc == 0), stop=(jc == 31))
                    tv = sT.inc(ins)
                    if o + 2 < 8:
                        sy.wait_ge(sT.h, tv)
                        ldw[par] = sW2[par].inc(
                            sy.dma_start(out=w2b[par][:].rearrange("r (jc c) -> r jc c", c=128),
                                         in_=self.w2q_d[o + 2].rearrange("jc r c -> r jc c")), 16)
                    ve.wait_ge(sT.h, tv)
                    ins = ve.scalar_tensor_tensor(xw[o][:, _s(1 + to, 1024)], pp[0:128, :],
                                                  mcol[:, 2:3], xw[o][:, _s(1 + to, 1024)],
                                                  ALU.mult, ALU.add)
                    ev2 = sV.inc(ve.drain())
            te.wait_ge(sV.h, sV.v)
            self.sT.inc(te.nop())
        self.qbarrier(sc, ve, te, gp, sy)
        with ExitStack() as fs:
            scr_ln = fs.enter_context(nc.sbuf_tensor("scr_ln", [128, OWN], BF16))
            x2b = [fs.enter_context(nc.sbuf_tensor(f"x2b{i}", [128, OWN], BF16))
                   for i in range(8)]
            self.emit_ln(xw, 1, OWN, ps_big, scr_ln)
            if self.debug:
                sy.wait_ge(self.sV.h, self.sV.v)
                sy.wait_ge(self.sT.h, self.sT.v)
                for dc in range(8):
                    self.sO.inc(sy.dma_start(out=self.dbg["d_r2"][_s(dc * 128, 128), :],
                                             in_=xw[dc][:, 0:W]), 16)
            for dc in range(8):
                sc.wait_ge(self.sT.h, self.sT.v)
                sc.activation(x2b[dc][:, :], xw[dc][:, 1:1 + OWN], AF.Identity)
            sv = self.sS.inc(sc.drain())
            sy.wait_ge(self.sS.h, sv)
            gp.wait_ge(self.sS.h, sv)
            for dc in range(8):
                self.sO.inc(sy.dma_start(out=self.x2own_d[dc], in_=xw[dc][:, 1:1 + OWN]), 16)
                self.sW.inc(gp.dma_start(out=self.cc_in[_s(dc * 128, 128), :],
                                         in_=x2b[dc][:, :]), 16)
            gp.wait_ge(self.sW.h, self.sW.v)
            self.sG.inc(gp.collective_compute(
                "AllGather", ALU.bypass,
                replica_groups=[[0, 1], [2, 3], [4, 5], [6, 7]],
                ins=[self.cc_in], outs=[self.cc_out]))
        st2.close()
        self.qbarrier(gp, sy, te, ve, sc)
        self.xw_stack.close()

    def phase3(self):
        nc, te, ve, sc, gp, sy = self.nc, self.te, self.ve, self.sc, self.gp, self.sy
        d = self.din
        sW, sX, sT, sV, sS, sG = self.sW, self.sX, self.sT, self.sV, self.sS, self.sG
        pr = self.pr
        half = 512

        st3 = ExitStack()

        def S3(name, p, f, dt):
            return st3.enter_context(nc.sbuf_tensor(name, [p, f], dt))

        X2T = [S3(f"X2T{i}", 128, OWN, BF16) for i in range(8)]
        qb2r = S3("qb2r", 1, 512, BF16)
        kb2r = S3("kb2r", 1, 512, BF16)
        zc = S3("zc", 64, 8, F32)
        A2 = S3("A2", 128, 512, BF16)
        self._b1sb = {}
        for nm, ncols in (("qb1_r", 8), ("kb1_r", 8), ("vb_s", 4), ("gb_s", 4), ("pb_r", 8)):
            t = S3("sb_" + nm, 128, ncols, F32)
            self.sX.inc(sy.dma_start(out=t[:], in_=d[nm]), 16)
            self._b1sb[nm] = t
        bias_ld = self.sX.v

        stA = ExitStack()

        def SA(name, p, f, dt):
            return stA.enter_context(nc.sbuf_tensor(name, [p, f], dt))

        x2f = [SA(f"x2f{i}", 128, NT, BF16) for i in range(8)]
        t1 = [SA(f"t1_{i}", 128, 512, BF16) for i in range(8)]
        q2n = [SA(f"q2n{i}", 128, 512, BF16) for i in range(4)]
        k520 = SA("k520", 128, 4 * 520, BF16)
        sp1 = SA("sp1", 128, 512, F32)
        sp2 = SA("sp2", 128, 512, F32)
        w1b = [SA("p3w0", 128, 1024, BF16), SA("p3w1", 128, 1024, BF16)]
        w2b = [SA("p3v0", 128, 512, BF16), SA("p3v1", 128, 512, BF16)]
        zr = SA("zr", 66, 1024, F32)
        vT = SA("vT", 128, NT, BF16)
        Op = SA("Op", 128, NT, BF16)
        OpSW = SA("OpSW", 128, NT, BF16)
        Osc = SA("Osc", 64, NT, BF16)

        sQ1 = [Sem(nc, self.st, "sQ10"), Sem(nc, self.st, "sQ11")]
        sQ2 = [Sem(nc, self.st, "sQ20"), Sem(nc, self.st, "sQ21")]
        sXo = [Sem(nc, self.st, "sXo0"), Sem(nc, self.st, "sXo1")]
        self.extra_sems += sQ1 + sQ2 + sXo

        with ExitStack() as pstk:
            pq = [pstk.enter_context(nc.psum_tensor("pq0", [128, 512], F32)),
                  pstk.enter_context(nc.psum_tensor("pq1", [128, 512], F32))]
            psA = pstk.enter_context(nc.psum_tensor("psA", [66, 512], F32))
            pq2 = [pstk.enter_context(nc.psum_tensor(f"pq2_{i}", [128, 512], F32))
                   for i in range(4)]

            sy.wait_ge(sG.h, sG.v)
            for dc in range(8):
                sX.inc(sy.dma_start(out=x2f[dc][:, 0:OWN],
                                    in_=self.cc_out[_s(dc * 128, 128), :]), 16)
                sX.inc(sy.dma_start(out=x2f[dc][:, OWN:NT],
                                    in_=self.cc_out[_s(D + dc * 128, 128), :]), 16)
            self.sW.inc(gp.dma_start(out=qb2r[:], in_=d["qb2_row"]), 16)
            self.sW.inc(gp.dma_start(out=kb2r[:], in_=d["kb2_row"]), 16)
            b2_ld = self.sW.v
            x2_ld = sX.v

            def load_w1chunk(nm, o, par):
                srcv = d[nm][:, _s(o * 128, 128)].rearrange("(dc r) c -> r dc c", r=128)
                dstv = w1b[par][:].rearrange("r (dc c) -> r dc c", c=128)
                return sQ1[par].inc(gp.dma_start(out=dstv, in_=srcv), 16)

            def load_w2chunk(nm, o, par):
                return sQ2[par].inc(
                    gp.dma_start(out=w2b[par][:], in_=d[nm][_s(o * 128, 128), :]), 16)

            for tg in range(8 if self.p3_sect >= 2 else 0):
                to = tg * 512
                for side, w1nm, w2nm, b1c, b2r in (
                        (0, "qw1T", "qw2Ts", "qb1_r", qb2r),
                        (1, "kw1T", "kw2Ts", "kb1_r", kb2r)):
                    l1 = [0, 0]
                    for o in range(8):
                        par = o % 2
                        gp.wait_ge(sT.h, sT.v)
                        l1[par] = load_w1chunk(w1nm, o, par)
                        te.wait_ge(sQ1[par].h, l1[par])
                        te.wait_ge(sX.h, max(x2_ld, bias_ld))
                        te.wait_ge(sW.h, b2_ld)
                        if o >= 1:
                            te.wait_ge(sS.h, sS.v)
                        for dc in range(8):
                            ins = te.matmul(pq[par][:, 0:512],
                                            w1b[par][:, _s(dc * 128, 128)],
                                            x2f[dc][:, _s(to, 512)],
                                            start=(dc == 0), stop=(dc == 7))
                        tv = sT.inc(ins)
                        sc.wait_ge(sT.h, tv)
                        sc.activation(t1[o][:, :], pq[par][:, 0:512], AF.Gelu,
                                      bias=self._b1sb[b1c][:, _s(o, 1)])
                        self.sS.inc(sc.drain())
                    l2 = [0, 0]
                    for o in range(8):
                        par = o % 2
                        gp.wait_ge(sT.h, sT.v)
                        l2[par] = load_w2chunk(w2nm, o, par)
                        te.wait_ge(sQ2[par].h, l2[par])
                        te.wait_ge(sS.h, sS.v)
                        if o == 0:
                            te.wait_ge(sV.h, sV.v)
                        for tc in range(4):
                            ins = te.matmul(pq2[tc][:, 0:512],
                                            t1[o][:, _s(tc * 128, 128)],
                                            w2b[par][:, 0:512],
                                            start=(o == 0), stop=False)
                        sT.inc(ins)
                    for tc in range(4):
                        ins = te.matmul(pq2[tc][:, 0:512], self.okk[0:1, 0:128],
                                        b2r[:, 0:512], start=False, stop=True)
                    tv = sT.inc(ins)
                    for tc in range(4):
                        sc.wait_ge(sT.h, tv)
                        sc.wait_ge(sV.h, sV.v)
                        sc.activation(sp1[:, :], pq2[tc][:, 0:512], AF.Abs)
                        sc.drain()
                        sc.activation(sp1[:, :], sp1[:, :], AF.Exp, scale=-1.0)
                        sc.drain()
                        sc.activation(sp2[:, :], sp1[:, :], AF.Ln, bias=1.0)
                        sc.drain()
                        sc.activation(sp1[:, :], pq2[tc][:, 0:512], AF.Relu)
                        sv = self.sS.inc(sc.drain())
                        ve.wait_ge(sS.h, sv)
                        if side == 0:
                            ve.tensor_add(q2n[tc][:, :], sp1[:, :], sp2[:, :])
                            sV.inc(ve.drain())
                        else:
                            ve.tensor_add(sp2[:, :], sp1[:, :], sp2[:, :])
                            ve.drain()
                            dst = k520[:, _s(tc * 520, 520)].rearrange(
                                "r (h c) -> r h c", c=65)[:, :, 0:64]
                            ve.tensor_copy(dst, sp2[:, :].rearrange("r (h c) -> r h c", c=64))
                            ve.drain()
                            for hh in range(8):
                                ve.reduce_sum(k520[:, _s(tc * 520 + hh * 65 + 64, 1)],
                                              k520[:, _s(tc * 520 + hh * 65, 64)], axis=AX)
                            sV.inc(ve.drain())
                te.wait_ge(sV.h, sV.v)
                for tc in range(4):
                    for hh in range(8):
                        ins = te.matmul(psA[0:65, _s(hh * 64, 64)],
                                        k520[:, _s(tc * 520 + hh * 65, 65)],
                                        q2n[tc][:, _s(hh * 64, 64)],
                                        start=(tg == 0 and tc == 0 and hh == 0),
                                        stop=(tg == 7 and tc == 3 and hh == 7))
                sT.inc(ins)

            if self.p3_sect < 3:
                stA.close()
                st3.close()
                return
            tv = sT.v
            ve.wait_ge(sT.h, tv)
            ve.tensor_scalar(A2[0:64, 0:512], psA[0:64, 0:512], 0.125, None, ALU.mult)
            ve.drain()
            for hh in range(8):
                ve.tensor_add(A2[0:64, _s(hh * 64, 64)], A2[0:64, _s(hh * 64, 64)],
                              self.ident_b[0:64, 0:64])
            ve.drain()
            ve.tensor_scalar(zr[64:65, 0:512], psA[64:65, 0:512], 0.125, 1024.0,
                             ALU.mult, ALU.add)
            ve.drain()
            ve.reciprocal(zr[64:65, _s(512, 512)], zr[64:65, 0:512])
            sv = sV.inc(ve.drain())
            gp.wait_ge(sV.h, sv)
            self.sW.inc(gp.dma_start(out=A2[64:128, 0:512], in_=A2[0:64, 0:512]), 16)
            gp.wait_ge(sW.h, sW.v)
            a2dma = self.sG.inc(gp.drain())
            te.wait_ge(sV.h, sv)
            for wave in range(2):
                for i in range(4):
                    hh = wave * 4 + i
                    ins = te.transpose(pq2[i][0:64, 0:1],
                                       zr[64:65, _s(512 + hh * 64, 64)],
                                       self.ident_f[64:65, 64:65])
                tv = sT.inc(ins)
                ve.wait_ge(sT.h, tv)
                for i in range(4):
                    ve.tensor_copy(zc[:, _s(wave * 4 + i, 1)], pq2[i][0:64, 0:1])
                sv = sV.inc(ve.drain())
                te.wait_ge(sV.h, sv)

            if self.p3_sect < 4:
                stA.close()
                st3.close()
                return
            for pp_ in range(4 if self.p3_sect >= 4 else 0):
                for which, wnm, bcol in ((0, "vwTs", "vb_s"), (1, "gwTs", "gb_s")):
                    gp.wait_ge(sT.h, sT.v)
                    srcv = d[wnm][:, _s(pp_ * 128, 128)].rearrange("(dc r) c -> r dc c", r=128)
                    dstv = w1b[0][:].rearrange("r (dc c) -> r dc c", c=128)
                    lw0 = sQ1[0].inc(gp.dma_start(out=dstv, in_=srcv), 16)
                    te.wait_ge(sQ1[0].h, lw0)
                    te.wait_ge(sV.h, sV.v)
                    for s in range(8):
                        par = s % 2
                        if s >= 2:
                            te.wait_ge(sS.h, sS.v)
                        for dc in range(8):
                            ins = te.matmul(pq[par][:, 0:512],
                                            w1b[0][:, _s(dc * 128, 128)],
                                            x2f[dc][:, _s(s * 512, 512)],
                                            start=(dc == 0), stop=(dc == 7))
                        tv = sT.inc(ins)
                        sc.wait_ge(sT.h, tv)
                        sc.activation(vT[:, _s(s * 512, 512)], pq[par][:, 0:512], AF.Gelu,
                                      bias=self._b1sb[bcol][:, _s(pp_, 1)])
                        self.sS.inc(sc.drain())
                    if which == 0:
                        te.wait_ge(sS.h, sS.v)
                        te.wait_ge(sG.h, a2dma)
                        ev = 0
                        for hh in range(2):
                            h = 2 * pp_ + hh
                            for s in range(8):
                                par = s % 2
                                if s >= 2 or hh == 1:
                                    te.wait_ge(sV.h, ev)
                                if hh == 0:
                                    ins = te.matmul(pq2[par][0:64, 0:512],
                                                    A2[0:64, _s(h * 64, 64)],
                                                    vT[0:64, _s(s * 512, 512)],
                                                    start=True, stop=True)
                                else:
                                    ins = te.matmul(pq2[par][0:64, 0:512],
                                                    A2[64:128, _s(h * 64, 64)],
                                                    vT[64:128, _s(s * 512, 512)],
                                                    start=True, stop=True)
                                tv = sT.inc(ins)
                                ve.wait_ge(sT.h, tv)
                                dst = Op[0:64, _s(s * 512, 512)] if hh == 0 \
                                    else Osc[:, _s(s * 512, 512)]
                                ve.tensor_scalar(dst, pq2[par][0:64, 0:512],
                                                 zc[:, _s(h, 1)], None, ALU.mult)
                                ev = sV.inc(ve.drain())
                        gp.wait_ge(sV.h, ev)
                        self.sW.inc(gp.dma_start(out=Op[64:128, :], in_=Osc[:, :]), 16)
                        gp.wait_ge(sW.h, sW.v)
                        self.sG.inc(gp.drain())
                ve.wait_ge(sG.h, sG.v)
                ve.wait_ge(sS.h, sS.v)
                ve.tensor_mul(Op[:, :], Op[:, :], vT[:, :])
                gv = sV.inc(ve.drain())
                te.wait_ge(sV.h, gv)
                for s in range(8):
                    par = s % 2
                    if s >= 2:
                        te.wait_ge(sV.h, sV.v)
                    ins = te.matmul(pq[par][:, 0:512], self.swap64[:],
                                    Op[:, _s(s * 512, 512)], start=True, stop=True)
                    tv = sT.inc(ins)
                    ve.wait_ge(sT.h, tv)
                    ve.tensor_copy(OpSW[:, _s(s * 512, 512)], pq[par][:, 0:512])
                    sV.inc(ve.drain())
                for hh in range(2):
                    h = 2 * pp_ + hh
                    for pc in range(8):
                        for jh in range(2):
                            j = 2 * pc + jh
                            srcT = Op if jh == hh else OpSW
                            ve.tensor_copy(X2T[pc][_s(jh * 64, 64), _s(h * 256, 256)],
                                           srcT[_s(jh * 64, 64), j:NT:16])
                    sV.inc(ve.drain())
            te.wait_ge(sV.h, sV.v)
            self.sT.inc(te.nop())
        if self.p3_sect < 5:
            stA.close()
            st3.close()
            return
        self.qbarrier(te, ve, sc, gp, sy)
        stA.close()
        r3 = [self.SB(f"r3_{i}", 128, OWN, F32, side="right") for i in range(8)]
        self.r3 = r3
        with ExitStack() as pstk:
            pp2 = [pstk.enter_context(nc.psum_tensor("pp2_0", [128, 1024], F32)),
                   pstk.enter_context(nc.psum_tensor("pp2_1", [128, 1024], F32))]
            psLN = pstk.enter_context(nc.psum_tensor("psLN", [128, 2048], F32))
            wpf = st3.enter_context(nc.sbuf_tensor("p3wp", [128, 8192], BF16))
            xo = [st3.enter_context(nc.sbuf_tensor(f"xo{i}", [128, OWN], F32))
                  for i in range(2)]
            scr3 = st3.enter_context(nc.sbuf_tensor("scr3", [128, OWN], BF16))
            srcv = d["pwT"][:].rearrange("(dc r) c -> r dc c", r=128)
            dstv = wpf[:].rearrange("r (dc c) -> r dc c", c=1024)
            wp_done = sW.inc(gp.dma_start(out=dstv, in_=srcv), 16)
            xol = [0, 0]
            for o in range(2):
                xol[o] = sXo[o].inc(sy.dma_start(out=xo[o][:], in_=self.x2own_d[o]), 16)
            te.wait_ge(sW.h, wp_done)
            ev3 = 0
            for o in range(8):
                par = o % 2
                if o >= 1:
                    te.wait_ge(sV.h, ev3)
                for th in range(2):
                    for dc in range(8):
                        for ss in range(2):
                            ins = te.matmul(pp2[th][:, _s(ss * 512, 512)],
                                            wpf[:, _s(dc * 1024 + o * 128, 128)],
                                            X2T[dc][:, _s(th * 1024 + ss * 512, 512)],
                                            start=(dc == 0), stop=(dc == 7))
                tv = sT.inc(ins)
                ve.wait_ge(sT.h, tv)
                ve.wait_ge(sXo[par].h, xol[par])
                for th in range(2):
                    ins = ve.scalar_tensor_tensor(r3[o][:, _s(th * 1024, 1024)],
                                                  pp2[th][0:128, :],
                                                  self._b1sb["pb_r"][:, _s(o, 1)],
                                                  xo[par][:, _s(th * 1024, 1024)],
                                                  ALU.add, ALU.add)
                ev3 = sV.inc(ve.drain())
                if o + 2 < 8:
                    sy.wait_ge(sV.h, ev3)
                    xol[par] = sXo[par].inc(
                        sy.dma_start(out=xo[par][:], in_=self.x2own_d[o + 2]), 16)
            te.wait_ge(sV.h, sV.v)
            self.sT.inc(te.nop())
            if self.p3_sect >= 6:
                self.emit_ln(r3, 0, OWN, psLN, scr3)
            if self.debug and self.p3_sect >= 7:
                sy.wait_ge(self.sV.h, self.sV.v)
                sy.wait_ge(self.sT.h, self.sT.v)
                for dc in range(8):
                    self.sO.inc(sy.dma_start(out=self.dbg["d_r3"][_s(dc * 128, 128), :],
                                             in_=r3[dc][:, 0:OWN]), 16)
        st3.close()

    def phase4(self):
        nc, te, ve, sc, gp, sy = self.nc, self.te, self.ve, self.sc, self.gp, self.sy
        d = self.din
        sW, sX, sT, sV, sS, sG = self.sW, self.sX, self.sT, self.sV, self.sS, self.sG
        r3 = self.r3
        BIG = 1.0e30

        self.qbarrier(gp, sy, te, ve, sc)
        st4 = ExitStack()

        def S4(name, p, f, dt):
            return st4.enter_context(nc.sbuf_tensor(name, [p, f], dt))

        x3 = [S4(f"x3_{i}", 128, OWN, BF16) for i in range(8)]
        cmT = S4("cmT", 16, OWN, BF16)
        e16b = S4("e16b", 16, OWN, BF16)
        be2b = S4("be2b", 16, D, BF16)
        be1c = S4("be1c", 128, 64, F32)
        be3c = S4("be3c", 128, 64, F32)
        bsc = S4("bsc", 128, 24, F32)
        sE = [Sem(nc, self.st, "sE0"), Sem(nc, self.st, "sE1")]
        sE2 = [Sem(nc, self.st, "sE20"), Sem(nc, self.st, "sE21")]
        self.extra_sems += sE + sE2

        for dc in range(8):
            sc.wait_ge(sT.h, sT.v)
            sc.activation(x3[dc][:, :], r3[dc][:, :], AF.Identity)
        x3v = self.sS.inc(sc.drain())
        self.sW.inc(gp.dma_start(out=e16b[:], in_=d["E16"]), 16)
        self.sW.inc(gp.dma_start(out=be2b[:], in_=d["be2"]), 16)
        w_small = self.sW.v
        self.sX.inc(sy.dma_start(out=be1c[:], in_=d["be1_c"]), 16)
        self.sX.inc(sy.dma_start(out=be3c[:], in_=d["be3_c"]), 16)
        self.sX.inc(sy.dma_start(out=bsc[:, 0:8], in_=d["bs1_r"]), 16)
        self.sX.inc(sy.dma_start(out=bsc[:, 8:16], in_=d["bs2_r"]), 16)
        self.sX.inc(sy.dma_start(out=bsc[:, 16:24], in_=d["bs3_r"]), 16)
        x_small = self.sX.v

        with ExitStack() as gs_:
            def SG(name, p, f, dt):
                return gs_.enter_context(nc.sbuf_tensor(name, [p, f], dt))
            gwb = SG("gwb", 128, 8 * 16, BF16)
            gbr = SG("gbr", 1, 16, F32)
            gbb = SG("gbb", 128, 16, F32)
            sgT = SG("sgT", 16, OWN, F32)
            sgN = SG("sgN", 128, 256, F32)
            sgO = SG("sgO", 128, 256, F32)
            comb = SG("comb", 128, 256, F32)
            p6 = SG("p6", 128, 24, F32)
            g4 = SG("g4", 128, 24, F32)
            cN = SG("cN", 128, 8, F32)
            sgM = SG("sgM", 128, 16, F32)
            sgM2 = SG("sgM2", 128, 16, F32)
            sgE = SG("sgE", 128, 16, F32)
            psG = gs_.enter_context(nc.psum_tensor("psG", [16, OWN], F32))
            ptr = gs_.enter_context(nc.psum_tensor("ptr", [128, 128], F32))

            srcv = d["gwT"][:].rearrange("(dc r) c -> r dc c", r=128)
            dstv = gwb[:].rearrange("r (dc c) -> r dc c", c=16)
            gl = self.sW.inc(gp.dma_start(out=dstv, in_=srcv), 16)
            self.sX.inc(sy.dma_start(out=gbr[:], in_=d["gb_row"]), 16)
            gb_ld = self.sX.v
            te.wait_ge(sW.h, gl)
            te.wait_ge(sS.h, x3v)
            for (o, n) in [(i * 512, 512) for i in range(4)]:
                for dc in range(8):
                    ins = te.matmul(psG[:, _s(o, n)], gwb[:, _s(dc * 16, 16)],
                                    x3[dc][:, _s(o, n)], start=(dc == 0), stop=(dc == 7))
            tv = sT.inc(ins)
            ve.wait_ge(sT.h, tv)
            ve.tensor_copy(sgT[:, :], psG[0:16, 0:OWN])
            sv = sV.inc(ve.drain())
            te.wait_ge(sX.h, gb_ld)
            te.wait_ge(sV.h, sv)
            ins = te.matmul(ptr[0:128, 0:16], self.ones_row_f[:], gbr[0:1, 0:16],
                            start=True, stop=True)
            tv = sT.inc(ins)
            ve.wait_ge(sT.h, tv)
            ve.tensor_copy(gbb[:, :], ptr[0:128, 0:16])
            sv = sV.inc(ve.drain())

            GS, GS2, GSN, EQG, KPG = 0, 4, 8, 12, 16
            for t in range(16):
                S = _s(t * 16, 16)
                te.wait_ge(sV.h, sV.v)
                ins = te.transpose(ptr[0:128, 0:16], sgT[:, _s(t * 128, 128)],
                                   self.ident_f[0:16, 0:16])
                tv = sT.inc(ins)
                ve.wait_ge(sT.h, tv)
                ve.tensor_copy(sgN[:, S], ptr[0:128, 0:16])
                ve.drain()
                ve.reduce_max(cN[:, 0:1], sgN[:, S], axis=AX, negate=True)
                nm = sV.inc(ve.drain())
                sc.wait_ge(sV.h, nm)
                sc.activation(sgO[:, S], sgN[:, S], AF.Exp, bias=cN[:, 0:1],
                              accum_out=cN[:, 1:2])
                ex = self.sS.inc(sc.drain())
                ve.wait_ge(sS.h, ex)
                ve.reciprocal(cN[:, 2:3], cN[:, 1:2])
                ve.drain()
                ve.tensor_scalar(sgO[:, S], sgO[:, S], cN[:, 2:3], None, ALU.mult)
                ve.drain()
                ve.tensor_add(sgN[:, S], sgO[:, S], gbb[:, :])
                ve.drain()
                for i, (a, b) in enumerate([(0, 1), (0, 2), (0, 3), (1, 2), (1, 3), (2, 3)]):
                    ve.tensor_add(p6[:, _s(4 * i, 4)],
                                  sgN[:, t * 16 + a:(t + 1) * 16:4],
                                  sgN[:, t * 16 + b:(t + 1) * 16:4])
                ve.drain()
                ve.tensor_max(g4[:, _s(GS, 4)], p6[:, 0:4], p6[:, 4:8])
                ve.drain()
                for i in range(2, 6):
                    ve.tensor_max(g4[:, _s(GS, 4)], g4[:, _s(GS, 4)], p6[:, _s(4 * i, 4)])
                    ve.drain()
                ve.reduce_max(cN[:, 3:4], g4[:, _s(GS, 4)], axis=AX, negate=True)
                sv = sV.inc(ve.drain())
                sc.wait_ge(sV.h, sv)
                sc.activation(g4[:, _s(GSN, 4)], g4[:, _s(GS, 4)], AF.Sign, bias=cN[:, 3:4])
                sc.drain()
                sc.activation(g4[:, _s(EQG, 4)], g4[:, _s(GSN, 4)], AF.Abs)
                ab = self.sS.inc(sc.drain())
                ve.wait_ge(sS.h, ab)
                ve.tensor_scalar(g4[:, _s(EQG, 4)], g4[:, _s(EQG, 4)], -1.0, 1.0,
                                 ALU.mult, ALU.add)
                ve.drain()
                ve.reduce_sum(cN[:, 4:5], g4[:, _s(EQG, 4)], axis=AX)
                ve.drain()
                ve.scalar_tensor_tensor(g4[:, _s(GS2, 4)], g4[:, _s(EQG, 4)], -BIG,
                                        g4[:, _s(GS, 4)], ALU.mult, ALU.add)
                ve.drain()
                ve.reduce_max(cN[:, 5:6], g4[:, _s(GS2, 4)], axis=AX)
                sv = sV.inc(ve.drain())
                sc.wait_ge(sV.h, sv)
                sc.activation(cN[:, 6:7], cN[:, 4:5], AF.Sign, bias=-1.5)
                sg2 = self.sS.inc(sc.drain())
                ve.wait_ge(sS.h, sg2)
                ve.tensor_scalar(cN[:, 6:7], cN[:, 6:7], 1.0, 1.0, ALU.add, ALU.min)
                ve.drain()
                ve.tensor_scalar(cN[:, 7:8], cN[:, 3:4], -1.0, None, ALU.mult)
                ve.drain()
                ve.tensor_sub(cN[:, 7:8], cN[:, 7:8], cN[:, 5:6])
                ve.drain()
                ve.tensor_mul(cN[:, 7:8], cN[:, 7:8], cN[:, 6:7])
                ve.drain()
                ve.tensor_add(cN[:, 7:8], cN[:, 7:8], cN[:, 5:6])
                ve.drain()
                ve.tensor_scalar(cN[:, 7:8], cN[:, 7:8], -1.0, None, ALU.mult)
                sv = sV.inc(ve.drain())
                sc.wait_ge(sV.h, sv)
                sc.activation(g4[:, _s(KPG, 4)], g4[:, _s(GS, 4)], AF.Sign, bias=cN[:, 7:8])
                kp = self.sS.inc(sc.drain())
                ve.wait_ge(sS.h, kp)
                ve.tensor_scalar(g4[:, _s(KPG, 4)], g4[:, _s(KPG, 4)], 1.0, 1.0,
                                 ALU.add, ALU.min)
                ve.drain()
                ve.tensor_scalar(g4[:, _s(GS2, 4)], g4[:, _s(KPG, 4)], -1.0, None, ALU.add)
                ve.drain()
                for i in range(4):
                    ve.scalar_tensor_tensor(sgM[:, i:16:4], g4[:, _s(GS2, 4)], BIG,
                                            sgN[:, t * 16 + i:(t + 1) * 16:4],
                                            ALU.mult, ALU.add)
                ve.drain()
                ve.reduce_max(cN[:, 3:4], sgM[:, :], axis=AX, negate=True)
                sv = sV.inc(ve.drain())
                sc.wait_ge(sV.h, sv)
                sc.activation(sgE[:, :], sgM[:, :], AF.Sign, bias=cN[:, 3:4])
                sc.drain()
                sc.activation(sgE[:, :], sgE[:, :], AF.Abs)
                ab = self.sS.inc(sc.drain())
                ve.wait_ge(sS.h, ab)
                ve.tensor_scalar(sgE[:, :], sgE[:, :], -1.0, 1.0, ALU.mult, ALU.add)
                ve.drain()
                ve.reduce_sum(cN[:, 4:5], sgE[:, :], axis=AX)
                ve.drain()
                ve.scalar_tensor_tensor(sgM2[:, :], sgE[:, :], -BIG, sgM[:, :],
                                        ALU.mult, ALU.add)
                ve.drain()
                ve.reduce_max(cN[:, 5:6], sgM2[:, :], axis=AX)
                sv = sV.inc(ve.drain())
                sc.wait_ge(sV.h, sv)
                sc.activation(cN[:, 6:7], cN[:, 4:5], AF.Sign, bias=-1.5)
                sg2 = self.sS.inc(sc.drain())
                ve.wait_ge(sS.h, sg2)
                ve.tensor_scalar(cN[:, 6:7], cN[:, 6:7], 1.0, 1.0, ALU.add, ALU.min)
                ve.drain()
                ve.tensor_scalar(cN[:, 7:8], cN[:, 3:4], -1.0, None, ALU.mult)
                ve.drain()
                ve.tensor_sub(cN[:, 7:8], cN[:, 7:8], cN[:, 5:6])
                ve.drain()
                ve.tensor_mul(cN[:, 7:8], cN[:, 7:8], cN[:, 6:7])
                ve.drain()
                ve.tensor_add(cN[:, 7:8], cN[:, 7:8], cN[:, 5:6])
                ve.drain()
                ve.tensor_scalar(cN[:, 7:8], cN[:, 7:8], -1.0, None, ALU.mult)
                sv = sV.inc(ve.drain())
                sc.wait_ge(sV.h, sv)
                sc.activation(sgE[:, :], sgM[:, :], AF.Sign, bias=cN[:, 7:8])
                kp = self.sS.inc(sc.drain())
                ve.wait_ge(sS.h, kp)
                ve.tensor_scalar(sgE[:, :], sgE[:, :], 1.0, 1.0, ALU.add, ALU.min)
                ve.drain()
                ve.tensor_mul(comb[:, S], sgO[:, S], sgE[:, :])
                sV.inc(ve.drain())
            for t in range(16):
                te.wait_ge(sV.h, sV.v)
                ins = te.transpose(ptr[0:16, 0:128], comb[:, _s(t * 16, 16)],
                                   self.ident_f[:])
                tv = sT.inc(ins)
                ve.wait_ge(sT.h, tv)
                ve.tensor_copy(cmT[:, _s(t * 128, 128)], ptr[0:16, 0:128])
                sV.inc(ve.drain())
            te.wait_ge(sV.h, sV.v)
            self.sT.inc(te.nop())
            if self.debug:
                gp.wait_ge(sV.h, sV.v)
                self.sW.inc(gp.dma_start(out=self.dbg["d_cmb"], in_=cmT[:, :]), 16)
        self.qbarrier(gp, sy, te, ve, sc)
        with ExitStack() as ss_:
            def SS(name, p, f, dt):
                return ss_.enter_context(nc.sbuf_tensor(name, [p, f], dt))
            hs = [SS(f"hs{i}", 128, OWN, BF16) for i in range(8)]
            s1b = SS("s1b", 128, OWN, BF16)
            wsb = [SS("wsb0", 128, 1024, BF16), SS("wsb1", 128, 1024, BF16)]
            psS4 = ss_.enter_context(nc.psum_tensor("psS4", [128, OWN], F32))

            def ld_ws(nm, o, par):
                srcv = d[nm][:, _s(o * 128, 128)].rearrange("(dc r) c -> r dc c", r=128)
                dstv = wsb[par][:].rearrange("r (dc c) -> r dc c", c=128)
                return sE[par].inc(gp.dma_start(out=dstv, in_=srcv), 16)

            for o in range(8):
                par = o % 2
                gp.wait_ge(sT.h, sT.v)
                l1 = ld_ws("ws1", o, par)
                te.wait_ge(sE[par].h, l1)
                te.wait_ge(sV.h, sV.v)
                for (off, n) in [(i * 512, 512) for i in range(4)]:
                    for dc in range(8):
                        ins = te.matmul(psS4[:, _s(off, n)], wsb[par][:, _s(dc * 128, 128)],
                                        x3[dc][:, _s(off, n)], start=(dc == 0), stop=(dc == 7))
                tv = sT.inc(ins)
                sc.wait_ge(sT.h, tv)
                sc.activation(s1b[:, :], psS4[0:128, 0:OWN], AF.Silu, bias=bsc[:, _s(o, 1)])
                sl = self.sS.inc(sc.drain())
                gp.wait_ge(sS.h, sl)
                l3 = ld_ws("ws3", o, par)
                te.wait_ge(sE[par].h, l3)
                for (off, n) in [(i * 512, 512) for i in range(4)]:
                    for dc in range(8):
                        ins = te.matmul(psS4[:, _s(off, n)], wsb[par][:, _s(dc * 128, 128)],
                                        x3[dc][:, _s(off, n)], start=(dc == 0), stop=(dc == 7))
                tv = sT.inc(ins)
                ve.wait_ge(sT.h, tv)
                ve.scalar_tensor_tensor(hs[o][:, :], psS4[0:128, 0:OWN],
                                        bsc[:, _s(16 + o, 1)], s1b[:, :],
                                        ALU.add, ALU.mult)
                sV.inc(ve.drain())
            for o in range(8):
                par = o % 2
                gp.wait_ge(sT.h, sT.v)
                l2 = ld_ws("ws2", o, par)
                te.wait_ge(sE[par].h, l2)
                te.wait_ge(sV.h, sV.v)
                for (off, n) in [(i * 512, 512) for i in range(4)]:
                    for dc in range(8):
                        ins = te.matmul(psS4[:, _s(off, n)], wsb[par][:, _s(dc * 128, 128)],
                                        hs[dc][:, _s(off, n)], start=(dc == 0), stop=(dc == 7))
                tv = sT.inc(ins)
                ve.wait_ge(sT.h, tv)
                ve.scalar_tensor_tensor(r3[o][:, :], psS4[0:128, 0:OWN],
                                        bsc[:, _s(8 + o, 1)], r3[o][:, :],
                                        ALU.add, ALU.add)
                sV.inc(ve.drain())
            te.wait_ge(sV.h, sV.v)
            self.sT.inc(te.nop())
            if self.debug:
                sy.wait_ge(sV.h, sV.v)
                for dc in range(8):
                    self.sO.inc(sy.dma_start(out=self.dbg["d_z"][_s(dc * 128, 128), :],
                                             in_=r3[dc][:, :]), 16)
        self.qbarrier(gp, sy, te, ve, sc)
        with ExitStack() as es_:
            def SE(name, p, f, dt):
                return es_.enter_context(nc.sbuf_tensor(name, [p, f], dt))
            we1e = [SE("we1e0", 128, 4096, BF16), SE("we1e1", 128, 4096, BF16)]
            we3e = [SE("we3e0", 128, 4096, BF16), SE("we3e1", 128, 4096, BF16)]
            we2e = SE("we2e", 128, 4096, BF16)
            heg = SE("heg", 128, 2048, BF16)
            s1e = SE("s1e", 128, 512, BF16)
            cbce = SE("cbce", 128, 512, BF16)
            ph = es_.enter_context(nc.psum_tensor("ph4", [128, 512], F32))
            pcb = es_.enter_context(nc.psum_tensor("pcb", [128, 512], F32))
            pp4 = [es_.enter_context(nc.psum_tensor("pp4_0", [128, 512], F32)),
                   es_.enter_context(nc.psum_tensor("pp4_1", [128, 512], F32))]

            def ld_e(nm, e, dstt, sem):
                srcv = d[nm][e].rearrange("(dc r) c -> r dc c", r=128)
                cw_ = srcv.shape[2] if hasattr(srcv, "shape") else None
                ncol = 512 if nm in ("We1", "We3") else 1024
                dstv = dstt[:].rearrange("r (dc c) -> r dc c", c=ncol)
                return sem.inc(gp.dma_start(out=dstv, in_=srcv), 16)

            l1 = [0, 0]
            l3 = [0, 0]
            l2 = 0
            l1[0] = ld_e("We1", 0, we1e[0], sE[0])
            l3[0] = ld_e("We3", 0, we3e[0], sE[0])
            l2 = ld_e("We2", 0, we2e, sE2[0])
            for e in range(NE):
                par = e % 2
                if e + 1 < NE:
                    gp.wait_ge(sT.h, sT.v)
                    l1[1 - par] = ld_e("We1", e + 1, we1e[1 - par], sE[1 - par])
                    l3[1 - par] = ld_e("We3", e + 1, we3e[1 - par], sE[1 - par])
                te.wait_ge(sE[par].h, max(l1[par], l3[par]))
                te.wait_ge(sE2[par if False else 0].h, l2)
                for tg in range(4):
                    to = tg * 512
                    te.wait_ge(sV.h, sV.v)
                    ins = te.matmul(pcb[:, 0:512], e16b[:, _s(e * 128, 128)],
                                    cmT[:, _s(to, 512)], start=True, stop=True)
                    tv = sT.inc(ins)
                    ve.wait_ge(sT.h, tv)
                    ve.tensor_copy(cbce[:, :], pcb[0:128, 0:512])
                    cbv = sV.inc(ve.drain())
                    for ich in range(4):
                        if not (tg == 0 and ich == 0):
                            te.wait_ge(sV.h, sV.v)
                        for dc in range(8):
                            ins = te.matmul(ph[:, 0:512],
                                            we1e[par][:, _s(dc * 512 + ich * 128, 128)],
                                            x3[dc][:, _s(to, 512)],
                                            start=(dc == 0), stop=(dc == 7))
                        tv = sT.inc(ins)
                        sc.wait_ge(sT.h, tv)
                        sc.wait_ge(sV.h, sV.v)
                        sc.activation(s1e[:, :], ph[0:128, 0:512], AF.Silu,
                                      bias=be1c[:, _s(e * 4 + ich, 1)])
                        sl = self.sS.inc(sc.drain())
                        te.wait_ge(sS.h, sl)
                        for dc in range(8):
                            ins = te.matmul(ph[:, 0:512],
                                            we3e[par][:, _s(dc * 512 + ich * 128, 128)],
                                            x3[dc][:, _s(to, 512)],
                                            start=(dc == 0), stop=(dc == 7))
                        tv = sT.inc(ins)
                        ve.wait_ge(sT.h, tv)
                        ve.wait_ge(sV.h, cbv)
                        ve.scalar_tensor_tensor(heg[:, _s(ich * 512, 512)], ph[0:128, 0:512],
                                                be3c[:, _s(e * 4 + ich, 1)], s1e[:, :],
                                                ALU.add, ALU.mult)
                        ve.drain()
                        ve.tensor_mul(heg[:, _s(ich * 512, 512)], heg[:, _s(ich * 512, 512)],
                                      cbce[:, :])
                        sV.inc(ve.drain())
                    te.wait_ge(sV.h, sV.v)
                    ev = 0
                    for o in range(8):
                        op = o % 2
                        if o >= 2:
                            te.wait_ge(sV.h, ev)
                        for ich in range(4):
                            ins = te.matmul(pp4[op][:, 0:512],
                                            we2e[:, _s(ich * 1024 + o * 128, 128)],
                                            heg[:, _s(ich * 512, 512)],
                                            start=(ich == 0), stop=(ich == 3))
                        tv = sT.inc(ins)
                        ve.wait_ge(sT.h, tv)
                        ve.tensor_add(r3[o][:, _s(to, 512)], r3[o][:, _s(to, 512)],
                                      pp4[op][0:128, 0:512])
                        ev = sV.inc(ve.drain())
                if e + 1 < NE:
                    gp.wait_ge(sT.h, sT.v)
                    l2 = ld_e("We2", e + 1, we2e, sE2[0])
            te.wait_ge(sV.h, sV.v)
            for o in range(8):
                op = o % 2
                for tg in range(4):
                    ins = te.matmul(pp4[op][:, 0:512], be2b[:, _s(o * 128, 128)],
                                    cmT[:, _s(tg * 512, 512)], start=True, stop=True)
                    tv = sT.inc(ins)
                    ve.wait_ge(sT.h, tv)
                    ve.tensor_add(r3[o][:, _s(tg * 512, 512)], r3[o][:, _s(tg * 512, 512)],
                                  pp4[op][0:128, 0:512])
                    sv = sV.inc(ve.drain())
                    te.wait_ge(sV.h, sv)
            te.wait_ge(sV.h, sV.v)
            self.sT.inc(te.nop())
        self.qbarrier(gp, sy, te, ve, sc)
        with ExitStack() as ls_:
            psL4 = ls_.enter_context(nc.psum_tensor("psL4", [128, OWN], F32))
            scr4 = ls_.enter_context(nc.sbuf_tensor("scr4", [128, OWN], BF16))
            self.emit_ln(r3, 0, OWN, psL4, scr4)
        st4.close()

    def phase5(self):
        nc, te, ve, sc, gp, sy = self.nc, self.te, self.ve, self.sc, self.gp, self.sy
        sT, sV, sO = self.sT, self.sV, self.sO
        r3 = self.r3
        self.qbarrier(gp, sy, te, ve, sc)
        sY = [Sem(nc, self.st, "sY0"), Sem(nc, self.st, "sY1")]
        self.extra_sems += sY
        with ExitStack() as os_:
            pst = [os_.enter_context(nc.psum_tensor(f"pst{i}", [128, 128], F32))
                   for i in range(4)]
            ysb = [os_.enter_context(nc.sbuf_tensor(f"ysb{i}", [128, 1024], F32))
                   for i in range(2)]
            dma_ct = [0, 0]
            for t in range(16):
                par = t % 2
                if t >= 2:
                    ve.wait_ge(sY[par].h, dma_ct[par])
                for w in range(2):
                    te.wait_ge(sV.h, sV.v)
                    for i in range(4):
                        dc = w * 4 + i
                        ins = te.transpose(pst[i][:, 0:128], r3[dc][:, _s(t * 128, 128)],
                                           self.ident_f[:])
                    tv = sT.inc(ins)
                    ve.wait_ge(sT.h, tv)
                    for i in range(4):
                        dc = w * 4 + i
                        ve.tensor_copy(ysb[par][:, _s(dc * 128, 128)], pst[i][:, 0:128])
                    sV.inc(ve.drain())
                sy.wait_ge(sV.h, sV.v)
                dma_ct[par] = sY[par].inc(
                    sy.dma_start(out=self.y_out[_s(t * 128, 128), :], in_=ysb[par][:, :]), 16)
            sy.wait_ge(sY[0].h, sY[0].v)
            sy.wait_ge(sY[1].h, sY[1].v)

    def debug_tail(self):
        gp, sy = self.gp, self.sy
        z = self.st.enter_context(self.nc.sbuf_tensor("zout", [128, D], F32))
        gp.wait_ge(self.sG.h, self.sG.v)
        gp.wait_ge(self.sW.h, self.sW.v)
        gp.wait_ge(self.sX.h, self.sX.v)
        gp.wait_ge(self.sO.h, self.sO.v)
        gp.wait_ge(self.sV.h, self.sV.v)
        gp.wait_ge(self.sS.h, self.sS.v)
        gp.wait_ge(self.sT.h, self.sT.v)
        gp.memset(z[:], 0.0)
        self.sG.inc(gp.drain())
        sy.wait_ge(self.sG.h, self.sG.v)
        sy.wait_ge(self.sV.h, self.sV.v)
        sy.wait_ge(self.sS.h, self.sS.v)
        sy.wait_ge(self.sT.h, self.sT.v)
        for t in range(16):
            self.sO.inc(sy.dma_start(out=self.y_out[_s(t * 128, 128), :], in_=z[:]), 16)
        sy.wait_ge(self.sO.h, self.sO.v)


def stage_inputs(inputs):
    x = np.asarray(inputs["L0_attn_query"], np.float32)
    perm_wp = np.empty(D, np.int64)
    for h in range(16):
        for e in range(HE):
            perm_wp[h * HE + e] = e * 16 + h
    base = {}
    base["wqT"] = np.ascontiguousarray(np.asarray(inputs["L0_attn_Wq"], np.float32).transpose(1, 0, 2).reshape(D, D))
    base["wkT"] = np.ascontiguousarray(np.asarray(inputs["L0_attn_Wk"], np.float32).transpose(1, 0, 2).reshape(D, D))
    base["wvT"] = np.ascontiguousarray(np.asarray(inputs["L0_attn_Wv"], np.float32).transpose(1, 0, 2).reshape(D, D))
    base["wpT"] = np.ascontiguousarray(np.asarray(inputs["L0_attn_Wp"], np.float32)[perm_wp, :])
    base["bp_r"] = np.ascontiguousarray(np.asarray(inputs["L0_attn_bp"], np.float32).reshape(8, 128).T)
    base["w1T"] = np.ascontiguousarray(np.asarray(inputs["L0_ffn_w1"], np.float32).T)
    base["cw"] = np.ascontiguousarray(np.asarray(inputs["L0_ffn_conv_w"], np.float32).reshape(32, 128, 3).transpose(1, 0, 2).reshape(128, 96))
    base["cb"] = np.ascontiguousarray(np.asarray(inputs["L0_ffn_conv_b"], np.float32).reshape(32, 128).T)
    base["w2T"] = np.ascontiguousarray(np.asarray(inputs["L0_ffn_w2"], np.float32).T)
    base["qw1T"] = np.ascontiguousarray(np.asarray(inputs["L1_attn_q_w1"], np.float32).T)
    base["qb1_r"] = np.ascontiguousarray(np.asarray(inputs["L1_attn_q_b1"], np.float32).reshape(8, 128).T)
    base["kw1T"] = np.ascontiguousarray(np.asarray(inputs["L1_attn_k_w1"], np.float32).T)
    base["kb1_r"] = np.ascontiguousarray(np.asarray(inputs["L1_attn_k_b1"], np.float32).reshape(8, 128).T)
    base["pwT"] = np.ascontiguousarray(np.asarray(inputs["L1_attn_p_w"], np.float32).T)
    base["pb_r"] = np.ascontiguousarray(np.asarray(inputs["L1_attn_p_b"], np.float32).reshape(8, 128).T)
    base["gwT"] = np.ascontiguousarray(np.asarray(inputs["L1_ffn_gate_w"], np.float32).T)
    base["gb_row"] = np.ascontiguousarray(np.asarray(inputs["L1_ffn_gate_b"], np.float32).reshape(1, 16))
    base["We1"] = np.ascontiguousarray(np.asarray(inputs["L1_ffn_We1"], np.float32))
    base["be1_c"] = np.ascontiguousarray(
        np.asarray(inputs["L1_ffn_be1"], np.float32).reshape(NE, 4, 128).transpose(2, 0, 1).reshape(128, 64))
    base["We2"] = np.ascontiguousarray(np.asarray(inputs["L1_ffn_We2"], np.float32))
    base["be2"] = np.ascontiguousarray(np.asarray(inputs["L1_ffn_be2"], np.float32))
    base["We3"] = np.ascontiguousarray(np.asarray(inputs["L1_ffn_We3"], np.float32))
    base["be3_c"] = np.ascontiguousarray(
        np.asarray(inputs["L1_ffn_be3"], np.float32).reshape(NE, 4, 128).transpose(2, 0, 1).reshape(128, 64))
    e16 = np.zeros((16, 2048), np.float32)
    for e in range(16):
        e16[e, e * 128:(e + 1) * 128] = 1.0
    base["E16"] = e16
    base["ws1"] = np.ascontiguousarray(np.asarray(inputs["L1_ffn_ws1"], np.float32))
    base["bs1_r"] = np.ascontiguousarray(np.asarray(inputs["L1_ffn_bs1"], np.float32).reshape(8, 128).T)
    base["ws2"] = np.ascontiguousarray(np.asarray(inputs["L1_ffn_ws2"], np.float32))
    base["bs2_r"] = np.ascontiguousarray(np.asarray(inputs["L1_ffn_bs2"], np.float32).reshape(8, 128).T)
    base["ws3"] = np.ascontiguousarray(np.asarray(inputs["L1_ffn_ws3"], np.float32))
    base["bs3_r"] = np.ascontiguousarray(np.asarray(inputs["L1_ffn_bs3"], np.float32).reshape(8, 128).T)
    base["ident"] = np.eye(128, dtype=np.float32)
    base["ones_row"] = np.ones((1, 128), np.float32)
    e2a = np.zeros((128, 66), np.float32)
    e2a[0:64, 64] = 1.0
    e2b = np.zeros((128, 66), np.float32)
    e2b[64:128, 64] = 1.0
    base["e2a"], base["e2b"] = e2a, e2b
    sw = np.zeros((128, 128), np.float32)
    sw[np.arange(64), np.arange(64) + 64] = 1.0
    sw[np.arange(64) + 64, np.arange(64)] = 1.0
    base["swap64"] = sw

    qw2T = np.asarray(inputs["L1_attn_q_w2"], np.float32).T
    kw2T = np.asarray(inputs["L1_attn_k_w2"], np.float32).T
    vwT = np.asarray(inputs["L1_attn_v_w"], np.float32).T
    gwT = np.asarray(inputs["L1_attn_g_w"], np.float32).T
    qb2 = np.asarray(inputs["L1_attn_q_b2"], np.float32)
    kb2 = np.asarray(inputs["L1_attn_k_b2"], np.float32)
    vb = np.asarray(inputs["L1_attn_v_b"], np.float32)
    gb = np.asarray(inputs["L1_attn_g_b"], np.float32)

    maps = []
    for c in range(8):
        b, half = c // 2, c % 2
        m = dict(base)
        xb = x[b]
        m["xTf"] = np.ascontiguousarray(xb.T)
        w = np.zeros((W, D), np.float32)
        s0 = half * OWN
        lo, hi = s0 - 1, s0 + OWN + 1
        cl, ch = max(lo, 0), min(hi, NT)
        w[cl - lo: cl - lo + (ch - cl)] = xb[cl:ch]
        m["xTw"] = np.ascontiguousarray(w.T)
        mk = np.zeros((128, 2), np.float32)
        mk[:, 0] = 0.0 if lo < 0 else 1.0
        mk[:, 1] = 0.0 if hi > NT else 1.0
        m["masks"] = mk
        hs = slice(half * 512, half * 512 + 512)
        m["qw2Ts"] = np.ascontiguousarray(qw2T[:, hs])
        m["kw2Ts"] = np.ascontiguousarray(kw2T[:, hs])
        m["vwTs"] = np.ascontiguousarray(vwT[:, hs])
        m["gwTs"] = np.ascontiguousarray(gwT[:, hs])
        m["qb2_row"] = np.ascontiguousarray(qb2[hs].reshape(1, 512))
        m["kb2_row"] = np.ascontiguousarray(kb2[hs].reshape(1, 512))
        m["vb_s"] = np.ascontiguousarray(vb[hs].reshape(4, 128).T)
        m["gb_s"] = np.ascontiguousarray(gb[hs].reshape(4, 128).T)
        maps.append(m)
    return maps


def kernel(**inputs):
    maps = stage_inputs(inputs)
    nc = Prog(upto=99).build()
    res = run_bass_kernel_spmd(nc, maps, core_ids=list(range(8)))
    out = np.empty((4, NT, D), np.float32)
    for c in range(8):
        b, half = c // 2, c % 2
        out[b, half * OWN:(half + 1) * OWN, :] = res.results[c]["y"]
    return out


# version 02
# speedup vs baseline: 1.2142x; 1.2142x over previous
import numpy as np
import ml_dtypes
from contextlib import ExitStack

import concourse.bass as bass
import concourse.mybir as mybir
from concourse.bass_utils import run_bass_kernel_spmd

F32 = mybir.dt.float32
BF16 = mybir.dt.bfloat16
AF = mybir.ActivationFunctionType
ALU = mybir.AluOpType
AX = mybir.AxisListType.X

MAGIC = 12582912.0
D = 1024
NT = 4096
OWN = 2048
W = 2050
HE = 64
NE = 16
LNEPS = 1e-5
WSL = [(0, 512), (512, 512), (1024, 512), (1536, 512), (2048, 2)]
NSL = [(i * 512, 512) for i in range(8)]


class Sem:
    def __init__(self, nc, st, name):
        self.h = st.enter_context(nc.semaphore(name))
        self.v = 0

    def inc(self, instr, k=1):
        instr.then_inc(self.h, k)
        self.v += k
        return self.v

    def wait(self, eng, v=None):
        eng.wait_ge(self.h, self.v if v is None else v)


def _s(lo, n):
    return slice(lo, lo + n)


class Prog:

    def __init__(self, upto=99, debug=0, p1_pairs=8, p1_proj=True, p1_sect=9, p3_sect=9):
        self.upto = upto
        self.debug = debug
        self.p1_pairs = p1_pairs
        self.p1_proj = p1_proj
        self.p1_sect = p1_sect
        self.p1_norm = 3
        self.p3_sect = p3_sect
        nc = self.nc = bass.Bass("TRN2", target_bir_lowering=False)
        self.te, self.ve, self.sc, self.gp, self.sy = (
            nc.tensor, nc.vector, nc.scalar, nc.gpsimd, nc.sync)
        self.st = ExitStack()
        self.din = {}
        self._declare_dram()
        self._alloc_common()

    def _declare_dram(self):
        nc = self.nc

        def DI(name, shape):
            self.din[name] = nc.dram_tensor(name, list(shape), F32, kind="ExternalInput").ap()

        DI("xTf", (D, NT)); DI("xTw", (D, W)); DI("masks", (128, 2))
        DI("wqT", (D, D)); DI("wkT", (D, D)); DI("wvT", (D, D)); DI("wpT", (D, D)); DI("bp_r", (128, 8))
        DI("w1T", (D, 4096)); DI("cw", (128, 96)); DI("cb", (128, 32)); DI("w2T", (4096, D))
        DI("qw1T", (D, D)); DI("qb1_r", (128, 8)); DI("kw1T", (D, D)); DI("kb1_r", (128, 8))
        DI("qw2Ts", (D, 512)); DI("qb2_row", (1, 512)); DI("kw2Ts", (D, 512)); DI("kb2_row", (1, 512))
        DI("vwTs", (D, 512)); DI("vb_s", (128, 4)); DI("gwTs", (D, 512)); DI("gb_s", (128, 4))
        DI("pwT", (D, D)); DI("pb_r", (128, 8))
        DI("gwT", (D, 16)); DI("gb_row", (1, 16))
        DI("We1", (NE, D, 512)); DI("be1_c", (128, 64)); DI("We2", (NE, 512, D)); DI("be2", (NE, D))
        DI("We3", (NE, D, 512)); DI("be3_c", (128, 64)); DI("E16", (16, 2048))
        DI("ws1", (D, D)); DI("bs1_r", (128, 8)); DI("ws2", (D, D)); DI("bs2_r", (128, 8))
        DI("ws3", (D, D)); DI("bs3_r", (128, 8))
        DI("ident", (128, 128)); DI("ones_row", (1, 128)); DI("e2a", (128, 66)); DI("e2b", (128, 66))
        DI("swap64", (128, 128))

        self.y_out = nc.dram_tensor("y", [OWN, D], F32, kind="ExternalOutput").ap()
        self.dbg = {}
        if self.debug:
            self.dbg["d_r1"] = nc.dram_tensor("d_r1", [D, W], F32, kind="ExternalOutput").ap()
            self.dbg["d_att"] = nc.dram_tensor("d_att", [D, W], F32, kind="ExternalOutput").ap()
            self.dbg["d_r2"] = nc.dram_tensor("d_r2", [D, W], F32, kind="ExternalOutput").ap()
            self.dbg["d_r3"] = nc.dram_tensor("d_r3", [D, OWN], F32, kind="ExternalOutput").ap()
            self.dbg["d_cmb"] = nc.dram_tensor("d_cmb", [16, OWN], F32, kind="ExternalOutput").ap()
            self.dbg["d_z"] = nc.dram_tensor("d_z", [D, OWN], F32, kind="ExternalOutput").ap()

        self.w1q_d = nc.dram_tensor("w1q_d", [32, 8, 128, 128], BF16).ap()
        self.w2q_d = nc.dram_tensor("w2q_d", [8, 32, 128, 128], BF16).ap()
        self.h2_d = nc.dram_tensor("h2_d", [32, 128, OWN], BF16).ap()
        self.x2own_d = nc.dram_tensor("x2own_d", [8, 128, OWN], F32).ap()
        self.cc_in = nc.dram_tensor("cc_in", [D, OWN], BF16).ap()
        self.cc_out = nc.dram_tensor("cc_out", [2 * D, OWN], BF16).ap()

    def _alloc_common(self):
        nc, st = self.nc, self.st
        E = st.enter_context

        def SB(name, p, f, dt, side="left"):
            return E(nc.sbuf_tensor(name, [p, f], dt, side=side))
        self.SB = SB

        self.ident_f = SB("ident_f", 128, 128, F32)
        self.ident_b = SB("ident_b", 128, 128, BF16)
        self.ones_row_f = SB("ones_row_f", 1, 128, F32)
        self.ones_row_b = SB("ones_row_b", 1, 128, BF16)
        self.ones_col_b = SB("ones_col_b", 128, 1, BF16)
        self.ones_col_f = SB("ones_col_f", 128, 1, F32)
        self.e2a = SB("e2a_sb", 128, 66, BF16)
        self.e2b = SB("e2b_sb", 128, 66, BF16)
        self.okk = SB("okk", 66, 128, BF16)
        self.swap64 = SB("swap64_sb", 128, 128, BF16)
        self.masks_sb = SB("masks_sb", 128, 2, F32)
        self.bp_sb = SB("bp_sb", 128, 8, F32)
        self.cw_sb = SB("cw_sb", 128, 96, F32)
        self.cb_sb = SB("cb_sb", 128, 32, F32)
        self.sc0 = SB("sc0", 1, 16, F32)
        self.pr = SB("pr", 1, 6528, F32, side="right")
        self.PRA, self.PRB, self.PRC = 0, 2176, 4352

        self.sW = Sem(nc, st, "sW")
        self.sX = Sem(nc, st, "sX")
        self.sT = Sem(nc, st, "sT")
        self.sV = Sem(nc, st, "sV")
        self.sS = Sem(nc, st, "sS")
        self.sG = Sem(nc, st, "sG")
        self.sO = Sem(nc, st, "sO")

        sy, gp = self.sy, self.gp
        d = self.din
        for src, dst in (("ident", self.ident_f), ("ones_row", self.ones_row_f),
                         ("masks", self.masks_sb), ("bp_r", self.bp_sb),
                         ("cw", self.cw_sb), ("cb", self.cb_sb)):
            self.sX.inc(sy.dma_start(out=dst[:], in_=d[src]), 16)
        for src, dst in (("ident", self.ident_b), ("ones_row", self.ones_row_b),
                         ("e2a", self.e2a), ("e2b", self.e2b), ("swap64", self.swap64)):
            self.sW.inc(gp.dma_start(out=dst[:], in_=d[src]), 16)
        gp.wait_ge(self.sW.h, self.sW.v)
        gp.memset(self.ones_col_b[:], 1.0)
        gp.memset(self.ones_col_f[:], 1.0)
        gp.memset(self.okk[0:1, :], 1.0)
        gp.memset(self.okk[64:65, :], 1.0)
        self._consts = {}
        for v in (LNEPS, MAGIC, -MAGIC, 1.0, -1.0, -1.5):
            t = SB(f"c_{len(self._consts)}", 128, 1, F32)
            gp.memset(t[:], float(v))
            nc.const_aps.register(F32, float(v), t[:])
            self._consts[float(v)] = t
        self.sG.inc(self.gp.drain())

    def build(self):
        lp = self.nc.allow_low_precision("bf16 matmul operands are intentional")
        lp.__enter__()
        if self.upto >= 1:
            self.phase1()
        if self.upto >= 2:
            self.phase2()
        if self.upto >= 3:
            self.phase3()
        if self.upto >= 4:
            self.phase4()
            self.phase5()
        if self.upto < 4:
            self.debug_tail()
        self.st.close()
        return self.nc

    def qbarrier(self, *engs):
        sems = [self.sT, self.sV, self.sS, self.sG, self.sW, self.sX, self.sO]
        sems += getattr(self, "extra_sems", [])
        for e in engs:
            for s in sems:
                e.wait_ge(s.h, s.v)

    def emit_ln(self, x_tiles, off, width, ps_big, scr_b):
        te, ve, sc = self.te, self.ve, self.sc
        sT, sV, sS = self.sT, self.sV, self.sS
        pr, PRA, PRB, PRC = self.pr, self.PRA, self.PRB, self.PRC
        sls = [(o, n) for (o, n) in
               [(i * 512, min(512, width - i * 512)) for i in range((width + 511) // 512)]]
        for which, fn, seg in ((0, AF.Identity, PRA), (1, AF.Square, PRB)):
            for dc in range(8):
                sc.wait_ge(sT.h, sT.v)
                sc.activation(scr_b[:, 0:width], x_tiles[dc][:, _s(off, width)], fn)
                self.sS.inc(sc.drain())
                te.wait_ge(sS.h, sS.v)
                if dc == 0:
                    te.wait_ge(sV.h, sV.v)
                for (o, n) in sls:
                    ins = te.matmul(ps_big[0:1, _s(o, n)], self.ones_col_b[:],
                                    scr_b[:, _s(o, n)], start=(dc == 0), stop=(dc == 7))
                sT.inc(ins)
            ve.wait_ge(sT.h, sT.v)
            ins = ve.tensor_copy(pr[0:1, _s(seg, width)], ps_big[0:1, 0:width])
            sV.inc(ve.drain())
        ve.tensor_scalar_mul(pr[0:1, _s(PRA, width)], pr[0:1, _s(PRA, width)], 1.0 / 1024.0)
        ve.tensor_scalar_mul(pr[0:1, _s(PRB, width)], pr[0:1, _s(PRB, width)], 1.0 / 1024.0)
        sV.inc(ve.drain())
        sc.wait_ge(sV.h, sV.v)
        sc.activation(pr[0:1, _s(PRC, width)], pr[0:1, _s(PRA, width)], AF.Square)
        self.sS.inc(sc.drain())
        ve.wait_ge(sS.h, sS.v)
        ve.tensor_sub(pr[0:1, _s(PRB, width)], pr[0:1, _s(PRB, width)], pr[0:1, _s(PRC, width)])
        sV.inc(ve.drain())
        sc.wait_ge(sV.h, sV.v)
        sc.activation(pr[0:1, _s(PRC, width)], pr[0:1, _s(PRB, width)], AF.Sqrt, bias=LNEPS)
        self.sS.inc(sc.drain())
        ve.wait_ge(sS.h, sS.v)
        ve.reciprocal(pr[0:1, _s(PRB, width)], pr[0:1, _s(PRC, width)])
        sV.inc(ve.drain())
        for seg, op in ((PRA, ALU.subtract), (PRB, ALU.mult)):
            te.wait_ge(sV.h, sV.v)
            for (o, n) in sls:
                ins = te.matmul(ps_big[0:128, _s(o, n)], self.ones_row_f[:],
                                pr[0:1, _s(seg + o, n)], start=True, stop=True)
            sT.inc(ins)
            ve.wait_ge(sT.h, sT.v)
            for dc in range(8):
                ins = ve.tensor_tensor(x_tiles[dc][:, _s(off, width)],
                                       x_tiles[dc][:, _s(off, width)],
                                       ps_big[0:128, 0:width], op)
            sV.inc(ve.drain())
        te.wait_ge(sV.h, sV.v)
        self.sT.inc(te.nop())

    def phase1(self):
        nc, te, ve, sc, gp, sy = self.nc, self.te, self.ve, self.sc, self.gp, self.sy
        d, SB = self.din, self.SB
        sW, sX, sT, sV, sS, sG = self.sW, self.sX, self.sT, self.sV, self.sS, self.sG

        p1st = ExitStack()
        att = [p1st.enter_context(nc.sbuf_tensor(f"att{i}", [128, W], BF16))
               for i in range(8)]

        with ExitStack() as ph:
            EE = ph.enter_context

            def S(name, p, f, dt):
                return EE(nc.sbuf_tensor(name, [p, f], dt))

            xTf_b = [S(f"xTf{i}", 128, NT, BF16) for i in range(8)]
            xTw_b = [S(f"xTw{i}", 128, W, BF16) for i in range(8)]
            kpA = S("kpA", 66, NT, BF16)
            kpB = S("kpB", 66, NT, BF16)
            qpA = S("qpA", 66, W, BF16)
            qpB = S("qpB", 66, W, BF16)
            vg = S("vg", 128, 32 * 130, BF16)
            pt = [S("pt0", 128, 1024, BF16), S("pt1", 128, 1024, BF16)]
            wq_p = S("wq_p", 128, 1024, BF16)
            wk_p = S("wk_p", 128, 1024, BF16)
            wv_p = S("wv_p", 128, 1024, BF16)
            sq_b = S("sq_b", 128, 512, BF16)
            n64 = S("n64", 66, 512, BF16)
            t64 = S("t64", 66, 16, F32)
            rb = S("rb", 64, 1024, BF16)
            osc = S("osc", 64, W, BF16)

            ps_S = [EE(nc.psum_tensor("ps_S0", [128, 1024], F32)),
                    EE(nc.psum_tensor("ps_S1", [128, 1024], F32))]
            ps_av = EE(nc.psum_tensor("ps_av", [66, 1024], F32))
            ps_x = [EE(nc.psum_tensor("ps_x0", [128, 512], F32)),
                    EE(nc.psum_tensor("ps_x1", [128, 512], F32))]

            for dc in range(8):
                sW.inc(gp.dma_start(out=xTf_b[dc][:], in_=d["xTf"][_s(dc * 128, 128), :]), 16)
                sW.inc(gp.dma_start(out=xTw_b[dc][:], in_=d["xTw"][_s(dc * 128, 128), :]), 16)
            ld_x_done = sW.v
            ve.memset(kpA[64:65, 0:NT], 1.0)
            ve.memset(kpB[64:65, 0:NT], 1.0)
            sV.inc(ve.drain())

            def load_pair_w(p):
                for nm, dst in (("wqT", wq_p), ("wkT", wk_p)):
                    src = d[nm][:, _s(p * 128, 128)].rearrange("(dc r) c -> r dc c", r=128)
                    dstv = dst[:].rearrange("r (dc c) -> r dc c", c=128)
                    sW.inc(gp.dma_start(out=dstv, in_=src), 16)
                return sW.v

            def load_pair_wv(p):
                src = d["wvT"][:, _s(p * 128, 128)].rearrange("(dc r) c -> r dc c", r=128)
                dstv = wv_p[:].rearrange("r (dc c) -> r dc c", c=128)
                sW.inc(gp.dma_start(out=dstv, in_=src), 16)
                return sW.v

            vg_ev = [0, 0]
            kp_ev = [0, 0]
            kp_sq = [0, 0]
            dma_osc = 0

            for p in range(self.p1_pairs):
                wv_done = load_pair_wv(p)
                w_done = load_pair_w(p)

                gp.wait_ge(sW.h, sW.v)
                gp.wait_ge(sT.h, sT.v)
                gp.memset(vg[:, :], 1.0)
                self.sG.inc(gp.drain())
                te.wait_ge(sG.h, sG.v)
                te.wait_ge(sV.h, sV.v)
                for kt in range(32):
                    par = kt % 2
                    if kt >= 2:
                        te.wait_ge(sV.h, vg_ev[par])
                    for dc in range(8):
                        ins = te.matmul(ps_x[par][:, 0:128],
                                        xTf_b[dc][:, _s(kt * 128, 128)],
                                        wv_p[:, _s(dc * 128, 128)],
                                        start=(dc == 0), stop=(dc == 7))
                    tv = sT.inc(ins)
                    ve.wait_ge(sT.h, tv)
                    dst = vg[:, _s(kt * 130, 130)].rearrange("r (h c) -> r h c", c=65)[:, :, 0:64]
                    src = ps_x[par][:, 0:128].rearrange("r (h c) -> r h c", c=64)
                    ins = ve.tensor_copy(dst, src)
                    vg_ev[par] = sV.inc(ve.drain())

                if self.p1_sect < 2:
                    continue
                gp.wait_ge(sW.h, sW.v)
                self.sG.inc(gp.drain())
                ve.wait_ge(sG.h, sG.v)
                ve.memset(t64[64:65, 0:2], 0.0)
                sV.inc(ve.drain())
                te.wait_ge(sG.h, sG.v)
                te.wait_ge(sV.h, sV.v)
                for si, (o, n) in enumerate(NSL):
                    par = si % 2
                    if si >= 2:
                        te.wait_ge(sS.h, kp_sq[par])
                        te.wait_ge(sV.h, kp_ev[par])
                    for dc in range(8):
                        ins = te.matmul(ps_x[par][:, 0:n],
                                        wk_p[:, _s(dc * 128, 128)],
                                        xTf_b[dc][:, _s(o, n)],
                                        start=(dc == 0), stop=(dc == 7))
                    tv = sT.inc(ins)
                    ve.wait_ge(sT.h, tv)
                    ve.tensor_copy(kpA[0:64, _s(o, n)], ps_x[par][0:64, 0:n])
                    ve.tensor_copy(kpB[0:64, _s(o, n)], ps_x[par][64:128, 0:n])
                    kp_ev[par] = sV.inc(ve.drain())
                    if self.p1_norm >= 1:
                        sc.wait_ge(sT.h, tv)
                        sc.wait_ge(sV.h, kp_ev[par])
                        sc.activation(sq_b[:, 0:n], ps_x[par][:, 0:n], AF.Square)
                        kp_sq[par] = self.sS.inc(sc.drain())
                    if self.p1_norm >= 2:
                        te.wait_ge(sS.h, kp_sq[par])
                        if si >= 1:
                            te.wait_ge(sV.h, sV.v)
                        ins = te.matmul(ps_av[:, 0:n], self.e2a[:], sq_b[:, 0:n], start=True, stop=True)
                        ins = te.matmul(ps_av[:, _s(512, n)], self.e2b[:], sq_b[:, 0:n], start=True, stop=True)
                        tv = sT.inc(ins)
                    if self.p1_norm >= 3:
                        ve.wait_ge(sT.h, tv)
                        ve.reduce_max(t64[64:65, 2:3], ps_av[64:65, 0:n], axis=AX)
                        ve.reduce_max(t64[64:65, 3:4], ps_av[64:65, _s(512, n)], axis=AX)
                        ve.drain()
                        ve.tensor_max(t64[64:65, 0:1], t64[64:65, 0:1], t64[64:65, 2:3])
                        ve.tensor_max(t64[64:65, 1:2], t64[64:65, 1:2], t64[64:65, 3:4])
                        sV.inc(ve.drain())
                te.wait_ge(sV.h, sV.v)

                if self.p1_sect < 3:
                    continue
                for si, (o, n) in enumerate(WSL):
                    par = si % 2
                    for dc in range(8):
                        ins = te.matmul(ps_x[par][:, 0:n],
                                        wq_p[:, _s(dc * 128, 128)],
                                        xTw_b[dc][:, _s(o, n)],
                                        start=(dc == 0), stop=(dc == 7))
                    tv = sT.inc(ins)
                    ve.wait_ge(sT.h, tv)
                    ve.tensor_copy(qpA[0:64, _s(o, n)], ps_x[par][0:64, 0:n])
                    ve.tensor_copy(qpB[0:64, _s(o, n)], ps_x[par][64:128, 0:n])
                    sV.inc(ve.drain())
                    sc.wait_ge(sT.h, tv)
                    sc.wait_ge(sV.h, sV.v)
                    sc.activation(sq_b[:, 0:n], ps_x[par][:, 0:n], AF.Square)
                    sv = self.sS.inc(sc.drain())
                    te.wait_ge(sS.h, sv)
                    ins = te.matmul(ps_av[:, 0:n], self.e2a[:], sq_b[:, 0:n], start=True, stop=True)
                    ins = te.matmul(ps_av[:, _s(512, n)], self.e2b[:], sq_b[:, 0:n], start=True, stop=True)
                    tv = sT.inc(ins)
                    ve.wait_ge(sT.h, tv)
                    ve.tensor_copy(qpA[64:65, _s(o, n)], ps_av[64:65, 0:n])
                    ve.tensor_copy(qpB[64:65, _s(o, n)], ps_av[64:65, _s(512, n)])
                    sV.inc(ve.drain())
                    te.wait_ge(sV.h, sV.v)

                if self.p1_sect < 4:
                    continue
                sc.wait_ge(sV.h, sV.v)
                sc.activation(qpA[64:65, 0:W], qpA[64:65, 0:W], AF.Sqrt, scale=t64[64:65, 0:1])
                sc.drain()
                sc.activation(qpB[64:65, 0:W], qpB[64:65, 0:W], AF.Sqrt, scale=t64[64:65, 1:2])
                sv = self.sS.inc(sc.drain())
                ve.wait_ge(sS.h, sv)
                ve.tensor_scalar_mul(qpA[64:65, 0:W], qpA[64:65, 0:W], -1.0)
                ve.tensor_scalar_mul(qpB[64:65, 0:W], qpB[64:65, 0:W], -1.0)
                sV.inc(ve.drain())

                if self.p1_sect < 5:
                    continue
                for hh in range(2):
                    h = 2 * p + hh
                    vbase = hh

                    kpX = kpA if hh == 0 else kpB
                    qpX = qpA if hh == 0 else qpB

                    def s_mm(par, cl, cn, ql, qn):
                        return te.matmul(ps_S[par][:, _s(cl, cn)],
                                         kpX[0:65, _s(kt_ * 128, 128)],
                                         qpX[0:65, _s(ql, qn)], start=True, stop=True)

                    te.wait_ge(sV.h, sV.v)
                    if hh == 1:
                        te.wait_ge(sG.h, dma_osc)
                    for qh in range(2):
                        qo = qh * 1024
                        exp_ct = [0, 0]
                        av_last = 0

                        def emit_av(kt_a):
                            par_a = kt_a % 2
                            te.wait_ge(sS.h, exp_ct[par_a])
                            for ss_ in range(2):
                                ins_ = te.matmul(ps_av[0:65, _s(ss_ * 512, 512)],
                                                 vg[:, _s(kt_a * 130 + vbase * 65, 65)],
                                                 pt[par_a][:, _s(ss_ * 512, 512)],
                                                 start=(kt_a == 0), stop=(kt_a == 31))
                            return sT.inc(ins_)

                        for kt_ in range(32):
                            par = kt_ % 2
                            if kt_ >= 2:
                                te.wait_ge(sS.h, exp_ct[par])
                            for ss in range(2):
                                ins = s_mm(par, ss * 512, 512, qo + ss * 512, 512)
                            tv = sT.inc(ins)
                            sc.wait_ge(sT.h, tv)
                            sc.activation(pt[par][:, 0:1024], ps_S[par][:, 0:1024],
                                          AF.Exp, scale=0.125)
                            exp_ct[par] = self.sS.inc(sc.drain())
                            if kt_ >= 1:
                                av_last = emit_av(kt_ - 1)
                        av_last = emit_av(31)
                        ve.wait_ge(sT.h, av_last)
                        for ss in range(2):
                            if ss == 1:
                                ve.wait_ge(sT.h, tv)
                            ve.reciprocal(n64[64:65, 0:512],
                                          ps_av[64:65, _s(ss * 512, 512)])
                            sv = sV.inc(ve.drain())
                            te.wait_ge(sV.h, sv)
                            ins = te.matmul(ps_S[0][0:64, _s(ss * 512, 512)],
                                            self.okk[64:65, 0:64],
                                            n64[64:65, 0:512],
                                            start=True, stop=True)
                            tv = sT.inc(ins)
                        ve.wait_ge(sT.h, tv)
                        ve.tensor_copy(rb[:, 0:1024], ps_S[0][0:64, 0:1024])
                        ve.drain()
                        if hh == 0:
                            ins = ve.tensor_mul(att[h // 2][0:64, _s(qo, 1024)],
                                                ps_av[0:64, 0:1024], rb[:, 0:1024])
                        else:
                            ins = ve.tensor_mul(osc[:, _s(qo, 1024)],
                                                ps_av[0:64, 0:1024], rb[:, 0:1024])
                        sV.inc(ve.drain())
                        te.wait_ge(sV.h, sV.v)
                    if self.p1_sect < 6:
                        continue
                    for kt_ in range(32):
                        ins = te.matmul(ps_S[0][:, _s(kt_ * 2, 2)],
                                        kpX[0:65, _s(kt_ * 128, 128)],
                                        qpX[0:65, 2048:2050],
                                        start=(kt_ == 0), stop=(kt_ == 31))
                    tv = sT.inc(ins)
                    sc.wait_ge(sT.h, tv)
                    sc.activation(pt[0][:, 0:64], ps_S[0][:, 0:64], AF.Exp, scale=0.125)
                    sv = self.sS.inc(sc.drain())
                    te.wait_ge(sS.h, sv)
                    for kt_ in range(32):
                        ins = te.matmul(ps_av[0:65, 0:2],
                                        vg[:, _s(kt_ * 130 + vbase * 65, 65)],
                                        pt[0][:, _s(kt_ * 2, 2)],
                                        start=(kt_ == 0), stop=(kt_ == 31))
                    tv = sT.inc(ins)
                    ve.wait_ge(sT.h, tv)
                    ve.reciprocal(n64[64:65, _s(0, 2)], ps_av[64:65, 0:2])
                    sv = sV.inc(ve.drain())
                    te.wait_ge(sV.h, sv)
                    ins = te.matmul(ps_S[0][0:64, 0:2], self.okk[64:65, 0:64],
                                    n64[64:65, _s(0, 2)], start=True, stop=True)
                    tv = sT.inc(ins)
                    ve.wait_ge(sT.h, tv)
                    ve.tensor_copy(rb[:, 0:2], ps_S[0][0:64, 0:2])
                    ve.drain()
                    if hh == 0:
                        ins = ve.tensor_mul(att[h // 2][0:64, 2048:2050],
                                            ps_av[0:64, 0:2], rb[:, 0:2])
                    else:
                        ins = ve.tensor_mul(osc[:, 2048:2050], ps_av[0:64, 0:2], rb[:, 0:2])
                    sV.inc(ve.drain())
                    te.wait_ge(sV.h, sV.v)
                    if hh == 1:
                        gp.wait_ge(sV.h, sV.v)
                        sW.inc(gp.dma_start(out=att[h // 2][64:128, 0:W], in_=osc[:, 0:W]), 16)
                        gp.wait_ge(sW.h, sW.v)
                        dma_osc = self.sG.inc(gp.drain())

        if not self.p1_proj:
            if self.debug:
                gp.wait_ge(self.sV.h, self.sV.v)
                gp.wait_ge(self.sT.h, self.sT.v)
                for dc in range(8):
                    self.sW.inc(gp.dma_start(out=self.dbg["d_att"][_s(dc * 128, 128), :],
                                             in_=att[dc][:, 0:W]), 16)
            p1st.close()
            return
        self.xw_stack = ExitStack()
        xw = [self.xw_stack.enter_context(
            self.nc.sbuf_tensor(f"xw{i}", [128, W], F32, side="right")) for i in range(8)]
        self.xw = xw
        with ExitStack() as ph:
            EE = ph.enter_context
            wp_full = EE(nc.sbuf_tensor("wp_full", [128, 8192], BF16))
            scr_b = EE(nc.sbuf_tensor("p1scr", [128, W], BF16))
            ps_big = EE(nc.psum_tensor("ps_big", [128, 2560], F32))

            for eng in (sy, gp):
                eng.wait_ge(sT.h, sT.v)
                eng.wait_ge(sV.h, sV.v)
                eng.wait_ge(sS.h, sS.v)
                eng.wait_ge(sW.h, sW.v)
                eng.wait_ge(self.sG.h, self.sG.v)
            for dc in range(8):
                sX.inc(sy.dma_start(out=xw[dc][:], in_=d["xTw"][_s(dc * 128, 128), :]), 16)
            srcv = d["wpT"][:].rearrange("(dc r) c -> r dc c", r=128)
            dstv = wp_full[:].rearrange("r (dc c) -> r dc c", c=1024)
            wp_done = sW.inc(gp.dma_start(out=dstv, in_=srcv), 16)
            ld_x = sX.v
            proj_ev = 0
            te.wait_ge(sW.h, wp_done)
            for o in range(8):
                if o >= 1:
                    te.wait_ge(sV.h, proj_ev)
                for dc in range(8):
                    for (off, n) in WSL:
                        ins = te.matmul(ps_big[:, _s(off, n)],
                                        wp_full[:, _s(dc * 1024 + o * 128, 128)],
                                        att[dc][:, _s(off, n)], start=(dc == 0), stop=(dc == 7))
                tv = sT.inc(ins)
                ve.wait_ge(sT.h, tv)
                ve.wait_ge(sX.h, ld_x)
                ins = ve.scalar_tensor_tensor(xw[o][:, 0:W], ps_big[0:128, 0:W],
                                              self.bp_sb[:, _s(o, 1)], xw[o][:, 0:W],
                                              ALU.add, ALU.add)
                proj_ev = sV.inc(ve.drain())
            if self.debug:
                gp.wait_ge(self.sV.h, self.sV.v)
                for dc in range(8):
                    self.sW.inc(gp.dma_start(out=self.dbg["d_att"][_s(dc * 128, 128), :],
                                             in_=att[dc][:, 0:W]), 16)
            te.wait_ge(sV.h, sV.v)
            self.sT.inc(te.nop())
            self.emit_ln(xw, 0, W, ps_big, scr_b)
            if self.debug:
                sy.wait_ge(self.sV.h, self.sV.v)
                sy.wait_ge(self.sT.h, self.sT.v)
                for dc in range(8):
                    self.sO.inc(sy.dma_start(out=self.dbg["d_r1"][_s(dc * 128, 128), :],
                                             in_=xw[dc][:, 0:W]), 16)
        p1st.close()

    def phase2(self):
        nc, te, ve, sc, gp, sy = self.nc, self.te, self.ve, self.sc, self.gp, self.sy
        d = self.din
        sW, sX, sT, sV, sS, sG = self.sW, self.sX, self.sT, self.sV, self.sS, self.sG
        pr, PRA, PRB, PRC = self.pr, self.PRA, self.PRB, self.PRC
        xw = self.xw
        st2 = ExitStack()

        def S2(name, p, f, dt):
            return st2.enter_context(nc.sbuf_tensor(name, [p, f], dt))

        ps_big = st2.enter_context(nc.psum_tensor("p2big", [128, 2560], F32))
        ps_tr = st2.enter_context(nc.psum_tensor("p2tr", [128, 2176], BF16))
        sF = self.sF = [Sem(nc, self.st, "sF0"), Sem(nc, self.st, "sF1")]
        sH = [Sem(nc, self.st, "sH0"), Sem(nc, self.st, "sH1")]
        sH2 = [Sem(nc, self.st, "sH20"), Sem(nc, self.st, "sH21")]
        sW2 = [Sem(nc, self.st, "sW20"), Sem(nc, self.st, "sW21")]
        self.extra_sems = sF + sH + sH2 + sW2

        mcol = S2("mcol", 128, 4, F32)
        self.qbarrier(gp, sy, te, ve, sc)
        cwm = S2("cwm", 128, 96, F32)
        acc2 = S2("acc2", 128, OWN, BF16)
        am_cols = S2("am_cols", 128, 17, F32)
        amT = S2("amT", 17, 128, F32)
        gp.memset(am_cols[:], 0.0)
        self.sG.inc(gp.drain())

        def wstats_quant(nm, nrow, ncol, spill, spill_rear, mc):
            with ExitStack() as ws:
                wf = [ws.enter_context(nc.sbuf_tensor(f"wf_{nm}{i}", [128, ncol], F32))
                      for i in range(2)]
                wq = ws.enter_context(nc.sbuf_tensor(f"wqq_{nm}", [128, ncol], BF16))
                accc = ws.enter_context(nc.sbuf_tensor(f"accc_{nm}", [128, 2], F32))
                nch = nrow // 128
                sF = self.sF
                lded = [0, 0]
                for ch in range(2):
                    lded[ch] = sF[ch].inc(
                        sy.dma_start(out=wf[ch][:], in_=d[nm][_s(ch * 128, 128), :]), 16)
                for ch in range(nch):
                    par = ch % 2
                    ve.wait_ge(sF[par].h, lded[par])
                    ve.reduce_sum(accc[:, 1:2], wf[par][:, :], axis=AX,
                                  apply_absolute_value=True)
                    ve.drain()
                    if ch == 0:
                        ve.tensor_copy(accc[:, 0:1], accc[:, 1:2])
                    else:
                        ve.tensor_add(accc[:, 0:1], accc[:, 0:1], accc[:, 1:2])
                    sV.inc(ve.drain())
                    if ch + 2 < nch:
                        sy.wait_ge(sV.h, sV.v)
                        lded[par] = sF[par].inc(
                            sy.dma_start(out=wf[par][:], in_=d[nm][_s((ch + 2) * 128, 128), :]), 16)
                te.wait_ge(sV.h, sV.v)
                ins = te.matmul(ps_big[0:1, 0:1], self.ones_col_f[:], accc[:, 0:1],
                                start=True, stop=True)
                tv = sT.inc(ins)
                ve.wait_ge(sT.h, tv)
                ve.tensor_scalar(pr[0:1, 0:1], ps_big[0:1, 0:1], 1.0 / (nrow * ncol), 1e-5,
                                 ALU.mult, ALU.max)
                ve.drain()
                ve.reciprocal(pr[0:1, 1:2], pr[0:1, 0:1])
                sv = sV.inc(ve.drain())
                te.wait_ge(sV.h, sv)
                te.matmul(ps_big[0:128, 0:1], self.ones_row_f[:], pr[0:1, 0:1],
                          start=True, stop=True)
                ins = te.matmul(ps_big[0:128, 1:2], self.ones_row_f[:], pr[0:1, 1:2],
                                start=True, stop=True)
                tv = sT.inc(ins)
                ve.wait_ge(sT.h, tv)
                ve.tensor_copy(mcol[:, _s(mc, 2)], ps_big[0:128, 0:2])
                sv = sV.inc(ve.drain())
                prev_spill = self.sW.v
                for ch in range(nch):
                    par = ch % 2
                    sy.wait_ge(sV.h, sV.v)
                    lded[par] = sF[par].inc(
                        sy.dma_start(out=wf[par][:], in_=d[nm][_s(ch * 128, 128), :]), 16)
                    ve.wait_ge(sF[par].h, lded[par])
                    ve.tensor_scalar(wf[par][:], wf[par][:], mcol[:, _s(mc + 1, 1)], MAGIC,
                                     ALU.mult, ALU.add)
                    ve.drain()
                    ve.tensor_scalar(wf[par][:], wf[par][:], -MAGIC, None, ALU.add)
                    ve.drain()
                    ve.wait_ge(self.sW.h, prev_spill)
                    ve.tensor_scalar(wq[:, :], wf[par][:], -1.0, 1.0, ALU.max, ALU.min)
                    sv = sV.inc(ve.drain())
                    gp.wait_ge(sV.h, sv)
                    oap, iap = spill_rear(ch, wq)
                    prev_spill = self.sW.inc(gp.dma_start(out=oap, in_=iap), 16)
                gp.wait_ge(self.sW.h, self.sW.v)
                self.sG.inc(gp.drain())

        def w1_rear(ch, wq):
            return (self.w1q_d[:, ch].rearrange("jc r c -> r jc c"),
                    wq[:].rearrange("r (jc c) -> r jc c", c=128))

        def w2_rear(ch, wq):
            return (self.w2q_d[:, ch].rearrange("o r c -> r o c"),
                    wq[:].rearrange("r (o c) -> r o c", c=128))

        wstats_quant("w1T", D, 4096, self.w1q_d, w1_rear, 0)

        ve.wait_ge(sG.h, sG.v)
        ve.tensor_scalar(cwm[:, :], self.cw_sb[:, :], mcol[:, 0:1], None, ALU.mult)
        sV.inc(ve.drain())

        with ExitStack() as fs:
            def SF(name, p, f, dt):
                return fs.enter_context(nc.sbuf_tensor(name, [p, f], dt))
            xq = [SF(f"xq{i}", 128, W, BF16) for i in range(8)]
            scrf = SF("scrf", 128, W, F32)
            scr_b = SF("scr2b", 128, W, BF16)
            acc_b = SF("acc_b", 128, W, BF16)
            sc_bc = SF("sc_bc", 128, W, F32)
            inv_bc = SF("inv_bc", 128, W, F32)
            h_int = SF("h_int", 128, W, F32)
            t0 = SF("t0", 128, OWN, F32)
            h2b = [SF("h2b0", 128, OWN, BF16), SF("h2b1", 128, OWN, BF16)]
            w1b = [SF("w1b0", 128, 1024, BF16), SF("w1b1", 128, 1024, BF16)]

            for dc in range(8):
                sc.wait_ge(sV.h, sV.v)
                sc.wait_ge(sT.h, sT.v)
                sc.activation(scr_b[:, 0:W] if dc else acc_b[:, 0:W],
                              xw[dc][:, 0:W], AF.Abs)
                ssv = self.sS.inc(sc.drain())
                if dc:
                    ve.wait_ge(sS.h, ssv)
                    ve.tensor_max(acc_b[:, 0:W], acc_b[:, 0:W], scr_b[:, 0:W])
                    sV.inc(ve.drain())

            def cross_part_amax(acc_t, width, seg):
                ntile = (width + 127) // 128
                te.wait_ge(sV.h, sV.v)
                for t in range(ntile):
                    n = min(128, width - t * 128)
                    ins = te.transpose(ps_tr[0:n, _s(t * 128, 128)],
                                       acc_t[:, _s(t * 128, n)], self.ident_b[:])
                tv = sT.inc(ins)
                ve.wait_ge(sT.h, tv)
                for t in range(ntile):
                    n = min(128, width - t * 128)
                    ve.reduce_max(am_cols[0:n, _s(t, 1)], ps_tr[0:n, _s(t * 128, 128)],
                                  axis=AX)
                sv = sV.inc(ve.drain())
                te.wait_ge(sV.h, sv)
                ins = te.transpose(ps_big[0:17, 0:128], am_cols[:, 0:17], self.ident_f[:])
                tv = sT.inc(ins)
                ve.wait_ge(sT.h, tv)
                ve.tensor_copy(amT[:, :], ps_big[0:17, 0:128])
                sv = sV.inc(ve.drain())
                gp.wait_ge(sV.h, sv)
                self.sW.inc(gp.dma_start(out=pr[0:1, _s(seg, 2176)], in_=amT[:, :]), 16)
                gp.wait_ge(self.sW.h, self.sW.v)
                return self.sG.inc(gp.drain())

            gv = cross_part_amax(acc_b, W, PRA)
            ve.wait_ge(sG.h, gv)
            ve.tensor_scalar(pr[0:1, _s(PRA, W)], pr[0:1, _s(PRA, W)], 1e-5, None, ALU.max)
            ve.drain()
            ve.reciprocal(pr[0:1, _s(PRB, W)], pr[0:1, _s(PRA, W)])
            ve.drain()
            ve.tensor_scalar(pr[0:1, _s(PRB, W)], pr[0:1, _s(PRB, W)], 127.0, None, ALU.mult)
            ve.drain()
            ve.tensor_scalar(pr[0:1, _s(PRC, W)], pr[0:1, _s(PRA, W)], 1.0 / 127.0, None, ALU.mult)
            sv = sV.inc(ve.drain())

            def bcast_row(seg, width, dst_t):
                te.wait_ge(sV.h, sV.v)
                for (o, n) in [(i * 512, min(512, width - i * 512))
                               for i in range((width + 511) // 512)]:
                    ins = te.matmul(ps_big[0:128, _s(o, n)], self.ones_row_f[:],
                                    pr[0:1, _s(seg + o, n)], start=True, stop=True)
                tv = sT.inc(ins)
                ve.wait_ge(sT.h, tv)
                ve.tensor_copy(dst_t[:, 0:width], ps_big[0:128, 0:width])
                return sV.inc(ve.drain())

            bcast_row(PRB, W, sc_bc)
            bcast_row(PRC, W, inv_bc)
            for dc in range(8):
                ve.tensor_mul(scrf[:, 0:W], xw[dc][:, 0:W], sc_bc[:, 0:W])
                ve.drain()
                ve.tensor_scalar(scrf[:, 0:W], scrf[:, 0:W], MAGIC, -MAGIC, ALU.add, ALU.add)
                ve.drain()
                ve.tensor_mul(xq[dc][:, 0:W], scrf[:, 0:W], inv_bc[:, 0:W])
                sV.inc(ve.drain())

            sy.wait_ge(self.sG.h, self.sG.v)
            sy.wait_ge(self.sW.h, self.sW.v)
            ldv = [0, 0]
            for jc in range(2):
                ldv[jc] = sF[jc].inc(
                    sy.dma_start(out=w1b[jc][:].rearrange("r (dc c) -> r dc c", c=128),
                                 in_=self.w1q_d[jc].rearrange("dc r c -> r dc c")), 16)
            ev_ct = 0
            for jc in range(32):
                par = jc % 2
                te.wait_ge(sF[par].h, ldv[par])
                te.wait_ge(sV.h, sV.v)
                if jc >= 1:
                    te.wait_ge(sS.h, ev_ct)
                for dc in range(8):
                    for (off, n) in WSL:
                        ins = te.matmul(ps_big[:, _s(off, n)], w1b[par][:, _s(dc * 128, 128)],
                                        xq[dc][:, _s(off, n)], start=(dc == 0), stop=(dc == 7))
                tv = sT.inc(ins)
                if jc + 2 < 32:
                    sy.wait_ge(sT.h, tv)
                    ldv[par] = sF[par].inc(
                        sy.dma_start(out=w1b[par][:].rearrange("r (dc c) -> r dc c", c=128),
                                     in_=self.w1q_d[jc + 2].rearrange("dc r c -> r dc c")), 16)
                ve.wait_ge(sT.h, tv)
                ve.tensor_scalar(ps_big[:, 0:1], ps_big[:, 0:1], self.masks_sb[:, 0:1],
                                 None, ALU.mult)
                ve.tensor_scalar(ps_big[:, 2049:2050], ps_big[:, 2049:2050],
                                 self.masks_sb[:, 1:2], None, ALU.mult)
                mv = sV.inc(ve.drain())
                sc.wait_ge(sV.h, mv)
                sc.activation(h_int[:, 0:W], ps_big[0:128, 0:W], AF.Identity)
                ev_ct = self.sS.inc(sc.drain())
                sc.activation(t0[:, 0:OWN], h_int[:, 0:OWN], AF.Identity,
                              scale=cwm[:, _s(jc * 3, 1)], bias=self.cb_sb[:, _s(jc, 1)])
                cv = self.sS.inc(sc.drain())
                ve.wait_ge(sS.h, cv)
                ve.scalar_tensor_tensor(t0[:, 0:OWN], h_int[:, 1:1 + OWN],
                                        cwm[:, _s(jc * 3 + 1, 1)], t0[:, 0:OWN],
                                        ALU.mult, ALU.add)
                ve.drain()
                ve.scalar_tensor_tensor(t0[:, 0:OWN], h_int[:, 2:2 + OWN],
                                        cwm[:, _s(jc * 3 + 2, 1)], t0[:, 0:OWN],
                                        ALU.mult, ALU.add)
                cv2 = sV.inc(ve.drain())
                sc.wait_ge(sV.h, cv2)
                if jc >= 2:
                    sc.wait_ge(sH[par].h, 16 * (jc // 2))
                sc.activation(h2b[par][:, 0:OWN], t0[:, 0:OWN], AF.Silu)
                sc.drain()
                sc.activation(scr_b[:, 0:OWN], t0[:, 0:OWN], AF.Abs)
                av = self.sS.inc(sc.drain())
                ve.wait_ge(sS.h, av)
                if jc == 0:
                    ve.tensor_copy(acc2[:, :], scr_b[:, 0:OWN])
                else:
                    ve.tensor_max(acc2[:, :], acc2[:, :], scr_b[:, 0:OWN])
                sV.inc(ve.drain())
                sy.wait_ge(sS.h, av)
                sH[par].inc(sy.dma_start(out=self.h2_d[jc], in_=h2b[par][:, 0:OWN]), 16)

            gv = cross_part_amax(acc2, OWN, PRA)
            ve.wait_ge(sG.h, gv)
            ve.tensor_scalar(pr[0:1, _s(PRA, OWN)], pr[0:1, _s(PRA, OWN)], 1e-5, None, ALU.max)
            ve.drain()
            ve.reciprocal(pr[0:1, _s(PRB, OWN)], pr[0:1, _s(PRA, OWN)])
            ve.drain()
            ve.tensor_scalar(pr[0:1, _s(PRB, OWN)], pr[0:1, _s(PRB, OWN)], 127.0, None, ALU.mult)
            ve.drain()
            ve.tensor_scalar(pr[0:1, _s(PRC, OWN)], pr[0:1, _s(PRA, OWN)], 1.0 / 127.0,
                             None, ALU.mult)
            sV.inc(ve.drain())

        self.qbarrier(gp, sy, te, ve, sc)
        wstats_quant("w2T", 4096, D, self.w2q_d, w2_rear, 2)

        self.qbarrier(gp, sy, te, ve, sc)
        with ExitStack() as fs:
            def SF(name, p, f, dt):
                return fs.enter_context(nc.sbuf_tensor(name, [p, f], dt))
            h2q = [SF(f"h2q{i}", 128, 1024, BF16) for i in range(32)]
            h2l = [SF("h2l0", 128, 1024, BF16), SF("h2l1", 128, 1024, BF16)]
            w2b = [SF("w2b0", 128, 32 * 128, BF16), SF("w2b1", 128, 32 * 128, BF16)]
            s2bc = SF("s2bc", 128, 1024, F32)
            i2bc = SF("i2bc", 128, 1024, F32)
            scrf2 = SF("scrf2", 128, 1024, F32)

            for tg in range(2):
                to = tg * 1024
                bcast_done = 0
                te.wait_ge(sV.h, sV.v)
                for (o, n) in ((0, 512), (512, 512)):
                    ins = te.matmul(ps_big[0:128, _s(o, n)], self.ones_row_f[:],
                                    pr[0:1, _s(PRB + to + o, n)], start=True, stop=True)
                    ins2 = te.matmul(ps_big[0:128, _s(1024 + o, n)], self.ones_row_f[:],
                                     pr[0:1, _s(PRC + to + o, n)], start=True, stop=True)
                tv = sT.inc(ins2)
                ve.wait_ge(sT.h, tv)
                ve.tensor_copy(s2bc[:, :], ps_big[0:128, 0:1024])
                ve.tensor_copy(i2bc[:, :], ps_big[0:128, 1024:2048])
                bcast_done = sV.inc(ve.drain())
                ld2 = [0, 0]
                if tg == 0:
                    sy.wait_ge(sH[0].h, sH[0].v)
                    sy.wait_ge(sH[1].h, sH[1].v)
                for jc in range(2):
                    ld2[jc] = sH2[jc].inc(
                        sy.dma_start(out=h2l[jc][:], in_=self.h2_d[jc][:, _s(to, 1024)]), 16)
                for jc in range(32):
                    par = jc % 2
                    ve.wait_ge(sH2[par].h, ld2[par])
                    ve.wait_ge(sT.h, sT.v)
                    ve.tensor_mul(scrf2[:, :], h2l[par][:, :], s2bc[:, :])
                    ve.drain()
                    ve.tensor_scalar(scrf2[:, :], scrf2[:, :], MAGIC, -MAGIC, ALU.add, ALU.add)
                    ve.drain()
                    ve.tensor_mul(h2q[jc][:, :], scrf2[:, :], i2bc[:, :])
                    qv = sV.inc(ve.drain())
                    if jc + 2 < 32:
                        sy.wait_ge(sV.h, qv)
                        ld2[par] = sH2[par].inc(
                            sy.dma_start(out=h2l[par][:], in_=self.h2_d[jc + 2][:, _s(to, 1024)]), 16)
                ldw = [0, 0]
                if tg == 0:
                    sy.wait_ge(sG.h, sG.v)
                for o in range(2):
                    ldw[o] = sW2[o].inc(
                        sy.dma_start(out=w2b[o][:].rearrange("r (jc c) -> r jc c", c=128),
                                     in_=self.w2q_d[o].rearrange("jc r c -> r jc c")), 16)
                ev2 = 0
                for o in range(8):
                    par = o % 2
                    te.wait_ge(sW2[par].h, ldw[par])
                    te.wait_ge(sV.h, sV.v if o == 0 else ev2)
                    pp = ps_big[:, _s(par * 1024, 1024)]
                    for jc in range(32):
                        for (off, n) in ((0, 512), (512, 512)):
                            ins = te.matmul(pp[:, _s(off, n)], w2b[par][:, _s(jc * 128, 128)],
                                            h2q[jc][:, _s(off, n)],
                                            start=(jc == 0), stop=(jc == 31))
                    tv = sT.inc(ins)
                    if o + 2 < 8:
                        sy.wait_ge(sT.h, tv)
                        ldw[par] = sW2[par].inc(
                            sy.dma_start(out=w2b[par][:].rearrange("r (jc c) -> r jc c", c=128),
                                         in_=self.w2q_d[o + 2].rearrange("jc r c -> r jc c")), 16)
                    ve.wait_ge(sT.h, tv)
                    ins = ve.scalar_tensor_tensor(xw[o][:, _s(1 + to, 1024)], pp[0:128, :],
                                                  mcol[:, 2:3], xw[o][:, _s(1 + to, 1024)],
                                                  ALU.mult, ALU.add)
                    ev2 = sV.inc(ve.drain())
            te.wait_ge(sV.h, sV.v)
            self.sT.inc(te.nop())
        self.qbarrier(sc, ve, te, gp, sy)
        with ExitStack() as fs:
            scr_ln = fs.enter_context(nc.sbuf_tensor("scr_ln", [128, OWN], BF16))
            x2b = [fs.enter_context(nc.sbuf_tensor(f"x2b{i}", [128, OWN], BF16))
                   for i in range(8)]
            self.emit_ln(xw, 1, OWN, ps_big, scr_ln)
            if self.debug:
                sy.wait_ge(self.sV.h, self.sV.v)
                sy.wait_ge(self.sT.h, self.sT.v)
                for dc in range(8):
                    self.sO.inc(sy.dma_start(out=self.dbg["d_r2"][_s(dc * 128, 128), :],
                                             in_=xw[dc][:, 0:W]), 16)
            for dc in range(8):
                sc.wait_ge(self.sT.h, self.sT.v)
                sc.activation(x2b[dc][:, :], xw[dc][:, 1:1 + OWN], AF.Identity)
            sv = self.sS.inc(sc.drain())
            sy.wait_ge(self.sS.h, sv)
            gp.wait_ge(self.sS.h, sv)
            for dc in range(8):
                self.sO.inc(sy.dma_start(out=self.x2own_d[dc], in_=xw[dc][:, 1:1 + OWN]), 16)
                self.sW.inc(gp.dma_start(out=self.cc_in[_s(dc * 128, 128), :],
                                         in_=x2b[dc][:, :]), 16)
            gp.wait_ge(self.sW.h, self.sW.v)
            self.sG.inc(gp.collective_compute(
                "AllGather", ALU.bypass,
                replica_groups=[[0, 1], [2, 3], [4, 5], [6, 7]],
                ins=[self.cc_in], outs=[self.cc_out]))
        st2.close()
        self.qbarrier(gp, sy, te, ve, sc)
        self.xw_stack.close()

    def phase3(self):
        nc, te, ve, sc, gp, sy = self.nc, self.te, self.ve, self.sc, self.gp, self.sy
        d = self.din
        sW, sX, sT, sV, sS, sG = self.sW, self.sX, self.sT, self.sV, self.sS, self.sG
        pr = self.pr
        half = 512

        st3 = ExitStack()

        def S3(name, p, f, dt):
            return st3.enter_context(nc.sbuf_tensor(name, [p, f], dt))

        X2T = [S3(f"X2T{i}", 128, OWN, BF16) for i in range(8)]
        qb2r = S3("qb2r", 1, 512, BF16)
        kb2r = S3("kb2r", 1, 512, BF16)
        zc = S3("zc", 64, 8, F32)
        A2 = S3("A2", 128, 512, BF16)
        self._b1sb = {}
        for nm, ncols in (("qb1_r", 8), ("kb1_r", 8), ("vb_s", 4), ("gb_s", 4), ("pb_r", 8)):
            t = S3("sb_" + nm, 128, ncols, F32)
            self.sX.inc(sy.dma_start(out=t[:], in_=d[nm]), 16)
            self._b1sb[nm] = t
        bias_ld = self.sX.v

        stA = ExitStack()

        def SA(name, p, f, dt):
            return stA.enter_context(nc.sbuf_tensor(name, [p, f], dt))

        x2f = [SA(f"x2f{i}", 128, NT, BF16) for i in range(8)]
        t1 = [SA(f"t1_{i}", 128, 512, BF16) for i in range(8)]
        q2n = [SA(f"q2n{i}", 128, 512, BF16) for i in range(4)]
        k520 = SA("k520", 128, 4 * 520, BF16)
        sp1 = SA("sp1", 128, 512, F32)
        sp2 = SA("sp2", 128, 512, F32)
        w1b = [SA("p3w0", 128, 1024, BF16), SA("p3w1", 128, 1024, BF16)]
        w2b = [SA("p3v0", 128, 512, BF16), SA("p3v1", 128, 512, BF16)]
        zr = SA("zr", 66, 1024, F32)
        vT = SA("vT", 128, NT, BF16)
        Op = SA("Op", 128, NT, BF16)
        OpSW = SA("OpSW", 128, NT, BF16)
        Osc = SA("Osc", 64, NT, BF16)

        sQ1 = [Sem(nc, self.st, "sQ10"), Sem(nc, self.st, "sQ11")]
        sQ2 = [Sem(nc, self.st, "sQ20"), Sem(nc, self.st, "sQ21")]
        sXo = [Sem(nc, self.st, "sXo0"), Sem(nc, self.st, "sXo1")]
        self.extra_sems += sQ1 + sQ2 + sXo

        with ExitStack() as pstk:
            pq = [pstk.enter_context(nc.psum_tensor("pq0", [128, 512], F32)),
                  pstk.enter_context(nc.psum_tensor("pq1", [128, 512], F32))]
            psA = pstk.enter_context(nc.psum_tensor("psA", [66, 512], F32))
            pq2 = [pstk.enter_context(nc.psum_tensor(f"pq2_{i}", [128, 512], F32))
                   for i in range(4)]

            sy.wait_ge(sG.h, sG.v)
            for dc in range(8):
                sX.inc(sy.dma_start(out=x2f[dc][:, 0:OWN],
                                    in_=self.cc_out[_s(dc * 128, 128), :]), 16)
                sX.inc(sy.dma_start(out=x2f[dc][:, OWN:NT],
                                    in_=self.cc_out[_s(D + dc * 128, 128), :]), 16)
            self.sW.inc(gp.dma_start(out=qb2r[:], in_=d["qb2_row"]), 16)
            self.sW.inc(gp.dma_start(out=kb2r[:], in_=d["kb2_row"]), 16)
            b2_ld = self.sW.v
            x2_ld = sX.v

            def load_w1chunk(nm, o, par):
                srcv = d[nm][:, _s(o * 128, 128)].rearrange("(dc r) c -> r dc c", r=128)
                dstv = w1b[par][:].rearrange("r (dc c) -> r dc c", c=128)
                return sQ1[par].inc(gp.dma_start(out=dstv, in_=srcv), 16)

            def load_w2chunk(nm, o, par):
                return sQ2[par].inc(
                    gp.dma_start(out=w2b[par][:], in_=d[nm][_s(o * 128, 128), :]), 16)

            for tg in range(8 if self.p3_sect >= 2 else 0):
                to = tg * 512
                for side, w1nm, w2nm, b1c, b2r in (
                        (0, "qw1T", "qw2Ts", "qb1_r", qb2r),
                        (1, "kw1T", "kw2Ts", "kb1_r", kb2r)):
                    l1 = [0, 0]
                    for o in range(8):
                        par = o % 2
                        gp.wait_ge(sT.h, sT.v)
                        l1[par] = load_w1chunk(w1nm, o, par)
                        te.wait_ge(sQ1[par].h, l1[par])
                        te.wait_ge(sX.h, max(x2_ld, bias_ld))
                        te.wait_ge(sW.h, b2_ld)
                        if o >= 1:
                            te.wait_ge(sS.h, sS.v)
                        for dc in range(8):
                            ins = te.matmul(pq[par][:, 0:512],
                                            w1b[par][:, _s(dc * 128, 128)],
                                            x2f[dc][:, _s(to, 512)],
                                            start=(dc == 0), stop=(dc == 7))
                        tv = sT.inc(ins)
                        sc.wait_ge(sT.h, tv)
                        sc.activation(t1[o][:, :], pq[par][:, 0:512], AF.Gelu,
                                      bias=self._b1sb[b1c][:, _s(o, 1)])
                        self.sS.inc(sc.drain())
                    l2 = [0, 0]
                    for o in range(8):
                        par = o % 2
                        gp.wait_ge(sT.h, sT.v)
                        l2[par] = load_w2chunk(w2nm, o, par)
                        te.wait_ge(sQ2[par].h, l2[par])
                        te.wait_ge(sS.h, sS.v)
                        if o == 0:
                            te.wait_ge(sV.h, sV.v)
                        for tc in range(4):
                            ins = te.matmul(pq2[tc][:, 0:512],
                                            t1[o][:, _s(tc * 128, 128)],
                                            w2b[par][:, 0:512],
                                            start=(o == 0), stop=False)
                        sT.inc(ins)
                    for tc in range(4):
                        ins = te.matmul(pq2[tc][:, 0:512], self.okk[0:1, 0:128],
                                        b2r[:, 0:512], start=False, stop=True)
                    tv = sT.inc(ins)
                    for tc in range(4):
                        sc.wait_ge(sT.h, tv)
                        sc.wait_ge(sV.h, sV.v)
                        sc.activation(sp1[:, :], pq2[tc][:, 0:512], AF.Abs)
                        sc.drain()
                        sc.activation(sp1[:, :], sp1[:, :], AF.Exp, scale=-1.0)
                        sc.drain()
                        sc.activation(sp2[:, :], sp1[:, :], AF.Ln, bias=1.0)
                        sc.drain()
                        sc.activation(sp1[:, :], pq2[tc][:, 0:512], AF.Relu)
                        sv = self.sS.inc(sc.drain())
                        ve.wait_ge(sS.h, sv)
                        if side == 0:
                            ve.tensor_add(q2n[tc][:, :], sp1[:, :], sp2[:, :])
                            sV.inc(ve.drain())
                        else:
                            ve.tensor_add(sp2[:, :], sp1[:, :], sp2[:, :])
                            ve.drain()
                            dst = k520[:, _s(tc * 520, 520)].rearrange(
                                "r (h c) -> r h c", c=65)[:, :, 0:64]
                            ve.tensor_copy(dst, sp2[:, :].rearrange("r (h c) -> r h c", c=64))
                            ve.drain()
                            for hh in range(8):
                                ve.reduce_sum(k520[:, _s(tc * 520 + hh * 65 + 64, 1)],
                                              k520[:, _s(tc * 520 + hh * 65, 64)], axis=AX)
                            sV.inc(ve.drain())
                te.wait_ge(sV.h, sV.v)
                for tc in range(4):
                    for hh in range(8):
                        ins = te.matmul(psA[0:65, _s(hh * 64, 64)],
                                        k520[:, _s(tc * 520 + hh * 65, 65)],
                                        q2n[tc][:, _s(hh * 64, 64)],
                                        start=(tg == 0 and tc == 0 and hh == 0),
                                        stop=(tg == 7 and tc == 3 and hh == 7))
                sT.inc(ins)

            if self.p3_sect < 3:
                stA.close()
                st3.close()
                return
            tv = sT.v
            ve.wait_ge(sT.h, tv)
            ve.tensor_scalar(A2[0:64, 0:512], psA[0:64, 0:512], 0.125, None, ALU.mult)
            ve.drain()
            for hh in range(8):
                ve.tensor_add(A2[0:64, _s(hh * 64, 64)], A2[0:64, _s(hh * 64, 64)],
                              self.ident_b[0:64, 0:64])
            ve.drain()
            ve.tensor_scalar(zr[64:65, 0:512], psA[64:65, 0:512], 0.125, 1024.0,
                             ALU.mult, ALU.add)
            ve.drain()
            ve.reciprocal(zr[64:65, _s(512, 512)], zr[64:65, 0:512])
            sv = sV.inc(ve.drain())
            gp.wait_ge(sV.h, sv)
            self.sW.inc(gp.dma_start(out=A2[64:128, 0:512], in_=A2[0:64, 0:512]), 16)
            gp.wait_ge(sW.h, sW.v)
            a2dma = self.sG.inc(gp.drain())
            te.wait_ge(sV.h, sv)
            for wave in range(2):
                for i in range(4):
                    hh = wave * 4 + i
                    ins = te.transpose(pq2[i][0:64, 0:1],
                                       zr[64:65, _s(512 + hh * 64, 64)],
                                       self.ident_f[64:65, 64:65])
                tv = sT.inc(ins)
                ve.wait_ge(sT.h, tv)
                for i in range(4):
                    ve.tensor_copy(zc[:, _s(wave * 4 + i, 1)], pq2[i][0:64, 0:1])
                sv = sV.inc(ve.drain())
                te.wait_ge(sV.h, sv)

            if self.p3_sect < 4:
                stA.close()
                st3.close()
                return
            for pp_ in range(4 if self.p3_sect >= 4 else 0):
                for which, wnm, bcol in ((0, "vwTs", "vb_s"), (1, "gwTs", "gb_s")):
                    gp.wait_ge(sT.h, sT.v)
                    srcv = d[wnm][:, _s(pp_ * 128, 128)].rearrange("(dc r) c -> r dc c", r=128)
                    dstv = w1b[0][:].rearrange("r (dc c) -> r dc c", c=128)
                    lw0 = sQ1[0].inc(gp.dma_start(out=dstv, in_=srcv), 16)
                    te.wait_ge(sQ1[0].h, lw0)
                    te.wait_ge(sV.h, sV.v)
                    for s in range(8):
                        par = s % 2
                        if s >= 2:
                            te.wait_ge(sS.h, sS.v)
                        for dc in range(8):
                            ins = te.matmul(pq[par][:, 0:512],
                                            w1b[0][:, _s(dc * 128, 128)],
                                            x2f[dc][:, _s(s * 512, 512)],
                                            start=(dc == 0), stop=(dc == 7))
                        tv = sT.inc(ins)
                        sc.wait_ge(sT.h, tv)
                        sc.activation(vT[:, _s(s * 512, 512)], pq[par][:, 0:512], AF.Gelu,
                                      bias=self._b1sb[bcol][:, _s(pp_, 1)])
                        self.sS.inc(sc.drain())
                    if which == 0:
                        te.wait_ge(sS.h, sS.v)
                        te.wait_ge(sG.h, a2dma)
                        ev = 0
                        for hh in range(2):
                            h = 2 * pp_ + hh
                            for s in range(8):
                                par = s % 2
                                if s >= 2 or hh == 1:
                                    te.wait_ge(sV.h, ev)
                                if hh == 0:
                                    ins = te.matmul(pq2[par][0:64, 0:512],
                                                    A2[0:64, _s(h * 64, 64)],
                                                    vT[0:64, _s(s * 512, 512)],
                                                    start=True, stop=True)
                                else:
                                    ins = te.matmul(pq2[par][0:64, 0:512],
                                                    A2[64:128, _s(h * 64, 64)],
                                                    vT[64:128, _s(s * 512, 512)],
                                                    start=True, stop=True)
                                tv = sT.inc(ins)
                                ve.wait_ge(sT.h, tv)
                                dst = Op[0:64, _s(s * 512, 512)] if hh == 0 \
                                    else Osc[:, _s(s * 512, 512)]
                                ve.tensor_scalar(dst, pq2[par][0:64, 0:512],
                                                 zc[:, _s(h, 1)], None, ALU.mult)
                                ev = sV.inc(ve.drain())
                        gp.wait_ge(sV.h, ev)
                        self.sW.inc(gp.dma_start(out=Op[64:128, :], in_=Osc[:, :]), 16)
                        gp.wait_ge(sW.h, sW.v)
                        self.sG.inc(gp.drain())
                ve.wait_ge(sG.h, sG.v)
                ve.wait_ge(sS.h, sS.v)
                ve.tensor_mul(Op[:, :], Op[:, :], vT[:, :])
                gv = sV.inc(ve.drain())
                te.wait_ge(sV.h, gv)
                for s in range(8):
                    par = s % 2
                    if s >= 2:
                        te.wait_ge(sV.h, sV.v)
                    ins = te.matmul(pq[par][:, 0:512], self.swap64[:],
                                    Op[:, _s(s * 512, 512)], start=True, stop=True)
                    tv = sT.inc(ins)
                    ve.wait_ge(sT.h, tv)
                    ve.tensor_copy(OpSW[:, _s(s * 512, 512)], pq[par][:, 0:512])
                    sV.inc(ve.drain())
                for hh in range(2):
                    h = 2 * pp_ + hh
                    for pc in range(8):
                        for jh in range(2):
                            j = 2 * pc + jh
                            srcT = Op if jh == hh else OpSW
                            ve.tensor_copy(X2T[pc][_s(jh * 64, 64), _s(h * 256, 256)],
                                           srcT[_s(jh * 64, 64), j:NT:16])
                    sV.inc(ve.drain())
            te.wait_ge(sV.h, sV.v)
            self.sT.inc(te.nop())
        if self.p3_sect < 5:
            stA.close()
            st3.close()
            return
        self.qbarrier(te, ve, sc, gp, sy)
        stA.close()
        r3 = [self.SB(f"r3_{i}", 128, OWN, F32, side="right") for i in range(8)]
        self.r3 = r3
        with ExitStack() as pstk:
            pp2 = [pstk.enter_context(nc.psum_tensor("pp2_0", [128, 1024], F32)),
                   pstk.enter_context(nc.psum_tensor("pp2_1", [128, 1024], F32))]
            psLN = pstk.enter_context(nc.psum_tensor("psLN", [128, 2048], F32))
            wpf = st3.enter_context(nc.sbuf_tensor("p3wp", [128, 8192], BF16))
            xo = [st3.enter_context(nc.sbuf_tensor(f"xo{i}", [128, OWN], F32))
                  for i in range(2)]
            scr3 = st3.enter_context(nc.sbuf_tensor("scr3", [128, OWN], BF16))
            srcv = d["pwT"][:].rearrange("(dc r) c -> r dc c", r=128)
            dstv = wpf[:].rearrange("r (dc c) -> r dc c", c=1024)
            wp_done = sW.inc(gp.dma_start(out=dstv, in_=srcv), 16)
            xol = [0, 0]
            for o in range(2):
                xol[o] = sXo[o].inc(sy.dma_start(out=xo[o][:], in_=self.x2own_d[o]), 16)
            te.wait_ge(sW.h, wp_done)
            ev3 = 0
            for o in range(8):
                par = o % 2
                if o >= 1:
                    te.wait_ge(sV.h, ev3)
                for th in range(2):
                    for dc in range(8):
                        for ss in range(2):
                            ins = te.matmul(pp2[th][:, _s(ss * 512, 512)],
                                            wpf[:, _s(dc * 1024 + o * 128, 128)],
                                            X2T[dc][:, _s(th * 1024 + ss * 512, 512)],
                                            start=(dc == 0), stop=(dc == 7))
                tv = sT.inc(ins)
                ve.wait_ge(sT.h, tv)
                ve.wait_ge(sXo[par].h, xol[par])
                for th in range(2):
                    ins = ve.scalar_tensor_tensor(r3[o][:, _s(th * 1024, 1024)],
                                                  pp2[th][0:128, :],
                                                  self._b1sb["pb_r"][:, _s(o, 1)],
                                                  xo[par][:, _s(th * 1024, 1024)],
                                                  ALU.add, ALU.add)
                ev3 = sV.inc(ve.drain())
                if o + 2 < 8:
                    sy.wait_ge(sV.h, ev3)
                    xol[par] = sXo[par].inc(
                        sy.dma_start(out=xo[par][:], in_=self.x2own_d[o + 2]), 16)
            te.wait_ge(sV.h, sV.v)
            self.sT.inc(te.nop())
            if self.p3_sect >= 6:
                self.emit_ln(r3, 0, OWN, psLN, scr3)
            if self.debug and self.p3_sect >= 7:
                sy.wait_ge(self.sV.h, self.sV.v)
                sy.wait_ge(self.sT.h, self.sT.v)
                for dc in range(8):
                    self.sO.inc(sy.dma_start(out=self.dbg["d_r3"][_s(dc * 128, 128), :],
                                             in_=r3[dc][:, 0:OWN]), 16)
        st3.close()

    def phase4(self):
        nc, te, ve, sc, gp, sy = self.nc, self.te, self.ve, self.sc, self.gp, self.sy
        d = self.din
        sW, sX, sT, sV, sS, sG = self.sW, self.sX, self.sT, self.sV, self.sS, self.sG
        r3 = self.r3
        BIG = 1.0e30

        self.qbarrier(gp, sy, te, ve, sc)
        st4 = ExitStack()

        def S4(name, p, f, dt):
            return st4.enter_context(nc.sbuf_tensor(name, [p, f], dt))

        x3 = [S4(f"x3_{i}", 128, OWN, BF16) for i in range(8)]
        cmT = S4("cmT", 16, OWN, BF16)
        e16b = S4("e16b", 16, OWN, BF16)
        be2b = S4("be2b", 16, D, BF16)
        be1c = S4("be1c", 128, 64, F32)
        be3c = S4("be3c", 128, 64, F32)
        bsc = S4("bsc", 128, 24, F32)
        sE = [Sem(nc, self.st, "sE0"), Sem(nc, self.st, "sE1")]
        sE2 = [Sem(nc, self.st, "sE20"), Sem(nc, self.st, "sE21")]
        self.extra_sems += sE + sE2

        for dc in range(8):
            sc.wait_ge(sT.h, sT.v)
            sc.activation(x3[dc][:, :], r3[dc][:, :], AF.Identity)
        x3v = self.sS.inc(sc.drain())
        self.sW.inc(gp.dma_start(out=e16b[:], in_=d["E16"]), 16)
        self.sW.inc(gp.dma_start(out=be2b[:], in_=d["be2"]), 16)
        w_small = self.sW.v
        self.sX.inc(sy.dma_start(out=be1c[:], in_=d["be1_c"]), 16)
        self.sX.inc(sy.dma_start(out=be3c[:], in_=d["be3_c"]), 16)
        self.sX.inc(sy.dma_start(out=bsc[:, 0:8], in_=d["bs1_r"]), 16)
        self.sX.inc(sy.dma_start(out=bsc[:, 8:16], in_=d["bs2_r"]), 16)
        self.sX.inc(sy.dma_start(out=bsc[:, 16:24], in_=d["bs3_r"]), 16)
        x_small = self.sX.v

        with ExitStack() as gs_:
            def SG(name, p, f, dt):
                return gs_.enter_context(nc.sbuf_tensor(name, [p, f], dt))
            gwb = SG("gwb", 128, 8 * 16, BF16)
            gbr = SG("gbr", 1, 16, F32)
            gbb = SG("gbb", 128, 16, F32)
            sgT = SG("sgT", 16, OWN, F32)
            sgN = SG("sgN", 128, 256, F32)
            sgO = SG("sgO", 128, 256, F32)
            comb = SG("comb", 128, 256, F32)
            p6 = SG("p6", 128, 24, F32)
            g4 = SG("g4", 128, 24, F32)
            cN = SG("cN", 128, 8, F32)
            sgM = SG("sgM", 128, 16, F32)
            sgM2 = SG("sgM2", 128, 16, F32)
            sgE = SG("sgE", 128, 16, F32)
            psG = gs_.enter_context(nc.psum_tensor("psG", [16, OWN], F32))
            ptr = gs_.enter_context(nc.psum_tensor("ptr", [128, 128], F32))

            srcv = d["gwT"][:].rearrange("(dc r) c -> r dc c", r=128)
            dstv = gwb[:].rearrange("r (dc c) -> r dc c", c=16)
            gl = self.sW.inc(gp.dma_start(out=dstv, in_=srcv), 16)
            self.sX.inc(sy.dma_start(out=gbr[:], in_=d["gb_row"]), 16)
            gb_ld = self.sX.v
            te.wait_ge(sW.h, gl)
            te.wait_ge(sS.h, x3v)
            for (o, n) in [(i * 512, 512) for i in range(4)]:
                for dc in range(8):
                    ins = te.matmul(psG[:, _s(o, n)], gwb[:, _s(dc * 16, 16)],
                                    x3[dc][:, _s(o, n)], start=(dc == 0), stop=(dc == 7))
            tv = sT.inc(ins)
            ve.wait_ge(sT.h, tv)
            ve.tensor_copy(sgT[:, :], psG[0:16, 0:OWN])
            sv = sV.inc(ve.drain())
            te.wait_ge(sX.h, gb_ld)
            te.wait_ge(sV.h, sv)
            ins = te.matmul(ptr[0:128, 0:16], self.ones_row_f[:], gbr[0:1, 0:16],
                            start=True, stop=True)
            tv = sT.inc(ins)
            ve.wait_ge(sT.h, tv)
            ve.tensor_copy(gbb[:, :], ptr[0:128, 0:16])
            sv = sV.inc(ve.drain())

            GS, GS2, GSN, EQG, KPG = 0, 4, 8, 12, 16
            for t in range(16):
                S = _s(t * 16, 16)
                te.wait_ge(sV.h, sV.v)
                ins = te.transpose(ptr[0:128, 0:16], sgT[:, _s(t * 128, 128)],
                                   self.ident_f[0:16, 0:16])
                tv = sT.inc(ins)
                ve.wait_ge(sT.h, tv)
                ve.tensor_copy(sgN[:, S], ptr[0:128, 0:16])
                ve.drain()
                ve.reduce_max(cN[:, 0:1], sgN[:, S], axis=AX, negate=True)
                nm = sV.inc(ve.drain())
                sc.wait_ge(sV.h, nm)
                sc.activation(sgO[:, S], sgN[:, S], AF.Exp, bias=cN[:, 0:1],
                              accum_out=cN[:, 1:2])
                ex = self.sS.inc(sc.drain())
                ve.wait_ge(sS.h, ex)
                ve.reciprocal(cN[:, 2:3], cN[:, 1:2])
                ve.drain()
                ve.tensor_scalar(sgO[:, S], sgO[:, S], cN[:, 2:3], None, ALU.mult)
                ve.drain()
                ve.tensor_add(sgN[:, S], sgO[:, S], gbb[:, :])
                ve.drain()
                for i, (a, b) in enumerate([(0, 1), (0, 2), (0, 3), (1, 2), (1, 3), (2, 3)]):
                    ve.tensor_add(p6[:, _s(4 * i, 4)],
                                  sgN[:, t * 16 + a:(t + 1) * 16:4],
                                  sgN[:, t * 16 + b:(t + 1) * 16:4])
                ve.drain()
                ve.tensor_max(g4[:, _s(GS, 4)], p6[:, 0:4], p6[:, 4:8])
                ve.drain()
                for i in range(2, 6):
                    ve.tensor_max(g4[:, _s(GS, 4)], g4[:, _s(GS, 4)], p6[:, _s(4 * i, 4)])
                    ve.drain()
                ve.reduce_max(cN[:, 3:4], g4[:, _s(GS, 4)], axis=AX, negate=True)
                sv = sV.inc(ve.drain())
                sc.wait_ge(sV.h, sv)
                sc.activation(g4[:, _s(GSN, 4)], g4[:, _s(GS, 4)], AF.Sign, bias=cN[:, 3:4])
                sc.drain()
                sc.activation(g4[:, _s(EQG, 4)], g4[:, _s(GSN, 4)], AF.Abs)
                ab = self.sS.inc(sc.drain())
                ve.wait_ge(sS.h, ab)
                ve.tensor_scalar(g4[:, _s(EQG, 4)], g4[:, _s(EQG, 4)], -1.0, 1.0,
                                 ALU.mult, ALU.add)
                ve.drain()
                ve.reduce_sum(cN[:, 4:5], g4[:, _s(EQG, 4)], axis=AX)
                ve.drain()
                ve.scalar_tensor_tensor(g4[:, _s(GS2, 4)], g4[:, _s(EQG, 4)], -BIG,
                                        g4[:, _s(GS, 4)], ALU.mult, ALU.add)
                ve.drain()
                ve.reduce_max(cN[:, 5:6], g4[:, _s(GS2, 4)], axis=AX)
                sv = sV.inc(ve.drain())
                sc.wait_ge(sV.h, sv)
                sc.activation(cN[:, 6:7], cN[:, 4:5], AF.Sign, bias=-1.5)
                sg2 = self.sS.inc(sc.drain())
                ve.wait_ge(sS.h, sg2)
                ve.tensor_scalar(cN[:, 6:7], cN[:, 6:7], 1.0, 1.0, ALU.add, ALU.min)
                ve.drain()
                ve.tensor_scalar(cN[:, 7:8], cN[:, 3:4], -1.0, None, ALU.mult)
                ve.drain()
                ve.tensor_sub(cN[:, 7:8], cN[:, 7:8], cN[:, 5:6])
                ve.drain()
                ve.tensor_mul(cN[:, 7:8], cN[:, 7:8], cN[:, 6:7])
                ve.drain()
                ve.tensor_add(cN[:, 7:8], cN[:, 7:8], cN[:, 5:6])
                ve.drain()
                ve.tensor_scalar(cN[:, 7:8], cN[:, 7:8], -1.0, None, ALU.mult)
                sv = sV.inc(ve.drain())
                sc.wait_ge(sV.h, sv)
                sc.activation(g4[:, _s(KPG, 4)], g4[:, _s(GS, 4)], AF.Sign, bias=cN[:, 7:8])
                kp = self.sS.inc(sc.drain())
                ve.wait_ge(sS.h, kp)
                ve.tensor_scalar(g4[:, _s(KPG, 4)], g4[:, _s(KPG, 4)], 1.0, 1.0,
                                 ALU.add, ALU.min)
                ve.drain()
                ve.tensor_scalar(g4[:, _s(GS2, 4)], g4[:, _s(KPG, 4)], -1.0, None, ALU.add)
                ve.drain()
                for i in range(4):
                    ve.scalar_tensor_tensor(sgM[:, i:16:4], g4[:, _s(GS2, 4)], BIG,
                                            sgN[:, t * 16 + i:(t + 1) * 16:4],
                                            ALU.mult, ALU.add)
                ve.drain()
                ve.reduce_max(cN[:, 3:4], sgM[:, :], axis=AX, negate=True)
                sv = sV.inc(ve.drain())
                sc.wait_ge(sV.h, sv)
                sc.activation(sgE[:, :], sgM[:, :], AF.Sign, bias=cN[:, 3:4])
                sc.drain()
                sc.activation(sgE[:, :], sgE[:, :], AF.Abs)
                ab = self.sS.inc(sc.drain())
                ve.wait_ge(sS.h, ab)
                ve.tensor_scalar(sgE[:, :], sgE[:, :], -1.0, 1.0, ALU.mult, ALU.add)
                ve.drain()
                ve.reduce_sum(cN[:, 4:5], sgE[:, :], axis=AX)
                ve.drain()
                ve.scalar_tensor_tensor(sgM2[:, :], sgE[:, :], -BIG, sgM[:, :],
                                        ALU.mult, ALU.add)
                ve.drain()
                ve.reduce_max(cN[:, 5:6], sgM2[:, :], axis=AX)
                sv = sV.inc(ve.drain())
                sc.wait_ge(sV.h, sv)
                sc.activation(cN[:, 6:7], cN[:, 4:5], AF.Sign, bias=-1.5)
                sg2 = self.sS.inc(sc.drain())
                ve.wait_ge(sS.h, sg2)
                ve.tensor_scalar(cN[:, 6:7], cN[:, 6:7], 1.0, 1.0, ALU.add, ALU.min)
                ve.drain()
                ve.tensor_scalar(cN[:, 7:8], cN[:, 3:4], -1.0, None, ALU.mult)
                ve.drain()
                ve.tensor_sub(cN[:, 7:8], cN[:, 7:8], cN[:, 5:6])
                ve.drain()
                ve.tensor_mul(cN[:, 7:8], cN[:, 7:8], cN[:, 6:7])
                ve.drain()
                ve.tensor_add(cN[:, 7:8], cN[:, 7:8], cN[:, 5:6])
                ve.drain()
                ve.tensor_scalar(cN[:, 7:8], cN[:, 7:8], -1.0, None, ALU.mult)
                sv = sV.inc(ve.drain())
                sc.wait_ge(sV.h, sv)
                sc.activation(sgE[:, :], sgM[:, :], AF.Sign, bias=cN[:, 7:8])
                kp = self.sS.inc(sc.drain())
                ve.wait_ge(sS.h, kp)
                ve.tensor_scalar(sgE[:, :], sgE[:, :], 1.0, 1.0, ALU.add, ALU.min)
                ve.drain()
                ve.tensor_mul(comb[:, S], sgO[:, S], sgE[:, :])
                sV.inc(ve.drain())
            for t in range(16):
                te.wait_ge(sV.h, sV.v)
                ins = te.transpose(ptr[0:16, 0:128], comb[:, _s(t * 16, 16)],
                                   self.ident_f[:])
                tv = sT.inc(ins)
                ve.wait_ge(sT.h, tv)
                ve.tensor_copy(cmT[:, _s(t * 128, 128)], ptr[0:16, 0:128])
                sV.inc(ve.drain())
            te.wait_ge(sV.h, sV.v)
            self.sT.inc(te.nop())
            if self.debug:
                gp.wait_ge(sV.h, sV.v)
                self.sW.inc(gp.dma_start(out=self.dbg["d_cmb"], in_=cmT[:, :]), 16)
        self.qbarrier(gp, sy, te, ve, sc)
        with ExitStack() as ss_:
            def SS(name, p, f, dt):
                return ss_.enter_context(nc.sbuf_tensor(name, [p, f], dt))
            hs = [SS(f"hs{i}", 128, OWN, BF16) for i in range(8)]
            s1b = SS("s1b", 128, OWN, BF16)
            wsb = [SS("wsb0", 128, 1024, BF16), SS("wsb1", 128, 1024, BF16)]
            psS4 = ss_.enter_context(nc.psum_tensor("psS4", [128, OWN], F32))

            def ld_ws(nm, o, par):
                srcv = d[nm][:, _s(o * 128, 128)].rearrange("(dc r) c -> r dc c", r=128)
                dstv = wsb[par][:].rearrange("r (dc c) -> r dc c", c=128)
                return sE[par].inc(gp.dma_start(out=dstv, in_=srcv), 16)

            for o in range(8):
                par = o % 2
                gp.wait_ge(sT.h, sT.v)
                l1 = ld_ws("ws1", o, par)
                te.wait_ge(sE[par].h, l1)
                te.wait_ge(sV.h, sV.v)
                for (off, n) in [(i * 512, 512) for i in range(4)]:
                    for dc in range(8):
                        ins = te.matmul(psS4[:, _s(off, n)], wsb[par][:, _s(dc * 128, 128)],
                                        x3[dc][:, _s(off, n)], start=(dc == 0), stop=(dc == 7))
                tv = sT.inc(ins)
                sc.wait_ge(sT.h, tv)
                sc.activation(s1b[:, :], psS4[0:128, 0:OWN], AF.Silu, bias=bsc[:, _s(o, 1)])
                sl = self.sS.inc(sc.drain())
                gp.wait_ge(sS.h, sl)
                l3 = ld_ws("ws3", o, par)
                te.wait_ge(sE[par].h, l3)
                for (off, n) in [(i * 512, 512) for i in range(4)]:
                    for dc in range(8):
                        ins = te.matmul(psS4[:, _s(off, n)], wsb[par][:, _s(dc * 128, 128)],
                                        x3[dc][:, _s(off, n)], start=(dc == 0), stop=(dc == 7))
                tv = sT.inc(ins)
                ve.wait_ge(sT.h, tv)
                ve.scalar_tensor_tensor(hs[o][:, :], psS4[0:128, 0:OWN],
                                        bsc[:, _s(16 + o, 1)], s1b[:, :],
                                        ALU.add, ALU.mult)
                sV.inc(ve.drain())
            for o in range(8):
                par = o % 2
                gp.wait_ge(sT.h, sT.v)
                l2 = ld_ws("ws2", o, par)
                te.wait_ge(sE[par].h, l2)
                te.wait_ge(sV.h, sV.v)
                for (off, n) in [(i * 512, 512) for i in range(4)]:
                    for dc in range(8):
                        ins = te.matmul(psS4[:, _s(off, n)], wsb[par][:, _s(dc * 128, 128)],
                                        hs[dc][:, _s(off, n)], start=(dc == 0), stop=(dc == 7))
                tv = sT.inc(ins)
                ve.wait_ge(sT.h, tv)
                ve.scalar_tensor_tensor(r3[o][:, :], psS4[0:128, 0:OWN],
                                        bsc[:, _s(8 + o, 1)], r3[o][:, :],
                                        ALU.add, ALU.add)
                sV.inc(ve.drain())
            te.wait_ge(sV.h, sV.v)
            self.sT.inc(te.nop())
            if self.debug:
                sy.wait_ge(sV.h, sV.v)
                for dc in range(8):
                    self.sO.inc(sy.dma_start(out=self.dbg["d_z"][_s(dc * 128, 128), :],
                                             in_=r3[dc][:, :]), 16)
        self.qbarrier(gp, sy, te, ve, sc)
        with ExitStack() as es_:
            def SE(name, p, f, dt):
                return es_.enter_context(nc.sbuf_tensor(name, [p, f], dt))
            we1e = [SE("we1e0", 128, 4096, BF16), SE("we1e1", 128, 4096, BF16)]
            we3e = [SE("we3e0", 128, 4096, BF16), SE("we3e1", 128, 4096, BF16)]
            we2e = SE("we2e", 128, 4096, BF16)
            heg = SE("heg", 128, 2048, BF16)
            s1e = SE("s1e", 128, 512, BF16)
            cbce = SE("cbce", 128, 512, BF16)
            ph = es_.enter_context(nc.psum_tensor("ph4", [128, 512], F32))
            phB = es_.enter_context(nc.psum_tensor("ph4b", [128, 512], F32))
            pcb = es_.enter_context(nc.psum_tensor("pcb", [128, 512], F32))
            pp4 = [es_.enter_context(nc.psum_tensor("pp4_0", [128, 512], F32)),
                   es_.enter_context(nc.psum_tensor("pp4_1", [128, 512], F32))]

            def ld_e(nm, e, dstt, sem):
                srcv = d[nm][e].rearrange("(dc r) c -> r dc c", r=128)
                cw_ = srcv.shape[2] if hasattr(srcv, "shape") else None
                ncol = 512 if nm in ("We1", "We3") else 1024
                dstv = dstt[:].rearrange("r (dc c) -> r dc c", c=ncol)
                return sem.inc(gp.dma_start(out=dstv, in_=srcv), 16)

            l1 = [0, 0]
            l3 = [0, 0]
            l2 = 0
            sl4 = 0
            ev4 = 0
            l1[0] = ld_e("We1", 0, we1e[0], sE[0])
            l3[0] = ld_e("We3", 0, we3e[0], sE[0])
            l2 = ld_e("We2", 0, we2e, sE2[0])
            for e in range(NE):
                par = e % 2
                if e + 1 < NE:
                    gp.wait_ge(sT.h, sT.v)
                    l1[1 - par] = ld_e("We1", e + 1, we1e[1 - par], sE[1 - par])
                    l3[1 - par] = ld_e("We3", e + 1, we3e[1 - par], sE[1 - par])
                te.wait_ge(sE[par].h, max(l1[par], l3[par]))
                te.wait_ge(sE2[par if False else 0].h, l2)
                for tg in range(4):
                    to = tg * 512
                    te.wait_ge(sV.h, sV.v)
                    ins = te.matmul(pcb[:, 0:512], e16b[:, _s(e * 128, 128)],
                                    cmT[:, _s(to, 512)], start=True, stop=True)
                    tv = sT.inc(ins)
                    ve.wait_ge(sT.h, tv)
                    ve.tensor_copy(cbce[:, :], pcb[0:128, 0:512])
                    cbv = sV.inc(ve.drain())
                    for ich in range(4):
                        te.wait_ge(sS.h, sl4)
                        for dc in range(8):
                            ins = te.matmul(ph[:, 0:512],
                                            we1e[par][:, _s(dc * 512 + ich * 128, 128)],
                                            x3[dc][:, _s(to, 512)],
                                            start=(dc == 0), stop=(dc == 7))
                        tv = sT.inc(ins)
                        te.wait_ge(sV.h, ev4)
                        for dc in range(8):
                            ins2 = te.matmul(phB[:, 0:512],
                                             we3e[par][:, _s(dc * 512 + ich * 128, 128)],
                                             x3[dc][:, _s(to, 512)],
                                             start=(dc == 0), stop=(dc == 7))
                        tv2 = sT.inc(ins2)
                        sc.wait_ge(sT.h, tv)
                        sc.wait_ge(sV.h, ev4)
                        sc.activation(s1e[:, :], ph[0:128, 0:512], AF.Silu,
                                      bias=be1c[:, _s(e * 4 + ich, 1)])
                        sl4 = self.sS.inc(sc.drain())
                        ve.wait_ge(sT.h, tv2)
                        ve.wait_ge(sS.h, sl4)
                        ve.wait_ge(sV.h, cbv)
                        ve.scalar_tensor_tensor(heg[:, _s(ich * 512, 512)], phB[0:128, 0:512],
                                                be3c[:, _s(e * 4 + ich, 1)], s1e[:, :],
                                                ALU.add, ALU.mult)
                        ve.drain()
                        ve.tensor_mul(heg[:, _s(ich * 512, 512)], heg[:, _s(ich * 512, 512)],
                                      cbce[:, :])
                        ev4 = sV.inc(ve.drain())
                    te.wait_ge(sV.h, sV.v)
                    ev = 0
                    for o in range(8):
                        op = o % 2
                        if o >= 2:
                            te.wait_ge(sV.h, ev)
                        for ich in range(4):
                            ins = te.matmul(pp4[op][:, 0:512],
                                            we2e[:, _s(ich * 1024 + o * 128, 128)],
                                            heg[:, _s(ich * 512, 512)],
                                            start=(ich == 0), stop=(ich == 3))
                        tv = sT.inc(ins)
                        ve.wait_ge(sT.h, tv)
                        ve.tensor_add(r3[o][:, _s(to, 512)], r3[o][:, _s(to, 512)],
                                      pp4[op][0:128, 0:512])
                        ev = sV.inc(ve.drain())
                if e + 1 < NE:
                    gp.wait_ge(sT.h, sT.v)
                    l2 = ld_e("We2", e + 1, we2e, sE2[0])
            te.wait_ge(sV.h, sV.v)
            for o in range(8):
                op = o % 2
                for tg in range(4):
                    ins = te.matmul(pp4[op][:, 0:512], be2b[:, _s(o * 128, 128)],
                                    cmT[:, _s(tg * 512, 512)], start=True, stop=True)
                    tv = sT.inc(ins)
                    ve.wait_ge(sT.h, tv)
                    ve.tensor_add(r3[o][:, _s(tg * 512, 512)], r3[o][:, _s(tg * 512, 512)],
                                  pp4[op][0:128, 0:512])
                    sv = sV.inc(ve.drain())
                    te.wait_ge(sV.h, sv)
            te.wait_ge(sV.h, sV.v)
            self.sT.inc(te.nop())
        self.qbarrier(gp, sy, te, ve, sc)
        with ExitStack() as ls_:
            psL4 = ls_.enter_context(nc.psum_tensor("psL4", [128, OWN], F32))
            scr4 = ls_.enter_context(nc.sbuf_tensor("scr4", [128, OWN], BF16))
            self.emit_ln(r3, 0, OWN, psL4, scr4)
        st4.close()

    def phase5(self):
        nc, te, ve, sc, gp, sy = self.nc, self.te, self.ve, self.sc, self.gp, self.sy
        sT, sV, sO = self.sT, self.sV, self.sO
        r3 = self.r3
        self.qbarrier(gp, sy, te, ve, sc)
        sY = [Sem(nc, self.st, "sY0"), Sem(nc, self.st, "sY1")]
        self.extra_sems += sY
        with ExitStack() as os_:
            pst = [os_.enter_context(nc.psum_tensor(f"pst{i}", [128, 128], F32))
                   for i in range(4)]
            ysb = [os_.enter_context(nc.sbuf_tensor(f"ysb{i}", [128, 1024], F32))
                   for i in range(2)]
            dma_ct = [0, 0]
            for t in range(16):
                par = t % 2
                if t >= 2:
                    ve.wait_ge(sY[par].h, dma_ct[par])
                for w in range(2):
                    te.wait_ge(sV.h, sV.v)
                    for i in range(4):
                        dc = w * 4 + i
                        ins = te.transpose(pst[i][:, 0:128], r3[dc][:, _s(t * 128, 128)],
                                           self.ident_f[:])
                    tv = sT.inc(ins)
                    ve.wait_ge(sT.h, tv)
                    for i in range(4):
                        dc = w * 4 + i
                        ve.tensor_copy(ysb[par][:, _s(dc * 128, 128)], pst[i][:, 0:128])
                    sV.inc(ve.drain())
                sy.wait_ge(sV.h, sV.v)
                dma_ct[par] = sY[par].inc(
                    sy.dma_start(out=self.y_out[_s(t * 128, 128), :], in_=ysb[par][:, :]), 16)
            sy.wait_ge(sY[0].h, sY[0].v)
            sy.wait_ge(sY[1].h, sY[1].v)

    def debug_tail(self):
        gp, sy = self.gp, self.sy
        z = self.st.enter_context(self.nc.sbuf_tensor("zout", [128, D], F32))
        gp.wait_ge(self.sG.h, self.sG.v)
        gp.wait_ge(self.sW.h, self.sW.v)
        gp.wait_ge(self.sX.h, self.sX.v)
        gp.wait_ge(self.sO.h, self.sO.v)
        gp.wait_ge(self.sV.h, self.sV.v)
        gp.wait_ge(self.sS.h, self.sS.v)
        gp.wait_ge(self.sT.h, self.sT.v)
        gp.memset(z[:], 0.0)
        self.sG.inc(gp.drain())
        sy.wait_ge(self.sG.h, self.sG.v)
        sy.wait_ge(self.sV.h, self.sV.v)
        sy.wait_ge(self.sS.h, self.sS.v)
        sy.wait_ge(self.sT.h, self.sT.v)
        for t in range(16):
            self.sO.inc(sy.dma_start(out=self.y_out[_s(t * 128, 128), :], in_=z[:]), 16)
        sy.wait_ge(self.sO.h, self.sO.v)


def stage_inputs(inputs):
    x = np.asarray(inputs["L0_attn_query"], np.float32)
    perm_wp = np.empty(D, np.int64)
    for h in range(16):
        for e in range(HE):
            perm_wp[h * HE + e] = e * 16 + h
    base = {}
    base["wqT"] = np.ascontiguousarray(np.asarray(inputs["L0_attn_Wq"], np.float32).transpose(1, 0, 2).reshape(D, D))
    base["wkT"] = np.ascontiguousarray(np.asarray(inputs["L0_attn_Wk"], np.float32).transpose(1, 0, 2).reshape(D, D))
    base["wvT"] = np.ascontiguousarray(np.asarray(inputs["L0_attn_Wv"], np.float32).transpose(1, 0, 2).reshape(D, D))
    base["wpT"] = np.ascontiguousarray(np.asarray(inputs["L0_attn_Wp"], np.float32)[perm_wp, :])
    base["bp_r"] = np.ascontiguousarray(np.asarray(inputs["L0_attn_bp"], np.float32).reshape(8, 128).T)
    base["w1T"] = np.ascontiguousarray(np.asarray(inputs["L0_ffn_w1"], np.float32).T)
    base["cw"] = np.ascontiguousarray(np.asarray(inputs["L0_ffn_conv_w"], np.float32).reshape(32, 128, 3).transpose(1, 0, 2).reshape(128, 96))
    base["cb"] = np.ascontiguousarray(np.asarray(inputs["L0_ffn_conv_b"], np.float32).reshape(32, 128).T)
    base["w2T"] = np.ascontiguousarray(np.asarray(inputs["L0_ffn_w2"], np.float32).T)
    base["qw1T"] = np.ascontiguousarray(np.asarray(inputs["L1_attn_q_w1"], np.float32).T)
    base["qb1_r"] = np.ascontiguousarray(np.asarray(inputs["L1_attn_q_b1"], np.float32).reshape(8, 128).T)
    base["kw1T"] = np.ascontiguousarray(np.asarray(inputs["L1_attn_k_w1"], np.float32).T)
    base["kb1_r"] = np.ascontiguousarray(np.asarray(inputs["L1_attn_k_b1"], np.float32).reshape(8, 128).T)
    base["pwT"] = np.ascontiguousarray(np.asarray(inputs["L1_attn_p_w"], np.float32).T)
    base["pb_r"] = np.ascontiguousarray(np.asarray(inputs["L1_attn_p_b"], np.float32).reshape(8, 128).T)
    base["gwT"] = np.ascontiguousarray(np.asarray(inputs["L1_ffn_gate_w"], np.float32).T)
    base["gb_row"] = np.ascontiguousarray(np.asarray(inputs["L1_ffn_gate_b"], np.float32).reshape(1, 16))
    base["We1"] = np.ascontiguousarray(np.asarray(inputs["L1_ffn_We1"], np.float32))
    base["be1_c"] = np.ascontiguousarray(
        np.asarray(inputs["L1_ffn_be1"], np.float32).reshape(NE, 4, 128).transpose(2, 0, 1).reshape(128, 64))
    base["We2"] = np.ascontiguousarray(np.asarray(inputs["L1_ffn_We2"], np.float32))
    base["be2"] = np.ascontiguousarray(np.asarray(inputs["L1_ffn_be2"], np.float32))
    base["We3"] = np.ascontiguousarray(np.asarray(inputs["L1_ffn_We3"], np.float32))
    base["be3_c"] = np.ascontiguousarray(
        np.asarray(inputs["L1_ffn_be3"], np.float32).reshape(NE, 4, 128).transpose(2, 0, 1).reshape(128, 64))
    e16 = np.zeros((16, 2048), np.float32)
    for e in range(16):
        e16[e, e * 128:(e + 1) * 128] = 1.0
    base["E16"] = e16
    base["ws1"] = np.ascontiguousarray(np.asarray(inputs["L1_ffn_ws1"], np.float32))
    base["bs1_r"] = np.ascontiguousarray(np.asarray(inputs["L1_ffn_bs1"], np.float32).reshape(8, 128).T)
    base["ws2"] = np.ascontiguousarray(np.asarray(inputs["L1_ffn_ws2"], np.float32))
    base["bs2_r"] = np.ascontiguousarray(np.asarray(inputs["L1_ffn_bs2"], np.float32).reshape(8, 128).T)
    base["ws3"] = np.ascontiguousarray(np.asarray(inputs["L1_ffn_ws3"], np.float32))
    base["bs3_r"] = np.ascontiguousarray(np.asarray(inputs["L1_ffn_bs3"], np.float32).reshape(8, 128).T)
    base["ident"] = np.eye(128, dtype=np.float32)
    base["ones_row"] = np.ones((1, 128), np.float32)
    e2a = np.zeros((128, 66), np.float32)
    e2a[0:64, 64] = 1.0
    e2b = np.zeros((128, 66), np.float32)
    e2b[64:128, 64] = 1.0
    base["e2a"], base["e2b"] = e2a, e2b
    sw = np.zeros((128, 128), np.float32)
    sw[np.arange(64), np.arange(64) + 64] = 1.0
    sw[np.arange(64) + 64, np.arange(64)] = 1.0
    base["swap64"] = sw

    qw2T = np.asarray(inputs["L1_attn_q_w2"], np.float32).T
    kw2T = np.asarray(inputs["L1_attn_k_w2"], np.float32).T
    vwT = np.asarray(inputs["L1_attn_v_w"], np.float32).T
    gwT = np.asarray(inputs["L1_attn_g_w"], np.float32).T
    qb2 = np.asarray(inputs["L1_attn_q_b2"], np.float32)
    kb2 = np.asarray(inputs["L1_attn_k_b2"], np.float32)
    vb = np.asarray(inputs["L1_attn_v_b"], np.float32)
    gb = np.asarray(inputs["L1_attn_g_b"], np.float32)

    maps = []
    for c in range(8):
        b, half = c // 2, c % 2
        m = dict(base)
        xb = x[b]
        m["xTf"] = np.ascontiguousarray(xb.T)
        w = np.zeros((W, D), np.float32)
        s0 = half * OWN
        lo, hi = s0 - 1, s0 + OWN + 1
        cl, ch = max(lo, 0), min(hi, NT)
        w[cl - lo: cl - lo + (ch - cl)] = xb[cl:ch]
        m["xTw"] = np.ascontiguousarray(w.T)
        mk = np.zeros((128, 2), np.float32)
        mk[:, 0] = 0.0 if lo < 0 else 1.0
        mk[:, 1] = 0.0 if hi > NT else 1.0
        m["masks"] = mk
        hs = slice(half * 512, half * 512 + 512)
        m["qw2Ts"] = np.ascontiguousarray(qw2T[:, hs])
        m["kw2Ts"] = np.ascontiguousarray(kw2T[:, hs])
        m["vwTs"] = np.ascontiguousarray(vwT[:, hs])
        m["gwTs"] = np.ascontiguousarray(gwT[:, hs])
        m["qb2_row"] = np.ascontiguousarray(qb2[hs].reshape(1, 512))
        m["kb2_row"] = np.ascontiguousarray(kb2[hs].reshape(1, 512))
        m["vb_s"] = np.ascontiguousarray(vb[hs].reshape(4, 128).T)
        m["gb_s"] = np.ascontiguousarray(gb[hs].reshape(4, 128).T)
        maps.append(m)
    return maps


def kernel(**inputs):
    maps = stage_inputs(inputs)
    nc = Prog(upto=99).build()
    res = run_bass_kernel_spmd(nc, maps, core_ids=list(range(8)))
    out = np.empty((4, NT, D), np.float32)
    for c in range(8):
        b, half = c // 2, c % 2
        out[b, half * OWN:(half + 1) * OWN, :] = res.results[c]["y"]
    return out


# version 03
# speedup vs baseline: 1.2856x; 1.0589x over previous
import numpy as np
import ml_dtypes
from contextlib import ExitStack

import concourse.bass as bass
import concourse.mybir as mybir
from concourse.bass_utils import run_bass_kernel_spmd

F32 = mybir.dt.float32
BF16 = mybir.dt.bfloat16
AF = mybir.ActivationFunctionType
ALU = mybir.AluOpType
AX = mybir.AxisListType.X

MAGIC = 12582912.0
D = 1024
NT = 4096
OWN = 2048
W = 2050
HE = 64
NE = 16
LNEPS = 1e-5
WSL = [(0, 512), (512, 512), (1024, 512), (1536, 512), (2048, 2)]
NSL = [(i * 512, 512) for i in range(8)]


class Sem:
    def __init__(self, nc, st, name):
        self.h = st.enter_context(nc.semaphore(name))
        self.v = 0

    def inc(self, instr, k=1):
        instr.then_inc(self.h, k)
        self.v += k
        return self.v

    def wait(self, eng, v=None):
        eng.wait_ge(self.h, self.v if v is None else v)


def _s(lo, n):
    return slice(lo, lo + n)


class Prog:

    def __init__(self, upto=99, debug=0, p1_pairs=8, p1_proj=True, p1_sect=9, p3_sect=9):
        self.upto = upto
        self.debug = debug
        self.p1_pairs = p1_pairs
        self.p1_proj = p1_proj
        self.p1_sect = p1_sect
        self.p1_norm = 3
        self.p3_sect = p3_sect
        nc = self.nc = bass.Bass("TRN2", target_bir_lowering=False)
        self.te, self.ve, self.sc, self.gp, self.sy = (
            nc.tensor, nc.vector, nc.scalar, nc.gpsimd, nc.sync)
        self.st = ExitStack()
        self.din = {}
        self._declare_dram()
        self._alloc_common()

    def _declare_dram(self):
        nc = self.nc

        def DI(name, shape):
            self.din[name] = nc.dram_tensor(name, list(shape), F32, kind="ExternalInput").ap()

        DI("xTf", (D, NT)); DI("xTw", (D, W)); DI("masks", (128, 2))
        DI("wqT", (D, D)); DI("wkT", (D, D)); DI("wvT", (D, D)); DI("wpT", (D, D)); DI("bp_r", (128, 8))
        DI("w1T", (D, 4096)); DI("cw", (128, 96)); DI("cb", (128, 32)); DI("w2T", (4096, D))
        DI("qw1T", (D, D)); DI("qb1_r", (128, 8)); DI("kw1T", (D, D)); DI("kb1_r", (128, 8))
        DI("qw2Ts", (D, 512)); DI("qb2_row", (1, 512)); DI("kw2Ts", (D, 512)); DI("kb2_row", (1, 512))
        DI("vwTs", (D, 512)); DI("vb_s", (128, 4)); DI("gwTs", (D, 512)); DI("gb_s", (128, 4))
        DI("pwT", (D, D)); DI("pb_r", (128, 8))
        DI("gwT", (D, 16)); DI("gb_row", (1, 16))
        DI("We1", (NE, D, 512)); DI("be1_c", (128, 64)); DI("We2", (NE, 512, D)); DI("be2", (NE, D))
        DI("We3", (NE, D, 512)); DI("be3_c", (128, 64)); DI("E16", (16, 2048))
        DI("ws1", (D, D)); DI("bs1_r", (128, 8)); DI("ws2", (D, D)); DI("bs2_r", (128, 8))
        DI("ws3", (D, D)); DI("bs3_r", (128, 8))
        DI("ident", (128, 128)); DI("ones_row", (1, 128)); DI("e2a", (128, 66)); DI("e2b", (128, 66))
        DI("swap64", (128, 128))

        self.y_out = nc.dram_tensor("y", [OWN, D], F32, kind="ExternalOutput").ap()
        self.dbg = {}
        if self.debug:
            self.dbg["d_r1"] = nc.dram_tensor("d_r1", [D, W], F32, kind="ExternalOutput").ap()
            self.dbg["d_att"] = nc.dram_tensor("d_att", [D, W], F32, kind="ExternalOutput").ap()
            self.dbg["d_r2"] = nc.dram_tensor("d_r2", [D, W], F32, kind="ExternalOutput").ap()
            self.dbg["d_r3"] = nc.dram_tensor("d_r3", [D, OWN], F32, kind="ExternalOutput").ap()
            self.dbg["d_cmb"] = nc.dram_tensor("d_cmb", [16, OWN], F32, kind="ExternalOutput").ap()
            self.dbg["d_z"] = nc.dram_tensor("d_z", [D, OWN], F32, kind="ExternalOutput").ap()

        self.w1q_d = nc.dram_tensor("w1q_d", [32, 8, 128, 128], BF16).ap()
        self.w2q_d = nc.dram_tensor("w2q_d", [8, 32, 128, 128], BF16).ap()
        self.h2_d = nc.dram_tensor("h2_d", [32, 128, OWN], BF16).ap()
        self.x2own_d = nc.dram_tensor("x2own_d", [8, 128, OWN], F32).ap()
        self.cc_in = nc.dram_tensor("cc_in", [D, OWN], BF16).ap()
        self.cc_out = nc.dram_tensor("cc_out", [2 * D, OWN], BF16).ap()

    def _alloc_common(self):
        nc, st = self.nc, self.st
        E = st.enter_context

        def SB(name, p, f, dt, side="left"):
            return E(nc.sbuf_tensor(name, [p, f], dt, side=side))
        self.SB = SB

        self.ident_f = SB("ident_f", 128, 128, F32)
        self.ident_b = SB("ident_b", 128, 128, BF16)
        self.ones_row_f = SB("ones_row_f", 1, 128, F32)
        self.ones_row_b = SB("ones_row_b", 1, 128, BF16)
        self.ones_col_b = SB("ones_col_b", 128, 1, BF16)
        self.ones_col_f = SB("ones_col_f", 128, 1, F32)
        self.e2a = SB("e2a_sb", 128, 66, BF16)
        self.e2b = SB("e2b_sb", 128, 66, BF16)
        self.okk = SB("okk", 66, 128, BF16)
        self.swap64 = SB("swap64_sb", 128, 128, BF16)
        self.masks_sb = SB("masks_sb", 128, 2, F32)
        self.bp_sb = SB("bp_sb", 128, 8, F32)
        self.cw_sb = SB("cw_sb", 128, 96, F32)
        self.cb_sb = SB("cb_sb", 128, 32, F32)
        self.sc0 = SB("sc0", 1, 16, F32)
        self.pr = SB("pr", 1, 6528, F32, side="right")
        self.PRA, self.PRB, self.PRC = 0, 2176, 4352

        self.sW = Sem(nc, st, "sW")
        self.sX = Sem(nc, st, "sX")
        self.sT = Sem(nc, st, "sT")
        self.sV = Sem(nc, st, "sV")
        self.sS = Sem(nc, st, "sS")
        self.sG = Sem(nc, st, "sG")
        self.sO = Sem(nc, st, "sO")

        sy, gp = self.sy, self.gp
        d = self.din
        for src, dst in (("ident", self.ident_f), ("ones_row", self.ones_row_f),
                         ("masks", self.masks_sb), ("bp_r", self.bp_sb),
                         ("cw", self.cw_sb), ("cb", self.cb_sb)):
            self.sX.inc(sy.dma_start(out=dst[:], in_=d[src]), 16)
        for src, dst in (("ident", self.ident_b), ("ones_row", self.ones_row_b),
                         ("e2a", self.e2a), ("e2b", self.e2b), ("swap64", self.swap64)):
            self.sW.inc(gp.dma_start(out=dst[:], in_=d[src]), 16)
        gp.wait_ge(self.sW.h, self.sW.v)
        gp.memset(self.ones_col_b[:], 1.0)
        gp.memset(self.ones_col_f[:], 1.0)
        gp.memset(self.okk[0:1, :], 1.0)
        gp.memset(self.okk[64:65, :], 1.0)
        self._consts = {}
        for v in (LNEPS, MAGIC, -MAGIC, 1.0, -1.0, -1.5):
            t = SB(f"c_{len(self._consts)}", 128, 1, F32)
            gp.memset(t[:], float(v))
            nc.const_aps.register(F32, float(v), t[:])
            self._consts[float(v)] = t
        self.sG.inc(self.gp.drain())

    def build(self):
        lp = self.nc.allow_low_precision("bf16 matmul operands are intentional")
        lp.__enter__()
        if self.upto >= 1:
            self.phase1()
        if self.upto >= 2:
            self.phase2()
        if self.upto >= 3:
            self.phase3()
        if self.upto >= 4:
            self.phase4()
            self.phase5()
        if self.upto < 4:
            self.debug_tail()
        self.st.close()
        return self.nc

    def qbarrier(self, *engs):
        sems = [self.sT, self.sV, self.sS, self.sG, self.sW, self.sX, self.sO]
        sems += getattr(self, "extra_sems", [])
        for e in engs:
            for s in sems:
                e.wait_ge(s.h, s.v)

    def emit_ln(self, x_tiles, off, width, ps_big, scr_b):
        te, ve, sc = self.te, self.ve, self.sc
        sT, sV, sS = self.sT, self.sV, self.sS
        pr, PRA, PRB, PRC = self.pr, self.PRA, self.PRB, self.PRC
        sls = [(o, n) for (o, n) in
               [(i * 512, min(512, width - i * 512)) for i in range((width + 511) // 512)]]
        for which, fn, seg in ((0, AF.Identity, PRA), (1, AF.Square, PRB)):
            for dc in range(8):
                sc.wait_ge(sT.h, sT.v)
                sc.activation(scr_b[:, 0:width], x_tiles[dc][:, _s(off, width)], fn)
                self.sS.inc(sc.drain())
                te.wait_ge(sS.h, sS.v)
                if dc == 0:
                    te.wait_ge(sV.h, sV.v)
                for (o, n) in sls:
                    ins = te.matmul(ps_big[0:1, _s(o, n)], self.ones_col_b[:],
                                    scr_b[:, _s(o, n)], start=(dc == 0), stop=(dc == 7))
                sT.inc(ins)
            ve.wait_ge(sT.h, sT.v)
            ins = ve.tensor_copy(pr[0:1, _s(seg, width)], ps_big[0:1, 0:width])
            sV.inc(ve.drain())
        ve.tensor_scalar_mul(pr[0:1, _s(PRA, width)], pr[0:1, _s(PRA, width)], 1.0 / 1024.0)
        ve.tensor_scalar_mul(pr[0:1, _s(PRB, width)], pr[0:1, _s(PRB, width)], 1.0 / 1024.0)
        sV.inc(ve.drain())
        sc.wait_ge(sV.h, sV.v)
        sc.activation(pr[0:1, _s(PRC, width)], pr[0:1, _s(PRA, width)], AF.Square)
        self.sS.inc(sc.drain())
        ve.wait_ge(sS.h, sS.v)
        ve.tensor_sub(pr[0:1, _s(PRB, width)], pr[0:1, _s(PRB, width)], pr[0:1, _s(PRC, width)])
        sV.inc(ve.drain())
        sc.wait_ge(sV.h, sV.v)
        sc.activation(pr[0:1, _s(PRC, width)], pr[0:1, _s(PRB, width)], AF.Sqrt, bias=LNEPS)
        self.sS.inc(sc.drain())
        ve.wait_ge(sS.h, sS.v)
        ve.reciprocal(pr[0:1, _s(PRB, width)], pr[0:1, _s(PRC, width)])
        sV.inc(ve.drain())
        for seg, op in ((PRA, ALU.subtract), (PRB, ALU.mult)):
            te.wait_ge(sV.h, sV.v)
            for (o, n) in sls:
                ins = te.matmul(ps_big[0:128, _s(o, n)], self.ones_row_f[:],
                                pr[0:1, _s(seg + o, n)], start=True, stop=True)
            sT.inc(ins)
            ve.wait_ge(sT.h, sT.v)
            for dc in range(8):
                ins = ve.tensor_tensor(x_tiles[dc][:, _s(off, width)],
                                       x_tiles[dc][:, _s(off, width)],
                                       ps_big[0:128, 0:width], op)
            sV.inc(ve.drain())
        te.wait_ge(sV.h, sV.v)
        self.sT.inc(te.nop())

    def phase1(self):
        nc, te, ve, sc, gp, sy = self.nc, self.te, self.ve, self.sc, self.gp, self.sy
        d, SB = self.din, self.SB
        sW, sX, sT, sV, sS, sG = self.sW, self.sX, self.sT, self.sV, self.sS, self.sG

        p1st = ExitStack()
        att = [p1st.enter_context(nc.sbuf_tensor(f"att{i}", [128, W], BF16))
               for i in range(8)]

        with ExitStack() as ph:
            EE = ph.enter_context

            def S(name, p, f, dt):
                return EE(nc.sbuf_tensor(name, [p, f], dt))

            xTf_b = [S(f"xTf{i}", 128, NT, BF16) for i in range(8)]
            xTw_b = [S(f"xTw{i}", 128, W, BF16) for i in range(8)]
            kpA = S("kpA", 66, NT, BF16)
            kpB = S("kpB", 66, NT, BF16)
            qpA = S("qpA", 66, W, BF16)
            qpB = S("qpB", 66, W, BF16)
            vg = S("vg", 128, 32 * 130, BF16)
            pt = [S("pt0", 128, 1024, BF16), S("pt1", 128, 1024, BF16),
                  S("pt2", 128, 1024, BF16)]
            wq_p = S("wq_p", 128, 1024, BF16)
            wk_p = S("wk_p", 128, 1024, BF16)
            wv_p = S("wv_p", 128, 1024, BF16)
            sq_b = S("sq_b", 128, 512, BF16)
            t64 = S("t64", 66, 16, F32)
            rb = S("rb", 64, 512, BF16)
            osc = S("osc", 64, W, BF16)

            ps_S = [EE(nc.psum_tensor("ps_S0", [128, 1024], F32)),
                    EE(nc.psum_tensor("ps_S1", [128, 1024], F32))]
            ps_av = EE(nc.psum_tensor("ps_av", [66, 1024], F32))
            ps_x = [EE(nc.psum_tensor("ps_x0", [128, 512], F32)),
                    EE(nc.psum_tensor("ps_x1", [128, 512], F32))]

            for dc in range(8):
                sW.inc(gp.dma_start(out=xTf_b[dc][:], in_=d["xTf"][_s(dc * 128, 128), :]), 16)
                sW.inc(gp.dma_start(out=xTw_b[dc][:], in_=d["xTw"][_s(dc * 128, 128), :]), 16)
            ld_x_done = sW.v
            ve.memset(kpA[64:65, 0:NT], 1.0)
            ve.memset(kpB[64:65, 0:NT], 1.0)
            sV.inc(ve.drain())

            def load_pair_w(p):
                for nm, dst in (("wqT", wq_p), ("wkT", wk_p)):
                    src = d[nm][:, _s(p * 128, 128)].rearrange("(dc r) c -> r dc c", r=128)
                    dstv = dst[:].rearrange("r (dc c) -> r dc c", c=128)
                    sW.inc(gp.dma_start(out=dstv, in_=src), 16)
                return sW.v

            def load_pair_wv(p):
                src = d["wvT"][:, _s(p * 128, 128)].rearrange("(dc r) c -> r dc c", r=128)
                dstv = wv_p[:].rearrange("r (dc c) -> r dc c", c=128)
                sW.inc(gp.dma_start(out=dstv, in_=src), 16)
                return sW.v

            vg_ev = [0, 0]
            kp_ev = [0, 0]
            kp_sq = [0, 0]
            dma_osc = 0

            for p in range(self.p1_pairs):
                wv_done = load_pair_wv(p)
                w_done = load_pair_w(p)

                gp.wait_ge(sW.h, sW.v)
                gp.wait_ge(sT.h, sT.v)
                gp.memset(vg[:, :], 1.0)
                self.sG.inc(gp.drain())
                te.wait_ge(sG.h, sG.v)
                te.wait_ge(sV.h, sV.v)
                for kt in range(32):
                    par = kt % 2
                    if kt >= 2:
                        te.wait_ge(sV.h, vg_ev[par])
                    for dc in range(8):
                        ins = te.matmul(ps_x[par][:, 0:128],
                                        xTf_b[dc][:, _s(kt * 128, 128)],
                                        wv_p[:, _s(dc * 128, 128)],
                                        start=(dc == 0), stop=(dc == 7))
                    tv = sT.inc(ins)
                    ve.wait_ge(sT.h, tv)
                    dst = vg[:, _s(kt * 130, 130)].rearrange("r (h c) -> r h c", c=65)[:, :, 0:64]
                    src = ps_x[par][:, 0:128].rearrange("r (h c) -> r h c", c=64)
                    ins = ve.tensor_copy(dst, src)
                    vg_ev[par] = sV.inc(ve.drain())

                if self.p1_sect < 2:
                    continue
                gp.wait_ge(sW.h, sW.v)
                self.sG.inc(gp.drain())
                ve.wait_ge(sG.h, sG.v)
                ve.memset(t64[64:65, 0:2], 0.0)
                sV.inc(ve.drain())
                te.wait_ge(sG.h, sG.v)
                te.wait_ge(sV.h, sV.v)
                for si, (o, n) in enumerate(NSL):
                    par = si % 2
                    if si >= 2:
                        te.wait_ge(sS.h, kp_sq[par])
                        te.wait_ge(sV.h, kp_ev[par])
                    for dc in range(8):
                        ins = te.matmul(ps_x[par][:, 0:n],
                                        wk_p[:, _s(dc * 128, 128)],
                                        xTf_b[dc][:, _s(o, n)],
                                        start=(dc == 0), stop=(dc == 7))
                    tv = sT.inc(ins)
                    ve.wait_ge(sT.h, tv)
                    ve.tensor_copy(kpA[0:64, _s(o, n)], ps_x[par][0:64, 0:n])
                    ve.tensor_copy(kpB[0:64, _s(o, n)], ps_x[par][64:128, 0:n])
                    kp_ev[par] = sV.inc(ve.drain())
                    if self.p1_norm >= 1:
                        sc.wait_ge(sT.h, tv)
                        sc.wait_ge(sV.h, kp_ev[par])
                        sc.activation(sq_b[:, 0:n], ps_x[par][:, 0:n], AF.Square)
                        kp_sq[par] = self.sS.inc(sc.drain())
                    if self.p1_norm >= 2:
                        te.wait_ge(sS.h, kp_sq[par])
                        if si >= 1:
                            te.wait_ge(sV.h, sV.v)
                        ins = te.matmul(ps_av[:, 0:n], self.e2a[:], sq_b[:, 0:n], start=True, stop=True)
                        ins = te.matmul(ps_av[:, _s(512, n)], self.e2b[:], sq_b[:, 0:n], start=True, stop=True)
                        tv = sT.inc(ins)
                    if self.p1_norm >= 3:
                        ve.wait_ge(sT.h, tv)
                        ve.reduce_max(t64[64:65, 2:3], ps_av[64:65, 0:n], axis=AX)
                        ve.reduce_max(t64[64:65, 3:4], ps_av[64:65, _s(512, n)], axis=AX)
                        ve.drain()
                        ve.tensor_max(t64[64:65, 0:1], t64[64:65, 0:1], t64[64:65, 2:3])
                        ve.tensor_max(t64[64:65, 1:2], t64[64:65, 1:2], t64[64:65, 3:4])
                        sV.inc(ve.drain())
                te.wait_ge(sV.h, sV.v)

                if self.p1_sect < 3:
                    continue
                for si, (o, n) in enumerate(WSL):
                    par = si % 2
                    for dc in range(8):
                        ins = te.matmul(ps_x[par][:, 0:n],
                                        wq_p[:, _s(dc * 128, 128)],
                                        xTw_b[dc][:, _s(o, n)],
                                        start=(dc == 0), stop=(dc == 7))
                    tv = sT.inc(ins)
                    ve.wait_ge(sT.h, tv)
                    ve.tensor_copy(qpA[0:64, _s(o, n)], ps_x[par][0:64, 0:n])
                    ve.tensor_copy(qpB[0:64, _s(o, n)], ps_x[par][64:128, 0:n])
                    sV.inc(ve.drain())
                    sc.wait_ge(sT.h, tv)
                    sc.wait_ge(sV.h, sV.v)
                    sc.activation(sq_b[:, 0:n], ps_x[par][:, 0:n], AF.Square)
                    sv = self.sS.inc(sc.drain())
                    te.wait_ge(sS.h, sv)
                    ins = te.matmul(ps_av[:, 0:n], self.e2a[:], sq_b[:, 0:n], start=True, stop=True)
                    ins = te.matmul(ps_av[:, _s(512, n)], self.e2b[:], sq_b[:, 0:n], start=True, stop=True)
                    tv = sT.inc(ins)
                    ve.wait_ge(sT.h, tv)
                    ve.tensor_copy(qpA[64:65, _s(o, n)], ps_av[64:65, 0:n])
                    ve.tensor_copy(qpB[64:65, _s(o, n)], ps_av[64:65, _s(512, n)])
                    sV.inc(ve.drain())
                    te.wait_ge(sV.h, sV.v)

                if self.p1_sect < 4:
                    continue
                sc.wait_ge(sV.h, sV.v)
                sc.activation(qpA[64:65, 0:W], qpA[64:65, 0:W], AF.Sqrt, scale=t64[64:65, 0:1])
                sc.drain()
                sc.activation(qpB[64:65, 0:W], qpB[64:65, 0:W], AF.Sqrt, scale=t64[64:65, 1:2])
                sv = self.sS.inc(sc.drain())
                ve.wait_ge(sS.h, sv)
                ve.tensor_scalar_mul(qpA[64:65, 0:W], qpA[64:65, 0:W], -1.0)
                ve.tensor_scalar_mul(qpB[64:65, 0:W], qpB[64:65, 0:W], -1.0)
                sV.inc(ve.drain())

                if self.p1_sect < 5:
                    continue
                for hh in range(2):
                    h = 2 * p + hh
                    vbase = hh

                    kpX = kpA if hh == 0 else kpB
                    qpX = qpA if hh == 0 else qpB

                    def s_mm(par, cl, cn, ql, qn):
                        return te.matmul(ps_S[par][:, _s(cl, cn)],
                                         kpX[0:65, _s(kt_ * 128, 128)],
                                         qpX[0:65, _s(ql, qn)], start=True, stop=True)

                    te.wait_ge(sV.h, sV.v)
                    if hh == 1:
                        te.wait_ge(sG.h, dma_osc)
                    for qh in range(2):
                        qo = qh * 1024
                        exp_ct3 = [0, 0, 0]
                        av_last = 0

                        def emit_av(kt_a):
                            par_a = kt_a % 3
                            te.wait_ge(sS.h, exp_ct3[par_a])
                            for ss_ in range(2):
                                ins_ = te.matmul(ps_av[0:65, _s(ss_ * 512, 512)],
                                                 vg[:, _s(kt_a * 130 + vbase * 65, 65)],
                                                 pt[par_a][:, _s(ss_ * 512, 512)],
                                                 start=(kt_a == 0), stop=(kt_a == 31))
                            return sT.inc(ins_)

                        for kt_ in range(32):
                            par = kt_ % 2
                            if kt_ >= 2:
                                te.wait_ge(sS.h, exp_ct3[(kt_ - 2) % 3])
                            for ss in range(2):
                                ins = s_mm(par, ss * 512, 512, qo + ss * 512, 512)
                            tv = sT.inc(ins)
                            sc.wait_ge(sT.h, tv)
                            sc.activation(pt[kt_ % 3][:, 0:1024], ps_S[par][:, 0:1024],
                                          AF.Exp, scale=0.125)
                            exp_ct3[kt_ % 3] = self.sS.inc(sc.drain())
                            if kt_ >= 2:
                                av_last = emit_av(kt_ - 2)
                        av_last = emit_av(30)
                        av_last = emit_av(31)
                        ve.wait_ge(sT.h, av_last)
                        for ss in range(2):
                            if ss == 1:
                                ve.wait_ge(sT.h, tv)
                            ve.reciprocal(sq_b[64:65, 0:512],
                                          ps_av[64:65, _s(ss * 512, 512)])
                            sv = sV.inc(ve.drain())
                            te.wait_ge(sV.h, sv)
                            ins = te.matmul(ps_S[0][0:64, _s(ss * 512, 512)],
                                            self.okk[64:65, 0:64],
                                            sq_b[64:65, 0:512],
                                            start=True, stop=True)
                            tv = sT.inc(ins)
                            ve.wait_ge(sT.h, tv)
                            ve.tensor_copy(rb[:, 0:512], ps_S[0][0:64, _s(ss * 512, 512)])
                            ve.drain()
                            dst = (att[h // 2][0:64, _s(qo + ss * 512, 512)] if hh == 0
                                   else osc[:, _s(qo + ss * 512, 512)])
                            ve.tensor_mul(dst, ps_av[0:64, _s(ss * 512, 512)],
                                          rb[:, 0:512])
                            sV.inc(ve.drain())
                        te.wait_ge(sV.h, sV.v)
                    if self.p1_sect < 6:
                        continue
                    for kt_ in range(32):
                        ins = te.matmul(ps_S[0][:, _s(kt_ * 2, 2)],
                                        kpX[0:65, _s(kt_ * 128, 128)],
                                        qpX[0:65, 2048:2050],
                                        start=(kt_ == 0), stop=(kt_ == 31))
                    tv = sT.inc(ins)
                    sc.wait_ge(sT.h, tv)
                    sc.activation(pt[0][:, 0:64], ps_S[0][:, 0:64], AF.Exp, scale=0.125)
                    sv = self.sS.inc(sc.drain())
                    te.wait_ge(sS.h, sv)
                    for kt_ in range(32):
                        ins = te.matmul(ps_av[0:65, 0:2],
                                        vg[:, _s(kt_ * 130 + vbase * 65, 65)],
                                        pt[0][:, _s(kt_ * 2, 2)],
                                        start=(kt_ == 0), stop=(kt_ == 31))
                    tv = sT.inc(ins)
                    ve.wait_ge(sT.h, tv)
                    ve.reciprocal(sq_b[64:65, _s(0, 2)], ps_av[64:65, 0:2])
                    sv = sV.inc(ve.drain())
                    te.wait_ge(sV.h, sv)
                    ins = te.matmul(ps_S[0][0:64, 0:2], self.okk[64:65, 0:64],
                                    sq_b[64:65, _s(0, 2)], start=True, stop=True)
                    tv = sT.inc(ins)
                    ve.wait_ge(sT.h, tv)
                    ve.tensor_copy(rb[:, 0:2], ps_S[0][0:64, 0:2])
                    ve.drain()
                    if hh == 0:
                        ins = ve.tensor_mul(att[h // 2][0:64, 2048:2050],
                                            ps_av[0:64, 0:2], rb[:, 0:2])
                    else:
                        ins = ve.tensor_mul(osc[:, 2048:2050], ps_av[0:64, 0:2], rb[:, 0:2])
                    sV.inc(ve.drain())
                    te.wait_ge(sV.h, sV.v)
                    if hh == 1:
                        gp.wait_ge(sV.h, sV.v)
                        sW.inc(gp.dma_start(out=att[h // 2][64:128, 0:W], in_=osc[:, 0:W]), 16)
                        gp.wait_ge(sW.h, sW.v)
                        dma_osc = self.sG.inc(gp.drain())

        if not self.p1_proj:
            if self.debug:
                gp.wait_ge(self.sV.h, self.sV.v)
                gp.wait_ge(self.sT.h, self.sT.v)
                for dc in range(8):
                    self.sW.inc(gp.dma_start(out=self.dbg["d_att"][_s(dc * 128, 128), :],
                                             in_=att[dc][:, 0:W]), 16)
            p1st.close()
            return
        self.xw_stack = ExitStack()
        xw = [self.xw_stack.enter_context(
            self.nc.sbuf_tensor(f"xw{i}", [128, W], F32, side="right")) for i in range(8)]
        self.xw = xw
        with ExitStack() as ph:
            EE = ph.enter_context
            wp_full = EE(nc.sbuf_tensor("wp_full", [128, 8192], BF16))
            scr_b = EE(nc.sbuf_tensor("p1scr", [128, W], BF16))
            ps_big = EE(nc.psum_tensor("ps_big", [128, 2560], F32))

            for eng in (sy, gp):
                eng.wait_ge(sT.h, sT.v)
                eng.wait_ge(sV.h, sV.v)
                eng.wait_ge(sS.h, sS.v)
                eng.wait_ge(sW.h, sW.v)
                eng.wait_ge(self.sG.h, self.sG.v)
            for dc in range(8):
                sX.inc(sy.dma_start(out=xw[dc][:], in_=d["xTw"][_s(dc * 128, 128), :]), 16)
            srcv = d["wpT"][:].rearrange("(dc r) c -> r dc c", r=128)
            dstv = wp_full[:].rearrange("r (dc c) -> r dc c", c=1024)
            wp_done = sW.inc(gp.dma_start(out=dstv, in_=srcv), 16)
            ld_x = sX.v
            proj_ev = 0
            te.wait_ge(sW.h, wp_done)
            for o in range(8):
                if o >= 1:
                    te.wait_ge(sV.h, proj_ev)
                for dc in range(8):
                    for (off, n) in WSL:
                        ins = te.matmul(ps_big[:, _s(off, n)],
                                        wp_full[:, _s(dc * 1024 + o * 128, 128)],
                                        att[dc][:, _s(off, n)], start=(dc == 0), stop=(dc == 7))
                tv = sT.inc(ins)
                ve.wait_ge(sT.h, tv)
                ve.wait_ge(sX.h, ld_x)
                ins = ve.scalar_tensor_tensor(xw[o][:, 0:W], ps_big[0:128, 0:W],
                                              self.bp_sb[:, _s(o, 1)], xw[o][:, 0:W],
                                              ALU.add, ALU.add)
                proj_ev = sV.inc(ve.drain())
            if self.debug:
                gp.wait_ge(self.sV.h, self.sV.v)
                for dc in range(8):
                    self.sW.inc(gp.dma_start(out=self.dbg["d_att"][_s(dc * 128, 128), :],
                                             in_=att[dc][:, 0:W]), 16)
            te.wait_ge(sV.h, sV.v)
            self.sT.inc(te.nop())
            self.emit_ln(xw, 0, W, ps_big, scr_b)
            if self.debug:
                sy.wait_ge(self.sV.h, self.sV.v)
                sy.wait_ge(self.sT.h, self.sT.v)
                for dc in range(8):
                    self.sO.inc(sy.dma_start(out=self.dbg["d_r1"][_s(dc * 128, 128), :],
                                             in_=xw[dc][:, 0:W]), 16)
        p1st.close()

    def phase2(self):
        nc, te, ve, sc, gp, sy = self.nc, self.te, self.ve, self.sc, self.gp, self.sy
        d = self.din
        sW, sX, sT, sV, sS, sG = self.sW, self.sX, self.sT, self.sV, self.sS, self.sG
        pr, PRA, PRB, PRC = self.pr, self.PRA, self.PRB, self.PRC
        xw = self.xw
        st2 = ExitStack()

        def S2(name, p, f, dt):
            return st2.enter_context(nc.sbuf_tensor(name, [p, f], dt))

        ps_big = st2.enter_context(nc.psum_tensor("p2big", [128, 2560], F32))
        ps_tr = st2.enter_context(nc.psum_tensor("p2tr", [128, 2176], BF16))
        sF = self.sF = [Sem(nc, self.st, "sF0"), Sem(nc, self.st, "sF1")]
        sH = [Sem(nc, self.st, "sH0"), Sem(nc, self.st, "sH1")]
        sH2 = [Sem(nc, self.st, "sH20"), Sem(nc, self.st, "sH21")]
        sW2 = [Sem(nc, self.st, "sW20"), Sem(nc, self.st, "sW21")]
        self.extra_sems = sF + sH + sH2 + sW2

        mcol = S2("mcol", 128, 4, F32)
        self.qbarrier(gp, sy, te, ve, sc)
        cwm = S2("cwm", 128, 96, F32)
        acc2 = S2("acc2", 128, OWN, BF16)
        am_cols = S2("am_cols", 128, 17, F32)
        amT = S2("amT", 17, 128, F32)
        gp.memset(am_cols[:], 0.0)
        self.sG.inc(gp.drain())

        def wstats_quant(nm, nrow, ncol, spill, spill_rear, mc):
            with ExitStack() as ws:
                wf = [ws.enter_context(nc.sbuf_tensor(f"wf_{nm}{i}", [128, ncol], F32))
                      for i in range(2)]
                wq = ws.enter_context(nc.sbuf_tensor(f"wqq_{nm}", [128, ncol], BF16))
                accc = ws.enter_context(nc.sbuf_tensor(f"accc_{nm}", [128, 2], F32))
                nch = nrow // 128
                sF = self.sF
                lded = [0, 0]
                for ch in range(2):
                    lded[ch] = sF[ch].inc(
                        sy.dma_start(out=wf[ch][:], in_=d[nm][_s(ch * 128, 128), :]), 16)
                for ch in range(nch):
                    par = ch % 2
                    ve.wait_ge(sF[par].h, lded[par])
                    ve.reduce_sum(accc[:, 1:2], wf[par][:, :], axis=AX,
                                  apply_absolute_value=True)
                    ve.drain()
                    if ch == 0:
                        ve.tensor_copy(accc[:, 0:1], accc[:, 1:2])
                    else:
                        ve.tensor_add(accc[:, 0:1], accc[:, 0:1], accc[:, 1:2])
                    sV.inc(ve.drain())
                    if ch + 2 < nch:
                        sy.wait_ge(sV.h, sV.v)
                        lded[par] = sF[par].inc(
                            sy.dma_start(out=wf[par][:], in_=d[nm][_s((ch + 2) * 128, 128), :]), 16)
                te.wait_ge(sV.h, sV.v)
                ins = te.matmul(ps_big[0:1, 0:1], self.ones_col_f[:], accc[:, 0:1],
                                start=True, stop=True)
                tv = sT.inc(ins)
                ve.wait_ge(sT.h, tv)
                ve.tensor_scalar(pr[0:1, 0:1], ps_big[0:1, 0:1], 1.0 / (nrow * ncol), 1e-5,
                                 ALU.mult, ALU.max)
                ve.drain()
                ve.reciprocal(pr[0:1, 1:2], pr[0:1, 0:1])
                sv = sV.inc(ve.drain())
                te.wait_ge(sV.h, sv)
                te.matmul(ps_big[0:128, 0:1], self.ones_row_f[:], pr[0:1, 0:1],
                          start=True, stop=True)
                ins = te.matmul(ps_big[0:128, 1:2], self.ones_row_f[:], pr[0:1, 1:2],
                                start=True, stop=True)
                tv = sT.inc(ins)
                ve.wait_ge(sT.h, tv)
                ve.tensor_copy(mcol[:, _s(mc, 2)], ps_big[0:128, 0:2])
                sv = sV.inc(ve.drain())
                prev_spill = self.sW.v
                for ch in range(nch):
                    par = ch % 2
                    sy.wait_ge(sV.h, sV.v)
                    lded[par] = sF[par].inc(
                        sy.dma_start(out=wf[par][:], in_=d[nm][_s(ch * 128, 128), :]), 16)
                    ve.wait_ge(sF[par].h, lded[par])
                    ve.tensor_scalar(wf[par][:], wf[par][:], mcol[:, _s(mc + 1, 1)], MAGIC,
                                     ALU.mult, ALU.add)
                    ve.drain()
                    ve.tensor_scalar(wf[par][:], wf[par][:], -MAGIC, None, ALU.add)
                    ve.drain()
                    ve.wait_ge(self.sW.h, prev_spill)
                    ve.tensor_scalar(wq[:, :], wf[par][:], -1.0, 1.0, ALU.max, ALU.min)
                    sv = sV.inc(ve.drain())
                    gp.wait_ge(sV.h, sv)
                    oap, iap = spill_rear(ch, wq)
                    prev_spill = self.sW.inc(gp.dma_start(out=oap, in_=iap), 16)
                gp.wait_ge(self.sW.h, self.sW.v)
                self.sG.inc(gp.drain())

        def w1_rear(ch, wq):
            return (self.w1q_d[:, ch].rearrange("jc r c -> r jc c"),
                    wq[:].rearrange("r (jc c) -> r jc c", c=128))

        def w2_rear(ch, wq):
            return (self.w2q_d[:, ch].rearrange("o r c -> r o c"),
                    wq[:].rearrange("r (o c) -> r o c", c=128))

        wstats_quant("w1T", D, 4096, self.w1q_d, w1_rear, 0)

        ve.wait_ge(sG.h, sG.v)
        ve.tensor_scalar(cwm[:, :], self.cw_sb[:, :], mcol[:, 0:1], None, ALU.mult)
        sV.inc(ve.drain())

        with ExitStack() as fs:
            def SF(name, p, f, dt):
                return fs.enter_context(nc.sbuf_tensor(name, [p, f], dt))
            xq = [SF(f"xq{i}", 128, W, BF16) for i in range(8)]
            scrf = SF("scrf", 128, W, F32)
            scr_b = SF("scr2b", 128, W, BF16)
            acc_b = SF("acc_b", 128, W, BF16)
            sc_bc = SF("sc_bc", 128, W, F32)
            inv_bc = SF("inv_bc", 128, W, F32)
            h_int = SF("h_int", 128, W, F32)
            t0 = SF("t0", 128, OWN, F32)
            h2b = [SF("h2b0", 128, OWN, BF16), SF("h2b1", 128, OWN, BF16)]
            w1b = [SF("w1b0", 128, 1024, BF16), SF("w1b1", 128, 1024, BF16)]

            for dc in range(8):
                sc.wait_ge(sV.h, sV.v)
                sc.wait_ge(sT.h, sT.v)
                sc.activation(scr_b[:, 0:W] if dc else acc_b[:, 0:W],
                              xw[dc][:, 0:W], AF.Abs)
                ssv = self.sS.inc(sc.drain())
                if dc:
                    ve.wait_ge(sS.h, ssv)
                    ve.tensor_max(acc_b[:, 0:W], acc_b[:, 0:W], scr_b[:, 0:W])
                    sV.inc(ve.drain())

            def cross_part_amax(acc_t, width, seg):
                ntile = (width + 127) // 128
                te.wait_ge(sV.h, sV.v)
                for t in range(ntile):
                    n = min(128, width - t * 128)
                    ins = te.transpose(ps_tr[0:n, _s(t * 128, 128)],
                                       acc_t[:, _s(t * 128, n)], self.ident_b[:])
                tv = sT.inc(ins)
                ve.wait_ge(sT.h, tv)
                for t in range(ntile):
                    n = min(128, width - t * 128)
                    ve.reduce_max(am_cols[0:n, _s(t, 1)], ps_tr[0:n, _s(t * 128, 128)],
                                  axis=AX)
                sv = sV.inc(ve.drain())
                te.wait_ge(sV.h, sv)
                ins = te.transpose(ps_big[0:17, 0:128], am_cols[:, 0:17], self.ident_f[:])
                tv = sT.inc(ins)
                ve.wait_ge(sT.h, tv)
                ve.tensor_copy(amT[:, :], ps_big[0:17, 0:128])
                sv = sV.inc(ve.drain())
                gp.wait_ge(sV.h, sv)
                self.sW.inc(gp.dma_start(out=pr[0:1, _s(seg, 2176)], in_=amT[:, :]), 16)
                gp.wait_ge(self.sW.h, self.sW.v)
                return self.sG.inc(gp.drain())

            gv = cross_part_amax(acc_b, W, PRA)
            ve.wait_ge(sG.h, gv)
            ve.tensor_scalar(pr[0:1, _s(PRA, W)], pr[0:1, _s(PRA, W)], 1e-5, None, ALU.max)
            ve.drain()
            ve.reciprocal(pr[0:1, _s(PRB, W)], pr[0:1, _s(PRA, W)])
            ve.drain()
            ve.tensor_scalar(pr[0:1, _s(PRB, W)], pr[0:1, _s(PRB, W)], 127.0, None, ALU.mult)
            ve.drain()
            ve.tensor_scalar(pr[0:1, _s(PRC, W)], pr[0:1, _s(PRA, W)], 1.0 / 127.0, None, ALU.mult)
            sv = sV.inc(ve.drain())

            def bcast_row(seg, width, dst_t):
                te.wait_ge(sV.h, sV.v)
                for (o, n) in [(i * 512, min(512, width - i * 512))
                               for i in range((width + 511) // 512)]:
                    ins = te.matmul(ps_big[0:128, _s(o, n)], self.ones_row_f[:],
                                    pr[0:1, _s(seg + o, n)], start=True, stop=True)
                tv = sT.inc(ins)
                ve.wait_ge(sT.h, tv)
                ve.tensor_copy(dst_t[:, 0:width], ps_big[0:128, 0:width])
                return sV.inc(ve.drain())

            bcast_row(PRB, W, sc_bc)
            bcast_row(PRC, W, inv_bc)
            for dc in range(8):
                ve.tensor_mul(scrf[:, 0:W], xw[dc][:, 0:W], sc_bc[:, 0:W])
                ve.drain()
                ve.tensor_scalar(scrf[:, 0:W], scrf[:, 0:W], MAGIC, -MAGIC, ALU.add, ALU.add)
                ve.drain()
                ve.tensor_mul(xq[dc][:, 0:W], scrf[:, 0:W], inv_bc[:, 0:W])
                sV.inc(ve.drain())

            sy.wait_ge(self.sG.h, self.sG.v)
            sy.wait_ge(self.sW.h, self.sW.v)
            ldv = [0, 0]
            for jc in range(2):
                ldv[jc] = sF[jc].inc(
                    sy.dma_start(out=w1b[jc][:].rearrange("r (dc c) -> r dc c", c=128),
                                 in_=self.w1q_d[jc].rearrange("dc r c -> r dc c")), 16)
            ev_ct = 0
            for jc in range(32):
                par = jc % 2
                te.wait_ge(sF[par].h, ldv[par])
                te.wait_ge(sV.h, sV.v)
                if jc >= 1:
                    te.wait_ge(sS.h, ev_ct)
                for dc in range(8):
                    for (off, n) in WSL:
                        ins = te.matmul(ps_big[:, _s(off, n)], w1b[par][:, _s(dc * 128, 128)],
                                        xq[dc][:, _s(off, n)], start=(dc == 0), stop=(dc == 7))
                tv = sT.inc(ins)
                if jc + 2 < 32:
                    sy.wait_ge(sT.h, tv)
                    ldv[par] = sF[par].inc(
                        sy.dma_start(out=w1b[par][:].rearrange("r (dc c) -> r dc c", c=128),
                                     in_=self.w1q_d[jc + 2].rearrange("dc r c -> r dc c")), 16)
                ve.wait_ge(sT.h, tv)
                ve.tensor_scalar(ps_big[:, 0:1], ps_big[:, 0:1], self.masks_sb[:, 0:1],
                                 None, ALU.mult)
                ve.tensor_scalar(ps_big[:, 2049:2050], ps_big[:, 2049:2050],
                                 self.masks_sb[:, 1:2], None, ALU.mult)
                mv = sV.inc(ve.drain())
                sc.wait_ge(sV.h, mv)
                sc.activation(h_int[:, 0:W], ps_big[0:128, 0:W], AF.Identity)
                ev_ct = self.sS.inc(sc.drain())
                sc.activation(t0[:, 0:OWN], h_int[:, 0:OWN], AF.Identity,
                              scale=cwm[:, _s(jc * 3, 1)], bias=self.cb_sb[:, _s(jc, 1)])
                cv = self.sS.inc(sc.drain())
                ve.wait_ge(sS.h, cv)
                ve.scalar_tensor_tensor(t0[:, 0:OWN], h_int[:, 1:1 + OWN],
                                        cwm[:, _s(jc * 3 + 1, 1)], t0[:, 0:OWN],
                                        ALU.mult, ALU.add)
                ve.drain()
                ve.scalar_tensor_tensor(t0[:, 0:OWN], h_int[:, 2:2 + OWN],
                                        cwm[:, _s(jc * 3 + 2, 1)], t0[:, 0:OWN],
                                        ALU.mult, ALU.add)
                cv2 = sV.inc(ve.drain())
                sc.wait_ge(sV.h, cv2)
                if jc >= 2:
                    sc.wait_ge(sH[par].h, 16 * (jc // 2))
                sc.activation(h2b[par][:, 0:OWN], t0[:, 0:OWN], AF.Silu)
                sc.drain()
                sc.activation(scr_b[:, 0:OWN], t0[:, 0:OWN], AF.Abs)
                av = self.sS.inc(sc.drain())
                ve.wait_ge(sS.h, av)
                if jc == 0:
                    ve.tensor_copy(acc2[:, :], scr_b[:, 0:OWN])
                else:
                    ve.tensor_max(acc2[:, :], acc2[:, :], scr_b[:, 0:OWN])
                sV.inc(ve.drain())
                sy.wait_ge(sS.h, av)
                sH[par].inc(sy.dma_start(out=self.h2_d[jc], in_=h2b[par][:, 0:OWN]), 16)

            gv = cross_part_amax(acc2, OWN, PRA)
            ve.wait_ge(sG.h, gv)
            ve.tensor_scalar(pr[0:1, _s(PRA, OWN)], pr[0:1, _s(PRA, OWN)], 1e-5, None, ALU.max)
            ve.drain()
            ve.reciprocal(pr[0:1, _s(PRB, OWN)], pr[0:1, _s(PRA, OWN)])
            ve.drain()
            ve.tensor_scalar(pr[0:1, _s(PRB, OWN)], pr[0:1, _s(PRB, OWN)], 127.0, None, ALU.mult)
            ve.drain()
            ve.tensor_scalar(pr[0:1, _s(PRC, OWN)], pr[0:1, _s(PRA, OWN)], 1.0 / 127.0,
                             None, ALU.mult)
            sV.inc(ve.drain())

        self.qbarrier(gp, sy, te, ve, sc)
        wstats_quant("w2T", 4096, D, self.w2q_d, w2_rear, 2)

        self.qbarrier(gp, sy, te, ve, sc)
        with ExitStack() as fs:
            def SF(name, p, f, dt):
                return fs.enter_context(nc.sbuf_tensor(name, [p, f], dt))
            h2q = [SF(f"h2q{i}", 128, 1024, BF16) for i in range(32)]
            h2l = [SF("h2l0", 128, 1024, BF16), SF("h2l1", 128, 1024, BF16)]
            w2b = [SF("w2b0", 128, 32 * 128, BF16), SF("w2b1", 128, 32 * 128, BF16)]
            s2bc = SF("s2bc", 128, 1024, F32)
            i2bc = SF("i2bc", 128, 1024, F32)
            scrf2 = SF("scrf2", 128, 1024, F32)

            for tg in range(2):
                to = tg * 1024
                bcast_done = 0
                te.wait_ge(sV.h, sV.v)
                for (o, n) in ((0, 512), (512, 512)):
                    ins = te.matmul(ps_big[0:128, _s(o, n)], self.ones_row_f[:],
                                    pr[0:1, _s(PRB + to + o, n)], start=True, stop=True)
                    ins2 = te.matmul(ps_big[0:128, _s(1024 + o, n)], self.ones_row_f[:],
                                     pr[0:1, _s(PRC + to + o, n)], start=True, stop=True)
                tv = sT.inc(ins2)
                ve.wait_ge(sT.h, tv)
                ve.tensor_copy(s2bc[:, :], ps_big[0:128, 0:1024])
                ve.tensor_copy(i2bc[:, :], ps_big[0:128, 1024:2048])
                bcast_done = sV.inc(ve.drain())
                ld2 = [0, 0]
                if tg == 0:
                    sy.wait_ge(sH[0].h, sH[0].v)
                    sy.wait_ge(sH[1].h, sH[1].v)
                for jc in range(2):
                    ld2[jc] = sH2[jc].inc(
                        sy.dma_start(out=h2l[jc][:], in_=self.h2_d[jc][:, _s(to, 1024)]), 16)
                for jc in range(32):
                    par = jc % 2
                    ve.wait_ge(sH2[par].h, ld2[par])
                    ve.wait_ge(sT.h, sT.v)
                    ve.tensor_mul(scrf2[:, :], h2l[par][:, :], s2bc[:, :])
                    ve.drain()
                    ve.tensor_scalar(scrf2[:, :], scrf2[:, :], MAGIC, -MAGIC, ALU.add, ALU.add)
                    ve.drain()
                    ve.tensor_mul(h2q[jc][:, :], scrf2[:, :], i2bc[:, :])
                    qv = sV.inc(ve.drain())
                    if jc + 2 < 32:
                        sy.wait_ge(sV.h, qv)
                        ld2[par] = sH2[par].inc(
                            sy.dma_start(out=h2l[par][:], in_=self.h2_d[jc + 2][:, _s(to, 1024)]), 16)
                ldw = [0, 0]
                if tg == 0:
                    sy.wait_ge(sG.h, sG.v)
                for o in range(2):
                    ldw[o] = sW2[o].inc(
                        sy.dma_start(out=w2b[o][:].rearrange("r (jc c) -> r jc c", c=128),
                                     in_=self.w2q_d[o].rearrange("jc r c -> r jc c")), 16)
                ev2 = 0
                for o in range(8):
                    par = o % 2
                    te.wait_ge(sW2[par].h, ldw[par])
                    te.wait_ge(sV.h, sV.v if o == 0 else ev2)
                    pp = ps_big[:, _s(par * 1024, 1024)]
                    for jc in range(32):
                        for (off, n) in ((0, 512), (512, 512)):
                            ins = te.matmul(pp[:, _s(off, n)], w2b[par][:, _s(jc * 128, 128)],
                                            h2q[jc][:, _s(off, n)],
                                            start=(jc == 0), stop=(jc == 31))
                    tv = sT.inc(ins)
                    if o + 2 < 8:
                        sy.wait_ge(sT.h, tv)
                        ldw[par] = sW2[par].inc(
                            sy.dma_start(out=w2b[par][:].rearrange("r (jc c) -> r jc c", c=128),
                                         in_=self.w2q_d[o + 2].rearrange("jc r c -> r jc c")), 16)
                    ve.wait_ge(sT.h, tv)
                    ins = ve.scalar_tensor_tensor(xw[o][:, _s(1 + to, 1024)], pp[0:128, :],
                                                  mcol[:, 2:3], xw[o][:, _s(1 + to, 1024)],
                                                  ALU.mult, ALU.add)
                    ev2 = sV.inc(ve.drain())
            te.wait_ge(sV.h, sV.v)
            self.sT.inc(te.nop())
        self.qbarrier(sc, ve, te, gp, sy)
        with ExitStack() as fs:
            scr_ln = fs.enter_context(nc.sbuf_tensor("scr_ln", [128, OWN], BF16))
            x2b = [fs.enter_context(nc.sbuf_tensor(f"x2b{i}", [128, OWN], BF16))
                   for i in range(8)]
            self.emit_ln(xw, 1, OWN, ps_big, scr_ln)
            if self.debug:
                sy.wait_ge(self.sV.h, self.sV.v)
                sy.wait_ge(self.sT.h, self.sT.v)
                for dc in range(8):
                    self.sO.inc(sy.dma_start(out=self.dbg["d_r2"][_s(dc * 128, 128), :],
                                             in_=xw[dc][:, 0:W]), 16)
            for dc in range(8):
                sc.wait_ge(self.sT.h, self.sT.v)
                sc.activation(x2b[dc][:, :], xw[dc][:, 1:1 + OWN], AF.Identity)
            sv = self.sS.inc(sc.drain())
            sy.wait_ge(self.sS.h, sv)
            gp.wait_ge(self.sS.h, sv)
            for dc in range(8):
                self.sO.inc(sy.dma_start(out=self.x2own_d[dc], in_=xw[dc][:, 1:1 + OWN]), 16)
                self.sW.inc(gp.dma_start(out=self.cc_in[_s(dc * 128, 128), :],
                                         in_=x2b[dc][:, :]), 16)
            gp.wait_ge(self.sW.h, self.sW.v)
            self.sG.inc(gp.collective_compute(
                "AllGather", ALU.bypass,
                replica_groups=[[0, 1], [2, 3], [4, 5], [6, 7]],
                ins=[self.cc_in], outs=[self.cc_out]))
        st2.close()
        self.qbarrier(gp, sy, te, ve, sc)
        self.xw_stack.close()

    def phase3(self):
        nc, te, ve, sc, gp, sy = self.nc, self.te, self.ve, self.sc, self.gp, self.sy
        d = self.din
        sW, sX, sT, sV, sS, sG = self.sW, self.sX, self.sT, self.sV, self.sS, self.sG
        pr = self.pr
        half = 512

        st3 = ExitStack()

        def S3(name, p, f, dt):
            return st3.enter_context(nc.sbuf_tensor(name, [p, f], dt))

        X2T = [S3(f"X2T{i}", 128, OWN, BF16) for i in range(8)]
        qb2r = S3("qb2r", 1, 512, BF16)
        kb2r = S3("kb2r", 1, 512, BF16)
        zc = S3("zc", 64, 8, F32)
        A2 = S3("A2", 128, 512, BF16)
        self._b1sb = {}
        for nm, ncols in (("qb1_r", 8), ("kb1_r", 8), ("vb_s", 4), ("gb_s", 4), ("pb_r", 8)):
            t = S3("sb_" + nm, 128, ncols, F32)
            self.sX.inc(sy.dma_start(out=t[:], in_=d[nm]), 16)
            self._b1sb[nm] = t
        bias_ld = self.sX.v

        stA = ExitStack()

        def SA(name, p, f, dt):
            return stA.enter_context(nc.sbuf_tensor(name, [p, f], dt))

        x2f = [SA(f"x2f{i}", 128, NT, BF16) for i in range(8)]
        t1 = [SA(f"t1_{i}", 128, 512, BF16) for i in range(8)]
        q2n = [SA(f"q2n{i}", 128, 512, BF16) for i in range(4)]
        k520 = SA("k520", 128, 4 * 520, BF16)
        sp1 = SA("sp1", 128, 512, F32)
        sp2 = SA("sp2", 128, 512, F32)
        w1b = [SA("p3w0", 128, 1024, BF16), SA("p3w1", 128, 1024, BF16)]
        w2b = [SA("p3v0", 128, 512, BF16), SA("p3v1", 128, 512, BF16)]
        zr = SA("zr", 66, 1024, F32)
        vT = SA("vT", 128, NT, BF16)
        Op = SA("Op", 128, NT, BF16)
        OpSW = SA("OpSW", 128, NT, BF16)
        Osc = SA("Osc", 64, NT, BF16)

        sQ1 = [Sem(nc, self.st, "sQ10"), Sem(nc, self.st, "sQ11")]
        sQ2 = [Sem(nc, self.st, "sQ20"), Sem(nc, self.st, "sQ21")]
        sXo = [Sem(nc, self.st, "sXo0"), Sem(nc, self.st, "sXo1")]
        self.extra_sems += sQ1 + sQ2 + sXo

        with ExitStack() as pstk:
            pq = [pstk.enter_context(nc.psum_tensor("pq0", [128, 512], F32)),
                  pstk.enter_context(nc.psum_tensor("pq1", [128, 512], F32))]
            psA = pstk.enter_context(nc.psum_tensor("psA", [66, 512], F32))
            pq2 = [pstk.enter_context(nc.psum_tensor(f"pq2_{i}", [128, 512], F32))
                   for i in range(4)]

            sy.wait_ge(sG.h, sG.v)
            for dc in range(8):
                sX.inc(sy.dma_start(out=x2f[dc][:, 0:OWN],
                                    in_=self.cc_out[_s(dc * 128, 128), :]), 16)
                sX.inc(sy.dma_start(out=x2f[dc][:, OWN:NT],
                                    in_=self.cc_out[_s(D + dc * 128, 128), :]), 16)
            self.sW.inc(gp.dma_start(out=qb2r[:], in_=d["qb2_row"]), 16)
            self.sW.inc(gp.dma_start(out=kb2r[:], in_=d["kb2_row"]), 16)
            b2_ld = self.sW.v
            x2_ld = sX.v

            def load_w1chunk(nm, o, par):
                srcv = d[nm][:, _s(o * 128, 128)].rearrange("(dc r) c -> r dc c", r=128)
                dstv = w1b[par][:].rearrange("r (dc c) -> r dc c", c=128)
                return sQ1[par].inc(gp.dma_start(out=dstv, in_=srcv), 16)

            def load_w2chunk(nm, o, par):
                return sQ2[par].inc(
                    gp.dma_start(out=w2b[par][:], in_=d[nm][_s(o * 128, 128), :]), 16)

            for tg in range(8 if self.p3_sect >= 2 else 0):
                to = tg * 512
                for side, w1nm, w2nm, b1c, b2r in (
                        (0, "qw1T", "qw2Ts", "qb1_r", qb2r),
                        (1, "kw1T", "kw2Ts", "kb1_r", kb2r)):
                    l1 = [0, 0]
                    for o in range(8):
                        par = o % 2
                        gp.wait_ge(sT.h, sT.v)
                        l1[par] = load_w1chunk(w1nm, o, par)
                        te.wait_ge(sQ1[par].h, l1[par])
                        te.wait_ge(sX.h, max(x2_ld, bias_ld))
                        te.wait_ge(sW.h, b2_ld)
                        if o >= 1:
                            te.wait_ge(sS.h, sS.v)
                        for dc in range(8):
                            ins = te.matmul(pq[par][:, 0:512],
                                            w1b[par][:, _s(dc * 128, 128)],
                                            x2f[dc][:, _s(to, 512)],
                                            start=(dc == 0), stop=(dc == 7))
                        tv = sT.inc(ins)
                        sc.wait_ge(sT.h, tv)
                        sc.activation(t1[o][:, :], pq[par][:, 0:512], AF.Gelu,
                                      bias=self._b1sb[b1c][:, _s(o, 1)])
                        self.sS.inc(sc.drain())
                    l2 = [0, 0]
                    for o in range(8):
                        par = o % 2
                        gp.wait_ge(sT.h, sT.v)
                        l2[par] = load_w2chunk(w2nm, o, par)
                        te.wait_ge(sQ2[par].h, l2[par])
                        te.wait_ge(sS.h, sS.v)
                        if o == 0:
                            te.wait_ge(sV.h, sV.v)
                        for tc in range(4):
                            ins = te.matmul(pq2[tc][:, 0:512],
                                            t1[o][:, _s(tc * 128, 128)],
                                            w2b[par][:, 0:512],
                                            start=(o == 0), stop=False)
                        sT.inc(ins)
                    for tc in range(4):
                        ins = te.matmul(pq2[tc][:, 0:512], self.okk[0:1, 0:128],
                                        b2r[:, 0:512], start=False, stop=True)
                    tv = sT.inc(ins)
                    for tc in range(4):
                        sc.wait_ge(sT.h, tv)
                        sc.wait_ge(sV.h, sV.v)
                        sc.activation(sp1[:, :], pq2[tc][:, 0:512], AF.Abs)
                        sc.drain()
                        sc.activation(sp1[:, :], sp1[:, :], AF.Exp, scale=-1.0)
                        sc.drain()
                        sc.activation(sp2[:, :], sp1[:, :], AF.Ln, bias=1.0)
                        sc.drain()
                        sc.activation(sp1[:, :], pq2[tc][:, 0:512], AF.Relu)
                        sv = self.sS.inc(sc.drain())
                        ve.wait_ge(sS.h, sv)
                        if side == 0:
                            ve.tensor_add(q2n[tc][:, :], sp1[:, :], sp2[:, :])
                            sV.inc(ve.drain())
                        else:
                            ve.tensor_add(sp2[:, :], sp1[:, :], sp2[:, :])
                            ve.drain()
                            dst = k520[:, _s(tc * 520, 520)].rearrange(
                                "r (h c) -> r h c", c=65)[:, :, 0:64]
                            ve.tensor_copy(dst, sp2[:, :].rearrange("r (h c) -> r h c", c=64))
                            ve.drain()
                            for hh in range(8):
                                ve.reduce_sum(k520[:, _s(tc * 520 + hh * 65 + 64, 1)],
                                              k520[:, _s(tc * 520 + hh * 65, 64)], axis=AX)
                            sV.inc(ve.drain())
                te.wait_ge(sV.h, sV.v)
                for tc in range(4):
                    for hh in range(8):
                        ins = te.matmul(psA[0:65, _s(hh * 64, 64)],
                                        k520[:, _s(tc * 520 + hh * 65, 65)],
                                        q2n[tc][:, _s(hh * 64, 64)],
                                        start=(tg == 0 and tc == 0 and hh == 0),
                                        stop=(tg == 7 and tc == 3 and hh == 7))
                sT.inc(ins)

            if self.p3_sect < 3:
                stA.close()
                st3.close()
                return
            tv = sT.v
            ve.wait_ge(sT.h, tv)
            ve.tensor_scalar(A2[0:64, 0:512], psA[0:64, 0:512], 0.125, None, ALU.mult)
            ve.drain()
            for hh in range(8):
                ve.tensor_add(A2[0:64, _s(hh * 64, 64)], A2[0:64, _s(hh * 64, 64)],
                              self.ident_b[0:64, 0:64])
            ve.drain()
            ve.tensor_scalar(zr[64:65, 0:512], psA[64:65, 0:512], 0.125, 1024.0,
                             ALU.mult, ALU.add)
            ve.drain()
            ve.reciprocal(zr[64:65, _s(512, 512)], zr[64:65, 0:512])
            sv = sV.inc(ve.drain())
            gp.wait_ge(sV.h, sv)
            self.sW.inc(gp.dma_start(out=A2[64:128, 0:512], in_=A2[0:64, 0:512]), 16)
            gp.wait_ge(sW.h, sW.v)
            a2dma = self.sG.inc(gp.drain())
            te.wait_ge(sV.h, sv)
            for wave in range(2):
                for i in range(4):
                    hh = wave * 4 + i
                    ins = te.transpose(pq2[i][0:64, 0:1],
                                       zr[64:65, _s(512 + hh * 64, 64)],
                                       self.ident_f[64:65, 64:65])
                tv = sT.inc(ins)
                ve.wait_ge(sT.h, tv)
                for i in range(4):
                    ve.tensor_copy(zc[:, _s(wave * 4 + i, 1)], pq2[i][0:64, 0:1])
                sv = sV.inc(ve.drain())
                te.wait_ge(sV.h, sv)

            if self.p3_sect < 4:
                stA.close()
                st3.close()
                return
            for pp_ in range(4 if self.p3_sect >= 4 else 0):
                for which, wnm, bcol in ((0, "vwTs", "vb_s"), (1, "gwTs", "gb_s")):
                    gp.wait_ge(sT.h, sT.v)
                    srcv = d[wnm][:, _s(pp_ * 128, 128)].rearrange("(dc r) c -> r dc c", r=128)
                    dstv = w1b[0][:].rearrange("r (dc c) -> r dc c", c=128)
                    lw0 = sQ1[0].inc(gp.dma_start(out=dstv, in_=srcv), 16)
                    te.wait_ge(sQ1[0].h, lw0)
                    te.wait_ge(sV.h, sV.v)
                    for s in range(8):
                        par = s % 2
                        if s >= 2:
                            te.wait_ge(sS.h, sS.v)
                        for dc in range(8):
                            ins = te.matmul(pq[par][:, 0:512],
                                            w1b[0][:, _s(dc * 128, 128)],
                                            x2f[dc][:, _s(s * 512, 512)],
                                            start=(dc == 0), stop=(dc == 7))
                        tv = sT.inc(ins)
                        sc.wait_ge(sT.h, tv)
                        sc.activation(vT[:, _s(s * 512, 512)], pq[par][:, 0:512], AF.Gelu,
                                      bias=self._b1sb[bcol][:, _s(pp_, 1)])
                        self.sS.inc(sc.drain())
                    if which == 0:
                        te.wait_ge(sS.h, sS.v)
                        te.wait_ge(sG.h, a2dma)
                        ev = 0
                        for hh in range(2):
                            h = 2 * pp_ + hh
                            for s in range(8):
                                par = s % 2
                                if s >= 2 or hh == 1:
                                    te.wait_ge(sV.h, ev)
                                if hh == 0:
                                    ins = te.matmul(pq2[par][0:64, 0:512],
                                                    A2[0:64, _s(h * 64, 64)],
                                                    vT[0:64, _s(s * 512, 512)],
                                                    start=True, stop=True)
                                else:
                                    ins = te.matmul(pq2[par][0:64, 0:512],
                                                    A2[64:128, _s(h * 64, 64)],
                                                    vT[64:128, _s(s * 512, 512)],
                                                    start=True, stop=True)
                                tv = sT.inc(ins)
                                ve.wait_ge(sT.h, tv)
                                dst = Op[0:64, _s(s * 512, 512)] if hh == 0 \
                                    else Osc[:, _s(s * 512, 512)]
                                ve.tensor_scalar(dst, pq2[par][0:64, 0:512],
                                                 zc[:, _s(h, 1)], None, ALU.mult)
                                ev = sV.inc(ve.drain())
                        gp.wait_ge(sV.h, ev)
                        self.sW.inc(gp.dma_start(out=Op[64:128, :], in_=Osc[:, :]), 16)
                        gp.wait_ge(sW.h, sW.v)
                        self.sG.inc(gp.drain())
                ve.wait_ge(sG.h, sG.v)
                ve.wait_ge(sS.h, sS.v)
                ve.tensor_mul(Op[:, :], Op[:, :], vT[:, :])
                gv = sV.inc(ve.drain())
                te.wait_ge(sV.h, gv)
                for s in range(8):
                    par = s % 2
                    if s >= 2:
                        te.wait_ge(sV.h, sV.v)
                    ins = te.matmul(pq[par][:, 0:512], self.swap64[:],
                                    Op[:, _s(s * 512, 512)], start=True, stop=True)
                    tv = sT.inc(ins)
                    ve.wait_ge(sT.h, tv)
                    ve.tensor_copy(OpSW[:, _s(s * 512, 512)], pq[par][:, 0:512])
                    sV.inc(ve.drain())
                for hh in range(2):
                    h = 2 * pp_ + hh
                    for pc in range(8):
                        for jh in range(2):
                            j = 2 * pc + jh
                            srcT = Op if jh == hh else OpSW
                            ve.tensor_copy(X2T[pc][_s(jh * 64, 64), _s(h * 256, 256)],
                                           srcT[_s(jh * 64, 64), j:NT:16])
                    sV.inc(ve.drain())
            te.wait_ge(sV.h, sV.v)
            self.sT.inc(te.nop())
        if self.p3_sect < 5:
            stA.close()
            st3.close()
            return
        self.qbarrier(te, ve, sc, gp, sy)
        stA.close()
        r3 = [self.SB(f"r3_{i}", 128, OWN, F32, side="right") for i in range(8)]
        self.r3 = r3
        with ExitStack() as pstk:
            pp2 = [pstk.enter_context(nc.psum_tensor("pp2_0", [128, 1024], F32)),
                   pstk.enter_context(nc.psum_tensor("pp2_1", [128, 1024], F32))]
            psLN = pstk.enter_context(nc.psum_tensor("psLN", [128, 2048], F32))
            wpf = st3.enter_context(nc.sbuf_tensor("p3wp", [128, 8192], BF16))
            xo = [st3.enter_context(nc.sbuf_tensor(f"xo{i}", [128, OWN], F32))
                  for i in range(2)]
            scr3 = st3.enter_context(nc.sbuf_tensor("scr3", [128, OWN], BF16))
            srcv = d["pwT"][:].rearrange("(dc r) c -> r dc c", r=128)
            dstv = wpf[:].rearrange("r (dc c) -> r dc c", c=1024)
            wp_done = sW.inc(gp.dma_start(out=dstv, in_=srcv), 16)
            xol = [0, 0]
            for o in range(2):
                xol[o] = sXo[o].inc(sy.dma_start(out=xo[o][:], in_=self.x2own_d[o]), 16)
            te.wait_ge(sW.h, wp_done)
            ev3 = 0
            for o in range(8):
                par = o % 2
                if o >= 1:
                    te.wait_ge(sV.h, ev3)
                for th in range(2):
                    for dc in range(8):
                        for ss in range(2):
                            ins = te.matmul(pp2[th][:, _s(ss * 512, 512)],
                                            wpf[:, _s(dc * 1024 + o * 128, 128)],
                                            X2T[dc][:, _s(th * 1024 + ss * 512, 512)],
                                            start=(dc == 0), stop=(dc == 7))
                tv = sT.inc(ins)
                ve.wait_ge(sT.h, tv)
                ve.wait_ge(sXo[par].h, xol[par])
                for th in range(2):
                    ins = ve.scalar_tensor_tensor(r3[o][:, _s(th * 1024, 1024)],
                                                  pp2[th][0:128, :],
                                                  self._b1sb["pb_r"][:, _s(o, 1)],
                                                  xo[par][:, _s(th * 1024, 1024)],
                                                  ALU.add, ALU.add)
                ev3 = sV.inc(ve.drain())
                if o + 2 < 8:
                    sy.wait_ge(sV.h, ev3)
                    xol[par] = sXo[par].inc(
                        sy.dma_start(out=xo[par][:], in_=self.x2own_d[o + 2]), 16)
            te.wait_ge(sV.h, sV.v)
            self.sT.inc(te.nop())
            if self.p3_sect >= 6:
                self.emit_ln(r3, 0, OWN, psLN, scr3)
            if self.debug and self.p3_sect >= 7:
                sy.wait_ge(self.sV.h, self.sV.v)
                sy.wait_ge(self.sT.h, self.sT.v)
                for dc in range(8):
                    self.sO.inc(sy.dma_start(out=self.dbg["d_r3"][_s(dc * 128, 128), :],
                                             in_=r3[dc][:, 0:OWN]), 16)
        st3.close()

    def phase4(self):
        nc, te, ve, sc, gp, sy = self.nc, self.te, self.ve, self.sc, self.gp, self.sy
        d = self.din
        sW, sX, sT, sV, sS, sG = self.sW, self.sX, self.sT, self.sV, self.sS, self.sG
        r3 = self.r3
        BIG = 1.0e30

        self.qbarrier(gp, sy, te, ve, sc)
        st4 = ExitStack()

        def S4(name, p, f, dt):
            return st4.enter_context(nc.sbuf_tensor(name, [p, f], dt))

        x3 = [S4(f"x3_{i}", 128, OWN, BF16) for i in range(8)]
        cmT = S4("cmT", 16, OWN, BF16)
        e16b = S4("e16b", 16, OWN, BF16)
        be2b = S4("be2b", 16, D, BF16)
        be1c = S4("be1c", 128, 64, F32)
        be3c = S4("be3c", 128, 64, F32)
        bsc = S4("bsc", 128, 24, F32)
        sE = [Sem(nc, self.st, "sE0"), Sem(nc, self.st, "sE1")]
        sE2 = [Sem(nc, self.st, "sE20"), Sem(nc, self.st, "sE21")]
        self.extra_sems += sE + sE2

        for dc in range(8):
            sc.wait_ge(sT.h, sT.v)
            sc.activation(x3[dc][:, :], r3[dc][:, :], AF.Identity)
        x3v = self.sS.inc(sc.drain())
        self.sW.inc(gp.dma_start(out=e16b[:], in_=d["E16"]), 16)
        self.sW.inc(gp.dma_start(out=be2b[:], in_=d["be2"]), 16)
        w_small = self.sW.v
        self.sX.inc(sy.dma_start(out=be1c[:], in_=d["be1_c"]), 16)
        self.sX.inc(sy.dma_start(out=be3c[:], in_=d["be3_c"]), 16)
        self.sX.inc(sy.dma_start(out=bsc[:, 0:8], in_=d["bs1_r"]), 16)
        self.sX.inc(sy.dma_start(out=bsc[:, 8:16], in_=d["bs2_r"]), 16)
        self.sX.inc(sy.dma_start(out=bsc[:, 16:24], in_=d["bs3_r"]), 16)
        x_small = self.sX.v

        with ExitStack() as gs_:
            def SG(name, p, f, dt):
                return gs_.enter_context(nc.sbuf_tensor(name, [p, f], dt))
            gwb = SG("gwb", 128, 8 * 16, BF16)
            gbr = SG("gbr", 1, 16, F32)
            gbb = SG("gbb", 128, 16, F32)
            sgT = SG("sgT", 16, OWN, F32)
            sgN = SG("sgN", 128, 256, F32)
            sgO = SG("sgO", 128, 256, F32)
            comb = SG("comb", 128, 256, F32)
            p6 = SG("p6", 128, 24, F32)
            g4 = SG("g4", 128, 24, F32)
            cN = SG("cN", 128, 8, F32)
            sgM = SG("sgM", 128, 16, F32)
            sgM2 = SG("sgM2", 128, 16, F32)
            sgE = SG("sgE", 128, 16, F32)
            psG = gs_.enter_context(nc.psum_tensor("psG", [16, OWN], F32))
            ptr = gs_.enter_context(nc.psum_tensor("ptr", [128, 128], F32))

            srcv = d["gwT"][:].rearrange("(dc r) c -> r dc c", r=128)
            dstv = gwb[:].rearrange("r (dc c) -> r dc c", c=16)
            gl = self.sW.inc(gp.dma_start(out=dstv, in_=srcv), 16)
            self.sX.inc(sy.dma_start(out=gbr[:], in_=d["gb_row"]), 16)
            gb_ld = self.sX.v
            te.wait_ge(sW.h, gl)
            te.wait_ge(sS.h, x3v)
            for (o, n) in [(i * 512, 512) for i in range(4)]:
                for dc in range(8):
                    ins = te.matmul(psG[:, _s(o, n)], gwb[:, _s(dc * 16, 16)],
                                    x3[dc][:, _s(o, n)], start=(dc == 0), stop=(dc == 7))
            tv = sT.inc(ins)
            ve.wait_ge(sT.h, tv)
            ve.tensor_copy(sgT[:, :], psG[0:16, 0:OWN])
            sv = sV.inc(ve.drain())
            te.wait_ge(sX.h, gb_ld)
            te.wait_ge(sV.h, sv)
            ins = te.matmul(ptr[0:128, 0:16], self.ones_row_f[:], gbr[0:1, 0:16],
                            start=True, stop=True)
            tv = sT.inc(ins)
            ve.wait_ge(sT.h, tv)
            ve.tensor_copy(gbb[:, :], ptr[0:128, 0:16])
            sv = sV.inc(ve.drain())

            GS, GS2, GSN, EQG, KPG = 0, 4, 8, 12, 16
            for t in range(16):
                S = _s(t * 16, 16)
                te.wait_ge(sV.h, sV.v)
                ins = te.transpose(ptr[0:128, 0:16], sgT[:, _s(t * 128, 128)],
                                   self.ident_f[0:16, 0:16])
                tv = sT.inc(ins)
                ve.wait_ge(sT.h, tv)
                ve.tensor_copy(sgN[:, S], ptr[0:128, 0:16])
                ve.drain()
                ve.reduce_max(cN[:, 0:1], sgN[:, S], axis=AX, negate=True)
                nm = sV.inc(ve.drain())
                sc.wait_ge(sV.h, nm)
                sc.activation(sgO[:, S], sgN[:, S], AF.Exp, bias=cN[:, 0:1],
                              accum_out=cN[:, 1:2])
                ex = self.sS.inc(sc.drain())
                ve.wait_ge(sS.h, ex)
                ve.reciprocal(cN[:, 2:3], cN[:, 1:2])
                ve.drain()
                ve.tensor_scalar(sgO[:, S], sgO[:, S], cN[:, 2:3], None, ALU.mult)
                ve.drain()
                ve.tensor_add(sgN[:, S], sgO[:, S], gbb[:, :])
                ve.drain()
                for i, (a, b) in enumerate([(0, 1), (0, 2), (0, 3), (1, 2), (1, 3), (2, 3)]):
                    ve.tensor_add(p6[:, _s(4 * i, 4)],
                                  sgN[:, t * 16 + a:(t + 1) * 16:4],
                                  sgN[:, t * 16 + b:(t + 1) * 16:4])
                ve.drain()
                ve.tensor_max(g4[:, _s(GS, 4)], p6[:, 0:4], p6[:, 4:8])
                ve.drain()
                for i in range(2, 6):
                    ve.tensor_max(g4[:, _s(GS, 4)], g4[:, _s(GS, 4)], p6[:, _s(4 * i, 4)])
                    ve.drain()
                ve.reduce_max(cN[:, 3:4], g4[:, _s(GS, 4)], axis=AX, negate=True)
                sv = sV.inc(ve.drain())
                sc.wait_ge(sV.h, sv)
                sc.activation(g4[:, _s(GSN, 4)], g4[:, _s(GS, 4)], AF.Sign, bias=cN[:, 3:4])
                sc.drain()
                sc.activation(g4[:, _s(EQG, 4)], g4[:, _s(GSN, 4)], AF.Abs)
                ab = self.sS.inc(sc.drain())
                ve.wait_ge(sS.h, ab)
                ve.tensor_scalar(g4[:, _s(EQG, 4)], g4[:, _s(EQG, 4)], -1.0, 1.0,
                                 ALU.mult, ALU.add)
                ve.drain()
                ve.reduce_sum(cN[:, 4:5], g4[:, _s(EQG, 4)], axis=AX)
                ve.drain()
                ve.scalar_tensor_tensor(g4[:, _s(GS2, 4)], g4[:, _s(EQG, 4)], -BIG,
                                        g4[:, _s(GS, 4)], ALU.mult, ALU.add)
                ve.drain()
                ve.reduce_max(cN[:, 5:6], g4[:, _s(GS2, 4)], axis=AX)
                sv = sV.inc(ve.drain())
                sc.wait_ge(sV.h, sv)
                sc.activation(cN[:, 6:7], cN[:, 4:5], AF.Sign, bias=-1.5)
                sg2 = self.sS.inc(sc.drain())
                ve.wait_ge(sS.h, sg2)
                ve.tensor_scalar(cN[:, 6:7], cN[:, 6:7], 1.0, 1.0, ALU.add, ALU.min)
                ve.drain()
                ve.tensor_scalar(cN[:, 7:8], cN[:, 3:4], -1.0, None, ALU.mult)
                ve.drain()
                ve.tensor_sub(cN[:, 7:8], cN[:, 7:8], cN[:, 5:6])
                ve.drain()
                ve.tensor_mul(cN[:, 7:8], cN[:, 7:8], cN[:, 6:7])
                ve.drain()
                ve.tensor_add(cN[:, 7:8], cN[:, 7:8], cN[:, 5:6])
                ve.drain()
                ve.tensor_scalar(cN[:, 7:8], cN[:, 7:8], -1.0, None, ALU.mult)
                sv = sV.inc(ve.drain())
                sc.wait_ge(sV.h, sv)
                sc.activation(g4[:, _s(KPG, 4)], g4[:, _s(GS, 4)], AF.Sign, bias=cN[:, 7:8])
                kp = self.sS.inc(sc.drain())
                ve.wait_ge(sS.h, kp)
                ve.tensor_scalar(g4[:, _s(KPG, 4)], g4[:, _s(KPG, 4)], 1.0, 1.0,
                                 ALU.add, ALU.min)
                ve.drain()
                ve.tensor_scalar(g4[:, _s(GS2, 4)], g4[:, _s(KPG, 4)], -1.0, None, ALU.add)
                ve.drain()
                for i in range(4):
                    ve.scalar_tensor_tensor(sgM[:, i:16:4], g4[:, _s(GS2, 4)], BIG,
                                            sgN[:, t * 16 + i:(t + 1) * 16:4],
                                            ALU.mult, ALU.add)
                ve.drain()
                ve.reduce_max(cN[:, 3:4], sgM[:, :], axis=AX, negate=True)
                sv = sV.inc(ve.drain())
                sc.wait_ge(sV.h, sv)
                sc.activation(sgE[:, :], sgM[:, :], AF.Sign, bias=cN[:, 3:4])
                sc.drain()
                sc.activation(sgE[:, :], sgE[:, :], AF.Abs)
                ab = self.sS.inc(sc.drain())
                ve.wait_ge(sS.h, ab)
                ve.tensor_scalar(sgE[:, :], sgE[:, :], -1.0, 1.0, ALU.mult, ALU.add)
                ve.drain()
                ve.reduce_sum(cN[:, 4:5], sgE[:, :], axis=AX)
                ve.drain()
                ve.scalar_tensor_tensor(sgM2[:, :], sgE[:, :], -BIG, sgM[:, :],
                                        ALU.mult, ALU.add)
                ve.drain()
                ve.reduce_max(cN[:, 5:6], sgM2[:, :], axis=AX)
                sv = sV.inc(ve.drain())
                sc.wait_ge(sV.h, sv)
                sc.activation(cN[:, 6:7], cN[:, 4:5], AF.Sign, bias=-1.5)
                sg2 = self.sS.inc(sc.drain())
                ve.wait_ge(sS.h, sg2)
                ve.tensor_scalar(cN[:, 6:7], cN[:, 6:7], 1.0, 1.0, ALU.add, ALU.min)
                ve.drain()
                ve.tensor_scalar(cN[:, 7:8], cN[:, 3:4], -1.0, None, ALU.mult)
                ve.drain()
                ve.tensor_sub(cN[:, 7:8], cN[:, 7:8], cN[:, 5:6])
                ve.drain()
                ve.tensor_mul(cN[:, 7:8], cN[:, 7:8], cN[:, 6:7])
                ve.drain()
                ve.tensor_add(cN[:, 7:8], cN[:, 7:8], cN[:, 5:6])
                ve.drain()
                ve.tensor_scalar(cN[:, 7:8], cN[:, 7:8], -1.0, None, ALU.mult)
                sv = sV.inc(ve.drain())
                sc.wait_ge(sV.h, sv)
                sc.activation(sgE[:, :], sgM[:, :], AF.Sign, bias=cN[:, 7:8])
                kp = self.sS.inc(sc.drain())
                ve.wait_ge(sS.h, kp)
                ve.tensor_scalar(sgE[:, :], sgE[:, :], 1.0, 1.0, ALU.add, ALU.min)
                ve.drain()
                ve.tensor_mul(comb[:, S], sgO[:, S], sgE[:, :])
                sV.inc(ve.drain())
            for t in range(16):
                te.wait_ge(sV.h, sV.v)
                ins = te.transpose(ptr[0:16, 0:128], comb[:, _s(t * 16, 16)],
                                   self.ident_f[:])
                tv = sT.inc(ins)
                ve.wait_ge(sT.h, tv)
                ve.tensor_copy(cmT[:, _s(t * 128, 128)], ptr[0:16, 0:128])
                sV.inc(ve.drain())
            te.wait_ge(sV.h, sV.v)
            self.sT.inc(te.nop())
            if self.debug:
                gp.wait_ge(sV.h, sV.v)
                self.sW.inc(gp.dma_start(out=self.dbg["d_cmb"], in_=cmT[:, :]), 16)
        self.qbarrier(gp, sy, te, ve, sc)
        with ExitStack() as ss_:
            def SS(name, p, f, dt):
                return ss_.enter_context(nc.sbuf_tensor(name, [p, f], dt))
            hs = [SS(f"hs{i}", 128, OWN, BF16) for i in range(8)]
            s1b = SS("s1b", 128, OWN, BF16)
            wsb = [SS("wsb0", 128, 1024, BF16), SS("wsb1", 128, 1024, BF16)]
            psS4 = ss_.enter_context(nc.psum_tensor("psS4", [128, OWN], F32))

            def ld_ws(nm, o, par):
                srcv = d[nm][:, _s(o * 128, 128)].rearrange("(dc r) c -> r dc c", r=128)
                dstv = wsb[par][:].rearrange("r (dc c) -> r dc c", c=128)
                return sE[par].inc(gp.dma_start(out=dstv, in_=srcv), 16)

            for o in range(8):
                par = o % 2
                gp.wait_ge(sT.h, sT.v)
                l1 = ld_ws("ws1", o, par)
                te.wait_ge(sE[par].h, l1)
                te.wait_ge(sV.h, sV.v)
                for (off, n) in [(i * 512, 512) for i in range(4)]:
                    for dc in range(8):
                        ins = te.matmul(psS4[:, _s(off, n)], wsb[par][:, _s(dc * 128, 128)],
                                        x3[dc][:, _s(off, n)], start=(dc == 0), stop=(dc == 7))
                tv = sT.inc(ins)
                sc.wait_ge(sT.h, tv)
                sc.activation(s1b[:, :], psS4[0:128, 0:OWN], AF.Silu, bias=bsc[:, _s(o, 1)])
                sl = self.sS.inc(sc.drain())
                gp.wait_ge(sS.h, sl)
                l3 = ld_ws("ws3", o, par)
                te.wait_ge(sE[par].h, l3)
                for (off, n) in [(i * 512, 512) for i in range(4)]:
                    for dc in range(8):
                        ins = te.matmul(psS4[:, _s(off, n)], wsb[par][:, _s(dc * 128, 128)],
                                        x3[dc][:, _s(off, n)], start=(dc == 0), stop=(dc == 7))
                tv = sT.inc(ins)
                ve.wait_ge(sT.h, tv)
                ve.scalar_tensor_tensor(hs[o][:, :], psS4[0:128, 0:OWN],
                                        bsc[:, _s(16 + o, 1)], s1b[:, :],
                                        ALU.add, ALU.mult)
                sV.inc(ve.drain())
            for o in range(8):
                par = o % 2
                gp.wait_ge(sT.h, sT.v)
                l2 = ld_ws("ws2", o, par)
                te.wait_ge(sE[par].h, l2)
                te.wait_ge(sV.h, sV.v)
                for (off, n) in [(i * 512, 512) for i in range(4)]:
                    for dc in range(8):
                        ins = te.matmul(psS4[:, _s(off, n)], wsb[par][:, _s(dc * 128, 128)],
                                        hs[dc][:, _s(off, n)], start=(dc == 0), stop=(dc == 7))
                tv = sT.inc(ins)
                ve.wait_ge(sT.h, tv)
                ve.scalar_tensor_tensor(r3[o][:, :], psS4[0:128, 0:OWN],
                                        bsc[:, _s(8 + o, 1)], r3[o][:, :],
                                        ALU.add, ALU.add)
                sV.inc(ve.drain())
            te.wait_ge(sV.h, sV.v)
            self.sT.inc(te.nop())
            if self.debug:
                sy.wait_ge(sV.h, sV.v)
                for dc in range(8):
                    self.sO.inc(sy.dma_start(out=self.dbg["d_z"][_s(dc * 128, 128), :],
                                             in_=r3[dc][:, :]), 16)
        self.qbarrier(gp, sy, te, ve, sc)
        with ExitStack() as es_:
            def SE(name, p, f, dt):
                return es_.enter_context(nc.sbuf_tensor(name, [p, f], dt))
            we1e = [SE("we1e0", 128, 4096, BF16), SE("we1e1", 128, 4096, BF16)]
            we3e = [SE("we3e0", 128, 4096, BF16), SE("we3e1", 128, 4096, BF16)]
            we2e = SE("we2e", 128, 4096, BF16)
            heg = SE("heg", 128, 2048, BF16)
            s1e = SE("s1e", 128, 512, BF16)
            cbce = SE("cbce", 128, 512, BF16)
            ph = es_.enter_context(nc.psum_tensor("ph4", [128, 512], F32))
            phB = es_.enter_context(nc.psum_tensor("ph4b", [128, 512], F32))
            pcb = es_.enter_context(nc.psum_tensor("pcb", [128, 512], F32))
            pp4 = [es_.enter_context(nc.psum_tensor(f"pp4_{i}", [128, 512], F32))
                   for i in range(4)]

            def ld_e(nm, e, dstt, sem):
                srcv = d[nm][e].rearrange("(dc r) c -> r dc c", r=128)
                cw_ = srcv.shape[2] if hasattr(srcv, "shape") else None
                ncol = 512 if nm in ("We1", "We3") else 1024
                dstv = dstt[:].rearrange("r (dc c) -> r dc c", c=ncol)
                return sem.inc(gp.dma_start(out=dstv, in_=srcv), 16)

            l1 = [0, 0]
            l3 = [0, 0]
            l2 = 0
            sl4 = 0
            ev4 = 0
            l1[0] = ld_e("We1", 0, we1e[0], sE[0])
            l3[0] = ld_e("We3", 0, we3e[0], sE[0])
            l2 = ld_e("We2", 0, we2e, sE2[0])
            for e in range(NE):
                par = e % 2
                if e + 1 < NE:
                    gp.wait_ge(sT.h, sT.v)
                    l1[1 - par] = ld_e("We1", e + 1, we1e[1 - par], sE[1 - par])
                    l3[1 - par] = ld_e("We3", e + 1, we3e[1 - par], sE[1 - par])
                te.wait_ge(sE[par].h, max(l1[par], l3[par]))
                te.wait_ge(sE2[par if False else 0].h, l2)
                for tg in range(4):
                    to = tg * 512
                    te.wait_ge(sV.h, sV.v)
                    ins = te.matmul(pcb[:, 0:512], e16b[:, _s(e * 128, 128)],
                                    cmT[:, _s(to, 512)], start=True, stop=True)
                    tv = sT.inc(ins)
                    ve.wait_ge(sT.h, tv)
                    ve.tensor_copy(cbce[:, :], pcb[0:128, 0:512])
                    cbv = sV.inc(ve.drain())
                    for ich in range(4):
                        te.wait_ge(sS.h, sl4)
                        for dc in range(8):
                            ins = te.matmul(ph[:, 0:512],
                                            we1e[par][:, _s(dc * 512 + ich * 128, 128)],
                                            x3[dc][:, _s(to, 512)],
                                            start=(dc == 0), stop=(dc == 7))
                        tv = sT.inc(ins)
                        te.wait_ge(sV.h, ev4)
                        for dc in range(8):
                            ins2 = te.matmul(phB[:, 0:512],
                                             we3e[par][:, _s(dc * 512 + ich * 128, 128)],
                                             x3[dc][:, _s(to, 512)],
                                             start=(dc == 0), stop=(dc == 7))
                        tv2 = sT.inc(ins2)
                        sc.wait_ge(sT.h, tv)
                        sc.wait_ge(sV.h, ev4)
                        sc.activation(s1e[:, :], ph[0:128, 0:512], AF.Silu,
                                      bias=be1c[:, _s(e * 4 + ich, 1)])
                        sl4 = self.sS.inc(sc.drain())
                        ve.wait_ge(sT.h, tv2)
                        ve.wait_ge(sS.h, sl4)
                        ve.wait_ge(sV.h, cbv)
                        ve.scalar_tensor_tensor(heg[:, _s(ich * 512, 512)], phB[0:128, 0:512],
                                                be3c[:, _s(e * 4 + ich, 1)], s1e[:, :],
                                                ALU.add, ALU.mult)
                        ve.drain()
                        ve.tensor_mul(heg[:, _s(ich * 512, 512)], heg[:, _s(ich * 512, 512)],
                                      cbce[:, :])
                        ev4 = sV.inc(ve.drain())
                    te.wait_ge(sV.h, sV.v)
                    evs = [0, 0, 0, 0]
                    for o in range(8):
                        op = o % 4
                        if o >= 4:
                            te.wait_ge(sV.h, evs[op])
                        for ich in range(4):
                            ins = te.matmul(pp4[op][:, 0:512],
                                            we2e[:, _s(ich * 1024 + o * 128, 128)],
                                            heg[:, _s(ich * 512, 512)],
                                            start=(ich == 0), stop=(ich == 3))
                        tv = sT.inc(ins)
                        ve.wait_ge(sT.h, tv)
                        ve.tensor_add(r3[o][:, _s(to, 512)], r3[o][:, _s(to, 512)],
                                      pp4[op][0:128, 0:512])
                        evs[op] = sV.inc(ve.drain())
                if e + 1 < NE:
                    gp.wait_ge(sT.h, sT.v)
                    l2 = ld_e("We2", e + 1, we2e, sE2[0])
            te.wait_ge(sV.h, sV.v)
            for o in range(8):
                op = o % 2
                for tg in range(4):
                    ins = te.matmul(pp4[op][:, 0:512], be2b[:, _s(o * 128, 128)],
                                    cmT[:, _s(tg * 512, 512)], start=True, stop=True)
                    tv = sT.inc(ins)
                    ve.wait_ge(sT.h, tv)
                    ve.tensor_add(r3[o][:, _s(tg * 512, 512)], r3[o][:, _s(tg * 512, 512)],
                                  pp4[op][0:128, 0:512])
                    sv = sV.inc(ve.drain())
                    te.wait_ge(sV.h, sv)
            te.wait_ge(sV.h, sV.v)
            self.sT.inc(te.nop())
        self.qbarrier(gp, sy, te, ve, sc)
        with ExitStack() as ls_:
            psL4 = ls_.enter_context(nc.psum_tensor("psL4", [128, OWN], F32))
            scr4 = ls_.enter_context(nc.sbuf_tensor("scr4", [128, OWN], BF16))
            self.emit_ln(r3, 0, OWN, psL4, scr4)
        st4.close()

    def phase5(self):
        nc, te, ve, sc, gp, sy = self.nc, self.te, self.ve, self.sc, self.gp, self.sy
        sT, sV, sO = self.sT, self.sV, self.sO
        r3 = self.r3
        self.qbarrier(gp, sy, te, ve, sc)
        sY = [Sem(nc, self.st, "sY0"), Sem(nc, self.st, "sY1")]
        self.extra_sems += sY
        with ExitStack() as os_:
            pst = [os_.enter_context(nc.psum_tensor(f"pst{i}", [128, 128], F32))
                   for i in range(4)]
            ysb = [os_.enter_context(nc.sbuf_tensor(f"ysb{i}", [128, 1024], F32))
                   for i in range(2)]
            dma_ct = [0, 0]
            for t in range(16):
                par = t % 2
                if t >= 2:
                    ve.wait_ge(sY[par].h, dma_ct[par])
                for w in range(2):
                    te.wait_ge(sV.h, sV.v)
                    for i in range(4):
                        dc = w * 4 + i
                        ins = te.transpose(pst[i][:, 0:128], r3[dc][:, _s(t * 128, 128)],
                                           self.ident_f[:])
                    tv = sT.inc(ins)
                    ve.wait_ge(sT.h, tv)
                    for i in range(4):
                        dc = w * 4 + i
                        ve.tensor_copy(ysb[par][:, _s(dc * 128, 128)], pst[i][:, 0:128])
                    sV.inc(ve.drain())
                sy.wait_ge(sV.h, sV.v)
                dma_ct[par] = sY[par].inc(
                    sy.dma_start(out=self.y_out[_s(t * 128, 128), :], in_=ysb[par][:, :]), 16)
            sy.wait_ge(sY[0].h, sY[0].v)
            sy.wait_ge(sY[1].h, sY[1].v)

    def debug_tail(self):
        gp, sy = self.gp, self.sy
        z = self.st.enter_context(self.nc.sbuf_tensor("zout", [128, D], F32))
        gp.wait_ge(self.sG.h, self.sG.v)
        gp.wait_ge(self.sW.h, self.sW.v)
        gp.wait_ge(self.sX.h, self.sX.v)
        gp.wait_ge(self.sO.h, self.sO.v)
        gp.wait_ge(self.sV.h, self.sV.v)
        gp.wait_ge(self.sS.h, self.sS.v)
        gp.wait_ge(self.sT.h, self.sT.v)
        gp.memset(z[:], 0.0)
        self.sG.inc(gp.drain())
        sy.wait_ge(self.sG.h, self.sG.v)
        sy.wait_ge(self.sV.h, self.sV.v)
        sy.wait_ge(self.sS.h, self.sS.v)
        sy.wait_ge(self.sT.h, self.sT.v)
        for t in range(16):
            self.sO.inc(sy.dma_start(out=self.y_out[_s(t * 128, 128), :], in_=z[:]), 16)
        sy.wait_ge(self.sO.h, self.sO.v)


def stage_inputs(inputs):
    x = np.asarray(inputs["L0_attn_query"], np.float32)
    perm_wp = np.empty(D, np.int64)
    for h in range(16):
        for e in range(HE):
            perm_wp[h * HE + e] = e * 16 + h
    base = {}
    base["wqT"] = np.ascontiguousarray(np.asarray(inputs["L0_attn_Wq"], np.float32).transpose(1, 0, 2).reshape(D, D))
    base["wkT"] = np.ascontiguousarray(np.asarray(inputs["L0_attn_Wk"], np.float32).transpose(1, 0, 2).reshape(D, D))
    base["wvT"] = np.ascontiguousarray(np.asarray(inputs["L0_attn_Wv"], np.float32).transpose(1, 0, 2).reshape(D, D))
    base["wpT"] = np.ascontiguousarray(np.asarray(inputs["L0_attn_Wp"], np.float32)[perm_wp, :])
    base["bp_r"] = np.ascontiguousarray(np.asarray(inputs["L0_attn_bp"], np.float32).reshape(8, 128).T)
    base["w1T"] = np.ascontiguousarray(np.asarray(inputs["L0_ffn_w1"], np.float32).T)
    base["cw"] = np.ascontiguousarray(np.asarray(inputs["L0_ffn_conv_w"], np.float32).reshape(32, 128, 3).transpose(1, 0, 2).reshape(128, 96))
    base["cb"] = np.ascontiguousarray(np.asarray(inputs["L0_ffn_conv_b"], np.float32).reshape(32, 128).T)
    base["w2T"] = np.ascontiguousarray(np.asarray(inputs["L0_ffn_w2"], np.float32).T)
    base["qw1T"] = np.ascontiguousarray(np.asarray(inputs["L1_attn_q_w1"], np.float32).T)
    base["qb1_r"] = np.ascontiguousarray(np.asarray(inputs["L1_attn_q_b1"], np.float32).reshape(8, 128).T)
    base["kw1T"] = np.ascontiguousarray(np.asarray(inputs["L1_attn_k_w1"], np.float32).T)
    base["kb1_r"] = np.ascontiguousarray(np.asarray(inputs["L1_attn_k_b1"], np.float32).reshape(8, 128).T)
    base["pwT"] = np.ascontiguousarray(np.asarray(inputs["L1_attn_p_w"], np.float32).T)
    base["pb_r"] = np.ascontiguousarray(np.asarray(inputs["L1_attn_p_b"], np.float32).reshape(8, 128).T)
    base["gwT"] = np.ascontiguousarray(np.asarray(inputs["L1_ffn_gate_w"], np.float32).T)
    base["gb_row"] = np.ascontiguousarray(np.asarray(inputs["L1_ffn_gate_b"], np.float32).reshape(1, 16))
    base["We1"] = np.ascontiguousarray(np.asarray(inputs["L1_ffn_We1"], np.float32))
    base["be1_c"] = np.ascontiguousarray(
        np.asarray(inputs["L1_ffn_be1"], np.float32).reshape(NE, 4, 128).transpose(2, 0, 1).reshape(128, 64))
    base["We2"] = np.ascontiguousarray(np.asarray(inputs["L1_ffn_We2"], np.float32))
    base["be2"] = np.ascontiguousarray(np.asarray(inputs["L1_ffn_be2"], np.float32))
    base["We3"] = np.ascontiguousarray(np.asarray(inputs["L1_ffn_We3"], np.float32))
    base["be3_c"] = np.ascontiguousarray(
        np.asarray(inputs["L1_ffn_be3"], np.float32).reshape(NE, 4, 128).transpose(2, 0, 1).reshape(128, 64))
    e16 = np.zeros((16, 2048), np.float32)
    for e in range(16):
        e16[e, e * 128:(e + 1) * 128] = 1.0
    base["E16"] = e16
    base["ws1"] = np.ascontiguousarray(np.asarray(inputs["L1_ffn_ws1"], np.float32))
    base["bs1_r"] = np.ascontiguousarray(np.asarray(inputs["L1_ffn_bs1"], np.float32).reshape(8, 128).T)
    base["ws2"] = np.ascontiguousarray(np.asarray(inputs["L1_ffn_ws2"], np.float32))
    base["bs2_r"] = np.ascontiguousarray(np.asarray(inputs["L1_ffn_bs2"], np.float32).reshape(8, 128).T)
    base["ws3"] = np.ascontiguousarray(np.asarray(inputs["L1_ffn_ws3"], np.float32))
    base["bs3_r"] = np.ascontiguousarray(np.asarray(inputs["L1_ffn_bs3"], np.float32).reshape(8, 128).T)
    base["ident"] = np.eye(128, dtype=np.float32)
    base["ones_row"] = np.ones((1, 128), np.float32)
    e2a = np.zeros((128, 66), np.float32)
    e2a[0:64, 64] = 1.0
    e2b = np.zeros((128, 66), np.float32)
    e2b[64:128, 64] = 1.0
    base["e2a"], base["e2b"] = e2a, e2b
    sw = np.zeros((128, 128), np.float32)
    sw[np.arange(64), np.arange(64) + 64] = 1.0
    sw[np.arange(64) + 64, np.arange(64)] = 1.0
    base["swap64"] = sw

    qw2T = np.asarray(inputs["L1_attn_q_w2"], np.float32).T
    kw2T = np.asarray(inputs["L1_attn_k_w2"], np.float32).T
    vwT = np.asarray(inputs["L1_attn_v_w"], np.float32).T
    gwT = np.asarray(inputs["L1_attn_g_w"], np.float32).T
    qb2 = np.asarray(inputs["L1_attn_q_b2"], np.float32)
    kb2 = np.asarray(inputs["L1_attn_k_b2"], np.float32)
    vb = np.asarray(inputs["L1_attn_v_b"], np.float32)
    gb = np.asarray(inputs["L1_attn_g_b"], np.float32)

    maps = []
    for c in range(8):
        b, half = c // 2, c % 2
        m = dict(base)
        xb = x[b]
        m["xTf"] = np.ascontiguousarray(xb.T)
        w = np.zeros((W, D), np.float32)
        s0 = half * OWN
        lo, hi = s0 - 1, s0 + OWN + 1
        cl, ch = max(lo, 0), min(hi, NT)
        w[cl - lo: cl - lo + (ch - cl)] = xb[cl:ch]
        m["xTw"] = np.ascontiguousarray(w.T)
        mk = np.zeros((128, 2), np.float32)
        mk[:, 0] = 0.0 if lo < 0 else 1.0
        mk[:, 1] = 0.0 if hi > NT else 1.0
        m["masks"] = mk
        hs = slice(half * 512, half * 512 + 512)
        m["qw2Ts"] = np.ascontiguousarray(qw2T[:, hs])
        m["kw2Ts"] = np.ascontiguousarray(kw2T[:, hs])
        m["vwTs"] = np.ascontiguousarray(vwT[:, hs])
        m["gwTs"] = np.ascontiguousarray(gwT[:, hs])
        m["qb2_row"] = np.ascontiguousarray(qb2[hs].reshape(1, 512))
        m["kb2_row"] = np.ascontiguousarray(kb2[hs].reshape(1, 512))
        m["vb_s"] = np.ascontiguousarray(vb[hs].reshape(4, 128).T)
        m["gb_s"] = np.ascontiguousarray(gb[hs].reshape(4, 128).T)
        maps.append(m)
    return maps


def kernel(**inputs):
    maps = stage_inputs(inputs)
    nc = Prog(upto=99).build()
    res = run_bass_kernel_spmd(nc, maps, core_ids=list(range(8)))
    out = np.empty((4, NT, D), np.float32)
    for c in range(8):
        b, half = c // 2, c % 2
        out[b, half * OWN:(half + 1) * OWN, :] = res.results[c]["y"]
    return out


# version 04
# speedup vs baseline: 1.3908x; 1.0818x over previous
import numpy as np
import ml_dtypes
from contextlib import ExitStack

import concourse.bass as bass
import concourse.mybir as mybir
from concourse.bass_utils import run_bass_kernel_spmd

F32 = mybir.dt.float32
BF16 = mybir.dt.bfloat16
AF = mybir.ActivationFunctionType
ALU = mybir.AluOpType
AX = mybir.AxisListType.X

MAGIC = 12582912.0
D = 1024
NT = 4096
OWN = 2048
W = 2050
HE = 64
NE = 16
LNEPS = 1e-5
WSL = [(0, 512), (512, 512), (1024, 512), (1536, 512), (2048, 2)]
NSL = [(i * 512, 512) for i in range(8)]


class Sem:
    def __init__(self, nc, st, name):
        self.h = st.enter_context(nc.semaphore(name))
        self.v = 0

    def inc(self, instr, k=1):
        instr.then_inc(self.h, k)
        self.v += k
        return self.v

    def wait(self, eng, v=None):
        eng.wait_ge(self.h, self.v if v is None else v)


def _s(lo, n):
    return slice(lo, lo + n)


class Prog:

    def __init__(self, upto=99, debug=0, p1_pairs=8, p1_proj=True, p1_sect=9, p3_sect=9):
        self.upto = upto
        self.debug = debug
        self.p1_pairs = p1_pairs
        self.p1_proj = p1_proj
        self.p1_sect = p1_sect
        self.p1_norm = 3
        self.p3_sect = p3_sect
        nc = self.nc = bass.Bass("TRN2", target_bir_lowering=False)
        self.te, self.ve, self.sc, self.gp, self.sy = (
            nc.tensor, nc.vector, nc.scalar, nc.gpsimd, nc.sync)
        self.st = ExitStack()
        self.din = {}
        self._declare_dram()
        self._alloc_common()

    def _declare_dram(self):
        nc = self.nc

        def DI(name, shape):
            self.din[name] = nc.dram_tensor(name, list(shape), F32, kind="ExternalInput").ap()

        DI("xTf", (D, NT)); DI("xTw", (D, W)); DI("masks", (128, 2))
        DI("wqT", (D, D)); DI("wkT", (D, D)); DI("wvT", (D, D)); DI("wpT", (D, D)); DI("bp_r", (128, 8))
        DI("w1T", (D, 4096)); DI("cw", (128, 96)); DI("cb", (128, 32)); DI("w2T", (4096, D))
        DI("qw1T", (D, D)); DI("qb1_r", (128, 8)); DI("kw1T", (D, D)); DI("kb1_r", (128, 8))
        DI("qw2Ts", (D, 512)); DI("qb2_row", (1, 512)); DI("kw2Ts", (D, 512)); DI("kb2_row", (1, 512))
        DI("vwTs", (D, 512)); DI("vb_s", (128, 4)); DI("gwTs", (D, 512)); DI("gb_s", (128, 4))
        DI("pwT", (D, D)); DI("pb_r", (128, 8))
        DI("gwT", (D, 16)); DI("gb_row", (1, 16))
        DI("We1", (NE, D, 512)); DI("be1_c", (128, 64)); DI("We2", (NE, 512, D)); DI("be2", (NE, D))
        DI("We3", (NE, D, 512)); DI("be3_c", (128, 64)); DI("E16", (16, 2048))
        DI("ws1", (D, D)); DI("bs1_r", (128, 8)); DI("ws2", (D, D)); DI("bs2_r", (128, 8))
        DI("ws3", (D, D)); DI("bs3_r", (128, 8))
        DI("ident", (128, 128)); DI("ones_row", (1, 128)); DI("e2a", (128, 66)); DI("e2b", (128, 66))
        DI("swap64", (128, 128))

        self.y_out = nc.dram_tensor("y", [OWN, D], F32, kind="ExternalOutput").ap()
        self.dbg = {}
        if self.debug:
            self.dbg["d_r1"] = nc.dram_tensor("d_r1", [D, W], F32, kind="ExternalOutput").ap()
            self.dbg["d_att"] = nc.dram_tensor("d_att", [D, W], F32, kind="ExternalOutput").ap()
            self.dbg["d_r2"] = nc.dram_tensor("d_r2", [D, W], F32, kind="ExternalOutput").ap()
            self.dbg["d_r3"] = nc.dram_tensor("d_r3", [D, OWN], F32, kind="ExternalOutput").ap()
            self.dbg["d_cmb"] = nc.dram_tensor("d_cmb", [16, OWN], F32, kind="ExternalOutput").ap()
            self.dbg["d_z"] = nc.dram_tensor("d_z", [D, OWN], F32, kind="ExternalOutput").ap()

        self.w1q_d = nc.dram_tensor("w1q_d", [32, 8, 128, 128], BF16).ap()
        self.w2q_d = nc.dram_tensor("w2q_d", [8, 32, 128, 128], BF16).ap()
        self.h2_d = nc.dram_tensor("h2_d", [32, 128, OWN], BF16).ap()
        self.x2own_d = nc.dram_tensor("x2own_d", [8, 128, OWN], F32).ap()
        self.cc_in = nc.dram_tensor("cc_in", [D, OWN], BF16).ap()
        self.cc_out = nc.dram_tensor("cc_out", [2 * D, OWN], BF16).ap()

    def _alloc_common(self):
        nc, st = self.nc, self.st
        E = st.enter_context

        def SB(name, p, f, dt, side="left"):
            return E(nc.sbuf_tensor(name, [p, f], dt, side=side))
        self.SB = SB

        self.ident_f = SB("ident_f", 128, 128, F32)
        self.ident_b = SB("ident_b", 128, 128, BF16)
        self.ones_row_f = SB("ones_row_f", 1, 128, F32)
        self.ones_row_b = SB("ones_row_b", 1, 128, BF16)
        self.ones_col_b = SB("ones_col_b", 128, 1, BF16)
        self.ones_col_f = SB("ones_col_f", 128, 1, F32)
        self.e2a = SB("e2a_sb", 128, 66, BF16)
        self.e2b = SB("e2b_sb", 128, 66, BF16)
        self.okk = SB("okk", 66, 128, BF16)
        self.swap64 = SB("swap64_sb", 128, 128, BF16)
        self.masks_sb = SB("masks_sb", 128, 2, F32)
        self.bp_sb = SB("bp_sb", 128, 8, F32)
        self.cw_sb = SB("cw_sb", 128, 96, F32)
        self.cb_sb = SB("cb_sb", 128, 32, F32)
        self.sc0 = SB("sc0", 1, 16, F32)
        self.pr = SB("pr", 1, 6528, F32, side="right")
        self.PRA, self.PRB, self.PRC = 0, 2176, 4352

        self.sW = Sem(nc, st, "sW")
        self.sX = Sem(nc, st, "sX")
        self.sT = Sem(nc, st, "sT")
        self.sV = Sem(nc, st, "sV")
        self.sS = Sem(nc, st, "sS")
        self.sG = Sem(nc, st, "sG")
        self.sO = Sem(nc, st, "sO")

        sy, gp = self.sy, self.gp
        d = self.din
        for src, dst in (("ident", self.ident_f), ("ones_row", self.ones_row_f),
                         ("masks", self.masks_sb), ("bp_r", self.bp_sb),
                         ("cw", self.cw_sb), ("cb", self.cb_sb)):
            self.sX.inc(sy.dma_start(out=dst[:], in_=d[src]), 16)
        for src, dst in (("ident", self.ident_b), ("ones_row", self.ones_row_b),
                         ("e2a", self.e2a), ("e2b", self.e2b), ("swap64", self.swap64)):
            self.sW.inc(gp.dma_start(out=dst[:], in_=d[src]), 16)
        gp.wait_ge(self.sW.h, self.sW.v)
        gp.memset(self.ones_col_b[:], 1.0)
        gp.memset(self.ones_col_f[:], 1.0)
        gp.memset(self.okk[0:1, :], 1.0)
        gp.memset(self.okk[64:65, :], 1.0)
        self._consts = {}
        for v in (LNEPS, MAGIC, -MAGIC, 1.0, -1.0, -1.5):
            t = SB(f"c_{len(self._consts)}", 128, 1, F32)
            gp.memset(t[:], float(v))
            nc.const_aps.register(F32, float(v), t[:])
            self._consts[float(v)] = t
        self.sG.inc(self.gp.drain())

    def build(self):
        lp = self.nc.allow_low_precision("bf16 matmul operands are intentional")
        lp.__enter__()
        if self.upto >= 1:
            self.phase1()
        if self.upto >= 2:
            self.phase2()
        if self.upto >= 3:
            self.phase3()
        if self.upto >= 4:
            self.phase4()
            self.phase5()
        if self.upto < 4:
            self.debug_tail()
        self.st.close()
        return self.nc

    def qbarrier(self, *engs):
        sems = [self.sT, self.sV, self.sS, self.sG, self.sW, self.sX, self.sO]
        sems += getattr(self, "extra_sems", [])
        for e in engs:
            for s in sems:
                e.wait_ge(s.h, s.v)

    def emit_ln(self, x_tiles, off, width, ps_big, scr_b):
        te, ve, sc = self.te, self.ve, self.sc
        sT, sV, sS = self.sT, self.sV, self.sS
        pr, PRA, PRB, PRC = self.pr, self.PRA, self.PRB, self.PRC
        sls = [(o, n) for (o, n) in
               [(i * 512, min(512, width - i * 512)) for i in range((width + 511) // 512)]]
        for which, fn, seg in ((0, AF.Identity, PRA), (1, AF.Square, PRB)):
            for dc in range(8):
                sc.wait_ge(sT.h, sT.v)
                sc.activation(scr_b[:, 0:width], x_tiles[dc][:, _s(off, width)], fn)
                self.sS.inc(sc.drain())
                te.wait_ge(sS.h, sS.v)
                if dc == 0:
                    te.wait_ge(sV.h, sV.v)
                for (o, n) in sls:
                    ins = te.matmul(ps_big[0:1, _s(o, n)], self.ones_col_b[:],
                                    scr_b[:, _s(o, n)], start=(dc == 0), stop=(dc == 7))
                sT.inc(ins)
            ve.wait_ge(sT.h, sT.v)
            ins = ve.tensor_copy(pr[0:1, _s(seg, width)], ps_big[0:1, 0:width])
            sV.inc(ve.drain())
        ve.tensor_scalar_mul(pr[0:1, _s(PRA, width)], pr[0:1, _s(PRA, width)], 1.0 / 1024.0)
        ve.tensor_scalar_mul(pr[0:1, _s(PRB, width)], pr[0:1, _s(PRB, width)], 1.0 / 1024.0)
        sV.inc(ve.drain())
        sc.wait_ge(sV.h, sV.v)
        sc.activation(pr[0:1, _s(PRC, width)], pr[0:1, _s(PRA, width)], AF.Square)
        self.sS.inc(sc.drain())
        ve.wait_ge(sS.h, sS.v)
        ve.tensor_sub(pr[0:1, _s(PRB, width)], pr[0:1, _s(PRB, width)], pr[0:1, _s(PRC, width)])
        sV.inc(ve.drain())
        sc.wait_ge(sV.h, sV.v)
        sc.activation(pr[0:1, _s(PRC, width)], pr[0:1, _s(PRB, width)], AF.Sqrt, bias=LNEPS)
        self.sS.inc(sc.drain())
        ve.wait_ge(sS.h, sS.v)
        ve.reciprocal(pr[0:1, _s(PRB, width)], pr[0:1, _s(PRC, width)])
        sV.inc(ve.drain())
        for seg, op in ((PRA, ALU.subtract), (PRB, ALU.mult)):
            te.wait_ge(sV.h, sV.v)
            for (o, n) in sls:
                ins = te.matmul(ps_big[0:128, _s(o, n)], self.ones_row_f[:],
                                pr[0:1, _s(seg + o, n)], start=True, stop=True)
            sT.inc(ins)
            ve.wait_ge(sT.h, sT.v)
            for dc in range(8):
                ins = ve.tensor_tensor(x_tiles[dc][:, _s(off, width)],
                                       x_tiles[dc][:, _s(off, width)],
                                       ps_big[0:128, 0:width], op)
            sV.inc(ve.drain())
        te.wait_ge(sV.h, sV.v)
        self.sT.inc(te.nop())

    def phase1(self):
        nc, te, ve, sc, gp, sy = self.nc, self.te, self.ve, self.sc, self.gp, self.sy
        d, SB = self.din, self.SB
        sW, sX, sT, sV, sS, sG = self.sW, self.sX, self.sT, self.sV, self.sS, self.sG

        p1st = ExitStack()
        att = [p1st.enter_context(nc.sbuf_tensor(f"att{i}", [128, W], BF16))
               for i in range(8)]

        with ExitStack() as ph:
            EE = ph.enter_context

            def S(name, p, f, dt):
                return EE(nc.sbuf_tensor(name, [p, f], dt))

            xTf_b = [S(f"xTf{i}", 128, NT, BF16) for i in range(8)]
            xTw_b = [S(f"xTw{i}", 128, W, BF16) for i in range(8)]
            kpA = S("kpA", 66, NT, BF16)
            kpB = S("kpB", 66, NT, BF16)
            qpA = S("qpA", 66, W, BF16)
            qpB = S("qpB", 66, W, BF16)
            vg = S("vg", 128, 32 * 130, BF16)
            pt = [S("pt0", 128, 1024, BF16), S("pt1", 128, 1024, BF16),
                  S("pt2", 128, 1024, BF16)]
            wq_p = S("wq_p", 128, 1024, BF16)
            wk_p = S("wk_p", 128, 1024, BF16)
            wv_p = S("wv_p", 128, 1024, BF16)
            sq_b = S("sq_b", 128, 512, BF16)
            t64 = S("t64", 66, 16, F32)
            rb = S("rb", 64, 512, BF16)
            osc = S("osc", 64, W, BF16)

            ps_S = [EE(nc.psum_tensor("ps_S0", [128, 1024], F32)),
                    EE(nc.psum_tensor("ps_S1", [128, 1024], F32))]
            ps_av = EE(nc.psum_tensor("ps_av", [66, 1024], F32))
            ps_x = [EE(nc.psum_tensor("ps_x0", [128, 512], F32)),
                    EE(nc.psum_tensor("ps_x1", [128, 512], F32))]

            for dc in range(8):
                sW.inc(gp.dma_start(out=xTf_b[dc][:], in_=d["xTf"][_s(dc * 128, 128), :]), 16)
                sW.inc(gp.dma_start(out=xTw_b[dc][:], in_=d["xTw"][_s(dc * 128, 128), :]), 16)
            ld_x_done = sW.v
            ve.memset(kpA[64:65, 0:NT], 1.0)
            ve.memset(kpB[64:65, 0:NT], 1.0)
            sV.inc(ve.drain())
            gp.memset(vg[:, :], 1.0)
            self.sG.inc(gp.drain())

            def load_pair_w(p):
                for nm, dst in (("wqT", wq_p), ("wkT", wk_p)):
                    src = d[nm][:, _s(p * 128, 128)].rearrange("(dc r) c -> r dc c", r=128)
                    dstv = dst[:].rearrange("r (dc c) -> r dc c", c=128)
                    sW.inc(gp.dma_start(out=dstv, in_=src), 16)
                return sW.v

            def load_pair_wv(p):
                src = d["wvT"][:, _s(p * 128, 128)].rearrange("(dc r) c -> r dc c", r=128)
                dstv = wv_p[:].rearrange("r (dc c) -> r dc c", c=128)
                sW.inc(gp.dma_start(out=dstv, in_=src), 16)
                return sW.v

            vg_ev = [0, 0]
            kp_ev = [0, 0]
            kp_sq = [0, 0]
            dma_osc = 0

            for p in range(self.p1_pairs):
                wv_done = load_pair_wv(p)
                w_done = load_pair_w(p)

                gp.wait_ge(sW.h, sW.v)
                self.sG.inc(gp.drain())
                te.wait_ge(sG.h, sG.v)
                te.wait_ge(sV.h, sV.v)
                for kt in range(32):
                    par = kt % 2
                    if kt >= 2:
                        te.wait_ge(sV.h, vg_ev[par])
                    for dc in range(8):
                        ins = te.matmul(ps_x[par][:, 0:128],
                                        xTf_b[dc][:, _s(kt * 128, 128)],
                                        wv_p[:, _s(dc * 128, 128)],
                                        start=(dc == 0), stop=(dc == 7))
                    tv = sT.inc(ins)
                    ve.wait_ge(sT.h, tv)
                    dst = vg[:, _s(kt * 130, 130)].rearrange("r (h c) -> r h c", c=65)[:, :, 0:64]
                    src = ps_x[par][:, 0:128].rearrange("r (h c) -> r h c", c=64)
                    ins = ve.tensor_copy(dst, src)
                    vg_ev[par] = sV.inc(ve.drain())

                if self.p1_sect < 2:
                    continue
                gp.wait_ge(sW.h, sW.v)
                self.sG.inc(gp.drain())
                ve.wait_ge(sG.h, sG.v)
                ve.memset(t64[64:65, 0:2], 0.0)
                sV.inc(ve.drain())
                te.wait_ge(sG.h, sG.v)
                te.wait_ge(sV.h, sV.v)
                for si, (o, n) in enumerate(NSL):
                    par = si % 2
                    if si >= 2:
                        te.wait_ge(sS.h, kp_sq[par])
                        te.wait_ge(sV.h, kp_ev[par])
                    for dc in range(8):
                        ins = te.matmul(ps_x[par][:, 0:n],
                                        wk_p[:, _s(dc * 128, 128)],
                                        xTf_b[dc][:, _s(o, n)],
                                        start=(dc == 0), stop=(dc == 7))
                    tv = sT.inc(ins)
                    ve.wait_ge(sT.h, tv)
                    ve.tensor_copy(kpA[0:64, _s(o, n)], ps_x[par][0:64, 0:n])
                    ve.tensor_copy(kpB[0:64, _s(o, n)], ps_x[par][64:128, 0:n])
                    kp_ev[par] = sV.inc(ve.drain())
                    if self.p1_norm >= 1:
                        sc.wait_ge(sT.h, tv)
                        sc.wait_ge(sV.h, kp_ev[par])
                        sc.activation(sq_b[:, 0:n], ps_x[par][:, 0:n], AF.Square)
                        kp_sq[par] = self.sS.inc(sc.drain())
                    if self.p1_norm >= 2:
                        te.wait_ge(sS.h, kp_sq[par])
                        if si >= 1:
                            te.wait_ge(sV.h, sV.v)
                        ins = te.matmul(ps_av[:, 0:n], self.e2a[:], sq_b[:, 0:n], start=True, stop=True)
                        ins = te.matmul(ps_av[:, _s(512, n)], self.e2b[:], sq_b[:, 0:n], start=True, stop=True)
                        tv = sT.inc(ins)
                    if self.p1_norm >= 3:
                        ve.wait_ge(sT.h, tv)
                        ve.reduce_max(t64[64:65, 2:3], ps_av[64:65, 0:n], axis=AX)
                        ve.reduce_max(t64[64:65, 3:4], ps_av[64:65, _s(512, n)], axis=AX)
                        ve.drain()
                        ve.tensor_max(t64[64:65, 0:1], t64[64:65, 0:1], t64[64:65, 2:3])
                        ve.tensor_max(t64[64:65, 1:2], t64[64:65, 1:2], t64[64:65, 3:4])
                        sV.inc(ve.drain())
                te.wait_ge(sV.h, sV.v)

                if self.p1_sect < 3:
                    continue
                for si, (o, n) in enumerate(WSL):
                    par = si % 2
                    for dc in range(8):
                        ins = te.matmul(ps_x[par][:, 0:n],
                                        wq_p[:, _s(dc * 128, 128)],
                                        xTw_b[dc][:, _s(o, n)],
                                        start=(dc == 0), stop=(dc == 7))
                    tv = sT.inc(ins)
                    ve.wait_ge(sT.h, tv)
                    ve.tensor_copy(qpA[0:64, _s(o, n)], ps_x[par][0:64, 0:n])
                    ve.tensor_copy(qpB[0:64, _s(o, n)], ps_x[par][64:128, 0:n])
                    sV.inc(ve.drain())
                    sc.wait_ge(sT.h, tv)
                    sc.wait_ge(sV.h, sV.v)
                    sc.activation(sq_b[:, 0:n], ps_x[par][:, 0:n], AF.Square)
                    sv = self.sS.inc(sc.drain())
                    te.wait_ge(sS.h, sv)
                    ins = te.matmul(ps_av[:, 0:n], self.e2a[:], sq_b[:, 0:n], start=True, stop=True)
                    ins = te.matmul(ps_av[:, _s(512, n)], self.e2b[:], sq_b[:, 0:n], start=True, stop=True)
                    tv = sT.inc(ins)
                    ve.wait_ge(sT.h, tv)
                    ve.tensor_copy(qpA[64:65, _s(o, n)], ps_av[64:65, 0:n])
                    ve.tensor_copy(qpB[64:65, _s(o, n)], ps_av[64:65, _s(512, n)])
                    sV.inc(ve.drain())
                    te.wait_ge(sV.h, sV.v)

                if self.p1_sect < 4:
                    continue
                sc.wait_ge(sV.h, sV.v)
                sc.activation(qpA[64:65, 0:W], qpA[64:65, 0:W], AF.Sqrt, scale=t64[64:65, 0:1])
                sc.drain()
                sc.activation(qpB[64:65, 0:W], qpB[64:65, 0:W], AF.Sqrt, scale=t64[64:65, 1:2])
                sv = self.sS.inc(sc.drain())
                ve.wait_ge(sS.h, sv)
                ve.tensor_scalar_mul(qpA[64:65, 0:W], qpA[64:65, 0:W], -1.0)
                ve.tensor_scalar_mul(qpB[64:65, 0:W], qpB[64:65, 0:W], -1.0)
                sV.inc(ve.drain())

                if self.p1_sect < 5:
                    continue
                for hh in range(2):
                    h = 2 * p + hh
                    vbase = hh

                    kpX = kpA if hh == 0 else kpB
                    qpX = qpA if hh == 0 else qpB

                    def s_mm(par, cl, cn, ql, qn):
                        return te.matmul(ps_S[par][:, _s(cl, cn)],
                                         kpX[0:65, _s(kt_ * 128, 128)],
                                         qpX[0:65, _s(ql, qn)], start=True, stop=True)

                    te.wait_ge(sV.h, sV.v)
                    if hh == 1:
                        te.wait_ge(sG.h, dma_osc)
                    for qh in range(2):
                        qo = qh * 1024
                        exp_ct3 = [0, 0, 0]
                        av_last = 0

                        def emit_av(kt_a):
                            par_a = kt_a % 3
                            te.wait_ge(sS.h, exp_ct3[par_a])
                            for ss_ in range(2):
                                ins_ = te.matmul(ps_av[0:65, _s(ss_ * 512, 512)],
                                                 vg[:, _s(kt_a * 130 + vbase * 65, 65)],
                                                 pt[par_a][:, _s(ss_ * 512, 512)],
                                                 start=(kt_a == 0), stop=(kt_a == 31))
                            return sT.inc(ins_)

                        for kt_ in range(32):
                            par = kt_ % 2
                            if kt_ >= 2:
                                te.wait_ge(sS.h, exp_ct3[(kt_ - 2) % 3])
                            for ss in range(2):
                                ins = s_mm(par, ss * 512, 512, qo + ss * 512, 512)
                            tv = sT.inc(ins)
                            sc.wait_ge(sT.h, tv)
                            sc.activation(pt[kt_ % 3][:, 0:1024], ps_S[par][:, 0:1024],
                                          AF.Exp, scale=0.125)
                            exp_ct3[kt_ % 3] = self.sS.inc(sc.drain())
                            if kt_ >= 2:
                                av_last = emit_av(kt_ - 2)
                        av_last = emit_av(30)
                        av_last = emit_av(31)
                        ve.wait_ge(sT.h, av_last)
                        for ss in range(2):
                            if ss == 1:
                                ve.wait_ge(sT.h, tv)
                            ve.reciprocal(sq_b[64:65, 0:512],
                                          ps_av[64:65, _s(ss * 512, 512)])
                            sv = sV.inc(ve.drain())
                            te.wait_ge(sV.h, sv)
                            ins = te.matmul(ps_S[0][0:64, _s(ss * 512, 512)],
                                            self.okk[64:65, 0:64],
                                            sq_b[64:65, 0:512],
                                            start=True, stop=True)
                            tv = sT.inc(ins)
                            ve.wait_ge(sT.h, tv)
                            ve.tensor_copy(rb[:, 0:512], ps_S[0][0:64, _s(ss * 512, 512)])
                            ve.drain()
                            dst = (att[h // 2][0:64, _s(qo + ss * 512, 512)] if hh == 0
                                   else osc[:, _s(qo + ss * 512, 512)])
                            ve.tensor_mul(dst, ps_av[0:64, _s(ss * 512, 512)],
                                          rb[:, 0:512])
                            sV.inc(ve.drain())
                        te.wait_ge(sV.h, sV.v)
                    if self.p1_sect < 6:
                        continue
                    for kt_ in range(32):
                        ins = te.matmul(ps_S[0][:, _s(kt_ * 2, 2)],
                                        kpX[0:65, _s(kt_ * 128, 128)],
                                        qpX[0:65, 2048:2050],
                                        start=(kt_ == 0), stop=(kt_ == 31))
                    tv = sT.inc(ins)
                    sc.wait_ge(sT.h, tv)
                    sc.activation(pt[0][:, 0:64], ps_S[0][:, 0:64], AF.Exp, scale=0.125)
                    sv = self.sS.inc(sc.drain())
                    te.wait_ge(sS.h, sv)
                    for kt_ in range(32):
                        ins = te.matmul(ps_av[0:65, 0:2],
                                        vg[:, _s(kt_ * 130 + vbase * 65, 65)],
                                        pt[0][:, _s(kt_ * 2, 2)],
                                        start=(kt_ == 0), stop=(kt_ == 31))
                    tv = sT.inc(ins)
                    ve.wait_ge(sT.h, tv)
                    ve.reciprocal(sq_b[64:65, _s(0, 2)], ps_av[64:65, 0:2])
                    sv = sV.inc(ve.drain())
                    te.wait_ge(sV.h, sv)
                    ins = te.matmul(ps_S[0][0:64, 0:2], self.okk[64:65, 0:64],
                                    sq_b[64:65, _s(0, 2)], start=True, stop=True)
                    tv = sT.inc(ins)
                    ve.wait_ge(sT.h, tv)
                    ve.tensor_copy(rb[:, 0:2], ps_S[0][0:64, 0:2])
                    ve.drain()
                    if hh == 0:
                        ins = ve.tensor_mul(att[h // 2][0:64, 2048:2050],
                                            ps_av[0:64, 0:2], rb[:, 0:2])
                    else:
                        ins = ve.tensor_mul(osc[:, 2048:2050], ps_av[0:64, 0:2], rb[:, 0:2])
                    sV.inc(ve.drain())
                    te.wait_ge(sV.h, sV.v)
                    if hh == 1:
                        gp.wait_ge(sV.h, sV.v)
                        sW.inc(gp.dma_start(out=att[h // 2][64:128, 0:W], in_=osc[:, 0:W]), 16)
                        gp.wait_ge(sW.h, sW.v)
                        dma_osc = self.sG.inc(gp.drain())

        if not self.p1_proj:
            if self.debug:
                gp.wait_ge(self.sV.h, self.sV.v)
                gp.wait_ge(self.sT.h, self.sT.v)
                for dc in range(8):
                    self.sW.inc(gp.dma_start(out=self.dbg["d_att"][_s(dc * 128, 128), :],
                                             in_=att[dc][:, 0:W]), 16)
            p1st.close()
            return
        self.xw_stack = ExitStack()
        xw = [self.xw_stack.enter_context(
            self.nc.sbuf_tensor(f"xw{i}", [128, W], F32, side="right")) for i in range(8)]
        self.xw = xw
        with ExitStack() as ph:
            EE = ph.enter_context
            wp_full = EE(nc.sbuf_tensor("wp_full", [128, 8192], BF16))
            scr_b = EE(nc.sbuf_tensor("p1scr", [128, W], BF16))
            ps_big = EE(nc.psum_tensor("ps_big", [128, 2560], F32))

            for eng in (sy, gp):
                eng.wait_ge(sT.h, sT.v)
                eng.wait_ge(sV.h, sV.v)
                eng.wait_ge(sS.h, sS.v)
                eng.wait_ge(sW.h, sW.v)
                eng.wait_ge(self.sG.h, self.sG.v)
            for dc in range(8):
                sX.inc(sy.dma_start(out=xw[dc][:], in_=d["xTw"][_s(dc * 128, 128), :]), 16)
            srcv = d["wpT"][:].rearrange("(dc r) c -> r dc c", r=128)
            dstv = wp_full[:].rearrange("r (dc c) -> r dc c", c=1024)
            wp_done = sW.inc(gp.dma_start(out=dstv, in_=srcv), 16)
            ld_x = sX.v
            proj_ev = 0
            te.wait_ge(sW.h, wp_done)
            for o in range(8):
                if o >= 1:
                    te.wait_ge(sV.h, proj_ev)
                for dc in range(8):
                    for (off, n) in WSL:
                        ins = te.matmul(ps_big[:, _s(off, n)],
                                        wp_full[:, _s(dc * 1024 + o * 128, 128)],
                                        att[dc][:, _s(off, n)], start=(dc == 0), stop=(dc == 7))
                tv = sT.inc(ins)
                ve.wait_ge(sT.h, tv)
                ve.wait_ge(sX.h, ld_x)
                ins = ve.scalar_tensor_tensor(xw[o][:, 0:W], ps_big[0:128, 0:W],
                                              self.bp_sb[:, _s(o, 1)], xw[o][:, 0:W],
                                              ALU.add, ALU.add)
                proj_ev = sV.inc(ve.drain())
            if self.debug:
                gp.wait_ge(self.sV.h, self.sV.v)
                for dc in range(8):
                    self.sW.inc(gp.dma_start(out=self.dbg["d_att"][_s(dc * 128, 128), :],
                                             in_=att[dc][:, 0:W]), 16)
            te.wait_ge(sV.h, sV.v)
            self.sT.inc(te.nop())
            self.emit_ln(xw, 0, W, ps_big, scr_b)
            if self.debug:
                sy.wait_ge(self.sV.h, self.sV.v)
                sy.wait_ge(self.sT.h, self.sT.v)
                for dc in range(8):
                    self.sO.inc(sy.dma_start(out=self.dbg["d_r1"][_s(dc * 128, 128), :],
                                             in_=xw[dc][:, 0:W]), 16)
        p1st.close()

    def phase2(self):
        nc, te, ve, sc, gp, sy = self.nc, self.te, self.ve, self.sc, self.gp, self.sy
        d = self.din
        sW, sX, sT, sV, sS, sG = self.sW, self.sX, self.sT, self.sV, self.sS, self.sG
        pr, PRA, PRB, PRC = self.pr, self.PRA, self.PRB, self.PRC
        xw = self.xw
        st2 = ExitStack()

        def S2(name, p, f, dt):
            return st2.enter_context(nc.sbuf_tensor(name, [p, f], dt))

        ps_big = st2.enter_context(nc.psum_tensor("p2big", [128, 2560], F32))
        ps_tr = st2.enter_context(nc.psum_tensor("p2tr", [128, 2176], BF16))
        sF = self.sF = [Sem(nc, self.st, "sF0"), Sem(nc, self.st, "sF1")]
        sH = [Sem(nc, self.st, "sH0"), Sem(nc, self.st, "sH1")]
        sH2 = [Sem(nc, self.st, "sH20"), Sem(nc, self.st, "sH21")]
        sW2 = [Sem(nc, self.st, "sW20"), Sem(nc, self.st, "sW21")]
        self.extra_sems = sF + sH + sH2 + sW2

        mcol = S2("mcol", 128, 4, F32)
        self.qbarrier(gp, sy, te, ve, sc)
        cwm = S2("cwm", 128, 96, F32)
        acc2 = S2("acc2", 128, OWN, BF16)
        am_cols = S2("am_cols", 128, 17, F32)
        amT = S2("amT", 17, 128, F32)
        gp.memset(am_cols[:], 0.0)
        self.sG.inc(gp.drain())

        def wstats_quant(nm, nrow, ncol, spill, spill_rear, mc):
            with ExitStack() as ws:
                wf = [ws.enter_context(nc.sbuf_tensor(f"wf_{nm}{i}", [128, ncol], F32))
                      for i in range(2)]
                wq = ws.enter_context(nc.sbuf_tensor(f"wqq_{nm}", [128, ncol], BF16))
                accc = ws.enter_context(nc.sbuf_tensor(f"accc_{nm}", [128, 2], F32))
                nch = nrow // 128
                sF = self.sF
                lded = [0, 0]
                for ch in range(2):
                    lded[ch] = sF[ch].inc(
                        sy.dma_start(out=wf[ch][:], in_=d[nm][_s(ch * 128, 128), :]), 16)
                for ch in range(nch):
                    par = ch % 2
                    ve.wait_ge(sF[par].h, lded[par])
                    ve.reduce_sum(accc[:, 1:2], wf[par][:, :], axis=AX,
                                  apply_absolute_value=True)
                    ve.drain()
                    if ch == 0:
                        ve.tensor_copy(accc[:, 0:1], accc[:, 1:2])
                    else:
                        ve.tensor_add(accc[:, 0:1], accc[:, 0:1], accc[:, 1:2])
                    sV.inc(ve.drain())
                    if ch + 2 < nch:
                        sy.wait_ge(sV.h, sV.v)
                        lded[par] = sF[par].inc(
                            sy.dma_start(out=wf[par][:], in_=d[nm][_s((ch + 2) * 128, 128), :]), 16)
                te.wait_ge(sV.h, sV.v)
                ins = te.matmul(ps_big[0:1, 0:1], self.ones_col_f[:], accc[:, 0:1],
                                start=True, stop=True)
                tv = sT.inc(ins)
                ve.wait_ge(sT.h, tv)
                ve.tensor_scalar(pr[0:1, 0:1], ps_big[0:1, 0:1], 1.0 / (nrow * ncol), 1e-5,
                                 ALU.mult, ALU.max)
                ve.drain()
                ve.reciprocal(pr[0:1, 1:2], pr[0:1, 0:1])
                sv = sV.inc(ve.drain())
                te.wait_ge(sV.h, sv)
                te.matmul(ps_big[0:128, 0:1], self.ones_row_f[:], pr[0:1, 0:1],
                          start=True, stop=True)
                ins = te.matmul(ps_big[0:128, 1:2], self.ones_row_f[:], pr[0:1, 1:2],
                                start=True, stop=True)
                tv = sT.inc(ins)
                ve.wait_ge(sT.h, tv)
                ve.tensor_copy(mcol[:, _s(mc, 2)], ps_big[0:128, 0:2])
                sv = sV.inc(ve.drain())
                prev_spill = self.sW.v
                for ch in range(nch):
                    par = ch % 2
                    sy.wait_ge(sV.h, sV.v)
                    lded[par] = sF[par].inc(
                        sy.dma_start(out=wf[par][:], in_=d[nm][_s(ch * 128, 128), :]), 16)
                    ve.wait_ge(sF[par].h, lded[par])
                    ve.tensor_scalar(wf[par][:], wf[par][:], mcol[:, _s(mc + 1, 1)], MAGIC,
                                     ALU.mult, ALU.add)
                    ve.drain()
                    ve.tensor_scalar(wf[par][:], wf[par][:], -MAGIC, None, ALU.add)
                    ve.drain()
                    ve.wait_ge(self.sW.h, prev_spill)
                    ve.tensor_scalar(wq[:, :], wf[par][:], -1.0, 1.0, ALU.max, ALU.min)
                    sv = sV.inc(ve.drain())
                    gp.wait_ge(sV.h, sv)
                    oap, iap = spill_rear(ch, wq)
                    prev_spill = self.sW.inc(gp.dma_start(out=oap, in_=iap), 16)
                gp.wait_ge(self.sW.h, self.sW.v)
                self.sG.inc(gp.drain())

        def w1_rear(ch, wq):
            return (self.w1q_d[:, ch].rearrange("jc r c -> r jc c"),
                    wq[:].rearrange("r (jc c) -> r jc c", c=128))

        def w2_rear(ch, wq):
            return (self.w2q_d[:, ch].rearrange("o r c -> r o c"),
                    wq[:].rearrange("r (o c) -> r o c", c=128))

        wstats_quant("w1T", D, 4096, self.w1q_d, w1_rear, 0)

        ve.wait_ge(sG.h, sG.v)
        ve.tensor_scalar(cwm[:, :], self.cw_sb[:, :], mcol[:, 0:1], None, ALU.mult)
        sV.inc(ve.drain())

        with ExitStack() as fs:
            def SF(name, p, f, dt):
                return fs.enter_context(nc.sbuf_tensor(name, [p, f], dt))
            xq = [SF(f"xq{i}", 128, W, BF16) for i in range(8)]
            scrf = SF("scrf", 128, W, F32)
            scr_b = SF("scr2b", 128, W, BF16)
            acc_b = SF("acc_b", 128, W, BF16)
            sc_bc = SF("sc_bc", 128, W, F32)
            inv_bc = SF("inv_bc", 128, W, F32)
            h_int = SF("h_int", 128, W, F32)
            t0 = SF("t0", 128, OWN, F32)
            h2b = [SF("h2b0", 128, OWN, BF16), SF("h2b1", 128, OWN, BF16)]
            w1b = [SF("w1b0", 128, 1024, BF16), SF("w1b1", 128, 1024, BF16)]

            for dc in range(8):
                sc.wait_ge(sV.h, sV.v)
                sc.wait_ge(sT.h, sT.v)
                sc.activation(scr_b[:, 0:W] if dc else acc_b[:, 0:W],
                              xw[dc][:, 0:W], AF.Abs)
                ssv = self.sS.inc(sc.drain())
                if dc:
                    ve.wait_ge(sS.h, ssv)
                    ve.tensor_max(acc_b[:, 0:W], acc_b[:, 0:W], scr_b[:, 0:W])
                    sV.inc(ve.drain())

            def cross_part_amax(acc_t, width, seg):
                ntile = (width + 127) // 128
                te.wait_ge(sV.h, sV.v)
                for t in range(ntile):
                    n = min(128, width - t * 128)
                    ins = te.transpose(ps_tr[0:n, _s(t * 128, 128)],
                                       acc_t[:, _s(t * 128, n)], self.ident_b[:])
                tv = sT.inc(ins)
                ve.wait_ge(sT.h, tv)
                for t in range(ntile):
                    n = min(128, width - t * 128)
                    ve.reduce_max(am_cols[0:n, _s(t, 1)], ps_tr[0:n, _s(t * 128, 128)],
                                  axis=AX)
                sv = sV.inc(ve.drain())
                te.wait_ge(sV.h, sv)
                ins = te.transpose(ps_big[0:17, 0:128], am_cols[:, 0:17], self.ident_f[:])
                tv = sT.inc(ins)
                ve.wait_ge(sT.h, tv)
                ve.tensor_copy(amT[:, :], ps_big[0:17, 0:128])
                sv = sV.inc(ve.drain())
                gp.wait_ge(sV.h, sv)
                self.sW.inc(gp.dma_start(out=pr[0:1, _s(seg, 2176)], in_=amT[:, :]), 16)
                gp.wait_ge(self.sW.h, self.sW.v)
                return self.sG.inc(gp.drain())

            gv = cross_part_amax(acc_b, W, PRA)
            ve.wait_ge(sG.h, gv)
            ve.tensor_scalar(pr[0:1, _s(PRA, W)], pr[0:1, _s(PRA, W)], 1e-5, None, ALU.max)
            ve.drain()
            ve.reciprocal(pr[0:1, _s(PRB, W)], pr[0:1, _s(PRA, W)])
            ve.drain()
            ve.tensor_scalar(pr[0:1, _s(PRB, W)], pr[0:1, _s(PRB, W)], 127.0, None, ALU.mult)
            ve.drain()
            ve.tensor_scalar(pr[0:1, _s(PRC, W)], pr[0:1, _s(PRA, W)], 1.0 / 127.0, None, ALU.mult)
            sv = sV.inc(ve.drain())

            def bcast_row(seg, width, dst_t):
                te.wait_ge(sV.h, sV.v)
                for (o, n) in [(i * 512, min(512, width - i * 512))
                               for i in range((width + 511) // 512)]:
                    ins = te.matmul(ps_big[0:128, _s(o, n)], self.ones_row_f[:],
                                    pr[0:1, _s(seg + o, n)], start=True, stop=True)
                tv = sT.inc(ins)
                ve.wait_ge(sT.h, tv)
                ve.tensor_copy(dst_t[:, 0:width], ps_big[0:128, 0:width])
                return sV.inc(ve.drain())

            bcast_row(PRB, W, sc_bc)
            bcast_row(PRC, W, inv_bc)
            for dc in range(8):
                ve.tensor_mul(scrf[:, 0:W], xw[dc][:, 0:W], sc_bc[:, 0:W])
                ve.drain()
                ve.tensor_scalar(scrf[:, 0:W], scrf[:, 0:W], MAGIC, -MAGIC, ALU.add, ALU.add)
                ve.drain()
                ve.tensor_mul(xq[dc][:, 0:W], scrf[:, 0:W], inv_bc[:, 0:W])
                sV.inc(ve.drain())

            sy.wait_ge(self.sG.h, self.sG.v)
            sy.wait_ge(self.sW.h, self.sW.v)
            ldv = [0, 0]
            for jc in range(2):
                ldv[jc] = sF[jc].inc(
                    sy.dma_start(out=w1b[jc][:].rearrange("r (dc c) -> r dc c", c=128),
                                 in_=self.w1q_d[jc].rearrange("dc r c -> r dc c")), 16)
            ev_ct = 0
            for jc in range(32):
                par = jc % 2
                te.wait_ge(sF[par].h, ldv[par])
                te.wait_ge(sV.h, sV.v)
                if jc >= 1:
                    te.wait_ge(sS.h, ev_ct)
                for dc in range(8):
                    for (off, n) in WSL:
                        ins = te.matmul(ps_big[:, _s(off, n)], w1b[par][:, _s(dc * 128, 128)],
                                        xq[dc][:, _s(off, n)], start=(dc == 0), stop=(dc == 7))
                tv = sT.inc(ins)
                if jc + 2 < 32:
                    sy.wait_ge(sT.h, tv)
                    ldv[par] = sF[par].inc(
                        sy.dma_start(out=w1b[par][:].rearrange("r (dc c) -> r dc c", c=128),
                                     in_=self.w1q_d[jc + 2].rearrange("dc r c -> r dc c")), 16)
                ve.wait_ge(sT.h, tv)
                ve.tensor_scalar(ps_big[:, 0:1], ps_big[:, 0:1], self.masks_sb[:, 0:1],
                                 None, ALU.mult)
                ve.tensor_scalar(ps_big[:, 2049:2050], ps_big[:, 2049:2050],
                                 self.masks_sb[:, 1:2], None, ALU.mult)
                mv = sV.inc(ve.drain())
                sc.wait_ge(sV.h, mv)
                sc.activation(h_int[:, 0:W], ps_big[0:128, 0:W], AF.Identity)
                ev_ct = self.sS.inc(sc.drain())
                sc.activation(t0[:, 0:OWN], h_int[:, 0:OWN], AF.Identity,
                              scale=cwm[:, _s(jc * 3, 1)], bias=self.cb_sb[:, _s(jc, 1)])
                cv = self.sS.inc(sc.drain())
                ve.wait_ge(sS.h, cv)
                ve.scalar_tensor_tensor(t0[:, 0:OWN], h_int[:, 1:1 + OWN],
                                        cwm[:, _s(jc * 3 + 1, 1)], t0[:, 0:OWN],
                                        ALU.mult, ALU.add)
                ve.drain()
                ve.scalar_tensor_tensor(t0[:, 0:OWN], h_int[:, 2:2 + OWN],
                                        cwm[:, _s(jc * 3 + 2, 1)], t0[:, 0:OWN],
                                        ALU.mult, ALU.add)
                cv2 = sV.inc(ve.drain())
                sc.wait_ge(sV.h, cv2)
                if jc >= 2:
                    sc.wait_ge(sH[par].h, 16 * (jc // 2))
                sc.activation(h2b[par][:, 0:OWN], t0[:, 0:OWN], AF.Silu)
                sc.drain()
                sc.activation(scr_b[:, 0:OWN], t0[:, 0:OWN], AF.Abs)
                av = self.sS.inc(sc.drain())
                ve.wait_ge(sS.h, av)
                if jc == 0:
                    ve.tensor_copy(acc2[:, :], scr_b[:, 0:OWN])
                else:
                    ve.tensor_max(acc2[:, :], acc2[:, :], scr_b[:, 0:OWN])
                sV.inc(ve.drain())
                sy.wait_ge(sS.h, av)
                sH[par].inc(sy.dma_start(out=self.h2_d[jc], in_=h2b[par][:, 0:OWN]), 16)

            gv = cross_part_amax(acc2, OWN, PRA)
            ve.wait_ge(sG.h, gv)
            ve.tensor_scalar(pr[0:1, _s(PRA, OWN)], pr[0:1, _s(PRA, OWN)], 1e-5, None, ALU.max)
            ve.drain()
            ve.reciprocal(pr[0:1, _s(PRB, OWN)], pr[0:1, _s(PRA, OWN)])
            ve.drain()
            ve.tensor_scalar(pr[0:1, _s(PRB, OWN)], pr[0:1, _s(PRB, OWN)], 127.0, None, ALU.mult)
            ve.drain()
            ve.tensor_scalar(pr[0:1, _s(PRC, OWN)], pr[0:1, _s(PRA, OWN)], 1.0 / 127.0,
                             None, ALU.mult)
            sV.inc(ve.drain())

        self.qbarrier(gp, sy, te, ve, sc)
        wstats_quant("w2T", 4096, D, self.w2q_d, w2_rear, 2)

        self.qbarrier(gp, sy, te, ve, sc)
        with ExitStack() as fs:
            def SF(name, p, f, dt):
                return fs.enter_context(nc.sbuf_tensor(name, [p, f], dt))
            h2q = [SF(f"h2q{i}", 128, 1024, BF16) for i in range(32)]
            h2l = [SF("h2l0", 128, 1024, BF16), SF("h2l1", 128, 1024, BF16)]
            w2b = [SF("w2b0", 128, 32 * 128, BF16), SF("w2b1", 128, 32 * 128, BF16)]
            s2bc = SF("s2bc", 128, 1024, F32)
            i2bc = SF("i2bc", 128, 1024, F32)
            scrf2 = SF("scrf2", 128, 1024, F32)

            for tg in range(2):
                to = tg * 1024
                bcast_done = 0
                te.wait_ge(sV.h, sV.v)
                for (o, n) in ((0, 512), (512, 512)):
                    ins = te.matmul(ps_big[0:128, _s(o, n)], self.ones_row_f[:],
                                    pr[0:1, _s(PRB + to + o, n)], start=True, stop=True)
                    ins2 = te.matmul(ps_big[0:128, _s(1024 + o, n)], self.ones_row_f[:],
                                     pr[0:1, _s(PRC + to + o, n)], start=True, stop=True)
                tv = sT.inc(ins2)
                ve.wait_ge(sT.h, tv)
                ve.tensor_copy(s2bc[:, :], ps_big[0:128, 0:1024])
                ve.tensor_copy(i2bc[:, :], ps_big[0:128, 1024:2048])
                bcast_done = sV.inc(ve.drain())
                ld2 = [0, 0]
                if tg == 0:
                    sy.wait_ge(sH[0].h, sH[0].v)
                    sy.wait_ge(sH[1].h, sH[1].v)
                for jc in range(2):
                    ld2[jc] = sH2[jc].inc(
                        sy.dma_start(out=h2l[jc][:], in_=self.h2_d[jc][:, _s(to, 1024)]), 16)
                for jc in range(32):
                    par = jc % 2
                    ve.wait_ge(sH2[par].h, ld2[par])
                    ve.wait_ge(sT.h, sT.v)
                    ve.tensor_mul(scrf2[:, :], h2l[par][:, :], s2bc[:, :])
                    ve.drain()
                    ve.tensor_scalar(scrf2[:, :], scrf2[:, :], MAGIC, -MAGIC, ALU.add, ALU.add)
                    ve.drain()
                    ve.tensor_mul(h2q[jc][:, :], scrf2[:, :], i2bc[:, :])
                    qv = sV.inc(ve.drain())
                    if jc + 2 < 32:
                        sy.wait_ge(sV.h, qv)
                        ld2[par] = sH2[par].inc(
                            sy.dma_start(out=h2l[par][:], in_=self.h2_d[jc + 2][:, _s(to, 1024)]), 16)
                ldw = [0, 0]
                if tg == 0:
                    sy.wait_ge(sG.h, sG.v)
                for o in range(2):
                    ldw[o] = sW2[o].inc(
                        sy.dma_start(out=w2b[o][:].rearrange("r (jc c) -> r jc c", c=128),
                                     in_=self.w2q_d[o].rearrange("jc r c -> r jc c")), 16)
                ev2 = 0
                for o in range(8):
                    par = o % 2
                    te.wait_ge(sW2[par].h, ldw[par])
                    te.wait_ge(sV.h, sV.v if o == 0 else ev2)
                    pp = ps_big[:, _s(par * 1024, 1024)]
                    for jc in range(32):
                        for (off, n) in ((0, 512), (512, 512)):
                            ins = te.matmul(pp[:, _s(off, n)], w2b[par][:, _s(jc * 128, 128)],
                                            h2q[jc][:, _s(off, n)],
                                            start=(jc == 0), stop=(jc == 31))
                    tv = sT.inc(ins)
                    if o + 2 < 8:
                        sy.wait_ge(sT.h, tv)
                        ldw[par] = sW2[par].inc(
                            sy.dma_start(out=w2b[par][:].rearrange("r (jc c) -> r jc c", c=128),
                                         in_=self.w2q_d[o + 2].rearrange("jc r c -> r jc c")), 16)
                    ve.wait_ge(sT.h, tv)
                    ins = ve.scalar_tensor_tensor(xw[o][:, _s(1 + to, 1024)], pp[0:128, :],
                                                  mcol[:, 2:3], xw[o][:, _s(1 + to, 1024)],
                                                  ALU.mult, ALU.add)
                    ev2 = sV.inc(ve.drain())
            te.wait_ge(sV.h, sV.v)
            self.sT.inc(te.nop())
        self.qbarrier(sc, ve, te, gp, sy)
        with ExitStack() as fs:
            scr_ln = fs.enter_context(nc.sbuf_tensor("scr_ln", [128, OWN], BF16))
            x2b = [fs.enter_context(nc.sbuf_tensor(f"x2b{i}", [128, OWN], BF16))
                   for i in range(8)]
            self.emit_ln(xw, 1, OWN, ps_big, scr_ln)
            if self.debug:
                sy.wait_ge(self.sV.h, self.sV.v)
                sy.wait_ge(self.sT.h, self.sT.v)
                for dc in range(8):
                    self.sO.inc(sy.dma_start(out=self.dbg["d_r2"][_s(dc * 128, 128), :],
                                             in_=xw[dc][:, 0:W]), 16)
            for dc in range(8):
                sc.wait_ge(self.sT.h, self.sT.v)
                sc.activation(x2b[dc][:, :], xw[dc][:, 1:1 + OWN], AF.Identity)
            sv = self.sS.inc(sc.drain())
            sy.wait_ge(self.sS.h, sv)
            gp.wait_ge(self.sS.h, sv)
            for dc in range(8):
                self.sO.inc(sy.dma_start(out=self.x2own_d[dc], in_=xw[dc][:, 1:1 + OWN]), 16)
                self.sW.inc(gp.dma_start(out=self.cc_in[_s(dc * 128, 128), :],
                                         in_=x2b[dc][:, :]), 16)
            gp.wait_ge(self.sW.h, self.sW.v)
            self.sG.inc(gp.collective_compute(
                "AllGather", ALU.bypass,
                replica_groups=[[0, 1], [2, 3], [4, 5], [6, 7]],
                ins=[self.cc_in], outs=[self.cc_out]))
        st2.close()
        self.qbarrier(gp, sy, te, ve, sc)
        self.xw_stack.close()

    def phase3(self):
        nc, te, ve, sc, gp, sy = self.nc, self.te, self.ve, self.sc, self.gp, self.sy
        d = self.din
        sW, sX, sT, sV, sS, sG = self.sW, self.sX, self.sT, self.sV, self.sS, self.sG
        pr = self.pr
        half = 512

        st3 = ExitStack()

        def S3(name, p, f, dt):
            return st3.enter_context(nc.sbuf_tensor(name, [p, f], dt))

        X2T = [S3(f"X2T{i}", 128, OWN, BF16) for i in range(8)]
        qb2r = S3("qb2r", 1, 512, BF16)
        kb2r = S3("kb2r", 1, 512, BF16)
        zc = S3("zc", 64, 8, F32)
        A2 = S3("A2", 128, 512, BF16)
        self._b1sb = {}
        for nm, ncols in (("qb1_r", 8), ("kb1_r", 8), ("vb_s", 4), ("gb_s", 4), ("pb_r", 8)):
            t = S3("sb_" + nm, 128, ncols, F32)
            self.sX.inc(sy.dma_start(out=t[:], in_=d[nm]), 16)
            self._b1sb[nm] = t
        bias_ld = self.sX.v

        stA = ExitStack()

        def SA(name, p, f, dt):
            return stA.enter_context(nc.sbuf_tensor(name, [p, f], dt))

        x2f = [SA(f"x2f{i}", 128, NT, BF16) for i in range(8)]
        t1 = [SA(f"t1_{i}", 128, 512, BF16) for i in range(8)]
        q2n = [SA(f"q2n{i}", 128, 512, BF16) for i in range(4)]
        k520 = SA("k520", 128, 4 * 520, BF16)
        sp1b = [SA("sp1", 128, 512, F32), SA("sp1b", 128, 512, F32)]
        sp2b = [SA("sp2", 128, 512, F32), SA("sp2b", 128, 512, F32)]
        w1b = [SA("p3w0", 128, 1024, BF16), SA("p3w1", 128, 1024, BF16)]
        w2b = [SA("p3v0", 128, 512, BF16), SA("p3v1", 128, 512, BF16)]
        zr = SA("zr", 66, 1024, F32)
        vT = SA("vT", 128, NT, BF16)
        Op = SA("Op", 128, NT, BF16)
        OpSW = SA("OpSW", 128, NT, BF16)
        Osc = SA("Osc", 64, NT, BF16)

        sQ1 = [Sem(nc, self.st, "sQ10"), Sem(nc, self.st, "sQ11")]
        sQ2 = [Sem(nc, self.st, "sQ20"), Sem(nc, self.st, "sQ21")]
        sXo = [Sem(nc, self.st, "sXo0"), Sem(nc, self.st, "sXo1")]
        self.extra_sems += sQ1 + sQ2 + sXo

        with ExitStack() as pstk:
            pq = [pstk.enter_context(nc.psum_tensor("pq0", [128, 512], F32)),
                  pstk.enter_context(nc.psum_tensor("pq1", [128, 512], F32))]
            psA = pstk.enter_context(nc.psum_tensor("psA", [66, 512], F32))
            pq2 = [pstk.enter_context(nc.psum_tensor(f"pq2_{i}", [128, 512], F32))
                   for i in range(4)]

            sy.wait_ge(sG.h, sG.v)
            for dc in range(8):
                sX.inc(sy.dma_start(out=x2f[dc][:, 0:OWN],
                                    in_=self.cc_out[_s(dc * 128, 128), :]), 16)
                sX.inc(sy.dma_start(out=x2f[dc][:, OWN:NT],
                                    in_=self.cc_out[_s(D + dc * 128, 128), :]), 16)
            self.sW.inc(gp.dma_start(out=qb2r[:], in_=d["qb2_row"]), 16)
            self.sW.inc(gp.dma_start(out=kb2r[:], in_=d["kb2_row"]), 16)
            b2_ld = self.sW.v
            x2_ld = sX.v

            def load_w1chunk(nm, o, par):
                srcv = d[nm][:, _s(o * 128, 128)].rearrange("(dc r) c -> r dc c", r=128)
                dstv = w1b[par][:].rearrange("r (dc c) -> r dc c", c=128)
                return sQ1[par].inc(gp.dma_start(out=dstv, in_=srcv), 16)

            def load_w2chunk(nm, o, par):
                return sQ2[par].inc(
                    gp.dma_start(out=w2b[par][:], in_=d[nm][_s(o * 128, 128), :]), 16)

            tv1s = [0, 0]
            tv2s = [0, 0]
            gls = [0, 0]
            spv = [0, 0]
            for tg in range(8 if self.p3_sect >= 2 else 0):
                to = tg * 512
                for side, w1nm, w2nm, b1c, b2r in (
                        (0, "qw1T", "qw2Ts", "qb1_r", qb2r),
                        (1, "kw1T", "kw2Ts", "kb1_r", kb2r)):
                    l1 = [0, 0]
                    for o in range(8):
                        par = o % 2
                        gp.wait_ge(sT.h, tv1s[par])
                        l1[par] = load_w1chunk(w1nm, o, par)
                        te.wait_ge(sQ1[par].h, l1[par])
                        te.wait_ge(sX.h, max(x2_ld, bias_ld))
                        te.wait_ge(sW.h, b2_ld)
                        te.wait_ge(sS.h, gls[par])
                        for dc in range(8):
                            ins = te.matmul(pq[par][:, 0:512],
                                            w1b[par][:, _s(dc * 128, 128)],
                                            x2f[dc][:, _s(to, 512)],
                                            start=(dc == 0), stop=(dc == 7))
                        tv = sT.inc(ins)
                        tv1s[par] = tv
                        sc.wait_ge(sT.h, tv)
                        sc.activation(t1[o][:, :], pq[par][:, 0:512], AF.Gelu,
                                      bias=self._b1sb[b1c][:, _s(o, 1)])
                        gls[par] = self.sS.inc(sc.drain())
                    l2 = [0, 0]
                    for o in range(8):
                        par = o % 2
                        gp.wait_ge(sT.h, tv2s[par])
                        l2[par] = load_w2chunk(w2nm, o, par)
                        te.wait_ge(sQ2[par].h, l2[par])
                        te.wait_ge(sS.h, sS.v)
                        if o == 0:
                            te.wait_ge(sV.h, sV.v)
                        for tc in range(4):
                            ins = te.matmul(pq2[tc][:, 0:512],
                                            t1[o][:, _s(tc * 128, 128)],
                                            w2b[par][:, 0:512],
                                            start=(o == 0), stop=False)
                        tv2s[par] = sT.inc(ins)
                    for tc in range(4):
                        ins = te.matmul(pq2[tc][:, 0:512], self.okk[0:1, 0:128],
                                        b2r[:, 0:512], start=False, stop=True)
                    tv = sT.inc(ins)
                    for tc in range(4):
                        tp = tc % 2
                        sp1, sp2 = sp1b[tp], sp2b[tp]
                        sc.wait_ge(sT.h, tv)
                        sc.wait_ge(sV.h, spv[tp])
                        sc.activation(sp1[:, :], pq2[tc][:, 0:512], AF.Abs)
                        sc.drain()
                        sc.activation(sp1[:, :], sp1[:, :], AF.Exp, scale=-1.0)
                        sc.drain()
                        sc.activation(sp2[:, :], sp1[:, :], AF.Ln, bias=1.0)
                        sc.drain()
                        sc.activation(sp1[:, :], pq2[tc][:, 0:512], AF.Relu)
                        sv = self.sS.inc(sc.drain())
                        ve.wait_ge(sS.h, sv)
                        if side == 0:
                            ve.tensor_add(q2n[tc][:, :], sp1[:, :], sp2[:, :])
                            spv[tp] = sV.inc(ve.drain())
                        else:
                            ve.tensor_add(sp2[:, :], sp1[:, :], sp2[:, :])
                            ve.drain()
                            dst = k520[:, _s(tc * 520, 520)].rearrange(
                                "r (h c) -> r h c", c=65)[:, :, 0:64]
                            ve.tensor_copy(dst, sp2[:, :].rearrange("r (h c) -> r h c", c=64))
                            ve.drain()
                            for hh in range(8):
                                ve.reduce_sum(k520[:, _s(tc * 520 + hh * 65 + 64, 1)],
                                              k520[:, _s(tc * 520 + hh * 65, 64)], axis=AX)
                            spv[tp] = sV.inc(ve.drain())
                te.wait_ge(sV.h, sV.v)
                for tc in range(4):
                    for hh in range(8):
                        ins = te.matmul(psA[0:65, _s(hh * 64, 64)],
                                        k520[:, _s(tc * 520 + hh * 65, 65)],
                                        q2n[tc][:, _s(hh * 64, 64)],
                                        start=(tg == 0 and tc == 0 and hh == 0),
                                        stop=(tg == 7 and tc == 3 and hh == 7))
                sT.inc(ins)

            if self.p3_sect < 3:
                stA.close()
                st3.close()
                return
            tv = sT.v
            ve.wait_ge(sT.h, tv)
            ve.tensor_scalar(A2[0:64, 0:512], psA[0:64, 0:512], 0.125, None, ALU.mult)
            ve.drain()
            for hh in range(8):
                ve.tensor_add(A2[0:64, _s(hh * 64, 64)], A2[0:64, _s(hh * 64, 64)],
                              self.ident_b[0:64, 0:64])
            ve.drain()
            ve.tensor_scalar(zr[64:65, 0:512], psA[64:65, 0:512], 0.125, 1024.0,
                             ALU.mult, ALU.add)
            ve.drain()
            ve.reciprocal(zr[64:65, _s(512, 512)], zr[64:65, 0:512])
            sv = sV.inc(ve.drain())
            gp.wait_ge(sV.h, sv)
            self.sW.inc(gp.dma_start(out=A2[64:128, 0:512], in_=A2[0:64, 0:512]), 16)
            gp.wait_ge(sW.h, sW.v)
            a2dma = self.sG.inc(gp.drain())
            te.wait_ge(sV.h, sv)
            for wave in range(2):
                for i in range(4):
                    hh = wave * 4 + i
                    ins = te.transpose(pq2[i][0:64, 0:1],
                                       zr[64:65, _s(512 + hh * 64, 64)],
                                       self.ident_f[64:65, 64:65])
                tv = sT.inc(ins)
                ve.wait_ge(sT.h, tv)
                for i in range(4):
                    ve.tensor_copy(zc[:, _s(wave * 4 + i, 1)], pq2[i][0:64, 0:1])
                sv = sV.inc(ve.drain())
                te.wait_ge(sV.h, sv)

            if self.p3_sect < 4:
                stA.close()
                st3.close()
                return
            for pp_ in range(4 if self.p3_sect >= 4 else 0):
                for which, wnm, bcol in ((0, "vwTs", "vb_s"), (1, "gwTs", "gb_s")):
                    gp.wait_ge(sT.h, sT.v)
                    srcv = d[wnm][:, _s(pp_ * 128, 128)].rearrange("(dc r) c -> r dc c", r=128)
                    dstv = w1b[0][:].rearrange("r (dc c) -> r dc c", c=128)
                    lw0 = sQ1[0].inc(gp.dma_start(out=dstv, in_=srcv), 16)
                    te.wait_ge(sQ1[0].h, lw0)
                    te.wait_ge(sV.h, sV.v)
                    for s in range(8):
                        par = s % 2
                        if s >= 2:
                            te.wait_ge(sS.h, sS.v)
                        for dc in range(8):
                            ins = te.matmul(pq[par][:, 0:512],
                                            w1b[0][:, _s(dc * 128, 128)],
                                            x2f[dc][:, _s(s * 512, 512)],
                                            start=(dc == 0), stop=(dc == 7))
                        tv = sT.inc(ins)
                        sc.wait_ge(sT.h, tv)
                        sc.activation(vT[:, _s(s * 512, 512)], pq[par][:, 0:512], AF.Gelu,
                                      bias=self._b1sb[bcol][:, _s(pp_, 1)])
                        self.sS.inc(sc.drain())
                    if which == 0:
                        te.wait_ge(sS.h, sS.v)
                        te.wait_ge(sG.h, a2dma)
                        ev = 0
                        for hh in range(2):
                            h = 2 * pp_ + hh
                            for s in range(8):
                                par = s % 2
                                if s >= 2 or hh == 1:
                                    te.wait_ge(sV.h, ev)
                                if hh == 0:
                                    ins = te.matmul(pq2[par][0:64, 0:512],
                                                    A2[0:64, _s(h * 64, 64)],
                                                    vT[0:64, _s(s * 512, 512)],
                                                    start=True, stop=True)
                                else:
                                    ins = te.matmul(pq2[par][0:64, 0:512],
                                                    A2[64:128, _s(h * 64, 64)],
                                                    vT[64:128, _s(s * 512, 512)],
                                                    start=True, stop=True)
                                tv = sT.inc(ins)
                                ve.wait_ge(sT.h, tv)
                                dst = Op[0:64, _s(s * 512, 512)] if hh == 0 \
                                    else Osc[:, _s(s * 512, 512)]
                                ve.tensor_scalar(dst, pq2[par][0:64, 0:512],
                                                 zc[:, _s(h, 1)], None, ALU.mult)
                                ev = sV.inc(ve.drain())
                        gp.wait_ge(sV.h, ev)
                        self.sW.inc(gp.dma_start(out=Op[64:128, :], in_=Osc[:, :]), 16)
                        gp.wait_ge(sW.h, sW.v)
                        self.sG.inc(gp.drain())
                ve.wait_ge(sG.h, sG.v)
                ve.wait_ge(sS.h, sS.v)
                ve.tensor_mul(Op[:, :], Op[:, :], vT[:, :])
                gv = sV.inc(ve.drain())
                te.wait_ge(sV.h, gv)
                for s in range(8):
                    par = s % 2
                    if s >= 2:
                        te.wait_ge(sV.h, sV.v)
                    ins = te.matmul(pq[par][:, 0:512], self.swap64[:],
                                    Op[:, _s(s * 512, 512)], start=True, stop=True)
                    tv = sT.inc(ins)
                    ve.wait_ge(sT.h, tv)
                    ve.tensor_copy(OpSW[:, _s(s * 512, 512)], pq[par][:, 0:512])
                    sV.inc(ve.drain())
                for hh in range(2):
                    h = 2 * pp_ + hh
                    for pc in range(8):
                        for jh in range(2):
                            j = 2 * pc + jh
                            srcT = Op if jh == hh else OpSW
                            ve.tensor_copy(X2T[pc][_s(jh * 64, 64), _s(h * 256, 256)],
                                           srcT[_s(jh * 64, 64), j:NT:16])
                    sV.inc(ve.drain())
            te.wait_ge(sV.h, sV.v)
            self.sT.inc(te.nop())
        if self.p3_sect < 5:
            stA.close()
            st3.close()
            return
        self.qbarrier(te, ve, sc, gp, sy)
        stA.close()
        r3 = [self.SB(f"r3_{i}", 128, OWN, F32, side="right") for i in range(8)]
        self.r3 = r3
        with ExitStack() as pstk:
            pp2 = [pstk.enter_context(nc.psum_tensor("pp2_0", [128, 1024], F32)),
                   pstk.enter_context(nc.psum_tensor("pp2_1", [128, 1024], F32))]
            psLN = pstk.enter_context(nc.psum_tensor("psLN", [128, 2048], F32))
            wpf = st3.enter_context(nc.sbuf_tensor("p3wp", [128, 8192], BF16))
            xo = [st3.enter_context(nc.sbuf_tensor(f"xo{i}", [128, OWN], F32))
                  for i in range(2)]
            scr3 = st3.enter_context(nc.sbuf_tensor("scr3", [128, OWN], BF16))
            srcv = d["pwT"][:].rearrange("(dc r) c -> r dc c", r=128)
            dstv = wpf[:].rearrange("r (dc c) -> r dc c", c=1024)
            wp_done = sW.inc(gp.dma_start(out=dstv, in_=srcv), 16)
            xol = [0, 0]
            for o in range(2):
                xol[o] = sXo[o].inc(sy.dma_start(out=xo[o][:], in_=self.x2own_d[o]), 16)
            te.wait_ge(sW.h, wp_done)
            ev3 = 0
            for o in range(8):
                par = o % 2
                if o >= 1:
                    te.wait_ge(sV.h, ev3)
                for th in range(2):
                    for dc in range(8):
                        for ss in range(2):
                            ins = te.matmul(pp2[th][:, _s(ss * 512, 512)],
                                            wpf[:, _s(dc * 1024 + o * 128, 128)],
                                            X2T[dc][:, _s(th * 1024 + ss * 512, 512)],
                                            start=(dc == 0), stop=(dc == 7))
                tv = sT.inc(ins)
                ve.wait_ge(sT.h, tv)
                ve.wait_ge(sXo[par].h, xol[par])
                for th in range(2):
                    ins = ve.scalar_tensor_tensor(r3[o][:, _s(th * 1024, 1024)],
                                                  pp2[th][0:128, :],
                                                  self._b1sb["pb_r"][:, _s(o, 1)],
                                                  xo[par][:, _s(th * 1024, 1024)],
                                                  ALU.add, ALU.add)
                ev3 = sV.inc(ve.drain())
                if o + 2 < 8:
                    sy.wait_ge(sV.h, ev3)
                    xol[par] = sXo[par].inc(
                        sy.dma_start(out=xo[par][:], in_=self.x2own_d[o + 2]), 16)
            te.wait_ge(sV.h, sV.v)
            self.sT.inc(te.nop())
            if self.p3_sect >= 6:
                self.emit_ln(r3, 0, OWN, psLN, scr3)
            if self.debug and self.p3_sect >= 7:
                sy.wait_ge(self.sV.h, self.sV.v)
                sy.wait_ge(self.sT.h, self.sT.v)
                for dc in range(8):
                    self.sO.inc(sy.dma_start(out=self.dbg["d_r3"][_s(dc * 128, 128), :],
                                             in_=r3[dc][:, 0:OWN]), 16)
        st3.close()

    def phase4(self):
        nc, te, ve, sc, gp, sy = self.nc, self.te, self.ve, self.sc, self.gp, self.sy
        d = self.din
        sW, sX, sT, sV, sS, sG = self.sW, self.sX, self.sT, self.sV, self.sS, self.sG
        r3 = self.r3
        BIG = 1.0e30

        self.qbarrier(gp, sy, te, ve, sc)
        st4 = ExitStack()

        def S4(name, p, f, dt):
            return st4.enter_context(nc.sbuf_tensor(name, [p, f], dt))

        x3 = [S4(f"x3_{i}", 128, OWN, BF16) for i in range(8)]
        cmT = S4("cmT", 16, OWN, BF16)
        e16b = S4("e16b", 16, OWN, BF16)
        be2b = S4("be2b", 16, D, BF16)
        be1c = S4("be1c", 128, 64, F32)
        be3c = S4("be3c", 128, 64, F32)
        bsc = S4("bsc", 128, 24, F32)
        sE = [Sem(nc, self.st, "sE0"), Sem(nc, self.st, "sE1")]
        sE2 = [Sem(nc, self.st, "sE20"), Sem(nc, self.st, "sE21")]
        self.extra_sems += sE + sE2

        for dc in range(8):
            sc.wait_ge(sT.h, sT.v)
            sc.activation(x3[dc][:, :], r3[dc][:, :], AF.Identity)
        x3v = self.sS.inc(sc.drain())
        self.sW.inc(gp.dma_start(out=e16b[:], in_=d["E16"]), 16)
        self.sW.inc(gp.dma_start(out=be2b[:], in_=d["be2"]), 16)
        w_small = self.sW.v
        self.sX.inc(sy.dma_start(out=be1c[:], in_=d["be1_c"]), 16)
        self.sX.inc(sy.dma_start(out=be3c[:], in_=d["be3_c"]), 16)
        self.sX.inc(sy.dma_start(out=bsc[:, 0:8], in_=d["bs1_r"]), 16)
        self.sX.inc(sy.dma_start(out=bsc[:, 8:16], in_=d["bs2_r"]), 16)
        self.sX.inc(sy.dma_start(out=bsc[:, 16:24], in_=d["bs3_r"]), 16)
        x_small = self.sX.v

        with ExitStack() as gs_:
            def SG(name, p, f, dt):
                return gs_.enter_context(nc.sbuf_tensor(name, [p, f], dt))
            gwb = SG("gwb", 128, 8 * 16, BF16)
            gbr = SG("gbr", 1, 16, F32)
            gbb = SG("gbb", 128, 16, F32)
            sgT = SG("sgT", 16, OWN, F32)
            sgN = SG("sgN", 128, 256, F32)
            sgO = SG("sgO", 128, 256, F32)
            comb = SG("comb", 128, 256, F32)
            p6 = SG("p6", 128, 24, F32)
            g4 = SG("g4", 128, 24, F32)
            cN = SG("cN", 128, 8, F32)
            sgM = SG("sgM", 128, 16, F32)
            sgM2 = SG("sgM2", 128, 16, F32)
            sgE = SG("sgE", 128, 16, F32)
            psG = gs_.enter_context(nc.psum_tensor("psG", [16, OWN], F32))
            ptr = gs_.enter_context(nc.psum_tensor("ptr", [128, 128], F32))

            srcv = d["gwT"][:].rearrange("(dc r) c -> r dc c", r=128)
            dstv = gwb[:].rearrange("r (dc c) -> r dc c", c=16)
            gl = self.sW.inc(gp.dma_start(out=dstv, in_=srcv), 16)
            self.sX.inc(sy.dma_start(out=gbr[:], in_=d["gb_row"]), 16)
            gb_ld = self.sX.v
            te.wait_ge(sW.h, gl)
            te.wait_ge(sS.h, x3v)
            for (o, n) in [(i * 512, 512) for i in range(4)]:
                for dc in range(8):
                    ins = te.matmul(psG[:, _s(o, n)], gwb[:, _s(dc * 16, 16)],
                                    x3[dc][:, _s(o, n)], start=(dc == 0), stop=(dc == 7))
            tv = sT.inc(ins)
            ve.wait_ge(sT.h, tv)
            ve.tensor_copy(sgT[:, :], psG[0:16, 0:OWN])
            sv = sV.inc(ve.drain())
            te.wait_ge(sX.h, gb_ld)
            te.wait_ge(sV.h, sv)
            ins = te.matmul(ptr[0:128, 0:16], self.ones_row_f[:], gbr[0:1, 0:16],
                            start=True, stop=True)
            tv = sT.inc(ins)
            ve.wait_ge(sT.h, tv)
            ve.tensor_copy(gbb[:, :], ptr[0:128, 0:16])
            sv = sV.inc(ve.drain())

            GS, GS2, GSN, EQG, KPG = 0, 4, 8, 12, 16
            for t in range(16):
                S = _s(t * 16, 16)
                te.wait_ge(sV.h, sV.v)
                ins = te.transpose(ptr[0:128, 0:16], sgT[:, _s(t * 128, 128)],
                                   self.ident_f[0:16, 0:16])
                tv = sT.inc(ins)
                ve.wait_ge(sT.h, tv)
                ve.tensor_copy(sgN[:, S], ptr[0:128, 0:16])
                ve.drain()
                ve.reduce_max(cN[:, 0:1], sgN[:, S], axis=AX, negate=True)
                nm = sV.inc(ve.drain())
                sc.wait_ge(sV.h, nm)
                sc.activation(sgO[:, S], sgN[:, S], AF.Exp, bias=cN[:, 0:1],
                              accum_out=cN[:, 1:2])
                ex = self.sS.inc(sc.drain())
                ve.wait_ge(sS.h, ex)
                ve.reciprocal(cN[:, 2:3], cN[:, 1:2])
                ve.drain()
                ve.tensor_scalar(sgO[:, S], sgO[:, S], cN[:, 2:3], None, ALU.mult)
                ve.drain()
                ve.tensor_add(sgN[:, S], sgO[:, S], gbb[:, :])
                ve.drain()
                for i, (a, b) in enumerate([(0, 1), (0, 2), (0, 3), (1, 2), (1, 3), (2, 3)]):
                    ve.tensor_add(p6[:, _s(4 * i, 4)],
                                  sgN[:, t * 16 + a:(t + 1) * 16:4],
                                  sgN[:, t * 16 + b:(t + 1) * 16:4])
                ve.drain()
                ve.tensor_max(g4[:, _s(GS, 4)], p6[:, 0:4], p6[:, 4:8])
                ve.drain()
                for i in range(2, 6):
                    ve.tensor_max(g4[:, _s(GS, 4)], g4[:, _s(GS, 4)], p6[:, _s(4 * i, 4)])
                    ve.drain()
                ve.reduce_max(cN[:, 3:4], g4[:, _s(GS, 4)], axis=AX, negate=True)
                sv = sV.inc(ve.drain())
                sc.wait_ge(sV.h, sv)
                sc.activation(g4[:, _s(GSN, 4)], g4[:, _s(GS, 4)], AF.Sign, bias=cN[:, 3:4])
                sc.drain()
                sc.activation(g4[:, _s(EQG, 4)], g4[:, _s(GSN, 4)], AF.Abs)
                ab = self.sS.inc(sc.drain())
                ve.wait_ge(sS.h, ab)
                ve.tensor_scalar(g4[:, _s(EQG, 4)], g4[:, _s(EQG, 4)], -1.0, 1.0,
                                 ALU.mult, ALU.add)
                ve.drain()
                ve.reduce_sum(cN[:, 4:5], g4[:, _s(EQG, 4)], axis=AX)
                ve.drain()
                ve.scalar_tensor_tensor(g4[:, _s(GS2, 4)], g4[:, _s(EQG, 4)], -BIG,
                                        g4[:, _s(GS, 4)], ALU.mult, ALU.add)
                ve.drain()
                ve.reduce_max(cN[:, 5:6], g4[:, _s(GS2, 4)], axis=AX)
                sv = sV.inc(ve.drain())
                sc.wait_ge(sV.h, sv)
                sc.activation(cN[:, 6:7], cN[:, 4:5], AF.Sign, bias=-1.5)
                sg2 = self.sS.inc(sc.drain())
                ve.wait_ge(sS.h, sg2)
                ve.tensor_scalar(cN[:, 6:7], cN[:, 6:7], 1.0, 1.0, ALU.add, ALU.min)
                ve.drain()
                ve.tensor_scalar(cN[:, 7:8], cN[:, 3:4], -1.0, None, ALU.mult)
                ve.drain()
                ve.tensor_sub(cN[:, 7:8], cN[:, 7:8], cN[:, 5:6])
                ve.drain()
                ve.tensor_mul(cN[:, 7:8], cN[:, 7:8], cN[:, 6:7])
                ve.drain()
                ve.tensor_add(cN[:, 7:8], cN[:, 7:8], cN[:, 5:6])
                ve.drain()
                ve.tensor_scalar(cN[:, 7:8], cN[:, 7:8], -1.0, None, ALU.mult)
                sv = sV.inc(ve.drain())
                sc.wait_ge(sV.h, sv)
                sc.activation(g4[:, _s(KPG, 4)], g4[:, _s(GS, 4)], AF.Sign, bias=cN[:, 7:8])
                kp = self.sS.inc(sc.drain())
                ve.wait_ge(sS.h, kp)
                ve.tensor_scalar(g4[:, _s(KPG, 4)], g4[:, _s(KPG, 4)], 1.0, 1.0,
                                 ALU.add, ALU.min)
                ve.drain()
                ve.tensor_scalar(g4[:, _s(GS2, 4)], g4[:, _s(KPG, 4)], -1.0, None, ALU.add)
                ve.drain()
                for i in range(4):
                    ve.scalar_tensor_tensor(sgM[:, i:16:4], g4[:, _s(GS2, 4)], BIG,
                                            sgN[:, t * 16 + i:(t + 1) * 16:4],
                                            ALU.mult, ALU.add)
                ve.drain()
                ve.reduce_max(cN[:, 3:4], sgM[:, :], axis=AX, negate=True)
                sv = sV.inc(ve.drain())
                sc.wait_ge(sV.h, sv)
                sc.activation(sgE[:, :], sgM[:, :], AF.Sign, bias=cN[:, 3:4])
                sc.drain()
                sc.activation(sgE[:, :], sgE[:, :], AF.Abs)
                ab = self.sS.inc(sc.drain())
                ve.wait_ge(sS.h, ab)
                ve.tensor_scalar(sgE[:, :], sgE[:, :], -1.0, 1.0, ALU.mult, ALU.add)
                ve.drain()
                ve.reduce_sum(cN[:, 4:5], sgE[:, :], axis=AX)
                ve.drain()
                ve.scalar_tensor_tensor(sgM2[:, :], sgE[:, :], -BIG, sgM[:, :],
                                        ALU.mult, ALU.add)
                ve.drain()
                ve.reduce_max(cN[:, 5:6], sgM2[:, :], axis=AX)
                sv = sV.inc(ve.drain())
                sc.wait_ge(sV.h, sv)
                sc.activation(cN[:, 6:7], cN[:, 4:5], AF.Sign, bias=-1.5)
                sg2 = self.sS.inc(sc.drain())
                ve.wait_ge(sS.h, sg2)
                ve.tensor_scalar(cN[:, 6:7], cN[:, 6:7], 1.0, 1.0, ALU.add, ALU.min)
                ve.drain()
                ve.tensor_scalar(cN[:, 7:8], cN[:, 3:4], -1.0, None, ALU.mult)
                ve.drain()
                ve.tensor_sub(cN[:, 7:8], cN[:, 7:8], cN[:, 5:6])
                ve.drain()
                ve.tensor_mul(cN[:, 7:8], cN[:, 7:8], cN[:, 6:7])
                ve.drain()
                ve.tensor_add(cN[:, 7:8], cN[:, 7:8], cN[:, 5:6])
                ve.drain()
                ve.tensor_scalar(cN[:, 7:8], cN[:, 7:8], -1.0, None, ALU.mult)
                sv = sV.inc(ve.drain())
                sc.wait_ge(sV.h, sv)
                sc.activation(sgE[:, :], sgM[:, :], AF.Sign, bias=cN[:, 7:8])
                kp = self.sS.inc(sc.drain())
                ve.wait_ge(sS.h, kp)
                ve.tensor_scalar(sgE[:, :], sgE[:, :], 1.0, 1.0, ALU.add, ALU.min)
                ve.drain()
                ve.tensor_mul(comb[:, S], sgO[:, S], sgE[:, :])
                sV.inc(ve.drain())
            for t in range(16):
                te.wait_ge(sV.h, sV.v)
                ins = te.transpose(ptr[0:16, 0:128], comb[:, _s(t * 16, 16)],
                                   self.ident_f[:])
                tv = sT.inc(ins)
                ve.wait_ge(sT.h, tv)
                ve.tensor_copy(cmT[:, _s(t * 128, 128)], ptr[0:16, 0:128])
                sV.inc(ve.drain())
            te.wait_ge(sV.h, sV.v)
            self.sT.inc(te.nop())
            if self.debug:
                gp.wait_ge(sV.h, sV.v)
                self.sW.inc(gp.dma_start(out=self.dbg["d_cmb"], in_=cmT[:, :]), 16)
        self.qbarrier(gp, sy, te, ve, sc)
        with ExitStack() as ss_:
            def SS(name, p, f, dt):
                return ss_.enter_context(nc.sbuf_tensor(name, [p, f], dt))
            hs = [SS(f"hs{i}", 128, OWN, BF16) for i in range(8)]
            s1b = SS("s1b", 128, OWN, BF16)
            wsb = [SS("wsb0", 128, 1024, BF16), SS("wsb1", 128, 1024, BF16)]
            psS4 = ss_.enter_context(nc.psum_tensor("psS4", [128, OWN], F32))

            def ld_ws(nm, o, par):
                srcv = d[nm][:, _s(o * 128, 128)].rearrange("(dc r) c -> r dc c", r=128)
                dstv = wsb[par][:].rearrange("r (dc c) -> r dc c", c=128)
                return sE[par].inc(gp.dma_start(out=dstv, in_=srcv), 16)

            for o in range(8):
                par = o % 2
                gp.wait_ge(sT.h, sT.v)
                l1 = ld_ws("ws1", o, par)
                te.wait_ge(sE[par].h, l1)
                te.wait_ge(sV.h, sV.v)
                for (off, n) in [(i * 512, 512) for i in range(4)]:
                    for dc in range(8):
                        ins = te.matmul(psS4[:, _s(off, n)], wsb[par][:, _s(dc * 128, 128)],
                                        x3[dc][:, _s(off, n)], start=(dc == 0), stop=(dc == 7))
                tv = sT.inc(ins)
                sc.wait_ge(sT.h, tv)
                sc.activation(s1b[:, :], psS4[0:128, 0:OWN], AF.Silu, bias=bsc[:, _s(o, 1)])
                sl = self.sS.inc(sc.drain())
                gp.wait_ge(sS.h, sl)
                l3 = ld_ws("ws3", o, par)
                te.wait_ge(sE[par].h, l3)
                for (off, n) in [(i * 512, 512) for i in range(4)]:
                    for dc in range(8):
                        ins = te.matmul(psS4[:, _s(off, n)], wsb[par][:, _s(dc * 128, 128)],
                                        x3[dc][:, _s(off, n)], start=(dc == 0), stop=(dc == 7))
                tv = sT.inc(ins)
                ve.wait_ge(sT.h, tv)
                ve.scalar_tensor_tensor(hs[o][:, :], psS4[0:128, 0:OWN],
                                        bsc[:, _s(16 + o, 1)], s1b[:, :],
                                        ALU.add, ALU.mult)
                sV.inc(ve.drain())
            for o in range(8):
                par = o % 2
                gp.wait_ge(sT.h, sT.v)
                l2 = ld_ws("ws2", o, par)
                te.wait_ge(sE[par].h, l2)
                te.wait_ge(sV.h, sV.v)
                for (off, n) in [(i * 512, 512) for i in range(4)]:
                    for dc in range(8):
                        ins = te.matmul(psS4[:, _s(off, n)], wsb[par][:, _s(dc * 128, 128)],
                                        hs[dc][:, _s(off, n)], start=(dc == 0), stop=(dc == 7))
                tv = sT.inc(ins)
                ve.wait_ge(sT.h, tv)
                ve.scalar_tensor_tensor(r3[o][:, :], psS4[0:128, 0:OWN],
                                        bsc[:, _s(8 + o, 1)], r3[o][:, :],
                                        ALU.add, ALU.add)
                sV.inc(ve.drain())
            te.wait_ge(sV.h, sV.v)
            self.sT.inc(te.nop())
            if self.debug:
                sy.wait_ge(sV.h, sV.v)
                for dc in range(8):
                    self.sO.inc(sy.dma_start(out=self.dbg["d_z"][_s(dc * 128, 128), :],
                                             in_=r3[dc][:, :]), 16)
        self.qbarrier(gp, sy, te, ve, sc)
        with ExitStack() as es_:
            def SE(name, p, f, dt):
                return es_.enter_context(nc.sbuf_tensor(name, [p, f], dt))
            we1e = [SE("we1e0", 128, 4096, BF16), SE("we1e1", 128, 4096, BF16)]
            we3e = [SE("we3e0", 128, 4096, BF16), SE("we3e1", 128, 4096, BF16)]
            we2e = SE("we2e", 128, 4096, BF16)
            heg = SE("heg", 128, 2048, BF16)
            s1e = SE("s1e", 128, 512, BF16)
            cbce = SE("cbce", 128, 512, BF16)
            ph = es_.enter_context(nc.psum_tensor("ph4", [128, 512], F32))
            phB = es_.enter_context(nc.psum_tensor("ph4b", [128, 512], F32))
            pcb = es_.enter_context(nc.psum_tensor("pcb", [128, 512], F32))
            pp4 = [es_.enter_context(nc.psum_tensor(f"pp4_{i}", [128, 512], F32))
                   for i in range(4)]

            def ld_e(nm, e, dstt, sem):
                srcv = d[nm][e].rearrange("(dc r) c -> r dc c", r=128)
                cw_ = srcv.shape[2] if hasattr(srcv, "shape") else None
                ncol = 512 if nm in ("We1", "We3") else 1024
                dstv = dstt[:].rearrange("r (dc c) -> r dc c", c=ncol)
                return sem.inc(gp.dma_start(out=dstv, in_=srcv), 16)

            l1 = [0, 0]
            l3 = [0, 0]
            l2 = 0
            sl4 = 0
            ev4 = 0
            l1[0] = ld_e("We1", 0, we1e[0], sE[0])
            l3[0] = ld_e("We3", 0, we3e[0], sE[0])
            l2 = ld_e("We2", 0, we2e, sE2[0])
            for e in range(NE):
                par = e % 2
                if e + 1 < NE:
                    gp.wait_ge(sT.h, sT.v)
                    l1[1 - par] = ld_e("We1", e + 1, we1e[1 - par], sE[1 - par])
                    l3[1 - par] = ld_e("We3", e + 1, we3e[1 - par], sE[1 - par])
                te.wait_ge(sE[par].h, max(l1[par], l3[par]))
                te.wait_ge(sE2[par if False else 0].h, l2)
                for tg in range(4):
                    to = tg * 512
                    te.wait_ge(sV.h, sV.v)
                    ins = te.matmul(pcb[:, 0:512], e16b[:, _s(e * 128, 128)],
                                    cmT[:, _s(to, 512)], start=True, stop=True)
                    tv = sT.inc(ins)
                    ve.wait_ge(sT.h, tv)
                    ve.tensor_copy(cbce[:, :], pcb[0:128, 0:512])
                    cbv = sV.inc(ve.drain())
                    for ich in range(4):
                        te.wait_ge(sS.h, sl4)
                        for dc in range(8):
                            ins = te.matmul(ph[:, 0:512],
                                            we1e[par][:, _s(dc * 512 + ich * 128, 128)],
                                            x3[dc][:, _s(to, 512)],
                                            start=(dc == 0), stop=(dc == 7))
                        tv = sT.inc(ins)
                        te.wait_ge(sV.h, ev4)
                        for dc in range(8):
                            ins2 = te.matmul(phB[:, 0:512],
                                             we3e[par][:, _s(dc * 512 + ich * 128, 128)],
                                             x3[dc][:, _s(to, 512)],
                                             start=(dc == 0), stop=(dc == 7))
                        tv2 = sT.inc(ins2)
                        sc.wait_ge(sT.h, tv)
                        sc.wait_ge(sV.h, ev4)
                        sc.activation(s1e[:, :], ph[0:128, 0:512], AF.Silu,
                                      bias=be1c[:, _s(e * 4 + ich, 1)])
                        sl4 = self.sS.inc(sc.drain())
                        ve.wait_ge(sT.h, tv2)
                        ve.wait_ge(sS.h, sl4)
                        ve.wait_ge(sV.h, cbv)
                        ve.scalar_tensor_tensor(heg[:, _s(ich * 512, 512)], phB[0:128, 0:512],
                                                be3c[:, _s(e * 4 + ich, 1)], s1e[:, :],
                                                ALU.add, ALU.mult)
                        ve.drain()
                        ve.tensor_mul(heg[:, _s(ich * 512, 512)], heg[:, _s(ich * 512, 512)],
                                      cbce[:, :])
                        ev4 = sV.inc(ve.drain())
                    te.wait_ge(sV.h, sV.v)
                    evs = [0, 0, 0, 0]
                    for o in range(8):
                        op = o % 4
                        if o >= 4:
                            te.wait_ge(sV.h, evs[op])
                        for ich in range(4):
                            ins = te.matmul(pp4[op][:, 0:512],
                                            we2e[:, _s(ich * 1024 + o * 128, 128)],
                                            heg[:, _s(ich * 512, 512)],
                                            start=(ich == 0), stop=(ich == 3))
                        tv = sT.inc(ins)
                        ve.wait_ge(sT.h, tv)
                        ve.tensor_add(r3[o][:, _s(to, 512)], r3[o][:, _s(to, 512)],
                                      pp4[op][0:128, 0:512])
                        evs[op] = sV.inc(ve.drain())
                if e + 1 < NE:
                    gp.wait_ge(sT.h, sT.v)
                    l2 = ld_e("We2", e + 1, we2e, sE2[0])
            te.wait_ge(sV.h, sV.v)
            for o in range(8):
                op = o % 2
                for tg in range(4):
                    ins = te.matmul(pp4[op][:, 0:512], be2b[:, _s(o * 128, 128)],
                                    cmT[:, _s(tg * 512, 512)], start=True, stop=True)
                    tv = sT.inc(ins)
                    ve.wait_ge(sT.h, tv)
                    ve.tensor_add(r3[o][:, _s(tg * 512, 512)], r3[o][:, _s(tg * 512, 512)],
                                  pp4[op][0:128, 0:512])
                    sv = sV.inc(ve.drain())
                    te.wait_ge(sV.h, sv)
            te.wait_ge(sV.h, sV.v)
            self.sT.inc(te.nop())
        self.qbarrier(gp, sy, te, ve, sc)
        with ExitStack() as ls_:
            psL4 = ls_.enter_context(nc.psum_tensor("psL4", [128, OWN], F32))
            scr4 = ls_.enter_context(nc.sbuf_tensor("scr4", [128, OWN], BF16))
            self.emit_ln(r3, 0, OWN, psL4, scr4)
        st4.close()

    def phase5(self):
        nc, te, ve, sc, gp, sy = self.nc, self.te, self.ve, self.sc, self.gp, self.sy
        sT, sV, sO = self.sT, self.sV, self.sO
        r3 = self.r3
        self.qbarrier(gp, sy, te, ve, sc)
        sY = [Sem(nc, self.st, "sY0"), Sem(nc, self.st, "sY1")]
        self.extra_sems += sY
        with ExitStack() as os_:
            pst = [os_.enter_context(nc.psum_tensor(f"pst{i}", [128, 128], F32))
                   for i in range(4)]
            ysb = [os_.enter_context(nc.sbuf_tensor(f"ysb{i}", [128, 1024], F32))
                   for i in range(2)]
            dma_ct = [0, 0]
            for t in range(16):
                par = t % 2
                if t >= 2:
                    ve.wait_ge(sY[par].h, dma_ct[par])
                for w in range(2):
                    te.wait_ge(sV.h, sV.v)
                    for i in range(4):
                        dc = w * 4 + i
                        ins = te.transpose(pst[i][:, 0:128], r3[dc][:, _s(t * 128, 128)],
                                           self.ident_f[:])
                    tv = sT.inc(ins)
                    ve.wait_ge(sT.h, tv)
                    for i in range(4):
                        dc = w * 4 + i
                        ve.tensor_copy(ysb[par][:, _s(dc * 128, 128)], pst[i][:, 0:128])
                    sV.inc(ve.drain())
                sy.wait_ge(sV.h, sV.v)
                dma_ct[par] = sY[par].inc(
                    sy.dma_start(out=self.y_out[_s(t * 128, 128), :], in_=ysb[par][:, :]), 16)
            sy.wait_ge(sY[0].h, sY[0].v)
            sy.wait_ge(sY[1].h, sY[1].v)

    def debug_tail(self):
        gp, sy = self.gp, self.sy
        z = self.st.enter_context(self.nc.sbuf_tensor("zout", [128, D], F32))
        gp.wait_ge(self.sG.h, self.sG.v)
        gp.wait_ge(self.sW.h, self.sW.v)
        gp.wait_ge(self.sX.h, self.sX.v)
        gp.wait_ge(self.sO.h, self.sO.v)
        gp.wait_ge(self.sV.h, self.sV.v)
        gp.wait_ge(self.sS.h, self.sS.v)
        gp.wait_ge(self.sT.h, self.sT.v)
        gp.memset(z[:], 0.0)
        self.sG.inc(gp.drain())
        sy.wait_ge(self.sG.h, self.sG.v)
        sy.wait_ge(self.sV.h, self.sV.v)
        sy.wait_ge(self.sS.h, self.sS.v)
        sy.wait_ge(self.sT.h, self.sT.v)
        for t in range(16):
            self.sO.inc(sy.dma_start(out=self.y_out[_s(t * 128, 128), :], in_=z[:]), 16)
        sy.wait_ge(self.sO.h, self.sO.v)


def stage_inputs(inputs):
    x = np.asarray(inputs["L0_attn_query"], np.float32)
    perm_wp = np.empty(D, np.int64)
    for h in range(16):
        for e in range(HE):
            perm_wp[h * HE + e] = e * 16 + h
    base = {}
    base["wqT"] = np.ascontiguousarray(np.asarray(inputs["L0_attn_Wq"], np.float32).transpose(1, 0, 2).reshape(D, D))
    base["wkT"] = np.ascontiguousarray(np.asarray(inputs["L0_attn_Wk"], np.float32).transpose(1, 0, 2).reshape(D, D))
    base["wvT"] = np.ascontiguousarray(np.asarray(inputs["L0_attn_Wv"], np.float32).transpose(1, 0, 2).reshape(D, D))
    base["wpT"] = np.ascontiguousarray(np.asarray(inputs["L0_attn_Wp"], np.float32)[perm_wp, :])
    base["bp_r"] = np.ascontiguousarray(np.asarray(inputs["L0_attn_bp"], np.float32).reshape(8, 128).T)
    base["w1T"] = np.ascontiguousarray(np.asarray(inputs["L0_ffn_w1"], np.float32).T)
    base["cw"] = np.ascontiguousarray(np.asarray(inputs["L0_ffn_conv_w"], np.float32).reshape(32, 128, 3).transpose(1, 0, 2).reshape(128, 96))
    base["cb"] = np.ascontiguousarray(np.asarray(inputs["L0_ffn_conv_b"], np.float32).reshape(32, 128).T)
    base["w2T"] = np.ascontiguousarray(np.asarray(inputs["L0_ffn_w2"], np.float32).T)
    base["qw1T"] = np.ascontiguousarray(np.asarray(inputs["L1_attn_q_w1"], np.float32).T)
    base["qb1_r"] = np.ascontiguousarray(np.asarray(inputs["L1_attn_q_b1"], np.float32).reshape(8, 128).T)
    base["kw1T"] = np.ascontiguousarray(np.asarray(inputs["L1_attn_k_w1"], np.float32).T)
    base["kb1_r"] = np.ascontiguousarray(np.asarray(inputs["L1_attn_k_b1"], np.float32).reshape(8, 128).T)
    base["pwT"] = np.ascontiguousarray(np.asarray(inputs["L1_attn_p_w"], np.float32).T)
    base["pb_r"] = np.ascontiguousarray(np.asarray(inputs["L1_attn_p_b"], np.float32).reshape(8, 128).T)
    base["gwT"] = np.ascontiguousarray(np.asarray(inputs["L1_ffn_gate_w"], np.float32).T)
    base["gb_row"] = np.ascontiguousarray(np.asarray(inputs["L1_ffn_gate_b"], np.float32).reshape(1, 16))
    base["We1"] = np.ascontiguousarray(np.asarray(inputs["L1_ffn_We1"], np.float32))
    base["be1_c"] = np.ascontiguousarray(
        np.asarray(inputs["L1_ffn_be1"], np.float32).reshape(NE, 4, 128).transpose(2, 0, 1).reshape(128, 64))
    base["We2"] = np.ascontiguousarray(np.asarray(inputs["L1_ffn_We2"], np.float32))
    base["be2"] = np.ascontiguousarray(np.asarray(inputs["L1_ffn_be2"], np.float32))
    base["We3"] = np.ascontiguousarray(np.asarray(inputs["L1_ffn_We3"], np.float32))
    base["be3_c"] = np.ascontiguousarray(
        np.asarray(inputs["L1_ffn_be3"], np.float32).reshape(NE, 4, 128).transpose(2, 0, 1).reshape(128, 64))
    e16 = np.zeros((16, 2048), np.float32)
    for e in range(16):
        e16[e, e * 128:(e + 1) * 128] = 1.0
    base["E16"] = e16
    base["ws1"] = np.ascontiguousarray(np.asarray(inputs["L1_ffn_ws1"], np.float32))
    base["bs1_r"] = np.ascontiguousarray(np.asarray(inputs["L1_ffn_bs1"], np.float32).reshape(8, 128).T)
    base["ws2"] = np.ascontiguousarray(np.asarray(inputs["L1_ffn_ws2"], np.float32))
    base["bs2_r"] = np.ascontiguousarray(np.asarray(inputs["L1_ffn_bs2"], np.float32).reshape(8, 128).T)
    base["ws3"] = np.ascontiguousarray(np.asarray(inputs["L1_ffn_ws3"], np.float32))
    base["bs3_r"] = np.ascontiguousarray(np.asarray(inputs["L1_ffn_bs3"], np.float32).reshape(8, 128).T)
    base["ident"] = np.eye(128, dtype=np.float32)
    base["ones_row"] = np.ones((1, 128), np.float32)
    e2a = np.zeros((128, 66), np.float32)
    e2a[0:64, 64] = 1.0
    e2b = np.zeros((128, 66), np.float32)
    e2b[64:128, 64] = 1.0
    base["e2a"], base["e2b"] = e2a, e2b
    sw = np.zeros((128, 128), np.float32)
    sw[np.arange(64), np.arange(64) + 64] = 1.0
    sw[np.arange(64) + 64, np.arange(64)] = 1.0
    base["swap64"] = sw

    qw2T = np.asarray(inputs["L1_attn_q_w2"], np.float32).T
    kw2T = np.asarray(inputs["L1_attn_k_w2"], np.float32).T
    vwT = np.asarray(inputs["L1_attn_v_w"], np.float32).T
    gwT = np.asarray(inputs["L1_attn_g_w"], np.float32).T
    qb2 = np.asarray(inputs["L1_attn_q_b2"], np.float32)
    kb2 = np.asarray(inputs["L1_attn_k_b2"], np.float32)
    vb = np.asarray(inputs["L1_attn_v_b"], np.float32)
    gb = np.asarray(inputs["L1_attn_g_b"], np.float32)

    maps = []
    for c in range(8):
        b, half = c // 2, c % 2
        m = dict(base)
        xb = x[b]
        m["xTf"] = np.ascontiguousarray(xb.T)
        w = np.zeros((W, D), np.float32)
        s0 = half * OWN
        lo, hi = s0 - 1, s0 + OWN + 1
        cl, ch = max(lo, 0), min(hi, NT)
        w[cl - lo: cl - lo + (ch - cl)] = xb[cl:ch]
        m["xTw"] = np.ascontiguousarray(w.T)
        mk = np.zeros((128, 2), np.float32)
        mk[:, 0] = 0.0 if lo < 0 else 1.0
        mk[:, 1] = 0.0 if hi > NT else 1.0
        m["masks"] = mk
        hs = slice(half * 512, half * 512 + 512)
        m["qw2Ts"] = np.ascontiguousarray(qw2T[:, hs])
        m["kw2Ts"] = np.ascontiguousarray(kw2T[:, hs])
        m["vwTs"] = np.ascontiguousarray(vwT[:, hs])
        m["gwTs"] = np.ascontiguousarray(gwT[:, hs])
        m["qb2_row"] = np.ascontiguousarray(qb2[hs].reshape(1, 512))
        m["kb2_row"] = np.ascontiguousarray(kb2[hs].reshape(1, 512))
        m["vb_s"] = np.ascontiguousarray(vb[hs].reshape(4, 128).T)
        m["gb_s"] = np.ascontiguousarray(gb[hs].reshape(4, 128).T)
        maps.append(m)
    return maps


def kernel(**inputs):
    maps = stage_inputs(inputs)
    nc = Prog(upto=99).build()
    res = run_bass_kernel_spmd(nc, maps, core_ids=list(range(8)))
    out = np.empty((4, NT, D), np.float32)
    for c in range(8):
        b, half = c // 2, c % 2
        out[b, half * OWN:(half + 1) * OWN, :] = res.results[c]["y"]
    return out
